# version 02
# baseline (speedup 1.0000x reference)
import numpy as np
from contextlib import ExitStack
import concourse.bass as bass
import concourse.mybir as mybir
from concourse.bass import IndirectOffsetOnAxis

F32 = mybir.dt.float32
BF16 = mybir.dt.bfloat16
I32 = mybir.dt.int32
AF = mybir.ActivationFunctionType
ALU = mybir.AluOpType
AX = mybir.AxisListType

ENGINES = ("sync", "scalar", "vector", "tensor", "gpsimd")

C = 1024
PC = C // 128
EPS = 1e-5



class _Op:
    __slots__ = ("engine", "fn", "reads", "writes", "dma", "tag", "idx",
                 "domain", "deps", "inc_needed", "wait_val", "ccinc1", "barrier",
                 "nobar")

    def __init__(self, engine, fn, reads, writes, dma, tag, idx):
        self.engine = engine
        self.fn = fn
        self.reads = tuple(reads)
        self.writes = tuple(writes)
        self.dma = dma
        self.tag = tag
        self.idx = idx
        if dma:
            assert len(self.writes) == 1, f"dma op {tag} needs exactly 1 write key"
            self.domain = "k_" + self.writes[0]
        else:
            self.domain = engine
        self.deps = []
        self.inc_needed = False
        self.wait_val = None
        self.ccinc1 = False
        self.barrier = False
        self.nobar = False


class Prog:
    def __init__(self):
        self.ops = []

    def op(self, engine, fn, reads=(), writes=(), dma=False, tag=""):
        o = _Op(engine, fn, reads, writes, dma, tag, len(self.ops))
        self.ops.append(o)
        return o

    def dma(self, engine, out_ap, in_ap, reads=(), writes=(), tag="", nobar=False, **kw):
        def fn(eng):
            return eng.dma_start(out=out_ap, in_=in_ap, **kw)
        o = self.op(engine, fn, reads=reads, writes=writes, dma=True, tag=tag)
        o.nobar = nobar
        return o

    def fence(self, engine, keys, tag="fence"):
        return self.op(engine, lambda e: e.nop(nofuse=True), reads=tuple(keys), tag=tag)

    def barrier(self, tag="barrier"):
        for e in ("sync", "scalar", "vector", "tensor", "gpsimd"):
            o = self.op(e, lambda eng: eng.nop(nofuse=True), tag=f"{tag}:{e}")
            o.barrier = True

    @staticmethod
    def _needs_edge(prod, cons):
        if prod.domain != cons.domain:
            return True
        if prod.dma and cons.dma:
            return False
        if prod.engine == "tensor" and cons.engine == "tensor" and not prod.dma:
            return False
        return True

    def analyze(self):
        last_writer = {}
        readers = {}
        last_per_domain = {}
        for o in self.ops:
            deps = {}
            if o.barrier:
                for dom, w in last_per_domain.items():
                    if not w.nobar and w is not o:
                        deps[id(w)] = w
            else:
                for k in o.reads:
                    w = last_writer.get(k)
                    if w is not None and self._needs_edge(w, o):
                        deps[id(w)] = w
                for k in o.writes:
                    w = last_writer.get(k)
                    if w is not None and self._needs_edge(w, o):
                        deps[id(w)] = w
                    for r in readers.get(k, ()):
                        if r is not o and self._needs_edge(r, o):
                            deps[id(r)] = r
            o.deps = list(deps.values())
            for k in o.reads:
                readers.setdefault(k, []).append(o)
            for k in o.writes:
                last_writer[k] = o
                readers[k] = []
            last_per_domain[o.domain] = o
        for o in self.ops:
            for d in o.deps:
                d.inc_needed = True
        dmaq = {}
        counts = {}
        for o in self.ops:
            if o.dma:
                o.inc_needed = True
                q = "sw" if o.engine == "gpsimd" else "hw"
                assert dmaq.setdefault(o.domain, q) == q,                     f"dma key {o.domain} used from both SWDGE and HWDGE queues"
            if o.inc_needed:
                step = 16 if (o.dma and not o.ccinc1) else 1
                counts[o.domain] = counts.get(o.domain, 0) + step
                o.wait_val = counts[o.domain]
        self.final_counts = counts

    def emit_into(self, nc, block, stack):
        self.analyze()
        sems = {}
        for dom in self.final_counts:
            sems[dom] = stack.enter_context(nc.semaphore("s_" + dom))
        streams = {e: [] for e in ENGINES}
        for o in self.ops:
            streams[o.engine].append(o)

        def run_stream(eng_obj, engine_name):
            watermark = {}
            for o in streams[engine_name]:
                for d in o.deps:
                    dom, v = d.domain, d.wait_val
                    if watermark.get(dom, 0) < v:
                        eng_obj.wait_ge(sems[dom], v)
                        watermark[dom] = v
                inst = o.fn(eng_obj)
                if o.inc_needed:
                    assert inst is not None, f"op '{o.tag}' needs inc but returned None"
                    inst.then_inc(sems[o.domain], 16 if (o.dma and not o.ccinc1) else 1)

        for name in ENGINES:
            getattr(block, name)(lambda eng, n=name: run_stream(eng, n))
        return sems




class Ctx:
    def __init__(self, nc, p, tok):
        self.nc = nc
        self.p = p
        self.tok = tok
        self.uid = 0
        self.scopes = [ExitStack()]

    def u(self, base):
        self.uid += 1
        return f"{base}{self.uid}"

    def push_scope(self):
        self.scopes.append(ExitStack())

    def pop_scope(self, barrier=True):
        self.scopes.pop().close()
        if barrier:
            self.p.barrier()

    def sb(self, name, shape, dt=F32, scope=-1):
        return self.scopes[scope].enter_context(self.nc.sbuf_tensor(name, shape, dt))

    def ps(self, name, shape, dt=F32, scope=-1):
        return self.scopes[scope].enter_context(self.nc.psum_tensor(name, shape, dt))

    def dram(self, name, shape, dt=F32, kind="Internal"):
        return self.nc.dram_tensor(name, shape, dt, kind=kind).ap()

    def input(self, name, shape, dt=F32):
        return self.nc.dram_tensor(name, shape, dt, kind="ExternalInput").ap()

    def output(self, name, shape, dt=F32):
        return self.nc.dram_tensor(name, shape, dt, kind="ExternalOutput").ap()

    def setup_common(self, ident_in, identf_in, const_vals=()):
        p = self.p
        self.ident_in = ident_in
        self.identf_in = identf_in
        self.ones_col_bf = self.sb("ones_col_bf", [128, 1], BF16, scope=0)
        self.ones_col_f32 = self.sb("ones_col_f32", [128, 1], F32, scope=0)
        self.ones_row_f32 = self.sb("ones_row_f32", [1, 128], F32, scope=0)
        p.op("vector", lambda e: e.memset(self.ones_col_bf[:], 1.0), writes=["ones"], tag="ones1")
        p.op("vector", lambda e: e.memset(self.ones_col_f32[:], 1.0), writes=["ones"], tag="ones2")
        p.op("vector", lambda e: e.memset(self.ones_row_f32[:], 1.0), writes=["ones"], tag="ones3")
        vals = sorted(set(float(v) for v in const_vals) | {EPS})
        pool = self.sb("constpool", [128, len(vals)], F32, scope=0)
        for i, v in enumerate(vals):
            p.op("vector", lambda e, v=v, i=i: e.memset(pool[:, i:i + 1], v),
                 writes=["ones"], tag=f"const_{i}")
            self.nc.const_aps.register(F32, v, pool[:, i:i + 1])
        self.ident = self.sb("ident", [128, 128], BF16, scope=0)
        p.dma("gpsimd", self.ident[:], self.ident_in, writes=["identk"], tag="ld_ident")
        self.ident_f32 = self.sb("ident_f32", [128, 128], F32, scope=0)
        p.dma("sync", self.ident_f32[:], self.identf_in, writes=["identfk"], tag="ld_identf")
        self.ln_tmp = self.sb("ln_tmp", [128, 512], F32, scope=0)




def emit_weight_load(ctx, key, w_dram, n_kc, m_cols, engine="gpsimd", dt=BF16,
                     rows_per_kc=128):
    tiles = []
    for kc in range(n_kc):
        t = ctx.sb(ctx.u(f"w_{key}_"), [rows_per_kc, m_cols], dt)
        ctx.p.dma(engine, t[:],
                  w_dram[kc * rows_per_kc:(kc + 1) * rows_per_kc, :],
                  writes=[key], tag=f"ldw_{key}_{kc}")
        tiles.append(t)
    return tiles


def mk_stream(ctx, tag, m_cols):
    bufs, keys = [], []
    for i in range(2):
        bufs.append(ctx.sb(ctx.u(f"ws_{tag}_"), [128, m_cols], BF16))
        keys.append(ctx.u(f"wsk_{tag}_"))
    return {"bufs": bufs, "keys": keys, "m_cols": m_cols}


def emit_proj_stream(ctx, tag, stream, w_dram, n_kc, in_fn, in_key, epilogue,
                     tp, mcols_off=0, n_mc=None, extra=None):
    p = ctx.p
    m_cols = stream["m_cols"]
    n_mc = (m_cols - mcols_off) // 128 if n_mc is None else n_mc
    assert tp == 512, "accumulation regions must be bank-aligned (tp=512)"
    GM = 8
    ctx.push_scope()
    ps = ctx.ps(ctx.u(f"pp_{tag}_"), [128, min(n_mc, GM), tp], F32)
    pk = ctx.u(f"ppk_{tag}_")
    bufs, keys = stream["bufs"], stream["keys"]

    def load(kc, g):
        p.dma("gpsimd", bufs[kc % 2][:], w_dram[kc * 128:(kc + 1) * 128, :],
              writes=[keys[kc % 2]], tag=f"ldws_{tag}_{g}_{kc}", nobar=True)
    for g0 in range(0, n_mc, GM):
        gn = min(GM, n_mc - g0)
        load(0, g0)
        if n_kc > 1:
            load(1, g0)
        for kc in range(n_kc):
            b = kc % 2
            for mi in range(gn):
                mc = g0 + mi
                p.op("tensor", lambda e, kc=kc, mc=mc, mi=mi, b=b: e.matmul(
                        ps[:, mi, :],
                        bufs[b][:, mcols_off + mc * 128:mcols_off + (mc + 1) * 128],
                        in_fn(kc),
                        start=(kc == 0),
                        stop=(kc == n_kc - 1 and extra is None)),
                     reads=[keys[b], in_key(kc) if callable(in_key) else in_key],
                     writes=[pk], tag=f"mm_{tag}_{kc}_{mc}")
            if kc + 2 < n_kc:
                load(kc + 2, g0)
        if extra is not None:
            extra(ps, pk)
        for mi in range(gn):
            epilogue(g0 + mi, ps[:, mi, :], pk)
    ctx.pop_scope()


def emit_ln_fm(ctx, tag, src_fn, src_key, out_fn=None, accum_res=None,
               n_chunks=PC, src_bf16=False, tok=None, gain_bias=None):
    nc, p = ctx.nc, ctx.p
    tok = ctx.tok if tok is None else tok
    TT = min(512, tok)
    n_t = tok // TT
    ctx.push_scope()
    sqt = ctx.sb(ctx.u(f"lnsq{tag}_"), [128, 2, TT], BF16)
    sqk = [ctx.u(f"lnsqk{tag}") for _ in range(2)]
    rows = ctx.sb(ctx.u(f"lnrows{tag}_"), [1, 3, tok], F32)
    rk = ctx.u(f"lnrk{tag}")
    prow = ctx.ps(ctx.u(f"lnprow{tag}_"), [1, TT], F32)
    pk = ctx.u(f"lnpk{tag}")
    ones_src = ctx.ones_col_bf if src_bf16 else ctx.ones_col_f32
    for tc in range(n_t):
        for ci in range(n_chunks):
            p.op("tensor", lambda e, ci=ci, tc=tc: e.matmul(
                    prow[:], ones_src[:], src_fn(ci)[:, tc * TT:(tc + 1) * TT],
                    start=(ci == 0), stop=(ci == n_chunks - 1)),
                 reads=[src_key, "ones"], writes=[pk], tag=f"lnsx_{tag}_{tc}_{ci}")
        p.op("vector", lambda e, tc=tc: e.tensor_copy(rows[:, 0, tc * TT:(tc + 1) * TT], prow[:]),
             reads=[pk], writes=[rk], tag=f"lnr0_{tag}_{tc}")
        for ci in range(n_chunks):
            b = ci % 2
            p.op("scalar", lambda e, ci=ci, tc=tc, b=b: e.activation(
                    sqt[:, b, :], src_fn(ci)[:, tc * TT:(tc + 1) * TT], AF.Square),
                 reads=[src_key], writes=[sqk[b]], tag=f"lnsq_{tag}_{tc}_{ci}")
            p.op("tensor", lambda e, ci=ci, b=b: e.matmul(
                    prow[:], ctx.ones_col_bf[:], sqt[:, b, :],
                    start=(ci == 0), stop=(ci == n_chunks - 1)),
                 reads=[sqk[b], "ones"], writes=[pk], tag=f"lnsq2_{tag}_{tc}_{ci}")
        p.op("vector", lambda e, tc=tc: e.tensor_copy(rows[:, 1, tc * TT:(tc + 1) * TT], prow[:]),
             reads=[pk], writes=[rk], tag=f"lnr1_{tag}_{tc}")
    inv = 1.0 / (n_chunks * 128)
    p.op("vector", lambda e: e.tensor_scalar(rows[:, 0, :], rows[:, 0, :], inv, None, ALU.mult),
         reads=[rk], writes=[rk], tag=f"ln_m_{tag}")
    p.op("vector", lambda e: e.tensor_scalar(rows[:, 1, :], rows[:, 1, :], inv, None, ALU.mult),
         reads=[rk], writes=[rk], tag=f"ln_v2_{tag}")
    p.op("vector", lambda e: e.tensor_mul(rows[:, 2, :], rows[:, 0, :], rows[:, 0, :]),
         reads=[rk], writes=[rk], tag=f"ln_msq_{tag}")
    p.op("vector", lambda e: e.tensor_sub(rows[:, 1, :], rows[:, 1, :], rows[:, 2, :]),
         reads=[rk], writes=[rk], tag=f"ln_var_{tag}")
    p.op("scalar", lambda e: e.activation(rows[:, 1, :], rows[:, 1, :], AF.Ln, bias=EPS),
         reads=[rk], writes=[rk], tag=f"ln_ln_{tag}")
    p.op("scalar", lambda e: e.activation(rows[:, 1, :], rows[:, 1, :], AF.Exp, scale=-0.5),
         reads=[rk], writes=[rk], tag=f"ln_exp_{tag}")
    p.op("vector", lambda e: e.tensor_mul(rows[:, 2, :], rows[:, 0, :], rows[:, 1, :]),
         reads=[rk], writes=[rk], tag=f"ln_mr_{tag}")
    pbc = ctx.ps(ctx.u(f"lnbc{tag}_"), [128, 2, TT], F32)
    bck = ctx.u(f"lnbck{tag}")
    for tc in range(n_t):
        sl = slice(tc * TT, (tc + 1) * TT)
        p.op("tensor", lambda e, sl=sl: e.matmul(
                pbc[:, 0, :], ctx.ones_row_f32[:], rows[:, 1, sl], start=True, stop=True),
             reads=[rk, "ones"], writes=[bck], tag=f"lnbcr_{tag}_{tc}")
        p.op("tensor", lambda e, sl=sl: e.matmul(
                pbc[:, 1, :], ctx.ones_row_f32[:], rows[:, 2, sl], start=True, stop=True),
             reads=[rk, "ones"], writes=[bck], tag=f"lnbcm_{tag}_{tc}")
        for ci in range(n_chunks):
            if accum_res is None:
                out_ap, out_key = out_fn(ci)
                dst = out_ap[:, sl]
                if gain_bias is not None:
                    g_fn, b_fn, gb_key = gain_bias
                    tmp = ctx.ln_tmp
                    p.op("vector", lambda e, ci=ci, sl=sl: e.scalar_tensor_tensor(
                            tmp[:, 0:TT], src_fn(ci)[:, sl], 1.0, pbc[:, 0, :],
                            ALU.mult, ALU.mult),
                         reads=[src_key, bck], writes=["ln_tmp"],
                         tag=f"lnm1_{tag}_{tc}_{ci}")
                    p.op("vector", lambda e: e.tensor_sub(
                            tmp[:, 0:TT], tmp[:, 0:TT], pbc[:, 1, :]),
                         reads=[bck, "ln_tmp"], writes=["ln_tmp"],
                         tag=f"lnm2_{tag}_{tc}_{ci}")
                    p.op("vector", lambda e, ci=ci, dst=dst: e.tensor_scalar(
                            dst, tmp[:, 0:TT], g_fn(ci), b_fn(ci),
                            ALU.mult, ALU.add),
                         reads=["ln_tmp", gb_key], writes=[out_key],
                         tag=f"lnm3_{tag}_{tc}_{ci}")
                    continue
                p.op("vector", lambda e, ci=ci, sl=sl, dst=dst: e.scalar_tensor_tensor(
                        dst, src_fn(ci)[:, sl], 1.0, pbc[:, 0, :], ALU.mult, ALU.mult),
                     reads=[src_key, bck], writes=[out_key], tag=f"lnm1_{tag}_{tc}_{ci}")
                p.op("vector", lambda e, dst=dst: e.tensor_sub(dst, dst, pbc[:, 1, :]),
                     reads=[bck, out_key], writes=[out_key], tag=f"lnm2_{tag}_{tc}_{ci}")
            else:
                res_fn, res_key = accum_res
                tmp = ctx.ln_tmp
                p.op("vector", lambda e, ci=ci, sl=sl: e.scalar_tensor_tensor(
                        tmp[:, 0:TT], src_fn(ci)[:, sl], 1.0, pbc[:, 0, :],
                        ALU.mult, ALU.mult),
                     reads=[src_key, bck], writes=["ln_tmp"], tag=f"lnm1_{tag}_{tc}_{ci}")
                p.op("vector", lambda e: e.tensor_sub(tmp[:, 0:TT], tmp[:, 0:TT], pbc[:, 1, :]),
                     reads=[bck, "ln_tmp"], writes=["ln_tmp"], tag=f"lnm2_{tag}_{tc}_{ci}")
                p.op("vector", lambda e, ci=ci, sl=sl: e.tensor_add(
                        res_fn(ci)[:, sl], res_fn(ci)[:, sl], tmp[:, 0:TT]),
                     reads=["ln_tmp", res_key], writes=[res_key],
                     tag=f"lnacc_{tag}_{tc}_{ci}")
    ctx.pop_scope()


def emit_ln_col(ctx, tag, col_fn, col_key, out_fn, out_key):
    p = ctx.p
    ctx.push_scope()
    colsq = ctx.sb(ctx.u("lncol_sq"), [128, PC], F32)
    sqk = ctx.u("lncolsqk")
    for ci in range(PC):
        p.op("scalar", lambda e, ci=ci: e.activation(colsq[:, ci:ci + 1], col_fn(ci), AF.Square),
             reads=[col_key], writes=[sqk], tag=f"lncsq_{tag}_{ci}")
    pr = ctx.ps(ctx.u("lncol_ps"), [1, 2], F32)
    prk = ctx.u("lncolpsk")
    rows = ctx.sb(ctx.u("lncol_rows"), [1, 4], F32)
    rk = ctx.u("lncolrk")
    for ci in range(PC):
        p.op("tensor", lambda e, ci=ci: e.matmul(
                pr[:, 0:1], ctx.ones_col_f32[:], col_fn(ci),
                start=(ci == 0), stop=(ci == PC - 1)),
             reads=[col_key, "ones"], writes=[prk], tag=f"lncs_{tag}_{ci}")
    for ci in range(PC):
        p.op("tensor", lambda e, ci=ci: e.matmul(
                pr[:, 1:2], ctx.ones_col_f32[:], colsq[:, ci:ci + 1],
                start=(ci == 0), stop=(ci == PC - 1)),
             reads=[sqk, "ones"], writes=[prk], tag=f"lncq_{tag}_{ci}")
    p.op("vector", lambda e: e.tensor_scalar(rows[:, 0:2], pr[:], 1.0 / C, None, ALU.mult),
         reads=[prk], writes=[rk], tag=f"lnc_m_{tag}")
    p.op("vector", lambda e: e.tensor_mul(rows[:, 2:3], rows[:, 0:1], rows[:, 0:1]),
         reads=[rk], writes=[rk], tag=f"lnc_msq_{tag}")
    p.op("vector", lambda e: e.tensor_sub(rows[:, 2:3], rows[:, 1:2], rows[:, 2:3]),
         reads=[rk], writes=[rk], tag=f"lnc_var_{tag}")
    p.op("scalar", lambda e: e.activation(rows[:, 2:3], rows[:, 2:3], AF.Ln, bias=EPS),
         reads=[rk], writes=[rk], tag=f"lnc_ln_{tag}")
    p.op("scalar", lambda e: e.activation(rows[:, 2:3], rows[:, 2:3], AF.Exp, scale=-0.5),
         reads=[rk], writes=[rk], tag=f"lnc_exp_{tag}")
    p.op("vector", lambda e: e.tensor_mul(rows[:, 3:4], rows[:, 0:1], rows[:, 2:3]),
         reads=[rk], writes=[rk], tag=f"lnc_mr_{tag}")
    pb = ctx.ps(ctx.u("lncol_pb"), [128, 2], F32)
    pbk = ctx.u("lncolpbk")
    p.op("tensor", lambda e: e.matmul(pb[:], ctx.ones_row_f32[:], rows[:, 2:4],
                                      start=True, stop=True),
         reads=[rk, "ones"], writes=[pbk], tag=f"lnc_bc_{tag}")
    for ci in range(PC):
        p.op("vector", lambda e, ci=ci: e.scalar_tensor_tensor(
                out_fn(ci), col_fn(ci), 1.0, pb[:, 0:1], ALU.mult, ALU.mult),
             reads=[col_key, pbk], writes=[out_key], tag=f"lnc_a1_{tag}_{ci}")
        p.op("vector", lambda e, ci=ci: e.tensor_sub(
                out_fn(ci), out_fn(ci), pb[:, 1:2]),
             reads=[pbk, out_key], writes=[out_key], tag=f"lnc_a2_{tag}_{ci}")
    ctx.pop_scope()



CLAMP_MAX = float(np.log(1.0 / 0.01))
L1F_COLS = [(0, 1), (1, 3), (3, 7), (7, 15)]


def emit_l1_ffn(ctx, x_fn, x_key, inp):
    nc, p, tok = ctx.nc, ctx.p, ctx.tok
    TP = min(512, tok)
    n_pass = tok // TP
    ctx.push_scope()

    st_W = mk_stream(ctx, "l1fW", C)
    st_Wp = mk_stream(ctx, "l1fWp", 256)
    w_wdt = emit_weight_load(ctx, "l1fWd", inp["Wdt"], PC, 15)
    wu = ctx.sb("l1f_wu", [15, C], BF16)
    p.dma("gpsimd", wu[:], inp["Wu"], writes=["l1fWu"], tag="ld_l1fWu")
    sim_sb = ctx.sb("l1f_sim", [128, 2, 4], F32)
    p.dma("sync", sim_sb[:], inp["sim"].rearrange("(a p) e -> p a e", p=128),
          writes=["l1fsim"], tag="ld_l1fsim", allow_slow_non_contiguous=True)
    simb_sb = ctx.sb("l1f_simb", [128, 2, 4], BF16)
    temp_sb = ctx.sb("l1f_temp", [1, 1], F32)
    p.dma("sync", temp_sb[:], inp["temp"], writes=["l1ftemp"], tag="ld_l1ftemp")
    bp_sb = ctx.sb("l1f_bp", [128, 2], F32)
    p.dma("sync", bp_sb[:], inp["bp"], writes=["l1fbp"], tag="ld_l1fbp")

    factor_bc = ctx.sb("l1f_facbc", [128, 4], F32)
    ctx.push_scope()
    simsq = ctx.sb(ctx.u("l1f_simsq"), [128, 2, 4], F32)
    for a in range(2):
        p.op("scalar", lambda e, a=a: e.activation(simsq[:, a, :], sim_sb[:, a, :], AF.Square),
             reads=["l1fsim"], writes=["simsq"], tag=f"l1f_simsq{a}")
    srow = ctx.sb(ctx.u("l1f_srow"), [1, 8], F32)
    ps4 = ctx.ps(ctx.u("l1f_ps4"), [128, 16], F32)
    for a in range(2):
        p.op("tensor", lambda e, a=a: e.matmul(ps4[0:1, 0:4], ctx.ones_col_f32[:],
                                               simsq[:, a, :],
                                               start=(a == 0), stop=(a == 1)),
             reads=["simsq", "ones"], writes=["ps4k"], tag=f"l1f_simss{a}")
    p.op("vector", lambda e: e.tensor_copy(srow[:, 0:4], ps4[0:1, 0:4]),
         reads=["ps4k"], writes=["srowk"], tag="l1f_srow")
    p.op("scalar", lambda e: e.activation(srow[:, 0:4], srow[:, 0:4], AF.Ln),
         reads=["srowk"], writes=["srowk"], tag="l1f_sln")
    p.op("scalar", lambda e: e.activation(srow[:, 0:4], srow[:, 0:4], AF.Exp, scale=0.5),
         reads=["srowk"], writes=["srowk"], tag="l1f_sexp")
    p.op("vector", lambda e: e.tensor_scalar_max(srow[:, 0:4], srow[:, 0:4], 1e-12),
         reads=["srowk"], writes=["srowk"], tag="l1f_smax")
    p.op("vector", lambda e: e.tensor_scalar_min(srow[:, 4:5], temp_sb[:], CLAMP_MAX),
         reads=["l1ftemp"], writes=["srowk"], tag="l1f_tclamp")
    p.op("scalar", lambda e: e.activation(srow[:, 4:5], srow[:, 4:5], AF.Exp),
         reads=["srowk"], writes=["srowk"], tag="l1f_texp")
    p.op("vector", lambda e: e.reciprocal(srow[:, 0:4], srow[:, 0:4]),
         reads=["srowk"], writes=["srowk"], tag="l1f_srcp")
    p.op("vector", lambda e: e.tensor_scalar(srow[:, 0:4], srow[:, 0:4], srow[:, 4:5],
                                             None, ALU.mult),
         reads=["srowk"], writes=["srowk"], tag="l1f_sfac")
    p.op("tensor", lambda e: e.matmul(ps4[:, 4:8], ctx.ones_row_f32[:], srow[:, 0:4],
                                      start=True, stop=True),
         reads=["srowk", "ones"], writes=["ps4k"], tag="l1f_facbcm")
    p.op("vector", lambda e: e.tensor_copy(factor_bc[:], ps4[:, 4:8]),
         reads=["ps4k"], writes=["facbck"], tag="l1f_facbcc")
    for a in range(2):
        p.op("vector", lambda e, a=a: e.tensor_copy(simb_sb[:, a, :], sim_sb[:, a, :]),
             reads=["l1fsim"], writes=["l1fsimb"], tag=f"l1f_simcast{a}")
    ctx.pop_scope()

    def do_pass(pass_i):
        t0 = pass_i * TP
        n_tt = TP // 128
        ctx.push_scope()
        ln_p = ctx.sb(ctx.u("l1f_ln"), [128, PC, TP], BF16)
        lnk = ctx.u("l1f_lnk")
        emit_ln_fm(ctx, f"l1fln{pass_i}",
                   lambda ci, t0=t0: x_fn(ci)[:, t0:t0 + TP], x_key,
                   out_fn=lambda ci, ln_p=ln_p, lnk=lnk: (ln_p[:, ci, :], lnk), tok=TP)
        proj = ctx.sb(ctx.u("l1f_proj"), [128, 2, TP], BF16)
        projk = ctx.u("l1f_projk")
        projsq = ctx.sb(ctx.u("l1f_projsq"), [128, 2, TP], BF16)
        projsqk = ctx.u("l1f_projsqk")

        def proj_ep(mc, psum, pkey):
            p.op("vector", lambda e, mc=mc, psum=psum: e.tensor_scalar(
                    proj[:, mc, :], psum, bp_sb[:, mc:mc + 1], None, ALU.add),
                 reads=[pkey, "l1fbp"], writes=[projk], tag=f"l1f_pev_{pass_i}_{mc}")
            p.op("scalar", lambda e, mc=mc: e.activation(
                    projsq[:, mc, :], proj[:, mc, :], AF.Square),
                 reads=[projk], writes=[projsqk], tag=f"l1f_psq_{pass_i}_{mc}")
        emit_proj_stream(ctx, f"l1fp{pass_i}", st_Wp, inp["Wpt"], PC,
                         lambda kc, ln_p=ln_p: ln_p[:, kc, :], lnk, proj_ep, TP)
        ctx.push_scope()
        rows = ctx.sb(ctx.u("l1f_rows"), [1, 2, TP], F32)
        rwk = ctx.u("l1frowsk")
        prow = ctx.ps(ctx.u("l1f_prow"), [1, TP], F32)
        prk = ctx.u("l1fprk")
        for mc in range(2):
            p.op("tensor", lambda e, mc=mc: e.matmul(
                    prow[:], ctx.ones_col_bf[:], projsq[:, mc, :],
                    start=(mc == 0), stop=(mc == 1)),
                 reads=[projsqk, "ones"], writes=[prk], tag=f"l1f_ss_{pass_i}_{mc}")
        p.op("vector", lambda e: e.tensor_copy(rows[:, 0, :], prow[:]),
             reads=[prk], writes=[rwk], tag=f"l1f_ssr_{pass_i}")
        p.op("scalar", lambda e: e.activation(rows[:, 1, :], rows[:, 0, :], AF.Ln),
             reads=[rwk], writes=[rwk], tag=f"l1f_lnr_{pass_i}")
        p.op("scalar", lambda e: e.activation(rows[:, 1, :], rows[:, 1, :], AF.Exp, scale=0.5),
             reads=[rwk], writes=[rwk], tag=f"l1f_expr_{pass_i}")
        p.op("vector", lambda e: e.tensor_scalar_max(rows[:, 1, :], rows[:, 1, :], 1e-12),
             reads=[rwk], writes=[rwk], tag=f"l1f_maxr_{pass_i}")
        p.op("vector", lambda e: e.reciprocal(rows[:, 1, :], rows[:, 1, :]),
             reads=[rwk], writes=[rwk], tag=f"l1f_rcpr_{pass_i}")
        pbc = ctx.ps(ctx.u("l1f_pbc"), [128, TP], F32)
        pbk = ctx.u("l1fpbk")
        p.op("tensor", lambda e: e.matmul(pbc[:], ctx.ones_row_f32[:], rows[:, 1, :],
                                          start=True, stop=True),
             reads=[rwk, "ones"], writes=[pbk], tag=f"l1f_bc_{pass_i}")
        for mc in range(2):
            p.op("vector", lambda e, mc=mc: e.tensor_mul(proj[:, mc, :], proj[:, mc, :], pbc[:]),
                 reads=[pbk, projk], writes=[projk], tag=f"l1f_scale_{pass_i}_{mc}")
        ctx.pop_scope()
        ret = ctx.sb(ctx.u("l1f_ret"), [128, n_tt, 4], F32)
        retk = ctx.u("l1f_retk")
        ctx.push_scope()
        lg_ps = ctx.ps(ctx.u("l1f_lgps"), [128, 4], F32)
        lgk = ctx.u("lgpsk")
        red = ctx.sb(ctx.u("l1f_red"), [128, 2], F32)
        redk = ctx.u("l1fredk")
        for tt in range(n_tt):
            sl = slice(tt * 128, (tt + 1) * 128)
            for a in range(2):
                p.op("tensor", lambda e, a=a, sl=sl: e.matmul(
                        lg_ps[:], proj[:, a, sl], simb_sb[:, a, :],
                        start=(a == 0), stop=(a == 1)),
                     reads=[projk, "l1fsimb"], writes=[lgk], tag=f"l1f_lg_{pass_i}_{tt}_{a}")
            p.op("vector", lambda e: e.tensor_mul(lg_ps[:], lg_ps[:], factor_bc[:]),
                 reads=[lgk, "facbck"], writes=[lgk], tag=f"l1f_lgs_{pass_i}_{tt}")
            p.op("vector", lambda e: e.reduce_max(red[:, 0:1], lg_ps[:], AX.X, negate=True),
                 reads=[lgk], writes=[redk], tag=f"l1f_nmax_{pass_i}_{tt}")
            p.op("scalar", lambda e, tt=tt: e.activation(
                    ret[:, tt, :], lg_ps[:], AF.Exp, bias=red[:, 0:1],
                    accum_out=red[:, 1:2]),
                 reads=[lgk, redk], writes=[retk, redk], tag=f"l1f_sm_{pass_i}_{tt}")
            p.op("vector", lambda e: e.reciprocal(red[:, 1:2], red[:, 1:2]),
                 reads=[redk], writes=[redk], tag=f"l1f_rl_{pass_i}_{tt}")
            p.op("vector", lambda e, tt=tt: e.tensor_scalar(
                    ret[:, tt, :], ret[:, tt, :], red[:, 1:2], None, ALU.mult),
                 reads=[retk, redk], writes=[retk], tag=f"l1f_lw_{pass_i}_{tt}")
            p.op("vector", lambda e, tt=tt: e.reduce_max(red[:, 0:1], ret[:, tt, :], AX.X),
                 reads=[retk], writes=[redk], tag=f"l1f_lmax_{pass_i}_{tt}")
            p.op("vector", lambda e, tt=tt: e.tensor_scalar(
                    ctx.ln_tmp[:, 0:4], ret[:, tt, :], red[:, 0:1], None, ALU.is_ge),
                 reads=[retk, redk], writes=["ln_tmp"], tag=f"l1f_mask_{pass_i}_{tt}")
            p.op("vector", lambda e, tt=tt: e.tensor_mul(
                    ret[:, tt, :], ret[:, tt, :], ctx.ln_tmp[:, 0:4]),
                 reads=["ln_tmp", retk], writes=[retk], tag=f"l1f_ret_{pass_i}_{tt}")
        ctx.pop_scope()
        scaled_fm = ctx.sb(ctx.u("l1f_scfm"), [15, TP], BF16)
        scfmk = ctx.u("l1f_scfmk")
        ctx.push_scope()
        dn_ps = ctx.ps(ctx.u("l1f_dnps"), [128, 15], F32)
        dnk = ctx.u("dnpsk")
        dn_bf = ctx.sb(ctx.u("l1f_dnbf"), [128, 15], BF16)
        dnbk = ctx.u("dnbfk")
        tr_ps = ctx.ps(ctx.u("l1f_trps"), [15, 128], BF16)
        trk = ctx.u("trpsk")
        for tt in range(n_tt):
            sl = slice(tt * 128, (tt + 1) * 128)
            for kc in range(PC):
                p.op("tensor", lambda e, kc=kc, sl=sl: e.matmul(
                        dn_ps[:], ln_p[:, kc, sl], w_wdt[kc][:],
                        start=(kc == 0), stop=(kc == PC - 1)),
                     reads=[lnk, "l1fWd"], writes=[dnk], tag=f"l1f_dn_{pass_i}_{tt}_{kc}")
            for ei, (c0, c1) in enumerate(L1F_COLS):
                p.op("vector", lambda e, tt=tt, ei=ei, c0=c0, c1=c1: e.tensor_scalar(
                        dn_bf[:, c0:c1], dn_ps[:, c0:c1], ret[:, tt, ei:ei + 1],
                        None, ALU.mult),
                     reads=[dnk, retk], writes=[dnbk], tag=f"l1f_sc_{pass_i}_{tt}_{ei}")
            p.op("tensor", lambda e: e.transpose(tr_ps[:], dn_bf[:], ctx.ident[:]),
                 reads=[dnbk, "identk"], writes=[trk], tag=f"l1f_tr_{pass_i}_{tt}")
            p.op("vector", lambda e, sl=sl: e.tensor_copy(scaled_fm[:, sl], tr_ps[:]),
                 reads=[trk], writes=[scfmk], tag=f"l1f_trc_{pass_i}_{tt}")
        ctx.pop_scope()
        outp = ctx.sb(ctx.u("l1f_out"), [128, PC, TP], BF16)
        outk = ctx.u("l1f_outk")

        def extra(ps, pk):
            for mc in range(PC):
                p.op("tensor", lambda e, mc=mc, ps=ps: e.matmul(
                        ps[:, mc, :], wu[:, mc * 128:(mc + 1) * 128], scaled_fm[:],
                        start=False, stop=True),
                     reads=["l1fWu", scfmk], writes=[pk], tag=f"l1f_up_{pass_i}_{mc}")

        def base_ep(mc, psum, pkey):
            p.op("scalar", lambda e, mc=mc, psum=psum: e.copy(outp[:, mc, :], psum),
                 reads=[pkey], writes=[outk], tag=f"l1f_oev_{pass_i}_{mc}")
        emit_proj_stream(ctx, f"l1fb{pass_i}", st_W, inp["Wt"], PC,
                         lambda kc, ln_p=ln_p: ln_p[:, kc, :], lnk, base_ep, TP,
                         extra=extra)
        emit_ln_fm(ctx, f"l1flno{pass_i}", lambda ci, outp=outp: outp[:, ci, :], outk,
                   accum_res=(lambda ci, t0=t0: x_fn(ci)[:, t0:t0 + TP], x_key),
                   src_bf16=True, tok=TP)
        ctx.pop_scope()

    for _pi in range(n_pass):
        do_pass(_pi)
    ctx.pop_scope()



L1A_H = 16
CHUNK = 128


def emit_tm_proj(ctx, tag, stream, w_dram, n_kc, mcols_off, n_cols, in_fn, in_key,
                 out_cb, tp):
    p = ctx.p
    n_tt = tp // 128
    n_nb = n_cols // 512
    ctx.push_scope()
    ps = ctx.ps(ctx.u(f"ptm_{tag}_"), [128, n_tt, n_nb, 512], F32)
    pk = ctx.u(f"ptmk_{tag}_")
    bufs, keys = stream["bufs"], stream["keys"]

    def load(kc):
        p.dma("gpsimd", bufs[kc % 2][:], w_dram[kc * 128:(kc + 1) * 128, :],
              writes=[keys[kc % 2]], tag=f"ldtm_{tag}_{kc}", nobar=True)
    load(0)
    if n_kc > 1:
        load(1)
    for kc in range(n_kc):
        b = kc % 2
        for tt in range(n_tt):
            for nb in range(n_nb):
                p.op("tensor", lambda e, kc=kc, tt=tt, nb=nb, b=b: e.matmul(
                        ps[:, tt, nb, :], in_fn(kc)[:, tt * 128:(tt + 1) * 128],
                        bufs[b][:, mcols_off + nb * 512:mcols_off + (nb + 1) * 512],
                        start=(kc == 0), stop=(kc == n_kc - 1)),
                     reads=[keys[b], in_key], writes=[pk],
                     tag=f"tmm_{tag}_{kc}_{tt}_{nb}")
        if kc + 2 < n_kc:
            load(kc + 2)
    for tt in range(n_tt):
        for nb in range(n_nb):
            out_cb(tt, nb, ps[:, tt, nb, :], pk)
    ctx.pop_scope()


def emit_l1_attn(ctx, x_fn, x_key, inp, groups8):
    nc, p, tok = ctx.nc, ctx.p, ctx.tok
    TP = min(512, tok)
    n_pass = tok // TP
    n_ch = TP // CHUNK
    n_tt = TP // 128
    H = L1A_H
    ctx.push_scope()

    st_qkv = mk_stream(ctx, "l1aqkv", 3072)
    st_out = mk_stream(ctx, "l1aout", C)
    wqm = ctx.sb("l1a_wqm", [128, 64], BF16)
    wkm = ctx.sb("l1a_wkm", [128, 64], BF16)
    for po in (0, 64):
        p.dma("gpsimd", wqm[po:po + 64, :], inp["wq_mapT"], writes=["l1a_wqm"], tag=f"ld_wqm{po}")
        p.dma("gpsimd", wkm[po:po + 64, :], inp["wk_mapT"], writes=["l1a_wkm"], tag=f"ld_wkm{po}")
    bq = ctx.sb("l1a_bq", [128, 2], F32)
    for po in (0, 64):
        p.dma("sync", bq[po:po + 64, 0:1], inp["bq"], writes=["l1a_bq"], tag=f"ld_bq{po}")
    p.op("vector", lambda e: e.tensor_scalar(bq[:, 1:2], bq[:, 0:1], -1.0, None, ALU.mult),
         reads=["l1a_bq"], writes=["l1a_bq"], tag="negbq")
    bk_rep = ctx.sb("l1a_bk", [128, 64], F32)
    p.dma("sync", bk_rep[:], inp["bk_rep"], writes=["l1a_bk"], tag="ld_bk")
    maskI = ctx.sb("l1a_mask", [128, 128], BF16)
    p.dma("gpsimd", maskI[:], inp["mask_incl"], writes=["l1a_mask"], tag="ld_mask")
    foldm = ctx.sb("l1a_fold", [128, 8], F32)
    p.dma("sync", foldm[:], inp["foldmask"], writes=["l1a_fold"], tag="ld_foldl1")

    q_fm = ctx.sb("l1a_qfm", [128, PC, tok], BF16)
    numer_d = ctx.dram("l1a_numd", [tok // 128, 128, H * 65], BF16)
    ntile = ctx.sb("l1a_nt", [128, H, 65], BF16)
    ntileB = ctx.sb("l1a_ntB", [128, H, 65], BF16)
    kv = ctx.sb("l1a_kv", [128, H, 65], F32)
    kv_bf = ctx.sb("l1a_kvbf", [128, H, 65], BF16)
    s_in = ctx.sb("l1a_sin", [128, H, 65], F32)
    s_in_bf = ctx.sb("l1a_sinbf", [128, H, 65], BF16)
    p.op("vector", lambda e: e.memset(kv[:], 0.0), writes=["l1a_kv"], tag="kv0")

    def emit_eq(pass_i, eq_sb, eq_key, tagp):
        t0 = pass_i * TP
        ctx.push_scope()
        pp = ctx.ps(ctx.u("l1a_pp"), [64, TP], F32)
        ppk = ctx.u("ppk")
        lrow = ctx.sb(ctx.u("l1a_lrow"), [1, 3, TP], F32)
        lrk = ctx.u("lrk")
        prow1 = ctx.ps(ctx.u("l1a_prow"), [1, TP], F32)
        prk = ctx.u("prk")
        pbc = ctx.ps(ctx.u("l1a_pbcq"), [64, TP], F32)
        pbck = ctx.u("pbck")
        for h in range(H):
            ci, po = h // 2, (h % 2) * 64
            p.op("tensor", lambda e, ci=ci, po=po: e.matmul(
                    pp[:], wqm[po:po + 64, :], q_fm[po:po + 64, ci, t0:t0 + TP],
                    start=True, stop=True),
                 reads=["l1a_wqm", "l1a_qfm"], writes=[ppk], tag=f"{tagp}_map_{h}")
            p.op("scalar", lambda e, h=h: e.activation(
                    eq_sb[0:64, h, :], pp[:], AF.Exp, bias=bq[0:64, 0:1]),
                 reads=[ppk, "l1a_bq"], writes=[eq_key], tag=f"{tagp}_eqp_{h}")
            p.op("scalar", lambda e, h=h: e.activation(
                    eq_sb[64:128, h, :], pp[:], AF.Exp, bias=bq[0:64, 1:2], scale=-1.0),
                 reads=[ppk, "l1a_bq"], writes=[eq_key], tag=f"{tagp}_eqm_{h}")
            p.op("tensor", lambda e, h=h: e.matmul(
                    prow1[:], ctx.ones_col_bf[0:64, :], eq_sb[0:64, h, :],
                    start=True, stop=True),
                 reads=[eq_key, "ones"], writes=[prk], tag=f"{tagp}_l1_{h}")
            p.op("vector", lambda e: e.tensor_copy(lrow[:, 0, :], prow1[:]),
                 reads=[prk], writes=[lrk], tag=f"{tagp}_l1c_{h}")
            p.op("tensor", lambda e, h=h: e.matmul(
                    prow1[:], ctx.ones_col_bf[64:128, :], eq_sb[64:128, h, :],
                    start=True, stop=True),
                 reads=[eq_key, "ones"], writes=[prk], tag=f"{tagp}_l2_{h}")
            p.op("vector", lambda e: e.tensor_copy(lrow[:, 1, :], prow1[:]),
                 reads=[prk], writes=[lrk], tag=f"{tagp}_l2c_{h}")
            p.op("vector", lambda e: e.reciprocal(lrow[:, 1, :], lrow[:, 1, :]),
                 reads=[lrk], writes=[lrk], tag=f"{tagp}_l2r_{h}")
            p.op("vector", lambda e: e.tensor_mul(lrow[:, 2, :], lrow[:, 0, :], lrow[:, 1, :]),
                 reads=[lrk], writes=[lrk], tag=f"{tagp}_rho_{h}")
            p.op("tensor", lambda e: e.matmul(
                    pbc[:], ctx.ones_row_f32[:, 0:64], lrow[:, 2, :],
                    start=True, stop=True),
                 reads=[lrk, "ones"], writes=[pbck], tag=f"{tagp}_rhob_{h}")
            p.op("vector", lambda e, h=h: e.tensor_mul(
                    eq_sb[64:128, h, :], eq_sb[64:128, h, :], pbc[:]),
                 reads=[pbck, eq_key], writes=[eq_key], tag=f"{tagp}_eqs_{h}")
        ctx.pop_scope()

    def phase_a(pass_i):
        t0 = pass_i * TP
        ctx.push_scope()
        ln_p = ctx.sb(ctx.u("l1a_ln"), [128, PC, TP], BF16)
        lnk = ctx.u("l1a_lnp")
        emit_ln_fm(ctx, f"l1aln{pass_i}",
                   lambda ci, t0=t0: x_fn(ci)[:, t0:t0 + TP], x_key,
                   out_fn=lambda ci, ln_p=ln_p, lnk=lnk: (ln_p[:, ci, :], lnk), tok=TP)
        k_fm = ctx.sb(ctx.u("l1a_kfm"), [128, PC, TP], BF16)
        kk = ctx.u("l1a_kfmk")

        def qk_ep(mc, psum, pkey, t0=t0):
            if mc < PC:
                p.op("vector", lambda e, mc=mc, psum=psum, t0=t0: e.tensor_copy(
                        q_fm[:, mc, t0:t0 + TP], psum),
                     reads=[pkey], writes=["l1a_qfm"], tag=f"qev_{pass_i}_{mc}")
            else:
                p.op("vector", lambda e, mc=mc, psum=psum: e.tensor_copy(
                        k_fm[:, mc - PC, :], psum),
                     reads=[pkey], writes=[kk], tag=f"kev_{pass_i}_{mc}")
        emit_proj_stream(ctx, f"l1aqk{pass_i}", st_qkv, inp["wqkvT"], PC,
                         lambda kc, ln_p=ln_p: ln_p[:, kc, :], lnk, qk_ep, TP,
                         mcols_off=0, n_mc=16)
        vaug = ctx.sb(ctx.u("l1a_vaug"), [128, n_tt, H, 65], BF16)
        vk = ctx.u("l1a_vaugk")
        p.op("vector", lambda e: e.memset(
                vaug.ap().rearrange("p a h o -> p (a h) o")[:, :, 64:65], 1.0),
             writes=[vk], tag=f"vones_{pass_i}")

        def v_cb(tt, nb, psum, pkey):
            p.op("vector", lambda e, tt=tt, nb=nb, psum=psum: e.tensor_copy(
                    vaug[:, tt, nb * 8:(nb + 1) * 8, 0:64],
                    psum.rearrange("p (h d) -> p h d", d=64)),
                 reads=[pkey], writes=[vk], tag=f"vev_{pass_i}_{tt}_{nb}")
        emit_tm_proj(ctx, f"l1av{pass_i}", st_qkv, inp["wqkvT"], PC, 2048, 1024,
                     lambda kc, ln_p=ln_p: ln_p[:, kc, :], lnk, v_cb, TP)
        eq_sb = ctx.sb(ctx.u("l1a_eq"), [128, H, TP], BF16)
        eqk = ctx.u("l1a_eqk")
        emit_eq(pass_i, eq_sb, eqk, f"eqA{pass_i}")
        phik_fm = ctx.sb(ctx.u("l1a_phikfm"), [128, H, TP], BF16)
        pfk = ctx.u("l1a_phikfmk")
        phik_tm = ctx.sb(ctx.u("l1a_phiktm"), [128, n_tt, H, 128], BF16)
        ptmk = ctx.u("l1a_phiktmk")
        ctx.push_scope()
        pkt = ctx.ps(ctx.u("l1a_pkt"), [128, 64], F32)
        pktk = ctx.u("pktk")
        phtm = ctx.sb(ctx.u("l1a_phtm"), [128, 128], BF16)
        phtmk = ctx.u("phtmk")
        red2 = ctx.sb(ctx.u("l1a_red"), [128, 4], F32)
        redk = ctx.u("redk2")
        ptr = ctx.ps(ctx.u("l1a_ptr"), [128, 128], BF16)
        ptrk = ctx.u("ptrk")
        for tt in range(n_tt):
            for h in range(H):
                ci, po = h // 2, (h % 2) * 64
                p.op("tensor", lambda e, tt=tt, ci=ci, po=po: e.matmul(
                        pkt[:], k_fm[po:po + 64, ci, tt * 128:(tt + 1) * 128],
                        wkm[po:po + 64, :], start=True, stop=True),
                     reads=[kk, "l1a_wkm"], writes=[pktk], tag=f"kmap_{pass_i}_{tt}_{h}")
                p.op("vector", lambda e: e.tensor_add(pkt[:], pkt[:], bk_rep[:]),
                     reads=[pktk, "l1a_bk"], writes=[pktk], tag=f"kbk_{pass_i}_{tt}_{h}")
                p.op("vector", lambda e: e.reduce_max(red2[:, 0:1], pkt[:], AX.X, negate=True),
                     reads=[pktk], writes=[redk], tag=f"km1_{pass_i}_{tt}_{h}")
                p.op("scalar", lambda e: e.activation(
                        phtm[:, 0:64], pkt[:], AF.Exp, bias=red2[:, 0:1],
                        accum_out=red2[:, 1:2]),
                     reads=[pktk, redk], writes=[phtmk, redk], tag=f"km2_{pass_i}_{tt}_{h}")
                p.op("vector", lambda e: e.tensor_reduce(red2[:, 2:3], pkt[:], AX.X, ALU.min),
                     reads=[pktk], writes=[redk], tag=f"km3_{pass_i}_{tt}_{h}")
                p.op("scalar", lambda e: e.activation(
                        phtm[:, 64:128], pkt[:], AF.Exp, bias=red2[:, 2:3], scale=-1.0,
                        accum_out=red2[:, 3:4]),
                     reads=[pktk, redk], writes=[phtmk, redk], tag=f"km4_{pass_i}_{tt}_{h}")
                p.op("vector", lambda e: e.reciprocal(red2[:, 1:2], red2[:, 1:2]),
                     reads=[redk], writes=[redk], tag=f"km5_{pass_i}_{tt}_{h}")
                p.op("vector", lambda e: e.reciprocal(red2[:, 3:4], red2[:, 3:4]),
                     reads=[redk], writes=[redk], tag=f"km6_{pass_i}_{tt}_{h}")
                p.op("vector", lambda e: e.tensor_scalar(
                        phtm[:, 0:64], phtm[:, 0:64], red2[:, 1:2], None, ALU.mult),
                     reads=[phtmk, redk], writes=[phtmk], tag=f"km7_{pass_i}_{tt}_{h}")
                p.op("vector", lambda e: e.tensor_scalar(
                        phtm[:, 64:128], phtm[:, 64:128], red2[:, 3:4], None, ALU.mult),
                     reads=[phtmk, redk], writes=[phtmk], tag=f"km8_{pass_i}_{tt}_{h}")
                p.op("tensor", lambda e: e.transpose(ptr[:], phtm[:], ctx.ident[:]),
                     reads=[phtmk, "identk"], writes=[ptrk], tag=f"ktr_{pass_i}_{tt}_{h}")
                p.op("vector", lambda e, tt=tt, h=h: e.tensor_copy(
                        phik_fm[:, h, tt * 128:(tt + 1) * 128], ptr[:]),
                     reads=[ptrk], writes=[pfk], tag=f"ktrc_{pass_i}_{tt}_{h}")
                p.op("vector", lambda e, tt=tt, h=h: e.tensor_copy(
                        phik_tm[:, tt, h, :], phtm[:]),
                     reads=[phtmk], writes=[ptmk], tag=f"ktms_{pass_i}_{tt}_{h}")
        ctx.pop_scope()
        ctx.push_scope()
        at_sb = ctx.sb(ctx.u("l1a_at"), [128, 128], BF16)
        atk = ctx.u("atk")
        pat = ctx.ps(ctx.u("l1a_pat"), [128, 128], F32)
        patk = ctx.u("patk")
        py = ctx.ps(ctx.u("l1a_py"), [128, 65], F32)
        pyk = ctx.u("pyk")
        pu = ctx.ps(ctx.u("l1a_pu"), [128, 65], F32)
        puk = ctx.u("puk")
        for cc in range(n_ch):
            csl = slice(cc * 128, (cc + 1) * 128)
            p.op("vector", lambda e: e.tensor_copy(kv_bf[:], kv[:]),
                 reads=["l1a_kv"], writes=["l1a_kvbf"], tag=f"kvbf_{pass_i}_{cc}")
            for h in range(H):
                p.op("tensor", lambda e, csl=csl, h=h: e.matmul(
                        pat[:], phik_fm[:, h, csl], eq_sb[:, h, csl],
                        start=True, stop=True),
                     reads=[pfk, eqk], writes=[patk], tag=f"at_{pass_i}_{cc}_{h}")
                p.op("vector", lambda e: e.tensor_mul(at_sb[:], pat[:], maskI[:]),
                     reads=[patk, "l1a_mask"], writes=[atk], tag=f"atm_{pass_i}_{cc}_{h}")
                p.op("tensor", lambda e, cc=cc, h=h: e.matmul(
                        py[:], at_sb[:], vaug[:, cc, h, :], start=True, stop=False),
                     reads=[atk, vk], writes=[pyk], tag=f"av_{pass_i}_{cc}_{h}")
                p.op("tensor", lambda e, csl=csl, h=h: e.matmul(
                        py[:], eq_sb[:, h, csl], kv_bf[:, h, :], start=False, stop=True),
                     reads=[eqk, "l1a_kvbf"], writes=[pyk], tag=f"ycr_{pass_i}_{cc}_{h}")
                p.op("vector", lambda e, h=h: e.tensor_copy(
                        ntile[:, h, :], py[:]),
                     reads=[pyk], writes=["l1a_nt"], tag=f"yev_{pass_i}_{cc}_{h}")
                p.op("tensor", lambda e, cc=cc, h=h: e.matmul(
                        pu[:], phik_tm[:, cc, h, :], vaug[:, cc, h, :],
                        start=True, stop=True),
                     reads=[ptmk, vk], writes=[puk], tag=f"u_{pass_i}_{cc}_{h}")
                p.op("vector", lambda e, h=h: e.tensor_add(kv[:, h, :], kv[:, h, :], pu[:]),
                     reads=[puk, "l1a_kv"], writes=["l1a_kv"], tag=f"kvup_{pass_i}_{cc}_{h}")
            gt = (t0 + cc * 128) // 128
            p.dma("gpsimd", numer_d[gt], ntile.ap().rearrange("p h o -> p (h o)"),
                  reads=["l1a_nt"], writes=["l1a_numd"], tag=f"nst_{pass_i}_{cc}")
        ctx.pop_scope()
        ctx.pop_scope()

    for _pi in range(n_pass):
        phase_a(_pi)

    n_rank = len(groups8[0])
    ctx.push_scope()
    ag_in = ctx.dram(ctx.u("l1a_agin"), [128, H * 65], F32)
    ag_out = ctx.dram(ctx.u("l1a_agout"), [n_rank * 128, H * 65], F32)
    p.dma("sync", ag_in, kv.ap().rearrange("p h o -> p (h o)"),
          reads=["l1a_kv"], writes=["l1a_agin"], tag="ag_in_st")
    p.op("gpsimd", lambda e: e.collective_compute(
            "AllGather", ALU.bypass, groups8, [ag_in.opt()], [ag_out.opt()]),
         reads=["l1a_agin"], writes=["l1a_agout"], dma=True, tag="l1a_ag").ccinc1 = True
    kvj = ctx.sb(ctx.u("l1a_kvj"), [128, H * 65], F32)
    kvjk = ctx.u("l1akvjk")
    p.op("vector", lambda e: e.memset(s_in[:], 0.0), writes=["l1a_sin"], tag="sin0")
    for j in range(n_rank):
        p.dma("sync", kvj[:], ag_out[j * 128:(j + 1) * 128, :],
              reads=["l1a_agout"], writes=[kvjk], tag=f"ldkv_{j}")
        p.op("vector", lambda e, j=j: e.scalar_tensor_tensor(
                s_in.ap().rearrange("p h o -> p (h o)"), kvj[:], foldm[:, j:j + 1],
                s_in.ap().rearrange("p h o -> p (h o)"), ALU.mult, ALU.add),
             reads=[kvjk, "l1a_fold", "l1a_sin"], writes=["l1a_sin"],
             tag=f"fold_{j}")
    p.op("vector", lambda e: e.tensor_copy(s_in_bf[:], s_in[:]),
         reads=["l1a_sin"], writes=["l1a_sinbf"], tag="sinbf")
    ctx.pop_scope()

    def phase_b(pass_i):
        t0 = pass_i * TP
        ctx.push_scope()
        eq_sb = ctx.sb(ctx.u("l1a_eqB"), [128, H, TP], BF16)
        eqk = ctx.u("l1a_eqBk")
        emit_eq(pass_i, eq_sb, eqk, f"eqB{pass_i}")
        y_fm = ctx.sb(ctx.u("l1a_yfm"), [128, PC, TP], BF16)
        yfk = ctx.u("l1a_yfmk")
        ctx.push_scope()
        pc2 = ctx.ps(ctx.u("l1a_pc2"), [128, 65], F32)
        pck = ctx.u("pc2k")
        ytm = ctx.sb(ctx.u("l1a_ytm"), [128, 64], BF16)
        ytmk = ctx.u("ytmk")
        ptr2 = ctx.ps(ctx.u("l1a_ptr2"), [64, 128], BF16)
        ptr2k = ctx.u("ptr2k")
        rcp = ctx.sb(ctx.u("l1a_rcp"), [128, 1], F32)
        rcpk = ctx.u("rcpk")
        for tt in range(n_tt):
            gt = (t0 + tt * 128) // 128
            csl = slice(tt * 128, (tt + 1) * 128)
            p.dma("gpsimd", ntileB.ap().rearrange("p h o -> p (h o)"), numer_d[gt],
                  reads=["l1a_numd"], writes=["l1a_ntB"], tag=f"nld_{pass_i}_{tt}")
            for h in range(H):
                p.op("tensor", lambda e, csl=csl, h=h: e.matmul(
                        pc2[:], eq_sb[:, h, csl], s_in_bf[:, h, :], start=True, stop=True),
                     reads=[eqk, "l1a_sinbf"], writes=[pck], tag=f"corr_{pass_i}_{tt}_{h}")
                p.op("vector", lambda e, h=h: e.tensor_add(
                        pc2[:], pc2[:], ntileB[:, h, :]),
                     reads=["l1a_ntB", pck], writes=[pck], tag=f"cadd_{pass_i}_{tt}_{h}")
                p.op("vector", lambda e: e.tensor_scalar_max(rcp[:], pc2[:, 64:65], 1e-6),
                     reads=[pck], writes=[rcpk], tag=f"dmax_{pass_i}_{tt}_{h}")
                p.op("vector", lambda e: e.reciprocal(rcp[:], rcp[:]),
                     reads=[rcpk], writes=[rcpk], tag=f"drcp_{pass_i}_{tt}_{h}")
                p.op("vector", lambda e: e.tensor_scalar(
                        ytm[:], pc2[:, 0:64], rcp[:], None, ALU.mult),
                     reads=[pck, rcpk], writes=[ytmk], tag=f"ydiv_{pass_i}_{tt}_{h}")
                p.op("tensor", lambda e: e.transpose(ptr2[:], ytm[:], ctx.ident[:]),
                     reads=[ytmk, "identk"], writes=[ptr2k], tag=f"ytr_{pass_i}_{tt}_{h}")
                ci, po = h // 2, (h % 2) * 64
                p.op("vector", lambda e, ci=ci, po=po, csl=csl: e.tensor_copy(
                        y_fm[po:po + 64, ci, csl], ptr2[:]),
                     reads=[ptr2k], writes=[yfk], tag=f"ytrc_{pass_i}_{tt}_{h}")
        ctx.pop_scope()
        outp = ctx.sb(ctx.u("l1a_out"), [128, PC, TP], BF16)
        outk = ctx.u("l1a_outk")

        def o_ep(mc, psum, pkey):
            p.op("scalar", lambda e, mc=mc, psum=psum: e.copy(outp[:, mc, :], psum),
                 reads=[pkey], writes=[outk], tag=f"oev_{pass_i}_{mc}")
        emit_proj_stream(ctx, f"l1ao{pass_i}", st_out, inp["w_outT"], PC,
                         lambda kc, y_fm=y_fm: y_fm[:, kc, :], yfk, o_ep, TP)
        emit_ln_fm(ctx, f"l1alno{pass_i}", lambda ci, outp=outp: outp[:, ci, :], outk,
                   accum_res=(lambda ci, t0=t0: x_fn(ci)[:, t0:t0 + TP], x_key),
                   src_bf16=True, tok=TP)
        ctx.pop_scope()

    for _pi in range(n_pass):
        phase_b(_pi)
    ctx.pop_scope()



WKV_L = 128


def emit_l0_attn(ctx, x_fn, x_key, inp, groups8):
    nc, p, tok = ctx.nc, ctx.p, ctx.tok
    TP = min(512, tok)
    n_pass = tok // TP
    n_ch = TP // WKV_L
    n_tt = TP // 128
    NPT = 16
    NG = 16
    ctx.push_scope()

    st_pad = mk_stream(ctx, "l0apad", 2048)
    st_sq = mk_stream(ctx, "l0asq", C)
    w_maaw1 = emit_weight_load(ctx, "l0a_maaw1", inp["maaw1"], PC, 128)
    maaw2 = ctx.sb("l0a_maaw2s", [128, C], BF16)
    p.dma("gpsimd", maaw2[:], inp["maaw2"], writes=["l0a_maaw2"], tag="ld_maaw2")
    w_dw1 = emit_weight_load(ctx, "l0a_dw1", inp["decayw1"], PC, 64)
    dw2 = ctx.sb("l0a_dw2s", [64, C], BF16)
    p.dma("gpsimd", dw2[:], inp["decayw2"], writes=["l0a_dw2"], tag="ld_dw2")
    dw2p = ctx.sb("l0a_dw2ps", [64, 2048], BF16)
    p.dma("gpsimd", dw2p[:], inp["decayw2p"], writes=["l0a_dw2p"], tag="ld_dw2p")
    maa = {}
    for nm in ("maax", "maaw", "maak", "maav", "maar"):
        t = ctx.sb(f"l0a_{nm}s", [128, PC], F32)
        p.dma("sync", t[:], inp[nm], writes=["l0a_maa"], tag=f"ld_{nm}")
        maa[nm] = t
    decvec_rep = ctx.sb("l0a_dvrep", [128, C], F32)
    p.dma("sync", decvec_rep[:], inp["decvec_rep"], writes=["l0a_dvrep"], tag="ld_dvrep")
    decvec_pad = ctx.sb("l0a_dvpad", [128, NPT], F32)
    p.dma("sync", decvec_pad[:], inp["decvec_pad"], writes=["l0a_dvpad"], tag="ld_dvpad")
    ublk = ctx.sb("l0a_ublks", [128, 64], BF16)
    p.dma("gpsimd", ublk[:], inp["ublk"], writes=["l0a_ublk"], tag="ld_ublk")
    lng_rep = ctx.sb("l0a_lngrep", [128, C], BF16)
    lnb_rep = ctx.sb("l0a_lnbrep", [128, C], BF16)
    p.dma("gpsimd", lng_rep[:], inp["lng_rep"], writes=["l0a_lngb"], tag="ld_lng")
    p.dma("gpsimd", lnb_rep[:], inp["lnb_rep"], writes=["l0a_lngb"], tag="ld_lnb")
    trif = ctx.sb("l0a_trifs", [128, 128], F32)
    p.dma("sync", trif[:], inp["tri_incl"], writes=["l0a_trif"], tag="ld_trif")
    mask4 = ctx.sb("l0a_mask4s", [128, 4 * 128], BF16)
    p.dma("gpsimd", mask4[:], inp["mask_strict4"], writes=["l0a_mask4"], tag="ld_mask4")
    xprev = ctx.sb("l0a_xprevs", [128, PC], F32)
    p.dma("sync", xprev[:], inp["xprev"], writes=["l0a_xprev"], tag="ld_xprev")
    foldm = ctx.sb("l0a_folds", [128, 8], F32)
    p.dma("sync", foldm[:], inp["foldmask"], writes=["l0a_fold"], tag="ld_foldl0")
    zrow = ctx.sb("l0a_zrow", [1, 128], BF16)
    p.op("vector", lambda e: e.memset(zrow[:], 0.0), writes=["l0a_zrow"], tag="zrow")
    zrow512 = ctx.sb("l0a_zrow512", [1, 512], BF16)
    p.op("vector", lambda e: e.memset(zrow512[:], 0.0), writes=["l0a_zrow"], tag="zrow512")

    n_chunk_tot = tok // WKV_L
    lnprev_all = ctx.sb("l0a_lnprev_all", [128, PC, tok // TP], BF16)
    rawu_d = ctx.dram("l0a_rawud", [n_chunk_tot, 128, NG * 16], BF16)
    dstore = ctx.sb("l0a_dst", [128, n_chunk_tot, NG], F32)
    S = ctx.sb("l0a_S", [128, NG, 16], F32)
    Sblk = ctx.sb("l0a_Sblk", [128, NG, 64], BF16)
    p.op("vector", lambda e: e.memset(S[:], 0.0), writes=["l0a_S"], tag="S0")
    p.op("vector", lambda e: e.memset(Sblk[:], 0.0), writes=["l0a_Sblk"], tag="Sblk0")

    def mixes_prep(pass_i, phase):
        t0 = pass_i * TP
        ln_p = ctx.sb(ctx.u("l0a_ln"), [128, PC, TP], BF16)
        lnk = ctx.u("l0a_lnk")
        xx = ctx.sb(ctx.u("l0a_xx"), [128, PC, TP], BF16)
        xxk = ctx.u("l0a_xxk")
        m_fm = ctx.sb(ctx.u("l0a_m"), [128, TP], BF16)
        mk = ctx.u("l0a_mk")
        tmpf = ctx.sb(ctx.u("l0a_tmpf"), [128, TP], F32)
        tfk = ctx.u("tfk")
        ctx.push_scope()
        emit_ln_fm(ctx, f"l0aln{phase}{pass_i}",
                   lambda ci, t0=t0: x_fn(ci)[:, t0:t0 + TP], x_key,
                   out_fn=lambda ci, ln_p=ln_p, lnk=lnk: (ln_p[:, ci, :], lnk), tok=TP)
        lpk = "l0a_lnpa"
        if phase == "A":
            if pass_i == 0:
                emit_ln_col(ctx, f"lp{phase}{pass_i}",
                            lambda ci: xprev[:, ci:ci + 1], "l0a_xprev",
                            lambda ci, pi=pass_i: lnprev_all[:, ci, pi:pi + 1], lpk)
            else:
                emit_ln_col(ctx, f"lp{phase}{pass_i}",
                            lambda ci, t0=t0: x_fn(ci)[:, t0 - 1:t0], x_key,
                            lambda ci, pi=pass_i: lnprev_all[:, ci, pi:pi + 1], lpk)
        for ci in range(PC):
            p.op("vector", lambda e, ci=ci: e.tensor_sub(
                    xx[:, ci, 1:TP], ln_p[:, ci, 0:TP - 1], ln_p[:, ci, 1:TP]),
                 reads=[lnk], writes=[xxk], tag=f"xx_{phase}{pass_i}_{ci}")
            p.op("vector", lambda e, ci=ci, pi=pass_i: e.tensor_sub(
                    xx[:, ci, 0:1], lnprev_all[:, ci, pi:pi + 1], ln_p[:, ci, 0:1]),
                 reads=[lnk, lpk], writes=[xxk], tag=f"xx0_{phase}{pass_i}_{ci}")
        xxx_t = ctx.sb(ctx.u("l0a_xxx"), [128, 2, TP], BF16)
        xxxk = [ctx.u("l0a_xxxk") for _ in range(2)]
        pm = ctx.ps(ctx.u("l0a_pm"), [128, TP], F32)
        pmk = ctx.u("pmk")
        for kc in range(PC):
            b = kc % 2
            p.op("vector", lambda e, kc=kc, b=b: e.scalar_tensor_tensor(
                    xxx_t[:, b, :], xx[:, kc, :], maa["maax"][:, kc:kc + 1],
                    ln_p[:, kc, :], ALU.mult, ALU.add),
                 reads=[xxk, lnk, "l0a_maa"], writes=[xxxk[b]],
                 tag=f"xxx_{phase}{pass_i}_{kc}")
            p.op("tensor", lambda e, kc=kc, b=b: e.matmul(
                    pm[:], w_maaw1[kc][:], xxx_t[:, b, :],
                    start=(kc == 0), stop=(kc == PC - 1)),
                 reads=["l0a_maaw1", xxxk[b]], writes=[pmk], tag=f"mm_m_{phase}{pass_i}_{kc}")
        p.op("scalar", lambda e: e.activation(m_fm[:], pm[:], AF.Tanh),
             reads=[pmk], writes=[mk], tag=f"mtanh_{phase}{pass_i}")
        ctx.pop_scope()
        return {"ln": (ln_p, lnk), "xx": (xx, xxk), "m": (m_fm, mk),
                "tmpf": (tmpf, tfk), "phase": phase, "pass_i": pass_i}

    _forder = {"w": 0, "k": 1, "v": 2, "r": 3}

    def mix_field(prep, f, xf, xfk):
        fi = _forder[f]
        ln_p, lnk = prep["ln"]
        xx, xxk = prep["xx"]
        m_fm, mk = prep["m"]
        tmpf, tfk = prep["tmpf"]
        phase, pass_i = prep["phase"], prep["pass_i"]
        ctx.push_scope()
        pf = ctx.ps(ctx.u("l0a_pf"), [128, TP], F32)
        pfk = ctx.u("pfk")
        for mc in range(PC):
            p.op("tensor", lambda e, fi=fi, mc=mc: e.matmul(
                    pf[:], maaw2[32 * fi:32 * fi + 32, mc * 128:(mc + 1) * 128],
                    m_fm[32 * fi:32 * fi + 32, :], start=True, stop=True,
                    tile_position=(32 * fi, 0)),
                 reads=["l0a_maaw2", mk], writes=[pfk], tag=f"mf_{phase}{pass_i}_{f}_{mc}")
            p.op("vector", lambda e, mc=mc, f=f: e.tensor_scalar(
                    tmpf[:], pf[:], maa["maa" + f][:, mc:mc + 1], None, ALU.add),
                 reads=[pfk, "l0a_maa"], writes=[tfk], tag=f"mfa_{phase}{pass_i}_{f}_{mc}")
            p.op("vector", lambda e, mc=mc: e.tensor_mul(tmpf[:], tmpf[:], xx[:, mc, :]),
                 reads=[tfk, xxk], writes=[tfk], tag=f"mfb_{phase}{pass_i}_{f}_{mc}")
            p.op("gpsimd", lambda e, mc=mc, xf=xf: e.tensor_add(
                    xf[:, mc, :], tmpf[:], ln_p[:, mc, :]),
                 reads=[tfk, lnk], writes=[xfk], tag=f"mfc_{phase}{pass_i}_{f}_{mc}")
        ctx.pop_scope()

    def decay_tanh(xw, xwk, tagp):
        wt = ctx.sb(ctx.u("l0a_wt"), [64, TP], BF16)
        wtk = ctx.u("l0a_wtk")
        ctx.push_scope()
        pd = ctx.ps(ctx.u("l0a_pd"), [64, TP], F32)
        pdk = ctx.u("pdk")
        for kc in range(PC):
            p.op("tensor", lambda e, kc=kc: e.matmul(
                    pd[:], w_dw1[kc][:], xw[:, kc, :],
                    start=(kc == 0), stop=(kc == PC - 1)),
                 reads=["l0a_dw1", xwk], writes=[pdk], tag=f"dmm_{tagp}_{kc}")
        p.op("scalar", lambda e: e.activation(wt[:], pd[:], AF.Tanh),
             reads=[pdk], writes=[wtk], tag=f"dtanh_{tagp}")
        ctx.pop_scope()
        return wt, wtk

    def phase_a(pass_i):
        ctx.push_scope()
        ktm = ctx.sb(ctx.u("l0a_ktm"), [128, n_tt, C], BF16)
        ktmk = ctx.u("ktmk")
        vtm = ctx.sb(ctx.u("l0a_vtm"), [128, n_tt, C], BF16)
        vtmk = ctx.u("vtmk")
        xfb = ctx.sb(ctx.u("l0a_xfb"), [128, PC, TP], BF16)
        xfbk = ctx.u("l0a_xfbk")
        prep = mixes_prep(pass_i, "A")
        mix_field(prep, "w", xfb, xfbk)
        wt, wtk = decay_tanh(xfb, xfbk, f"A{pass_i}")
        mix_field(prep, "k", xfb, xfbk)

        def k_cb(tt, nb, psum, pkey):
            p.op("vector", lambda e, tt=tt, nb=nb, psum=psum: e.tensor_copy(
                    ktm[:, tt, nb * 512:(nb + 1) * 512], psum),
                 reads=[pkey], writes=[ktmk], tag=f"kte_{pass_i}_{tt}_{nb}")
        emit_tm_proj(ctx, f"l0akt{pass_i}", st_sq, inp["Wk"], PC, 0, C,
                     lambda kc: xfb[:, kc, :], xfbk, k_cb, TP)
        mix_field(prep, "v", xfb, xfbk)

        def v_cb(tt, nb, psum, pkey):
            p.op("vector", lambda e, tt=tt, nb=nb, psum=psum: e.tensor_copy(
                    vtm[:, tt, nb * 512:(nb + 1) * 512], psum),
                 reads=[pkey], writes=[vtmk], tag=f"vte_{pass_i}_{tt}_{nb}")
        emit_tm_proj(ctx, f"l0avt{pass_i}", st_sq, inp["Wv"], PC, 0, C,
                     lambda kc: xfb[:, kc, :], xfbk, v_cb, TP)
        ctx.push_scope()
        pwp = ctx.ps(ctx.u("l0a_pwp"), [128, TP], F32)
        pwpk = ctx.u("pwpk")
        ewpad = ctx.sb(ctx.u("l0a_ewpad"), [128, TP], F32)
        ewpk = ctx.u("ewpadk")
        gl = ctx.sb(ctx.u("l0a_gl"), [128, n_ch, NPT], F32)
        glk = ctx.u("glk")
        for pt in range(NPT):
            p.op("tensor", lambda e, pt=pt: e.matmul(
                    pwp[:], dw2p[:, pt * 128:(pt + 1) * 128], wt[:],
                    start=True, stop=True),
                 reads=["l0a_dw2p", wtk], writes=[pwpk], tag=f"dpp_{pass_i}_{pt}")
            p.op("scalar", lambda e, pt=pt: e.activation(
                    ewpad[:], pwp[:], AF.Exp, bias=decvec_pad[:, pt:pt + 1]),
                 reads=[pwpk, "l0a_dvpad"], writes=[ewpk], tag=f"dpe_{pass_i}_{pt}")
            for cc in range(n_ch):
                p.op("vector", lambda e, pt=pt, cc=cc: e.reduce_sum(
                        gl[:, cc, pt:pt + 1], ewpad[:, cc * WKV_L:(cc + 1) * WKV_L], AX.X),
                     reads=[ewpk], writes=[glk], tag=f"dps_{pass_i}_{pt}_{cc}")
        for cc in range(n_ch):
            gchunk = pass_i * n_ch + cc
            p.op("scalar", lambda e, cc=cc, gchunk=gchunk: e.activation(
                    dstore[:, gchunk, :], gl[:, cc, :], AF.Exp, scale=-1.0),
                 reads=[glk], writes=["l0a_dstore"], tag=f"dexp_{pass_i}_{cc}")
        pw = ctx.ps(ctx.u("l0a_pw"), [128, C], F32)
        pwk = ctx.u("pwk")
        ewtm = ctx.sb(ctx.u("l0a_ewtm"), [128, C], F32)
        ewk = ctx.u("ewtmk")
        gtm = ctx.sb(ctx.u("l0a_gtm"), [128, C], F32)
        gk = ctx.u("gtmk")
        pg = ctx.ps(ctx.u("l0a_pg"), [128, C], F32)
        pgk = ctx.u("pgk")
        kptm = ctx.sb(ctx.u("l0a_kptm"), [128, C], BF16)
        kpk = ctx.u("kptmk")
        pru = ctx.ps(ctx.u("l0a_pru"), [128, NG * 16], F32)
        pruk = ctx.u("pruk")
        rutile = ctx.sb(ctx.u("l0a_rutile"), [128, NG * 16], BF16)
        for tt in range(n_tt):
            for nb in range(2):
                p.op("tensor", lambda e, tt=tt, nb=nb: e.matmul(
                        pw[:, nb * 512:(nb + 1) * 512], wt[:, tt * 128:(tt + 1) * 128],
                        dw2[:, nb * 512:(nb + 1) * 512], start=True, stop=True),
                     reads=[wtk, "l0a_dw2"], writes=[pwk], tag=f"wtm_{pass_i}_{tt}_{nb}")
            p.op("vector", lambda e: e.tensor_add(ewtm[:], pw[:], decvec_rep[:]),
                 reads=[pwk, "l0a_dvrep"], writes=[ewk], tag=f"ewadd_{pass_i}_{tt}")
            p.op("scalar", lambda e: e.activation(ewtm[:], ewtm[:], AF.Exp),
                 reads=[ewk], writes=[ewk], tag=f"ewexp_{pass_i}_{tt}")
            for nb in range(2):
                p.op("tensor", lambda e, nb=nb: e.matmul(
                        pg[:, nb * 512:(nb + 1) * 512], trif[:],
                        ewtm[:, nb * 512:(nb + 1) * 512], start=True, stop=True),
                     reads=[ewk, "l0a_trif"], writes=[pgk], tag=f"gtm_{pass_i}_{tt}_{nb}")
            p.op("scalar", lambda e: e.activation(gtm[:], pg[:], AF.Exp),
                 reads=[pgk], writes=[gk], tag=f"gexp_{pass_i}_{tt}")
            p.op("vector", lambda e, tt=tt: e.tensor_mul(kptm[:], ktm[:, tt, :], gtm[:]),
                 reads=[gk, ktmk], writes=[kpk], tag=f"kp_{pass_i}_{tt}")
            gchunk = pass_i * n_ch + tt
            for g in range(NG):
                p.op("tensor", lambda e, g=g: e.matmul(
                        pru[:, g * 16:g * 16 + 16], zrow[:], zrow[:, 0:16],
                        start=True, stop=False),
                     reads=["l0a_zrow"], writes=[pruk], tag=f"ruz_{pass_i}_{tt}_{g}")
                for j in range(4):
                    h = g * 4 + j
                    p.op("tensor", lambda e, g=g, j=j, h=h, tt=tt: e.matmul(
                            pru[32 * j:32 * j + 16, g * 16:g * 16 + 16],
                            kptm[:, h * 16:h * 16 + 16],
                            vtm[:, tt, h * 16:h * 16 + 16],
                            start=False, stop=False,
                            tile_position=(0, 32 * j)),
                         reads=[kpk, vtmk], writes=[pruk], tag=f"ru_{pass_i}_{tt}_{g}_{j}")
                p.op("tensor", lambda e, g=g: e.matmul(
                        pru[:, g * 16:g * 16 + 16], zrow[:], zrow[:, 0:16],
                        start=False, stop=True),
                     reads=["l0a_zrow"], writes=[pruk], tag=f"ruc_{pass_i}_{tt}_{g}")
            p.op("vector", lambda e: e.tensor_copy(rutile[:], pru[:]),
                 reads=[pruk], writes=["l0a_rut"], tag=f"ruev_{pass_i}_{tt}")
            p.dma("gpsimd", rawu_d[gchunk], rutile[:],
                  reads=["l0a_rut"], writes=["l0a_rawu"], tag=f"rust_{pass_i}_{tt}")
            p.op("vector", lambda e: e.tensor_add(
                    S.ap().rearrange("p g j -> p (g j)"),
                    S.ap().rearrange("p g j -> p (g j)"), rutile[:]),
                 reads=["l0a_rut", "l0a_S"], writes=["l0a_S"], tag=f"sadd_{pass_i}_{tt}")
            p.op("vector", lambda e, gchunk=gchunk: e.tensor_mul(
                    S[:], S[:], dstore[:, gchunk, :].unsqueeze(2).broadcast_to([128, NG, 16])),
                 reads=["l0a_dstore", "l0a_S"], writes=["l0a_S"], tag=f"smul_{pass_i}_{tt}")
        ctx.pop_scope()
        ctx.pop_scope()

    for _pi in range(n_pass):
        phase_a(_pi)

    ctx.push_scope()
    dtot = ctx.sb(ctx.u("l0a_dtot"), [128, NG], F32)
    dtk = ctx.u("dtotk")
    p.op("vector", lambda e: e.tensor_copy(dtot[:], dstore[:, 0, :]),
         reads=["l0a_dstore"], writes=[dtk], tag="dtot0")
    for gc in range(1, n_chunk_tot):
        p.op("vector", lambda e, gc=gc: e.tensor_mul(dtot[:], dtot[:], dstore[:, gc, :]),
             reads=["l0a_dstore", dtk], writes=[dtk], tag=f"dtot{gc}")
    n_rank = len(groups8[0])
    ag_in = ctx.dram(ctx.u("l0a_agin"), [128, NG * 16 + NG], F32)
    ag_out = ctx.dram(ctx.u("l0a_agout"), [n_rank * 128, NG * 16 + NG], F32)
    p.dma("sync", ag_in[:, 0:NG * 16], S.ap().rearrange("p g j -> p (g j)"),
          reads=["l0a_S"], writes=["l0a_agin"], tag="l0a_agst1")
    p.dma("sync", ag_in[:, NG * 16:], dtot[:],
          reads=[dtk], writes=["l0a_agin2"], tag="l0a_agst2")
    p.op("gpsimd", lambda e: e.collective_compute(
            "AllGather", ALU.bypass, groups8, [ag_in.opt()], [ag_out.opt()]),
         reads=["l0a_agin", "l0a_agin2"], writes=["l0a_agout"], dma=True,
         tag="l0a_ag").ccinc1 = True
    sg = ctx.sb(ctx.u("l0a_sg"), [128, NG, 16], F32)
    sgk = ctx.u("sgk")
    buf = ctx.sb(ctx.u("l0a_agbuf"), [128, NG * 16 + NG], F32)
    bk2 = ctx.u("agbufk")
    dfac = ctx.sb(ctx.u("l0a_dfac"), [128, NG], F32)
    dfk = ctx.u("dfack")
    p.op("vector", lambda e: e.memset(sg[:], 0.0), writes=[sgk], tag="sg0")
    for j in range(n_rank):
        p.dma("sync", buf[:], ag_out[j * 128:(j + 1) * 128, :],
              reads=["l0a_agout"], writes=[bk2], tag=f"l0a_ldag{j}")
        p.op("vector", lambda e: e.tensor_scalar(dfac[:], buf[:, NG * 16:], -1.0, None, ALU.add),
             reads=[bk2], writes=[dfk], tag=f"l0a_df1_{j}")
        p.op("vector", lambda e, j=j: e.tensor_scalar(
                dfac[:], dfac[:], foldm[:, j:j + 1], 1.0, ALU.mult, ALU.add),
             reads=[dfk, "l0a_fold"], writes=[dfk], tag=f"l0a_df2_{j}")
        p.op("vector", lambda e: e.tensor_mul(
                sg[:], sg[:], dfac.ap().unsqueeze(2).broadcast_to([128, NG, 16])),
             reads=[dfk, sgk], writes=[sgk], tag=f"l0a_sgm_{j}")
        p.op("vector", lambda e, j=j: e.scalar_tensor_tensor(
                sg.ap().rearrange("p g j -> p (g j)"), buf[:, 0:NG * 16],
                foldm[:, j:j + 1], sg.ap().rearrange("p g j -> p (g j)"),
                ALU.mult, ALU.add),
             reads=[bk2, "l0a_fold", sgk], writes=[sgk], tag=f"l0a_sga_{j}")
    p.op("vector", lambda e: e.tensor_copy(S[:], sg[:]),
         reads=[sgk], writes=["l0a_S"], tag="l0a_sreset")
    ctx.pop_scope()

    def phase_b(pass_i):
        t0 = pass_i * TP
        ctx.push_scope()
        rpad = ctx.sb(ctx.u("l0a_rpad"), [128, NPT, TP], BF16)
        rpk = ctx.u("rpadk")
        kpad = ctx.sb(ctx.u("l0a_kpad"), [128, NPT, TP], BF16)
        kpdk = ctx.u("kpadk")
        vtm = ctx.sb(ctx.u("l0a_vtmB"), [128, n_tt, C], BF16)
        vtmk = ctx.u("vtmBk")
        du = ctx.sb(ctx.u("l0a_du"), [128, n_tt, 64], F32)
        duk = ctx.u("duk")
        ytm = ctx.sb(ctx.u("l0a_ytm"), [128, n_tt, C], BF16)
        ytmk = ctx.u("ytmk")
        y_fm = ctx.sb(ctx.u("l0a_yfm"), [128, PC, TP], BF16)
        yfk = ctx.u("yfmk")
        ctx.push_scope()
        xfb = ctx.sb(ctx.u("l0a_xfbB"), [128, PC, TP], BF16)
        xfbk = ctx.u("l0a_xfbBk")
        prep = mixes_prep(pass_i, "B")

        def dbg_dump(f):
            if "dbg_xr" in inp:
                p.dma("gpsimd",
                      inp["dbg_x" + f][:, t0:t0 + TP].rearrange("(a p) t -> p a t", p=128),
                      xfb[:], reads=[xfbk],
                      writes=["dbgx" + f], tag=f"dbgx_{f}_{pass_i}")
        mix_field(prep, "w", xfb, xfbk)
        dbg_dump("w")
        wt, wtk = decay_tanh(xfb, xfbk, f"B{pass_i}")
        mix_field(prep, "r", xfb, xfbk)
        dbg_dump("r")

        def rp_ep(mc, psum, pkey):
            p.op("vector", lambda e, mc=mc, psum=psum: e.tensor_copy(rpad[:, mc, :], psum),
                 reads=[pkey], writes=[rpk], tag=f"rpev_{pass_i}_{mc}")
        emit_proj_stream(ctx, f"l0arp{pass_i}", st_pad, inp["Wrp"], PC,
                         lambda kc: xfb[:, kc, :], xfbk, rp_ep, TP)
        mix_field(prep, "k", xfb, xfbk)
        dbg_dump("k")

        def kp_ep(mc, psum, pkey):
            p.op("vector", lambda e, mc=mc, psum=psum: e.tensor_copy(kpad[:, mc, :], psum),
                 reads=[pkey], writes=[kpdk], tag=f"kpev_{pass_i}_{mc}")
        emit_proj_stream(ctx, f"l0akp{pass_i}", st_pad, inp["Wkp"], PC,
                         lambda kc: xfb[:, kc, :], xfbk, kp_ep, TP)
        mix_field(prep, "v", xfb, xfbk)
        dbg_dump("v")

        def v_cb(tt, nb, psum, pkey):
            p.op("vector", lambda e, tt=tt, nb=nb, psum=psum: e.tensor_copy(
                    vtm[:, tt, nb * 512:(nb + 1) * 512], psum),
                 reads=[pkey], writes=[vtmk], tag=f"vteB_{pass_i}_{tt}_{nb}")
        emit_tm_proj(ctx, f"l0avtB{pass_i}", st_sq, inp["Wv"], PC, 0, C,
                     lambda kc: xfb[:, kc, :], xfbk, v_cb, TP)
        ctx.push_scope()
        rk_sb = ctx.sb(ctx.u("l0a_rk"), [128, TP], BF16)
        rkk = ctx.u("rkk")
        pdu = ctx.ps(ctx.u("l0a_pdu"), [128, 64], F32)
        pduk = ctx.u("pduk")
        for pt in range(NPT):
            p.op("vector", lambda e, pt=pt: e.tensor_mul(
                    rk_sb[:], rpad[:, pt, :], kpad[:, pt, :]),
                 reads=[rpk, kpdk], writes=[rkk], tag=f"rk_{pass_i}_{pt}")
            for cc in range(n_ch):
                p.op("tensor", lambda e, pt=pt, cc=cc: e.matmul(
                        pdu[:, pt * 4:pt * 4 + 4], rk_sb[:, cc * WKV_L:(cc + 1) * WKV_L],
                        ublk[:, pt * 4:pt * 4 + 4], start=True, stop=True),
                     reads=[rkk, "l0a_ublk"], writes=[pduk], tag=f"du_{pass_i}_{pt}_{cc}")
                p.op("vector", lambda e, pt=pt, cc=cc: e.tensor_copy(
                        du[:, cc, pt * 4:pt * 4 + 4], pdu[:, pt * 4:pt * 4 + 4]),
                     reads=[pduk], writes=[duk], tag=f"duev_{pass_i}_{pt}_{cc}")
        ctx.pop_scope()
        ctx.push_scope()
        pwp = ctx.ps(ctx.u("l0a_pwpB"), [128, TP], F32)
        pwpk = ctx.u("pwpBk")
        ew_f = ctx.sb(ctx.u("l0a_ewf"), [128, TP], F32)
        ewfk = ctx.u("ewfk")
        g_f = ctx.sb(ctx.u("l0a_gf"), [128, TP], F32)
        gfk = ctx.u("gfk")
        es_f = ctx.sb(ctx.u("l0a_esf"), [128, TP], F32)
        esk = ctx.u("esfk")
        for pt in range(NPT):
            p.op("tensor", lambda e, pt=pt: e.matmul(
                    pwp[:], dw2p[:, pt * 128:(pt + 1) * 128], wt[:],
                    start=True, stop=True),
                 reads=["l0a_dw2p", wtk], writes=[pwpk], tag=f"dppB_{pass_i}_{pt}")
            p.op("scalar", lambda e, pt=pt: e.activation(
                    ew_f[:], pwp[:], AF.Exp, bias=decvec_pad[:, pt:pt + 1]),
                 reads=[pwpk, "l0a_dvpad"], writes=[ewfk], tag=f"dpeB_{pass_i}_{pt}")
            for cc in range(n_ch):
                sl = slice(cc * WKV_L, (cc + 1) * WKV_L)
                p.op("vector", lambda e, sl=sl: e.tensor_tensor_scan(
                        g_f[:, sl], ctx.ones_col_f32.ap().broadcast_to([128, WKV_L]),
                        ew_f[:, sl], 0.0, ALU.mult, ALU.add),
                     reads=[ewfk, "ones"], writes=[gfk], tag=f"scan_{pass_i}_{pt}_{cc}")
            p.op("scalar", lambda e: e.activation(es_f[:], g_f[:], AF.Exp),
                 reads=[gfk], writes=[esk], tag=f"ekp_{pass_i}_{pt}")
            p.op("vector", lambda e, pt=pt: e.tensor_mul(
                    kpad[:, pt, :], kpad[:, pt, :], es_f[:]),
                 reads=[esk, kpdk], writes=[kpdk], tag=f"kps_{pass_i}_{pt}")
            p.op("vector", lambda e: e.tensor_sub(g_f[:], ew_f[:], g_f[:]),
                 reads=[ewfk, gfk], writes=[gfk], tag=f"qarg_{pass_i}_{pt}")
            p.op("scalar", lambda e: e.activation(es_f[:], g_f[:], AF.Exp),
                 reads=[gfk], writes=[esk], tag=f"eqp_{pass_i}_{pt}")
            p.op("vector", lambda e, pt=pt: e.tensor_mul(
                    rpad[:, pt, :], rpad[:, pt, :], es_f[:]),
                 reads=[esk, rpk], writes=[rpk], tag=f"qps_{pass_i}_{pt}")
        ctx.pop_scope()
        ctx.pop_scope()
        ctx.push_scope()
        pat4 = ctx.ps(ctx.u("l0a_pat"), [128, 4, 512], F32)
        patk = ctx.u("patk")
        at_sb = ctx.sb(ctx.u("l0a_at"), [128, 4 * 128], BF16)
        atk = ctx.u("atk")
        pY = ctx.ps(ctx.u("l0a_pY"), [128, C], F32)
        pYk = ctx.u("pYk")
        vdu = ctx.sb(ctx.u("l0a_vdu"), [128, C], F32)
        vduk = ctx.u("vduk")
        rutileB = ctx.sb(ctx.u("l0a_rutileB"), [128, NG * 16], BF16)
        for cc in range(n_ch):
            csl = slice(cc * WKV_L, (cc + 1) * WKV_L)
            gchunk = pass_i * n_ch + cc
            for j in range(4):
                p.op("vector", lambda e, j=j: e.tensor_copy(
                        Sblk[32 * j:32 * j + 32, :, j * 16:(j + 1) * 16],
                        S[32 * j:32 * j + 32, :, :]),
                     reads=["l0a_S"], writes=["l0a_Sblk"], tag=f"sbf_{pass_i}_{cc}_{j}")
            for hb in range(2):
                p.op("tensor", lambda e, hb=hb: e.matmul(
                        pY[:, hb * 512:(hb + 1) * 512], zrow[:], zrow512[:],
                        start=True, stop=False),
                     reads=["l0a_zrow"], writes=[pYk], tag=f"yz_{pass_i}_{cc}_{hb}")
            for pt in range(NPT):
                for j in range(4):
                    p.op("tensor", lambda e, pt=pt, j=j, csl=csl: e.matmul(
                            pat4[:, j, 0:128],
                            kpad[32 * j:32 * j + 32, pt, csl],
                            rpad[32 * j:32 * j + 32, pt, csl],
                            start=True, stop=True, tile_position=(32 * j, 0)),
                         reads=[kpdk, rpk], writes=[patk],
                         tag=f"at_{pass_i}_{cc}_{pt}_{j}")
                p.op("vector", lambda e: e.tensor_mul(
                        at_sb.ap().rearrange("p (a b) -> p a b", b=128),
                        pat4[:, :, 0:128],
                        mask4.ap().rearrange("p (a b) -> p a b", b=128)),
                     reads=[patk, "l0a_mask4"], writes=[atk],
                     tag=f"atm_{pass_i}_{cc}_{pt}")
                for j in range(4):
                    h = pt * 4 + j
                    p.op("tensor", lambda e, j=j, h=h, cc=cc: e.matmul(
                            pY[:, h * 16:h * 16 + 16], at_sb[:, j * 128:(j + 1) * 128],
                            vtm[:, cc, h * 16:h * 16 + 16], start=False, stop=False),
                         reads=[atk, vtmk], writes=[pYk], tag=f"av_{pass_i}_{cc}_{pt}_{j}")
                p.op("tensor", lambda e, pt=pt, csl=csl: e.matmul(
                        pY[:, pt * 64:(pt + 1) * 64], rpad[:, pt, csl],
                        Sblk[:, pt, :], start=False, stop=False),
                     reads=[rpk, "l0a_Sblk"], writes=[pYk], tag=f"ycr_{pass_i}_{cc}_{pt}")
            for hb in range(2):
                p.op("tensor", lambda e, hb=hb: e.matmul(
                        pY[:, hb * 512:(hb + 1) * 512], zrow[:], zrow512[:],
                        start=False, stop=True),
                     reads=["l0a_zrow", pYk], writes=[pYk], tag=f"yc_{pass_i}_{cc}_{hb}")
            p.op("vector", lambda e, cc=cc: e.tensor_mul(
                    vdu.ap().rearrange("p (h j) -> p h j", j=16),
                    vtm[:, cc, :].rearrange("p (h j) -> p h j", j=16),
                    du[:, cc, :].unsqueeze(2).broadcast_to([128, 64, 16])),
                 reads=[vtmk, duk], writes=[vduk], tag=f"vdu_{pass_i}_{cc}")
            p.op("vector", lambda e, cc=cc: e.tensor_add(ytm[:, cc, :], vdu[:], pY[:]),
                 reads=[vduk, pYk], writes=[ytmk], tag=f"yadd_{pass_i}_{cc}")
            if "dbg_y" in inp:
                p.dma("gpsimd", inp["dbg_y"][t0 + cc * WKV_L:t0 + (cc + 1) * WKV_L, :],
                      ytm[:, cc, :], reads=[ytmk], writes=["dbgy"],
                      tag=f"dbgy_{pass_i}_{cc}")
            p.dma("gpsimd", rutileB[:], rawu_d[gchunk],
                  reads=["l0a_rawu"], writes=["l0a_rutB"], tag=f"ruld_{pass_i}_{cc}")
            p.op("vector", lambda e: e.tensor_add(
                    S.ap().rearrange("p g j -> p (g j)"),
                    S.ap().rearrange("p g j -> p (g j)"), rutileB[:]),
                 reads=["l0a_rutB", "l0a_S"], writes=["l0a_S"], tag=f"saddB_{pass_i}_{cc}")
            p.op("vector", lambda e, gchunk=gchunk: e.tensor_mul(
                    S[:], S[:], dstore[:, gchunk, :].unsqueeze(2).broadcast_to([128, NG, 16])),
                 reads=["l0a_dstore", "l0a_S"], writes=["l0a_S"], tag=f"smulB_{pass_i}_{cc}")
        ctx.pop_scope()
        ctx.push_scope()
        st2 = ctx.sb(ctx.u("l0a_st2"), [128, 6], F32)
        stk = ctx.u("st2k")
        ysq = ctx.sb(ctx.u("l0a_ysq"), [128, C], BF16)
        ysqk = ctx.u("ysqk")
        ptr3 = ctx.ps(ctx.u("l0a_ptr3"), [128, 128], BF16)
        ptr3k = ctx.u("ptr3k")
        for tt in range(n_tt):
            p.op("vector", lambda e, tt=tt: e.reduce_sum(st2[:, 0:1], ytm[:, tt, :], AX.X),
                 reads=[ytmk], writes=[stk], tag=f"gn_s_{pass_i}_{tt}")
            p.op("scalar", lambda e, tt=tt: e.activation(ysq[:], ytm[:, tt, :], AF.Square,
                                                         accum_out=st2[:, 1:2]),
                 reads=[ytmk], writes=[ysqk, stk], tag=f"gn_sq_{pass_i}_{tt}")
            p.op("vector", lambda e: e.tensor_scalar(st2[:, 2:4], st2[:, 0:2], 1.0 / C, None, ALU.mult),
                 reads=[stk], writes=[stk], tag=f"gn_m_{pass_i}_{tt}")
            p.op("vector", lambda e: e.tensor_mul(st2[:, 4:5], st2[:, 2:3], st2[:, 2:3]),
                 reads=[stk], writes=[stk], tag=f"gn_m2_{pass_i}_{tt}")
            p.op("vector", lambda e: e.tensor_sub(st2[:, 4:5], st2[:, 3:4], st2[:, 4:5]),
                 reads=[stk], writes=[stk], tag=f"gn_v_{pass_i}_{tt}")
            p.op("scalar", lambda e: e.activation(st2[:, 4:5], st2[:, 4:5], AF.Ln, bias=EPS),
                 reads=[stk], writes=[stk], tag=f"gn_l_{pass_i}_{tt}")
            p.op("scalar", lambda e: e.activation(st2[:, 4:5], st2[:, 4:5], AF.Exp, scale=-0.5),
                 reads=[stk], writes=[stk], tag=f"gn_e_{pass_i}_{tt}")
            p.op("vector", lambda e, tt=tt: e.tensor_scalar(
                    ytm[:, tt, :], ytm[:, tt, :], st2[:, 2:3], None, ALU.subtract),
                 reads=[stk, ytmk], writes=[ytmk], tag=f"gn_a1_{pass_i}_{tt}")
            p.op("vector", lambda e, tt=tt: e.tensor_scalar(
                    ytm[:, tt, :], ytm[:, tt, :], st2[:, 4:5], None, ALU.mult),
                 reads=[stk, ytmk], writes=[ytmk], tag=f"gn_a2_{pass_i}_{tt}")
            p.op("gpsimd", lambda e, tt=tt: e.tensor_mul(ytm[:, tt, :], ytm[:, tt, :], lng_rep[:]),
                 reads=["l0a_lngb", ytmk], writes=[ytmk], tag=f"gn_g_{pass_i}_{tt}")
            p.op("gpsimd", lambda e, tt=tt: e.tensor_add(ytm[:, tt, :], ytm[:, tt, :], lnb_rep[:]),
                 reads=["l0a_lngb", ytmk], writes=[ytmk], tag=f"gn_b_{pass_i}_{tt}")
            for ci in range(PC):
                p.op("tensor", lambda e, tt=tt, ci=ci: e.transpose(
                        ptr3[:], ytm[:, tt, ci * 128:(ci + 1) * 128], ctx.ident[:]),
                     reads=[ytmk, "identk"], writes=[ptr3k], tag=f"ytr_{pass_i}_{tt}_{ci}")
                p.op("vector", lambda e, tt=tt, ci=ci: e.tensor_copy(
                        y_fm[:, ci, tt * 128:(tt + 1) * 128], ptr3[:]),
                     reads=[ptr3k], writes=[yfk], tag=f"ytrc_{pass_i}_{tt}_{ci}")
        ctx.pop_scope()

        def o_ep(mc, psum, pkey):
            p.op("scalar", lambda e, mc=mc, psum=psum: e.copy(y_fm[:, mc, :], psum),
                 reads=[pkey], writes=[yfk], tag=f"l0a_oev_{pass_i}_{mc}")
        emit_proj_stream(ctx, f"l0ao{pass_i}", st_sq, inp["Wo"], PC,
                         lambda kc, y_fm=y_fm: y_fm[:, kc, :], yfk, o_ep, TP)
        emit_ln_fm(ctx, f"l0alno{pass_i}", lambda ci: y_fm[:, ci, :], yfk,
                   accum_res=(lambda ci, t0=t0: x_fn(ci)[:, t0:t0 + TP], x_key),
                   src_bf16=True, tok=TP)
        ctx.pop_scope()

    for _pi in range(n_pass):
        phase_b(_pi)
    ctx.pop_scope()




def emit_moe_prep(ctx, my_w1, my_w2, groups_all):
    p = ctx.p
    n = sum(len(g) for g in groups_all) // len(groups_all)
    assert [len(g) for g in groups_all] == [n] * len(groups_all)
    w1loc = ctx.dram(ctx.u("moe_w1loc"), [C, 4096], BF16)
    w2loc = ctx.dram(ctx.u("moe_w2loc"), [4096, C], BF16)
    w1all = ctx.dram(ctx.u("moe_w1all"), [n * C, 4096], BF16)
    w2all = ctx.dram(ctx.u("moe_w2all"), [n * 4096, C], BF16)
    p.dma("gpsimd", w1loc, my_w1, writes=["moe_w1loc"], tag="moec1", nobar=True)
    p.dma("gpsimd", w2loc, my_w2, writes=["moe_w2loc"], tag="moec2", nobar=True)
    o1 = p.op("gpsimd", lambda e: e.collective_compute(
            "AllGather", ALU.bypass, groups_all, [w1loc.opt()], [w1all.opt()]),
         reads=["moe_w1loc"], writes=["moe_w1all"], dma=True, tag="moeag1")
    o1.ccinc1 = True
    o1.nobar = True
    o2 = p.op("gpsimd", lambda e: e.collective_compute(
            "AllGather", ALU.bypass, groups_all, [w2loc.opt()], [w2all.opt()]),
         reads=["moe_w2loc"], writes=["moe_w2all"], dma=True, tag="moeag2")
    o2.ccinc1 = True
    o2.nobar = True
    return w1all, w2all


def emit_l0_ffn(ctx, x_fn, x_key, inp, groups4, moe_w1, moe_w2,
                moe_rk=((), ())):
    nc, p, tok = ctx.nc, ctx.p, ctx.tok
    TP = min(512, tok)
    n_pass = tok // TP
    NR = len(groups4[0])
    HL = 16 // NR
    T_ALL = tok * NR
    n_kt_all = T_ALL // 128
    ctx.push_scope()

    st_w = mk_stream(ctx, "l0f_w", C)
    st_w1 = mk_stream(ctx, "l0f_w1", 1024)
    st_w2 = mk_stream(ctx, "l0f_w2", C)
    biases = ctx.sb("l0f_biases", [128, 4, PC], F32)
    p.dma("sync", biases[:], inp["bqkvo"], writes=["l0f_b"], tag="ld_l0fb")
    bv_rep = ctx.sb("l0f_bvrep", [128, C], F32)
    p.dma("sync", bv_rep[:], inp["bv_rep"], writes=["l0f_bv"], tag="ld_bv")
    ln1g = ctx.sb("l0f_ln1g", [128, 2, PC], F32)
    p.dma("sync", ln1g[:], inp["ln1gb"], writes=["l0f_ln1"], tag="ld_ln1")
    ln2g = ctx.sb("l0f_ln2g", [128, 2, PC], F32)
    p.dma("sync", ln2g[:], inp["ln2gb"], writes=["l0f_ln2"], tag="ld_ln2")
    wr_sb = emit_weight_load(ctx, "l0f_wr", inp["Wr"], PC, 8)
    br_rep = ctx.sb("l0f_br", [128, 8], F32)
    p.dma("sync", br_rep[:], inp["br_rep"], writes=["l0f_br"], tag="ld_br")
    b1_sb = ctx.sb("l0f_b1", [128, 8, 32], F32)
    p.dma("sync", b1_sb[:], inp["b1"], writes=["l0f_b1"], tag="ld_b1")
    b2_sb = ctx.sb("l0f_b2", [128, 8, 8], F32)
    p.dma("sync", b2_sb[:], inp["b2"], writes=["l0f_b2"], tag="ld_b2")
    dmask = ctx.sb("l0f_dmask", [128, 4, 512], BF16)
    p.dma("gpsimd", dmask[:], inp["dmask"], writes=["l0f_dmask"], tag="ld_dmask")
    idxs = {}
    for nm in ("gq_idx", "gk_idx", "gv_idx", "go_idx"):
        w = inp[nm].shape[1]
        t = ctx.sb(f"l0f_{nm}", [128, w], I32)
        p.dma("sync", t[:], inp[nm], writes=["l0f_idx"], tag=f"ld_{nm}")
        idxs[nm] = t

    a_sb = ctx.sb("l0f_a", [128, PC, tok], F32)

    contrib = ctx.dram(ctx.u("l0f_contrib"), [48 * 64, tok], BF16)
    ag1_out = ctx.dram(ctx.u("l0f_ag1"), [NR * 48 * 64, tok], BF16)
    myq = ctx.dram(ctx.u("l0f_myq"), [HL * 64 * NR, tok], BF16)
    myk = ctx.dram(ctx.u("l0f_myk"), [HL * 64 * NR, tok], BF16)
    myv = ctx.dram(ctx.u("l0f_myv"), [HL * NR * (tok // 128), 128 * 64], BF16)
    ocontrib = ctx.dram(ctx.u("l0f_oc"), [HL * 64, T_ALL], BF16)
    ag2_out = ctx.dram(ctx.u("l0f_ag2"), [NR * HL * 64, T_ALL], BF16)
    wdram = ctx.dram(ctx.u("l0f_wdram"), [8, tok], F32)

    vthd = contrib[2048:3072, :].rearrange("(h r) t -> h (r t)", h=16) \
                                .rearrange("h (t d) -> t h d", d=64)

    def f1_pass(pass_i):
        t0 = pass_i * TP
        ctx.push_scope()
        h_sb = ctx.sb(ctx.u("l0f_h"), [128, PC, TP], BF16)
        hk = ctx.u("l0f_hk")
        qtmp = ctx.sb(ctx.u("l0f_qtmp"), [128, TP], BF16)
        qtk = ctx.u("qtmpk")
        emit_ln_fm(ctx, f"l0fsl{pass_i}",
                   lambda ci, t0=t0: x_fn(ci)[:, t0:t0 + TP], x_key,
                   out_fn=lambda ci, t0=t0: (a_sb[:, ci, t0:t0 + TP], "l0f_a"),
                   tok=TP)
        emit_ln_fm(ctx, f"l0fl1{pass_i}",
                   lambda ci, t0=t0: a_sb[:, ci, t0:t0 + TP], "l0f_a",
                   out_fn=lambda ci: (h_sb[:, ci, :], hk),
                   gain_bias=(lambda ci: ln1g[:, 0, ci:ci + 1],
                              lambda ci: ln1g[:, 1, ci:ci + 1], "l0f_ln1"),
                   tok=TP)
        for wi, wname in ((0, "Wq"), (1, "Wk")):
            slab0 = 0 if wi == 0 else 16

            def ep(mc, psum, pkey, wi=wi, wname=wname, slab0=slab0, t0=t0):
                p.op("vector", lambda e, psum=psum, wi=wi, mc=mc: e.tensor_scalar(
                        qtmp[:], psum, biases[:, wi, mc:mc + 1], None, ALU.add),
                     reads=[pkey, "l0f_b"], writes=[qtk], tag=f"qev_{pass_i}_{wname}_{mc}")
                r0 = slab0 * 64 + mc * 128
                p.dma("gpsimd", contrib[r0:r0 + 128, t0:t0 + TP], qtmp[:],
                      reads=[qtk], writes=["l0f_contrib"], tag=f"qst_{pass_i}_{wname}_{mc}")
            emit_proj_stream(ctx, f"l0f{wname}{pass_i}", st_w, inp[wname], PC,
                             lambda kc: h_sb[:, kc, :], hk, ep, TP)

        def v_cb(tt, nb, psum, pkey, t0=t0):
            p.op("vector", lambda e, psum=psum, nb=nb: e.tensor_add(
                    qtmp[:], psum, bv_rep[:, nb * 512:(nb + 1) * 512]),
                 reads=[pkey, "l0f_bv"], writes=[qtk], tag=f"vev_{pass_i}_{tt}_{nb}")
            ts0 = t0 + tt * 128
            p.dma("gpsimd",
                  vthd[ts0:ts0 + 128, nb * 8:(nb + 1) * 8, :],
                  qtmp.ap().rearrange("t (h d) -> t h d", d=64),
                  reads=[qtk], writes=["l0f_contrib"], tag=f"vst_{pass_i}_{tt}_{nb}")
        emit_tm_proj(ctx, f"l0fv{pass_i}", st_w, inp["Wv"], PC, 0, C,
                     lambda kc: h_sb[:, kc, :], hk, v_cb, TP)
        ctx.pop_scope()

    for _pi in range(n_pass):
        f1_pass(_pi)

    ctx.push_scope()
    p.op("gpsimd", lambda e: e.collective_compute(
            "AllGather", ALU.bypass, groups4, [contrib.opt()], [ag1_out.opt()]),
         reads=["l0f_contrib"], writes=["l0f_ag1"], dma=True, tag="l0f_ag1").ccinc1 = True
    gbuf = ctx.sb(ctx.u("l0f_gbuf"), [128, tok], BF16)
    gbk = ctx.u("l0f_gbufk")
    gvbuf = ctx.sb(ctx.u("l0f_gvbuf"), [128, 128 * 64], BF16)
    gvk = ctx.u("l0f_gvbufk")
    n_gq = (HL * 64 * NR) // 128
    for nm, dst in (("gq_idx", myq), ("gk_idx", myk)):
        for g in range(n_gq):
            p.op("gpsimd", lambda e, g=g, nm=nm: e.indirect_dma_start(
                    gbuf[:], None, ag1_out,
                    IndirectOffsetOnAxis(idxs[nm][:, g:g + 1], 0)),
                 reads=["l0f_ag1", "l0f_idx"], writes=[gbk], dma=True, tag=f"g{nm}_{g}")
            p.dma("gpsimd", dst[g * 128:(g + 1) * 128, :], gbuf[:],
                  reads=[gbk], writes=["l0f_myqk"], tag=f"g{nm}s_{g}")
    ag1_pieces = ag1_out.rearrange("(a b) t -> a (b t)", b=8192 // tok)
    n_vrows = HL * NR * (tok // 128)
    g = 0
    while g * 128 < n_vrows:
        rows = min(128, n_vrows - g * 128)
        p.op("gpsimd", lambda e, g=g, rows=rows: e.indirect_dma_start(
                gvbuf[0:rows, :], None, ag1_pieces,
                IndirectOffsetOnAxis(idxs["gv_idx"][0:rows, g:g + 1], 0)),
             reads=["l0f_ag1", "l0f_idx"], writes=[gvk], dma=True, tag=f"gv_{g}")
        p.dma("gpsimd", myv[g * 128:g * 128 + rows, :], gvbuf[0:rows, :],
              reads=[gvk], writes=["l0f_myv"], tag=f"gvs_{g}")
        g += 1
    ctx.pop_scope()

    myq_v = myq.rearrange("(h d r) t -> h d r t", h=HL, r=NR)
    myk_v = myk.rearrange("(h d r) t -> h d r t", h=HL, r=NR)
    myv_v = myv.rearrange("(h b) e -> h b e", h=HL)

    def f3_head(h):
        ctx.push_scope()
        qh = ctx.sb(ctx.u("l0f_qh"), [64, T_ALL], BF16)
        qhk = ctx.u("qhk")
        kh = ctx.sb(ctx.u("l0f_kh"), [64, T_ALL], BF16)
        khk = ctx.u("khk")
        p.dma("sync", qh.ap().rearrange("d (r t) -> d r t", r=NR), myq_v[h],
              reads=["l0f_myqk"], writes=[qhk], tag=f"ldqh_{h}")
        p.dma("sync", kh.ap().rearrange("d (r t) -> d r t", r=NR), myk_v[h],
              reads=["l0f_myqk"], writes=[khk], tag=f"ldkh_{h}")
        vaug = ctx.sb(ctx.u("l0f_vaug"), [128, n_kt_all, 65], BF16)
        vbk = ctx.u("vaugk")
        p.dma("sync", vaug[:, :, 0:64],
              myv_v[h].rearrange("b (t d) -> t b d", d=64),
              reads=["l0f_myv"], writes=[vbk], tag=f"ldvh_{h}")
        ncol = ctx.sb(ctx.u("l0f_ncol"), [128, 2, n_kt_all], F32)
        nck = ctx.u("ncolk")
        negb = ctx.sb(ctx.u("l0f_negb"), [128, n_kt_all], F32)
        nbk = ctx.u("negbk")
        eb = ctx.sb(ctx.u("l0f_eb"), [128, n_kt_all], F32)
        ebk = ctx.u("ebk")
        otile = ctx.sb(ctx.u("l0f_otile"), [64, 2, 128], BF16)
        otk = ctx.u("otilek")
        ctx.push_scope()
        sqt = ctx.sb(ctx.u("l0f_sqt"), [64, 512], BF16)
        sqk = ctx.u("sqtk")
        pnb = ctx.ps(ctx.u("l0f_pnb"), [128, 2, n_kt_all], F32)
        pnk = ctx.u("pnbk")
        zr64 = ctx.sb(ctx.u("l0f_zr"), [1, 128], BF16)
        p.op("vector", lambda e: e.memset(zr64[:], 0.0), writes=["l0f_zr"],
             tag=f"zr_{h}")
        for which, buf, bk in ((0, qh, qhk), (1, kh, khk)):
            p.op("tensor", lambda e, which=which: e.matmul(
                    pnb[:, which, :], zr64[:], zr64[:, 0:n_kt_all],
                    start=True, stop=False),
                 reads=["l0f_zr"], writes=[pnk], tag=f"nz_{h}_{which}")
            for tc in range(T_ALL // 512):
                p.op("scalar", lambda e, buf=buf, tc=tc: e.activation(
                        sqt[:], buf[:, tc * 512:(tc + 1) * 512], AF.Square),
                     reads=[bk], writes=[sqk], tag=f"nsq_{h}_{which}_{tc}")
                for j in range(4):
                    b = tc * 4 + j
                    p.op("tensor", lambda e, j=j, b=b, which=which: e.matmul(
                            pnb[:, which, b:b + 1], sqt[:, j * 128:(j + 1) * 128],
                            ctx.ones_col_bf[0:64, :], start=False, stop=False),
                         reads=[sqk, "ones"], writes=[pnk], tag=f"nmm_{h}_{which}_{b}")
            p.op("tensor", lambda e, which=which: e.matmul(
                    pnb[:, which, :], zr64[:], zr64[:, 0:n_kt_all],
                    start=False, stop=True),
                 reads=["l0f_zr", pnk], writes=[pnk], tag=f"nc_{h}_{which}")
            p.op("vector", lambda e, which=which: e.tensor_copy(
                    ncol[:, which, :], pnb[:, which, :]),
                 reads=[pnk], writes=[nck], tag=f"nev_{h}_{which}")
        qmax = ctx.sb(ctx.u("l0f_qmax"), [128, 1], F32)
        qmk = ctx.u("qmaxk")
        qrow = ctx.sb(ctx.u("l0f_qrow"), [1, 129], F32)
        qrk = ctx.u("qrowk")
        ptq = ctx.ps(ctx.u("l0f_ptq"), [1, 128], F32)
        ptqk = ctx.u("ptqk")
        pbq = ctx.ps(ctx.u("l0f_pbq"), [128, 1], F32)
        pbqk = ctx.u("pbqk")
        p.op("vector", lambda e: e.reduce_max(qmax[:], ncol[:, 0, :], AX.X),
             reads=[nck], writes=[qmk], tag=f"qm1_{h}")
        p.op("tensor", lambda e: e.transpose(ptq[:], qmax[:], ctx.ident_f32[:]),
             reads=[qmk, "identfk"], writes=[ptqk], tag=f"qtr_{h}")
        p.op("vector", lambda e: e.tensor_copy(qrow[:, 0:128], ptq[:]),
             reads=[ptqk], writes=[qrk], tag=f"qcp_{h}")
        p.op("vector", lambda e: e.reduce_max(qrow[:, 128:129], qrow[:, 0:128], AX.X),
             reads=[qrk], writes=[qrk], tag=f"qm2_{h}")
        p.op("tensor", lambda e: e.matmul(pbq[:], ctx.ones_row_f32[:],
                                          qrow[:, 128:129], start=True, stop=True),
             reads=[qrk, "ones"], writes=[pbqk], tag=f"qbc_{h}")
        p.op("vector", lambda e: e.tensor_copy(qmax[:], pbq[:]),
             reads=[pbqk], writes=[qmk], tag=f"qbe_{h}")
        p.op("vector", lambda e: e.tensor_scalar(negb[:], ncol[:, 1, :], qmax[:],
                                                 -1.0 / 16.0, ALU.add, ALU.mult),
             reads=[nck, qmk], writes=[nbk], tag=f"nb_{h}")
        p.op("scalar", lambda e: e.activation(eb[:], negb[:], AF.Exp, scale=-1.0),
             reads=[nbk], writes=[ebk], tag=f"eb_{h}")
        ctx.pop_scope()
        p.op("vector", lambda e: e.tensor_mul(
                vaug[:, :, 0:64], vaug[:, :, 0:64],
                eb.ap().unsqueeze(2).broadcast_to([128, n_kt_all, 64])),
             reads=[ebk, vbk], writes=[vbk], tag=f"vsc_{h}")
        p.op("vector", lambda e: e.tensor_copy(vaug[:, :, 64:65], eb.ap().unsqueeze(2)),
             reads=[ebk, vbk], writes=[vbk], tag=f"vac_{h}")
        ctx.push_scope()
        pst = ctx.ps(ctx.u("l0f_pst"), [128, 2, 512], F32)
        pstk = ctx.u("pstk")
        pt_sb = ctx.sb(ctx.u("l0f_pt"), [128, 2, 512], BF16)
        ptk = ctx.u("ptsbk")
        po = ctx.ps(ctx.u("l0f_po"), [128, 4, 512], F32)
        pok = ctx.u("pok")
        oy = ctx.sb(ctx.u("l0f_oy"), [128, 64], BF16)
        oyk = ctx.u("oyk")
        rcl = ctx.sb(ctx.u("l0f_rcl"), [128, 1], F32)
        rck = ctx.u("rclk")
        ptro = ctx.ps(ctx.u("l0f_ptro"), [64, 128], BF16)
        ptrok = ctx.u("ptrok")
        for qc in range(T_ALL // 512):
            n_kt = (qc + 1) * 4
            for ktp in range((n_kt + 1) // 2):
                for sub in range(2):
                    kt = ktp * 2 + sub
                    if kt >= n_kt:
                        continue
                    diag = kt >= qc * 4
                    p.op("tensor", lambda e, kt=kt, sub=sub, qc=qc, diag=diag: e.matmul(
                            pst[:, sub, :], kh[:, kt * 128:(kt + 1) * 128],
                            qh[:, qc * 512:(qc + 1) * 512],
                            start=True, stop=not diag),
                         reads=[khk, qhk], writes=[pstk], tag=f"qk_{h}_{qc}_{kt}")
                    if diag:
                        p.op("tensor", lambda e, kt=kt, sub=sub, qc=qc: e.matmul(
                                pst[:, sub, :], ctx.ident[:],
                                dmask[:, kt - qc * 4, :], start=False, stop=True),
                             reads=["identk", "l0f_dmask", pstk], writes=[pstk],
                             tag=f"dm_{h}_{qc}_{kt}")
                    p.op("scalar", lambda e, sub=sub, kt=kt: e.activation(
                            pt_sb[:, sub, :], pst[:, sub, :], AF.Exp,
                            bias=negb[:, kt:kt + 1], scale=0.125),
                         reads=[pstk, nbk], writes=[ptk], tag=f"exp_{h}_{qc}_{kt}")
                for sub in range(2):
                    kt = ktp * 2 + sub
                    if kt >= n_kt:
                        continue
                    for qs in range(4):
                        p.op("tensor", lambda e, kt=kt, sub=sub, qs=qs, nk=n_kt: e.matmul(
                                po[:, qs, 0:65],
                                pt_sb[:, sub, qs * 128:(qs + 1) * 128],
                                vaug[:, kt, :],
                                start=(kt == 0), stop=(kt == nk - 1)),
                             reads=[ptk, vbk], writes=[pok], tag=f"av_{h}_{qc}_{kt}_{qs}")
            for qs in range(4):
                ob = qs % 2
                p.op("vector", lambda e, qs=qs: e.reciprocal(rcl[:], po[:, qs, 64:65]),
                     reads=[pok], writes=[rck], tag=f"rl_{h}_{qc}_{qs}")
                p.op("vector", lambda e, qs=qs: e.tensor_scalar(
                        oy[:], po[:, qs, 0:64], rcl[:], None, ALU.mult),
                     reads=[pok, rck], writes=[oyk], tag=f"onorm_{h}_{qc}_{qs}")
                p.op("tensor", lambda e: e.transpose(ptro[:], oy[:], ctx.ident[:]),
                     reads=[oyk, "identk"], writes=[ptrok], tag=f"otr_{h}_{qc}_{qs}")
                p.op("vector", lambda e, ob=ob: e.tensor_copy(otile[:, ob, :], ptro[:]),
                     reads=[ptrok], writes=[otk], tag=f"oev_{h}_{qc}_{qs}")
                c0 = qc * 512 + qs * 128
                p.dma("gpsimd", ocontrib[h * 64:(h + 1) * 64, c0:c0 + 128],
                      otile[:, ob, :], reads=[otk], writes=["l0f_oc"],
                      tag=f"ost_{h}_{qc}_{qs}")
        ctx.pop_scope()
        ctx.pop_scope()

    for h in range(HL):
        f3_head(h)

    p.op("gpsimd", lambda e: e.collective_compute(
            "AllGather", ALU.bypass, groups4, [ocontrib.opt()], [ag2_out.opt()]),
         reads=["l0f_oc"], writes=["l0f_ag2"], dma=True, tag="l0f_ag2").ccinc1 = True
    ag2_pieces = ag2_out.rearrange("r (n u t) -> (r n u) t", n=NR, t=TP)

    def f4_pass(pass_i):
        t0 = pass_i * TP
        ctx.push_scope()
        h2_sb = ctx.sb(ctx.u("l0f_h2"), [128, PC, TP], BF16)
        h2k = ctx.u("l0f_h2k")
        wbc_all = ctx.sb(ctx.u("l0f_wbc"), [128, 8, TP], BF16)
        wbck = ctx.u("wbck")
        mid_sb = ctx.sb(ctx.u("l0f_mid"), [128, 8, TP], BF16)
        midk = ctx.u("midk")
        eo_sb = ctx.sb(ctx.u("l0f_eo"), [128, 8, TP], BF16)
        eok = ctx.u("eok")
        tws = ctx.sb(ctx.u("l0f_tws"), [128, TP], F32)
        twk = ctx.u("twsk")
        ctx.push_scope()
        o_pass = ctx.sb(ctx.u("l0f_op"), [128, PC, TP], BF16)
        opk = ctx.u("l0f_opk")
        for g in range(PC):
            p.op("gpsimd", lambda e, g=g: e.indirect_dma_start(
                    o_pass[:, g, :], None, ag2_pieces,
                    IndirectOffsetOnAxis(idxs["go_idx"][:, pass_i * PC + g:pass_i * PC + g + 1], 0)),
                 reads=["l0f_ag2", "l0f_idx"], writes=[opk], dma=True,
                 tag=f"go_{pass_i}_{g}")

        def ep(mc, psum, pkey, t0=t0):
            p.op("vector", lambda e, mc=mc, psum=psum, t0=t0: e.scalar_tensor_tensor(
                    a_sb[:, mc, t0:t0 + TP], psum, biases[:, 3, mc:mc + 1],
                    a_sb[:, mc, t0:t0 + TP], ALU.add, ALU.add),
                 reads=[pkey, "l0f_b", "l0f_a"], writes=["l0f_a"],
                 tag=f"aev_{pass_i}_{mc}")
        emit_proj_stream(ctx, f"l0fo{pass_i}", st_w, inp["Wo"], PC,
                         lambda kc: o_pass[:, kc, :], opk, ep, TP)
        ctx.pop_scope()
        emit_ln_fm(ctx, f"l0fl2{pass_i}",
                   lambda ci, t0=t0: a_sb[:, ci, t0:t0 + TP], "l0f_a",
                   out_fn=lambda ci: (h2_sb[:, ci, :], h2k),
                   gain_bias=(lambda ci: ln2g[:, 0, ci:ci + 1],
                              lambda ci: ln2g[:, 1, ci:ci + 1], "l0f_ln2"),
                   tok=TP)
        ctx.push_scope()
        w_tm = ctx.sb(ctx.u("l0f_wtm"), [128, TP // 128, 8], F32)
        wtmk = ctx.u("wtmk")
        wfm = ctx.sb(ctx.u("l0f_wfm"), [8, TP], F32)
        wfmk = ctx.u("wfmk")
        plg = ctx.ps(ctx.u("l0f_plg"), [128, 8], F32)
        plgk = ctx.u("plgk")
        ptw = ctx.ps(ctx.u("l0f_ptw"), [8, 128], F32)
        ptwk = ctx.u("ptwk")
        red = ctx.sb(ctx.u("l0f_red"), [128, 4], F32)
        redk = ctx.u("redl0f")
        etmp = ctx.sb(ctx.u("l0f_etmp"), [128, 8], F32)
        etk = ctx.u("etmpk")
        for tt in range(TP // 128):
            sl = slice(tt * 128, (tt + 1) * 128)
            for kc in range(PC):
                p.op("tensor", lambda e, kc=kc, sl=sl: e.matmul(
                        plg[:], h2_sb[:, kc, sl], wr_sb[kc][:],
                        start=(kc == 0), stop=(kc == PC - 1)),
                     reads=[h2k, "l0f_wr"], writes=[plgk], tag=f"rt_{pass_i}_{tt}_{kc}")
            p.op("vector", lambda e: e.tensor_add(plg[:], plg[:], br_rep[:]),
                 reads=[plgk, "l0f_br"], writes=[plgk], tag=f"rtb_{pass_i}_{tt}")
            p.op("vector", lambda e: e.reduce_max(red[:, 0:1], plg[:], AX.X, negate=True),
                 reads=[plgk], writes=[redk], tag=f"rm1_{pass_i}_{tt}")
            p.op("vector", lambda e: e.tensor_scalar(etmp[:], plg[:], red[:, 0:1],
                                                     0.0, ALU.add, ALU.is_ge),
                 reads=[plgk, redk], writes=[etk], tag=f"req_{pass_i}_{tt}")
            p.op("vector", lambda e: e.scalar_tensor_tensor(
                    etmp[:], etmp[:], -1e30, plg[:], ALU.mult, ALU.add),
                 reads=[etk, plgk], writes=[etk], tag=f"rrm_{pass_i}_{tt}")
            p.op("vector", lambda e: e.reduce_max(red[:, 1:2], etmp[:], AX.X),
                 reads=[etk], writes=[redk], tag=f"rm2_{pass_i}_{tt}")
            p.op("vector", lambda e: e.tensor_scalar(etmp[:], plg[:], red[:, 1:2],
                                                     None, ALU.is_ge),
                 reads=[plgk, redk], writes=[etk], tag=f"rkm_{pass_i}_{tt}")
            p.op("scalar", lambda e, tt=tt: e.activation(
                    w_tm[:, tt, :], plg[:], AF.Exp, bias=red[:, 0:1]),
                 reads=[plgk, redk], writes=[wtmk], tag=f"rexp_{pass_i}_{tt}")
            p.op("vector", lambda e, tt=tt: e.tensor_mul(
                    w_tm[:, tt, :], w_tm[:, tt, :], etmp[:]),
                 reads=[etk, wtmk], writes=[wtmk], tag=f"rmask_{pass_i}_{tt}")
            p.op("vector", lambda e, tt=tt: e.reduce_sum(red[:, 2:3], w_tm[:, tt, :], AX.X),
                 reads=[wtmk], writes=[redk], tag=f"rsum_{pass_i}_{tt}")
            p.op("vector", lambda e: e.reciprocal(red[:, 2:3], red[:, 2:3]),
                 reads=[redk], writes=[redk], tag=f"rrc_{pass_i}_{tt}")
            p.op("vector", lambda e, tt=tt: e.tensor_scalar(
                    w_tm[:, tt, :], w_tm[:, tt, :], red[:, 2:3], None, ALU.mult),
                 reads=[wtmk, redk], writes=[wtmk], tag=f"rw_{pass_i}_{tt}")
            p.op("tensor", lambda e, tt=tt: e.transpose(
                    ptw[:], w_tm[:, tt, :], ctx.ident_f32[:]),
                 reads=[wtmk, "identfk"], writes=[ptwk], tag=f"rtr_{pass_i}_{tt}")
            p.op("vector", lambda e, sl=sl: e.tensor_copy(wfm[:, sl], ptw[:]),
                 reads=[ptwk], writes=[wfmk], tag=f"rtrc_{pass_i}_{tt}")
        p.dma("sync", wdram[:, t0:t0 + TP], wfm[:], reads=[wfmk],
              writes=["l0f_wdram"], tag=f"wst_{pass_i}")
        wrow = ctx.sb(ctx.u("l0f_wrow"), [1, TP], F32)
        wrk = ctx.u("wrowk")
        pwb = ctx.ps(ctx.u("l0f_pwb"), [128, TP], F32)
        pwbk = ctx.u("pwbk")
        for e_i in range(8):
            p.dma("sync", wrow[:], wdram[e_i:e_i + 1, t0:t0 + TP],
                  reads=["l0f_wdram"], writes=[wrk], tag=f"wld_{pass_i}_{e_i}")
            p.op("tensor", lambda e: e.matmul(pwb[:], ctx.ones_row_f32[:], wrow[:],
                                              start=True, stop=True),
                 reads=[wrk, "ones"], writes=[pwbk], tag=f"wbc_{pass_i}_{e_i}")
            p.op("scalar", lambda e, e_i=e_i: e.copy(wbc_all[:, e_i, :], pwb[:]),
                 reads=[pwbk], writes=[wbck], tag=f"wbe_{pass_i}_{e_i}")
        ctx.pop_scope()
        ctx.push_scope()
        pmm = ctx.ps(ctx.u("l0f_pmm"), [128, 8, TP], F32)
        pmmk = ctx.u("pmmk")
        qs_of = {0: "gpsimd", 1: "sync"}

        def loadw1(e_i, fb, kc):
            b = kc % 2
            p.dma(qs_of[b], st_w1["bufs"][b][:],
                  moe_w1[e_i * C + kc * 128:e_i * C + (kc + 1) * 128,
                         fb * 1024:(fb + 1) * 1024],
                  reads=list(moe_rk[0]),
                  writes=[st_w1["keys"][b]], tag=f"w1l_{pass_i}_{e_i}_{fb}_{kc}",
                  nobar=True)

        def loadw2(e_i, fb, kcf):
            b = kcf % 2
            p.dma(qs_of[b], st_w2["bufs"][b][:],
                  moe_w2[e_i * 4096 + fb * 1024 + kcf * 128:
                         e_i * 4096 + fb * 1024 + (kcf + 1) * 128, :],
                  reads=list(moe_rk[1]),
                  writes=[st_w2["keys"][b]], tag=f"w2l_{pass_i}_{e_i}_{fb}_{kcf}",
                  nobar=True)
        for e_i in range(8):
            for fb in range(4):
                loadw1(e_i, fb, 0)
                loadw1(e_i, fb, 1)
                for kc in range(PC):
                    b = kc % 2
                    for fc in range(8):
                        p.op("tensor", lambda e, kc=kc, fc=fc, b=b, t0=t0: e.matmul(
                                pmm[:, fc, :],
                                st_w1["bufs"][b][:, fc * 128:(fc + 1) * 128],
                                h2_sb[:, kc, :],
                                start=(kc == 0), stop=(kc == PC - 1)),
                             reads=[st_w1["keys"][b], h2k], writes=[pmmk],
                             tag=f"w1m_{pass_i}_{e_i}_{fb}_{kc}_{fc}")
                    if kc + 2 < PC:
                        loadw1(e_i, fb, kc + 2)
                loadw2(e_i, fb, 0)
                loadw2(e_i, fb, 1)
                for fc in range(8):
                    p.op("scalar", lambda e, fc=fc, fb=fb, e_i=e_i: e.activation(
                            mid_sb[:, fc, :], pmm[:, fc, :], AF.Gelu,
                            bias=b1_sb[:, e_i, fb * 8 + fc:fb * 8 + fc + 1]),
                         reads=[pmmk, "l0f_b1"], writes=[midk],
                         tag=f"gel_{pass_i}_{e_i}_{fb}_{fc}")
                for kcf in range(8):
                    b = kcf % 2
                    for mc in range(PC):
                        p.op("tensor", lambda e, kcf=kcf, mc=mc, b=b: e.matmul(
                                pmm[:, mc, :],
                                st_w2["bufs"][b][:, mc * 128:(mc + 1) * 128],
                                mid_sb[:, kcf, :],
                                start=(kcf == 0), stop=(kcf == PC - 1)),
                             reads=[st_w2["keys"][b], midk], writes=[pmmk],
                             tag=f"w2m_{pass_i}_{e_i}_{fb}_{kcf}_{mc}")
                    if kcf + 2 < 8:
                        loadw2(e_i, fb, kcf + 2)
                for mc in range(PC):
                    if fb == 0:
                        p.op("vector", lambda e, mc=mc: e.tensor_copy(
                                eo_sb[:, mc, :], pmm[:, mc, :]),
                             reads=[pmmk], writes=[eok],
                             tag=f"eoc_{pass_i}_{e_i}_{fb}_{mc}")
                    elif fb < 3:
                        p.op("vector", lambda e, mc=mc: e.tensor_add(
                                eo_sb[:, mc, :], eo_sb[:, mc, :], pmm[:, mc, :]),
                             reads=[pmmk, eok], writes=[eok],
                             tag=f"eoa_{pass_i}_{e_i}_{fb}_{mc}")
                    else:
                        p.op("vector", lambda e, mc=mc, e_i=e_i: e.scalar_tensor_tensor(
                                tws[:], pmm[:, mc, :], b2_sb[:, e_i, mc:mc + 1],
                                eo_sb[:, mc, :], ALU.add, ALU.add),
                             reads=[pmmk, "l0f_b2", eok], writes=[twk],
                             tag=f"eow_{pass_i}_{e_i}_{mc}")
                        p.op("vector", lambda e, mc=mc, e_i=e_i: e.tensor_mul(
                                tws[:], tws[:], wbc_all[:, e_i, :]),
                             reads=[twk, wbck], writes=[twk],
                             tag=f"eowm_{pass_i}_{e_i}_{mc}")
                        p.op("gpsimd", lambda e, mc=mc, t0=t0: e.tensor_add(
                                a_sb[:, mc, t0:t0 + TP], a_sb[:, mc, t0:t0 + TP],
                                tws[:]),
                             reads=[twk, "l0f_a"], writes=["l0f_a"],
                             tag=f"acc_{pass_i}_{e_i}_{mc}")
        ctx.pop_scope()
        emit_ln_fm(ctx, f"l0flno{pass_i}",
                   lambda ci, t0=t0: a_sb[:, ci, t0:t0 + TP], "l0f_a",
                   accum_res=(lambda ci, t0=t0: x_fn(ci)[:, t0:t0 + TP], x_key),
                   tok=TP)
        ctx.pop_scope()

    for _pi in range(n_pass):
        f4_pass(_pi)
    ctx.pop_scope()


import ml_dtypes
from concourse.bass_utils import run_bass_kernel_spmd

NCORES = 8
TOK = 2048
NR4 = 4
HL = 16 // NR4
N_PASS = TOK // 512
GROUPS4 = [[0, 1, 2, 3], [4, 5, 6, 7]]
GROUPS8 = [[0, 1, 2, 3, 4, 5, 6, 7]]
BF16NP = ml_dtypes.bfloat16


def _stage_shell(nc, emit_fn):
    p = Prog()
    ctx = Ctx(nc, p, TOK)
    xin = ctx.input("xin", [C, TOK])
    ident_in = ctx.input("ident_in", [128, 128], BF16)
    identf_in = ctx.input("identf_in", [128, 128])
    xout = ctx.output("xout", [C, TOK])
    x_sb = ctx.sb("x_res", [128, PC, TOK], F32, scope=0)
    ctx.setup_common(ident_in, identf_in)
    p.dma("sync", x_sb[:], xin.rearrange("(a p) t -> p a t", p=128),
          writes=["x"], tag="ldx")
    emit_fn(ctx, lambda ci: x_sb[:, ci, :], "x")
    p.dma("gpsimd", xout.rearrange("(a p) t -> p a t", p=128), x_sb[:],
          reads=["x"], writes=["xout"], tag="stx")
    p.fence("sync", ["xout"])
    with nc.Block() as block:
        p.emit_into(nc, block, ctx.scopes[0])
    ctx.scopes[0].close()


def build_l0a(nc):
    def emit(ctx, x_fn, x_key):
        names = {
            "maax": [128, 8], "maaw": [128, 8], "maak": [128, 8],
            "maav": [128, 8], "maar": [128, 8],
            "maaw1": [C, 128], "maaw2": [128, C],
            "decayw1": [C, 64], "decayw2": [64, C], "decayw2p": [64, 2048],
            "decvec_rep": [128, C], "decvec_pad": [128, 16],
            "Wrp": [C, 2048], "Wkp": [C, 2048], "Wk": [C, C], "Wv": [C, C],
            "Wo": [C, C], "ublk": [128, 64], "lng_rep": [128, C],
            "lnb_rep": [128, C], "tri_incl": [128, 128], "xprev": [128, 8],
            "foldmask": [128, 8],
        }
        inp = {k: ctx.input("i_" + k, s) for k, s in names.items()}
        inp["mask_strict4"] = ctx.input("i_mask_strict4", [128, 512], BF16)
        emit_l0_attn(ctx, x_fn, x_key, inp, GROUPS4)
    _stage_shell(nc, emit)


def feeds_l0a(ins, x):
    decay = np.asarray(ins["L0_attn_time_decay"], np.float32).reshape(C)
    dpad = np.full((2048, 1), -1e9, np.float32)
    for h in range(64):
        dpad[32 * h:32 * h + 16, 0] = decay[16 * h:16 * h + 16]

    def padcols(W):
        Wp = np.zeros((W.shape[0], 2048), np.float32)
        for h in range(64):
            Wp[:, 32 * h:32 * h + 16] = W[:, 16 * h:16 * h + 16]
        return Wp
    u = np.asarray(ins["L0_attn_time_faaaa"], np.float32)
    ublk = np.zeros((128, 64), np.float32)
    for pt in range(16):
        for j in range(4):
            h = pt * 4 + j
            ublk[32 * j:32 * j + 16, pt * 4 + j] = u[h]
    g = lambda n: np.asarray(ins["L0_attn_" + n], np.float32)
    com = {
        "i_maax": np.ascontiguousarray(g("time_maa_x").reshape(8, 128).T),
        "i_maaw": np.ascontiguousarray(g("time_maa_w").reshape(8, 128).T),
        "i_maak": np.ascontiguousarray(g("time_maa_k").reshape(8, 128).T),
        "i_maav": np.ascontiguousarray(g("time_maa_v").reshape(8, 128).T),
        "i_maar": np.ascontiguousarray(g("time_maa_r").reshape(8, 128).T),
        "i_maaw1": g("time_maa_w1"),
        "i_maaw2": g("time_maa_w2").reshape(128, C),
        "i_decayw1": g("time_decay_w1"),
        "i_decayw2": g("time_decay_w2"),
        "i_decayw2p": padcols(g("time_decay_w2")),
        "i_decvec_rep": np.broadcast_to(decay, (128, C)).copy(),
        "i_decvec_pad": np.ascontiguousarray(dpad.reshape(16, 128).T),
        "i_Wrp": padcols(g("W_r")),
        "i_Wkp": padcols(g("W_k")),
        "i_Wk": g("W_k"), "i_Wv": g("W_v"), "i_Wo": g("W_o"),
        "i_ublk": ublk,
        "i_lng_rep": np.broadcast_to(g("ln_g"), (128, C)).copy(),
        "i_lnb_rep": np.broadcast_to(g("ln_b"), (128, C)).copy(),
        "i_tri_incl": np.triu(np.ones((128, 128), np.float32)),
        "i_mask_strict4": np.tile(np.triu(np.ones((128, 128)), 1),
                                  (1, 4)).astype(BF16NP),
        "ident_in": np.eye(128, dtype=BF16NP),
        "identf_in": np.eye(128, dtype=np.float32),
    }
    feeds = []
    for c in range(NCORES):
        be, r = c // NR4, c % NR4
        fm = np.zeros((128, 8), np.float32)
        fm[:, :r] = 1.0
        xp = (np.zeros(C, np.float32) if r == 0
              else np.asarray(x[be, r * TOK - 1], np.float32))
        f = dict(com)
        f["i_foldmask"] = fm
        f["i_xprev"] = np.ascontiguousarray(xp.reshape(8, 128).T)
        feeds.append(f)
    return feeds


def build_l0f(nc):
    def emit(ctx, x_fn, x_key):
        names = {
            "bqkvo": ([128, 4, PC], F32), "bv_rep": ([128, C], F32),
            "ln1gb": ([128, 2, PC], F32), "ln2gb": ([128, 2, PC], F32),
            "Wq": ([C, C], F32), "Wk": ([C, C], F32), "Wv": ([C, C], F32),
            "Wo": ([C, C], F32), "Wr": ([C, 8], F32), "br_rep": ([128, 8], F32),
            "b1": ([128, 8, 32], F32), "b2": ([128, 8, 8], F32),
            "dmask": ([128, 4, 512], BF16),
            "gq_idx": ([128, (HL * 64 * NR4) // 128], I32),
            "gk_idx": ([128, (HL * 64 * NR4) // 128], I32),
            "gv_idx": ([128, max(1, (HL * NR4 * (TOK // 128)) // 128)], I32),
            "go_idx": ([128, N_PASS * PC], I32),
        }
        inp = {k: ctx.input("i_" + k, s, dt) for k, (s, dt) in names.items()}
        my_w1 = ctx.input("i_myw1", [C, 4096])
        my_w2 = ctx.input("i_myw2", [4096, C])
        w1all, w2all = emit_moe_prep(ctx, my_w1, my_w2, GROUPS8)
        emit_l0_ffn(ctx, x_fn, x_key, inp, GROUPS4, w1all, w2all,
                    moe_rk=(("moe_w1all",), ("moe_w2all",)))
    _stage_shell(nc, emit)


def _l0f_idxs(rank):
    n_gq = (HL * 64 * NR4) // 128
    gq = np.zeros((128, n_gq), np.int32)
    gk = np.zeros((128, n_gq), np.int32)
    for gg in range(n_gq):
        for pp in range(128):
            flat = gg * 128 + pp
            h = flat // (64 * NR4)
            rem = flat % (64 * NR4)
            d = rem // NR4
            src = rem % NR4
            h_g = rank * HL + h
            gq[pp, gg] = src * 3072 + h_g * 64 + d
            gk[pp, gg] = src * 3072 + 1024 + h_g * 64 + d
    n_vrows = HL * NR4 * (TOK // 128)
    gv = np.zeros((128, max(1, n_vrows // 128)), np.int32)
    P1, P2, P3 = 3072 * TOK // 8192, 2048 * TOK // 8192, 64 * TOK // 8192
    for flat in range(n_vrows):
        h = flat // (NR4 * (TOK // 128))
        rem = flat % (NR4 * (TOK // 128))
        src = rem // (TOK // 128)
        bb = rem % (TOK // 128)
        h_g = rank * HL + h
        gv[flat % 128, flat // 128] = src * P1 + P2 + h_g * P3 + bb
    go = np.zeros((128, N_PASS * PC), np.int32)
    for pi in range(N_PASS):
        for gg in range(PC):
            for pp in range(128):
                go[pp, pi * PC + gg] = ((gg * 128 + pp) * NR4 + rank) * N_PASS + pi
    return gq, gk, gv, go


def feeds_l0f(ins):
    g = lambda n: np.asarray(ins["L0_ffn_" + n], np.float32)
    dmask = np.full((128, 4, 512), -1e30, np.float32)
    for kk in range(128):
        for o_ in range(4):
            dmask[kk, o_, o_ * 128 + kk:] = 0.0
    com = {
        "i_bqkvo": np.stack([g(n).reshape(PC, 128).T
                             for n in ("bq", "bk", "bv", "bo")], axis=1),
        "i_bv_rep": np.broadcast_to(g("bv"), (128, C)).copy(),
        "i_ln1gb": np.stack([g("ln1_g").reshape(PC, 128).T,
                             g("ln1_b").reshape(PC, 128).T], 1),
        "i_ln2gb": np.stack([g("ln2_g").reshape(PC, 128).T,
                             g("ln2_b").reshape(PC, 128).T], 1),
        "i_Wq": g("Wq"), "i_Wk": g("Wk"), "i_Wv": g("Wv"), "i_Wo": g("Wo"),
        "i_Wr": g("Wr"),
        "i_br_rep": np.broadcast_to(g("br"), (128, 8)).copy(),
        "i_b1": g("e_b1").reshape(8, 32, 128).transpose(2, 0, 1).copy(),
        "i_b2": g("e_b2").reshape(8, 8, 128).transpose(2, 0, 1).copy(),
        "i_dmask": dmask.astype(BF16NP),
        "ident_in": np.eye(128, dtype=BF16NP),
        "identf_in": np.eye(128, dtype=np.float32),
    }
    e_w1 = np.asarray(ins["L0_ffn_e_w1"], np.float32)
    e_w2 = np.asarray(ins["L0_ffn_e_w2"], np.float32)
    feeds = []
    for c in range(NCORES):
        r = c % NR4
        gq, gk, gv, go = _l0f_idxs(r)
        f = dict(com)
        f["i_gq_idx"] = gq
        f["i_gk_idx"] = gk
        f["i_gv_idx"] = gv
        f["i_go_idx"] = go
        f["i_myw1"] = np.ascontiguousarray(e_w1[c])
        f["i_myw2"] = np.ascontiguousarray(e_w2[c])
        feeds.append(f)
    return feeds


def build_l1a(nc):
    def emit(ctx, x_fn, x_key):
        inp = {
            "wqkvT": ctx.input("i_wqkvT", [C, 3072]),
            "w_outT": ctx.input("i_woutT", [C, C]),
            "wq_mapT": ctx.input("i_wqmT", [64, 64]),
            "wk_mapT": ctx.input("i_wkmT", [64, 64]),
            "bq": ctx.input("i_bqm", [64, 1]),
            "bk_rep": ctx.input("i_bkm", [128, 64]),
            "mask_incl": ctx.input("i_maski", [128, 128], BF16),
            "foldmask": ctx.input("i_foldmask", [128, 8]),
        }
        emit_l1_attn(ctx, x_fn, x_key, inp, GROUPS4)
    _stage_shell(nc, emit)


def feeds_l1a(ins):
    g = lambda n: np.asarray(ins["L1_attn_" + n], np.float32)
    com = {
        "i_wqkvT": np.ascontiguousarray(g("w_qkv").T),
        "i_woutT": np.ascontiguousarray(g("w_out").T),
        "i_wqmT": np.ascontiguousarray(g("wq_map").T),
        "i_wkmT": np.ascontiguousarray(g("wk_map").T),
        "i_bqm": g("bq_map").reshape(64, 1),
        "i_bkm": np.broadcast_to(g("bk_map"), (128, 64)).copy(),
        "i_maski": np.triu(np.ones((128, 128))).astype(BF16NP),
        "ident_in": np.eye(128, dtype=BF16NP),
        "identf_in": np.eye(128, dtype=np.float32),
    }
    feeds = []
    for c in range(NCORES):
        r = c % NR4
        fm = np.zeros((128, 8), np.float32)
        fm[:, :r] = 1.0
        f = dict(com)
        f["i_foldmask"] = fm
        feeds.append(f)
    return feeds


def build_l1f(nc):
    def emit(ctx, x_fn, x_key):
        inp = {
            "Wt": ctx.input("i_Wt", [C, C]),
            "Wpt": ctx.input("i_Wpt", [C, 256]),
            "bp": ctx.input("i_bp", [128, 2]),
            "sim": ctx.input("i_sim", [256, 4]),
            "temp": ctx.input("i_temp", [1, 1]),
            "Wdt": ctx.input("i_Wdt", [C, 15]),
            "Wu": ctx.input("i_Wu", [15, C]),
        }
        emit_l1_ffn(ctx, x_fn, x_key, inp)
    _stage_shell(nc, emit)


def feeds_l1f(ins):
    g = lambda n: np.asarray(ins["L1_ffn_" + n], np.float32)
    com = {
        "i_Wt": np.ascontiguousarray(g("W").T),
        "i_Wpt": np.ascontiguousarray(g("Wp").T),
        "i_bp": np.ascontiguousarray(g("bp").reshape(2, 128).T),
        "i_sim": g("sim"),
        "i_temp": g("temp").reshape(1, 1),
        "i_Wdt": np.ascontiguousarray(
            np.concatenate([g(f"Wd{i}") for i in range(4)], 0).T),
        "i_Wu": np.concatenate([np.ascontiguousarray(g(f"Wu{i}").T)
                                for i in range(4)], 0),
        "ident_in": np.eye(128, dtype=BF16NP),
        "identf_in": np.eye(128, dtype=np.float32),
    }
    return [dict(com) for _ in range(NCORES)]


def _launch(build_fn, feeds, xs):
    nc = bass.Bass("TRN2", target_bir_lowering=False, num_devices=NCORES)
    build_fn(nc)
    in_maps = []
    for c in range(NCORES):
        f = dict(feeds[c])
        f["xin"] = xs[c]
        in_maps.append(f)
    res = run_bass_kernel_spmd(nc, in_maps, core_ids=list(range(NCORES)))
    return [np.ascontiguousarray(np.asarray(res.results[c]["xout"],
                                            np.float32))
            for c in range(NCORES)]


def kernel(**inputs):
    ins = {k: np.asarray(v) for k, v in inputs.items()}
    x = np.asarray(ins["L0_attn_x"], np.float32)
    xs = [np.ascontiguousarray(x[c // NR4,
                                 (c % NR4) * TOK:(c % NR4 + 1) * TOK, :].T)
          for c in range(NCORES)]
    xs = _launch(build_l0a, feeds_l0a(ins, x), xs)
    xs = _launch(build_l0f, feeds_l0f(ins), xs)
    xs = _launch(build_l1a, feeds_l1a(ins), xs)
    xs = _launch(build_l1f, feeds_l1f(ins), xs)
    out = np.empty((2, 8192, C), np.float32)
    for c in range(NCORES):
        out[c // NR4, (c % NR4) * TOK:(c % NR4 + 1) * TOK, :] = xs[c].T
    return out


# version 03
# speedup vs baseline: 1.0171x; 1.0171x over previous
import numpy as np
from contextlib import ExitStack
import concourse.bass as bass
import concourse.mybir as mybir
from concourse.bass import IndirectOffsetOnAxis

F32 = mybir.dt.float32
BF16 = mybir.dt.bfloat16
I32 = mybir.dt.int32
AF = mybir.ActivationFunctionType
ALU = mybir.AluOpType
AX = mybir.AxisListType

ENGINES = ("sync", "scalar", "vector", "tensor", "gpsimd")

C = 1024
PC = C // 128
EPS = 1e-5



class _Op:
    __slots__ = ("engine", "fn", "reads", "writes", "dma", "tag", "idx",
                 "domain", "deps", "inc_needed", "wait_val", "ccinc1", "barrier",
                 "nobar")

    def __init__(self, engine, fn, reads, writes, dma, tag, idx):
        self.engine = engine
        self.fn = fn
        self.reads = tuple(reads)
        self.writes = tuple(writes)
        self.dma = dma
        self.tag = tag
        self.idx = idx
        if dma:
            assert len(self.writes) == 1, f"dma op {tag} needs exactly 1 write key"
            self.domain = "k_" + self.writes[0]
        else:
            self.domain = engine
        self.deps = []
        self.inc_needed = False
        self.wait_val = None
        self.ccinc1 = False
        self.barrier = False
        self.nobar = False


class Prog:
    def __init__(self):
        self.ops = []
        self.key_alias = {}

    def op(self, engine, fn, reads=(), writes=(), dma=False, tag=""):
        if self.key_alias:
            al = self.key_alias
            reads = [al.get(k, k) for k in reads]
            writes = [al.get(k, k) for k in writes]
        o = _Op(engine, fn, reads, writes, dma, tag, len(self.ops))
        self.ops.append(o)
        return o

    def dma(self, engine, out_ap, in_ap, reads=(), writes=(), tag="", nobar=False, **kw):
        def fn(eng):
            return eng.dma_start(out=out_ap, in_=in_ap, **kw)
        o = self.op(engine, fn, reads=reads, writes=writes, dma=True, tag=tag)
        o.nobar = nobar
        return o

    def fence(self, engine, keys, tag="fence"):
        return self.op(engine, lambda e: e.nop(nofuse=True), reads=tuple(keys), tag=tag)

    def barrier(self, tag="barrier"):
        for e in ("sync", "scalar", "vector", "tensor", "gpsimd"):
            o = self.op(e, lambda eng: eng.nop(nofuse=True), tag=f"{tag}:{e}")
            o.barrier = True

    @staticmethod
    def _needs_edge(prod, cons):
        if prod.domain != cons.domain:
            return True
        if prod.dma and cons.dma:
            return False
        if prod.engine == "tensor" and cons.engine == "tensor" and not prod.dma:
            return False
        return True

    def analyze(self):
        last_writer = {}
        readers = {}
        last_per_domain = {}
        for o in self.ops:
            deps = {}
            if o.barrier:
                for dom, w in last_per_domain.items():
                    if not w.nobar and w is not o:
                        deps[id(w)] = w
            else:
                for k in o.reads:
                    w = last_writer.get(k)
                    if w is not None and self._needs_edge(w, o):
                        deps[id(w)] = w
                for k in o.writes:
                    w = last_writer.get(k)
                    if w is not None and self._needs_edge(w, o):
                        deps[id(w)] = w
                    for r in readers.get(k, ()):
                        if r is not o and self._needs_edge(r, o):
                            deps[id(r)] = r
            o.deps = list(deps.values())
            for k in o.reads:
                readers.setdefault(k, []).append(o)
            for k in o.writes:
                last_writer[k] = o
                readers[k] = []
            last_per_domain[o.domain] = o
        for o in self.ops:
            for d in o.deps:
                d.inc_needed = True
        dmaq = {}
        counts = {}
        for o in self.ops:
            if o.dma:
                o.inc_needed = True
                q = "sw" if o.engine == "gpsimd" else "hw"
                assert dmaq.setdefault(o.domain, q) == q,                     f"dma key {o.domain} used from both SWDGE and HWDGE queues"
            if o.inc_needed:
                step = 16 if (o.dma and not o.ccinc1) else 1
                counts[o.domain] = counts.get(o.domain, 0) + step
                o.wait_val = counts[o.domain]
        self.final_counts = counts

    def emit_into(self, nc, block, stack):
        self.analyze()
        sems = {}
        for dom in self.final_counts:
            sems[dom] = stack.enter_context(nc.semaphore("s_" + dom))
        streams = {e: [] for e in ENGINES}
        for o in self.ops:
            streams[o.engine].append(o)

        def run_stream(eng_obj, engine_name):
            watermark = {}
            for o in streams[engine_name]:
                for d in o.deps:
                    dom, v = d.domain, d.wait_val
                    if watermark.get(dom, 0) < v:
                        eng_obj.wait_ge(sems[dom], v)
                        watermark[dom] = v
                inst = o.fn(eng_obj)
                if o.inc_needed:
                    assert inst is not None, f"op '{o.tag}' needs inc but returned None"
                    inst.then_inc(sems[o.domain], 16 if (o.dma and not o.ccinc1) else 1)

        for name in ENGINES:
            getattr(block, name)(lambda eng, n=name: run_stream(eng, n))
        return sems




class Ctx:
    def __init__(self, nc, p, tok):
        self.nc = nc
        self.p = p
        self.tok = tok
        self.uid = 0
        self.scopes = [ExitStack()]

    def u(self, base):
        self.uid += 1
        return f"{base}{self.uid}"

    def push_scope(self):
        self.scopes.append(ExitStack())

    def pop_scope(self, barrier=True):
        self.scopes.pop().close()
        if barrier:
            self.p.barrier()

    def sb(self, name, shape, dt=F32, scope=-1):
        return self.scopes[scope].enter_context(self.nc.sbuf_tensor(name, shape, dt))

    def ps(self, name, shape, dt=F32, scope=-1):
        return self.scopes[scope].enter_context(self.nc.psum_tensor(name, shape, dt))

    def dram(self, name, shape, dt=F32, kind="Internal"):
        return self.nc.dram_tensor(name, shape, dt, kind=kind).ap()

    def input(self, name, shape, dt=F32):
        return self.nc.dram_tensor(name, shape, dt, kind="ExternalInput").ap()

    def output(self, name, shape, dt=F32):
        return self.nc.dram_tensor(name, shape, dt, kind="ExternalOutput").ap()

    def setup_common(self, ident_in, identf_in, const_vals=()):
        p = self.p
        self.ident_in = ident_in
        self.identf_in = identf_in
        self.ones_col_bf = self.sb("ones_col_bf", [128, 1], BF16, scope=0)
        self.ones_col_f32 = self.sb("ones_col_f32", [128, 1], F32, scope=0)
        self.ones_row_f32 = self.sb("ones_row_f32", [1, 128], F32, scope=0)
        p.op("vector", lambda e: e.memset(self.ones_col_bf[:], 1.0), writes=["ones"], tag="ones1")
        p.op("vector", lambda e: e.memset(self.ones_col_f32[:], 1.0), writes=["ones"], tag="ones2")
        p.op("vector", lambda e: e.memset(self.ones_row_f32[:], 1.0), writes=["ones"], tag="ones3")
        vals = sorted(set(float(v) for v in const_vals) | {EPS})
        pool = self.sb("constpool", [128, len(vals)], F32, scope=0)
        for i, v in enumerate(vals):
            p.op("vector", lambda e, v=v, i=i: e.memset(pool[:, i:i + 1], v),
                 writes=["ones"], tag=f"const_{i}")
            self.nc.const_aps.register(F32, v, pool[:, i:i + 1])
        self.ident = self.sb("ident", [128, 128], BF16, scope=0)
        p.dma("gpsimd", self.ident[:], self.ident_in, writes=["identk"], tag="ld_ident")
        self.ident_f32 = self.sb("ident_f32", [128, 128], F32, scope=0)
        p.dma("sync", self.ident_f32[:], self.identf_in, writes=["identfk"], tag="ld_identf")
        self.ln_tmp = self.sb("ln_tmp", [128, 512], F32, scope=0)




def emit_weight_load(ctx, key, w_dram, n_kc, m_cols, engine="gpsimd", dt=BF16,
                     rows_per_kc=128):
    tiles = []
    for kc in range(n_kc):
        t = ctx.sb(ctx.u(f"w_{key}_"), [rows_per_kc, m_cols], dt)
        ctx.p.dma(engine, t[:],
                  w_dram[kc * rows_per_kc:(kc + 1) * rows_per_kc, :],
                  writes=[key], tag=f"ldw_{key}_{kc}")
        tiles.append(t)
    return tiles


def mk_stream(ctx, tag, m_cols):
    bufs, keys = [], []
    for i in range(2):
        bufs.append(ctx.sb(ctx.u(f"ws_{tag}_"), [128, m_cols], BF16))
        keys.append(ctx.u(f"wsk_{tag}_"))
    return {"bufs": bufs, "keys": keys, "m_cols": m_cols}


def emit_proj_stream(ctx, tag, stream, w_dram, n_kc, in_fn, in_key, epilogue,
                     tp, mcols_off=0, n_mc=None, extra=None):
    p = ctx.p
    m_cols = stream["m_cols"]
    n_mc = (m_cols - mcols_off) // 128 if n_mc is None else n_mc
    assert tp == 512, "accumulation regions must be bank-aligned (tp=512)"
    GM = 8
    ctx.push_scope()
    ps = ctx.ps(ctx.u(f"pp_{tag}_"), [128, min(n_mc, GM), tp], F32)
    pk = ctx.u(f"ppk_{tag}_")
    bufs, keys = stream["bufs"], stream["keys"]

    def load(kc, g):
        p.dma("gpsimd", bufs[kc % 2][:], w_dram[kc * 128:(kc + 1) * 128, :],
              writes=[keys[kc % 2]], tag=f"ldws_{tag}_{g}_{kc}", nobar=True)
    for g0 in range(0, n_mc, GM):
        gn = min(GM, n_mc - g0)
        load(0, g0)
        if n_kc > 1:
            load(1, g0)
        for kc in range(n_kc):
            b = kc % 2
            for mi in range(gn):
                mc = g0 + mi
                p.op("tensor", lambda e, kc=kc, mc=mc, mi=mi, b=b: e.matmul(
                        ps[:, mi, :],
                        bufs[b][:, mcols_off + mc * 128:mcols_off + (mc + 1) * 128],
                        in_fn(kc),
                        start=(kc == 0),
                        stop=(kc == n_kc - 1 and extra is None)),
                     reads=[keys[b], in_key(kc) if callable(in_key) else in_key],
                     writes=[pk], tag=f"mm_{tag}_{kc}_{mc}")
            if kc + 2 < n_kc:
                load(kc + 2, g0)
        if extra is not None:
            extra(ps, pk)
        for mi in range(gn):
            epilogue(g0 + mi, ps[:, mi, :], pk)
    ctx.pop_scope()


def emit_ln_fm(ctx, tag, src_fn, src_key, out_fn=None, accum_res=None,
               n_chunks=PC, src_bf16=False, tok=None, gain_bias=None):
    nc, p = ctx.nc, ctx.p
    tok = ctx.tok if tok is None else tok
    TT = min(512, tok)
    n_t = tok // TT
    ctx.push_scope()
    sqt = ctx.sb(ctx.u(f"lnsq{tag}_"), [128, 2, TT], BF16)
    sqk = [ctx.u(f"lnsqk{tag}") for _ in range(2)]
    rows = ctx.sb(ctx.u(f"lnrows{tag}_"), [1, 3, tok], F32)
    rk = ctx.u(f"lnrk{tag}")
    prow = ctx.ps(ctx.u(f"lnprow{tag}_"), [1, TT], F32)
    pk = ctx.u(f"lnpk{tag}")
    ones_src = ctx.ones_col_bf if src_bf16 else ctx.ones_col_f32
    for tc in range(n_t):
        for ci in range(n_chunks):
            p.op("tensor", lambda e, ci=ci, tc=tc: e.matmul(
                    prow[:], ones_src[:], src_fn(ci)[:, tc * TT:(tc + 1) * TT],
                    start=(ci == 0), stop=(ci == n_chunks - 1)),
                 reads=[src_key, "ones"], writes=[pk], tag=f"lnsx_{tag}_{tc}_{ci}")
        p.op("vector", lambda e, tc=tc: e.tensor_copy(rows[:, 0, tc * TT:(tc + 1) * TT], prow[:]),
             reads=[pk], writes=[rk], tag=f"lnr0_{tag}_{tc}")
        for ci in range(n_chunks):
            b = ci % 2
            p.op("scalar", lambda e, ci=ci, tc=tc, b=b: e.activation(
                    sqt[:, b, :], src_fn(ci)[:, tc * TT:(tc + 1) * TT], AF.Square),
                 reads=[src_key], writes=[sqk[b]], tag=f"lnsq_{tag}_{tc}_{ci}")
            p.op("tensor", lambda e, ci=ci, b=b: e.matmul(
                    prow[:], ctx.ones_col_bf[:], sqt[:, b, :],
                    start=(ci == 0), stop=(ci == n_chunks - 1)),
                 reads=[sqk[b], "ones"], writes=[pk], tag=f"lnsq2_{tag}_{tc}_{ci}")
        p.op("vector", lambda e, tc=tc: e.tensor_copy(rows[:, 1, tc * TT:(tc + 1) * TT], prow[:]),
             reads=[pk], writes=[rk], tag=f"lnr1_{tag}_{tc}")
    inv = 1.0 / (n_chunks * 128)
    p.op("vector", lambda e: e.tensor_scalar(rows[:, 0, :], rows[:, 0, :], inv, None, ALU.mult),
         reads=[rk], writes=[rk], tag=f"ln_m_{tag}")
    p.op("vector", lambda e: e.tensor_scalar(rows[:, 1, :], rows[:, 1, :], inv, None, ALU.mult),
         reads=[rk], writes=[rk], tag=f"ln_v2_{tag}")
    p.op("vector", lambda e: e.tensor_mul(rows[:, 2, :], rows[:, 0, :], rows[:, 0, :]),
         reads=[rk], writes=[rk], tag=f"ln_msq_{tag}")
    p.op("vector", lambda e: e.tensor_sub(rows[:, 1, :], rows[:, 1, :], rows[:, 2, :]),
         reads=[rk], writes=[rk], tag=f"ln_var_{tag}")
    p.op("scalar", lambda e: e.activation(rows[:, 1, :], rows[:, 1, :], AF.Ln, bias=EPS),
         reads=[rk], writes=[rk], tag=f"ln_ln_{tag}")
    p.op("scalar", lambda e: e.activation(rows[:, 1, :], rows[:, 1, :], AF.Exp, scale=-0.5),
         reads=[rk], writes=[rk], tag=f"ln_exp_{tag}")
    p.op("vector", lambda e: e.tensor_mul(rows[:, 2, :], rows[:, 0, :], rows[:, 1, :]),
         reads=[rk], writes=[rk], tag=f"ln_mr_{tag}")
    pbc = ctx.ps(ctx.u(f"lnbc{tag}_"), [128, 2, TT], F32)
    bck = ctx.u(f"lnbck{tag}")
    for tc in range(n_t):
        sl = slice(tc * TT, (tc + 1) * TT)
        p.op("tensor", lambda e, sl=sl: e.matmul(
                pbc[:, 0, :], ctx.ones_row_f32[:], rows[:, 1, sl], start=True, stop=True),
             reads=[rk, "ones"], writes=[bck], tag=f"lnbcr_{tag}_{tc}")
        p.op("tensor", lambda e, sl=sl: e.matmul(
                pbc[:, 1, :], ctx.ones_row_f32[:], rows[:, 2, sl], start=True, stop=True),
             reads=[rk, "ones"], writes=[bck], tag=f"lnbcm_{tag}_{tc}")
        for ci in range(n_chunks):
            if accum_res is None:
                out_ap, out_key = out_fn(ci)
                dst = out_ap[:, sl]
                if gain_bias is not None:
                    g_fn, b_fn, gb_key = gain_bias
                    tmp = ctx.ln_tmp
                    p.op("vector", lambda e, ci=ci, sl=sl: e.scalar_tensor_tensor(
                            tmp[:, 0:TT], src_fn(ci)[:, sl], 1.0, pbc[:, 0, :],
                            ALU.mult, ALU.mult),
                         reads=[src_key, bck], writes=["ln_tmp"],
                         tag=f"lnm1_{tag}_{tc}_{ci}")
                    p.op("vector", lambda e: e.tensor_sub(
                            tmp[:, 0:TT], tmp[:, 0:TT], pbc[:, 1, :]),
                         reads=[bck, "ln_tmp"], writes=["ln_tmp"],
                         tag=f"lnm2_{tag}_{tc}_{ci}")
                    p.op("vector", lambda e, ci=ci, dst=dst: e.tensor_scalar(
                            dst, tmp[:, 0:TT], g_fn(ci), b_fn(ci),
                            ALU.mult, ALU.add),
                         reads=["ln_tmp", gb_key], writes=[out_key],
                         tag=f"lnm3_{tag}_{tc}_{ci}")
                    continue
                p.op("vector", lambda e, ci=ci, sl=sl, dst=dst: e.scalar_tensor_tensor(
                        dst, src_fn(ci)[:, sl], 1.0, pbc[:, 0, :], ALU.mult, ALU.mult),
                     reads=[src_key, bck], writes=[out_key], tag=f"lnm1_{tag}_{tc}_{ci}")
                p.op("vector", lambda e, dst=dst: e.tensor_sub(dst, dst, pbc[:, 1, :]),
                     reads=[bck, out_key], writes=[out_key], tag=f"lnm2_{tag}_{tc}_{ci}")
            else:
                res_fn, res_key = accum_res
                tmp = ctx.ln_tmp
                p.op("vector", lambda e, ci=ci, sl=sl: e.scalar_tensor_tensor(
                        tmp[:, 0:TT], src_fn(ci)[:, sl], 1.0, pbc[:, 0, :],
                        ALU.mult, ALU.mult),
                     reads=[src_key, bck], writes=["ln_tmp"], tag=f"lnm1_{tag}_{tc}_{ci}")
                p.op("vector", lambda e: e.tensor_sub(tmp[:, 0:TT], tmp[:, 0:TT], pbc[:, 1, :]),
                     reads=[bck, "ln_tmp"], writes=["ln_tmp"], tag=f"lnm2_{tag}_{tc}_{ci}")
                p.op("vector", lambda e, ci=ci, sl=sl: e.tensor_add(
                        res_fn(ci)[:, sl], res_fn(ci)[:, sl], tmp[:, 0:TT]),
                     reads=["ln_tmp", res_key], writes=[res_key],
                     tag=f"lnacc_{tag}_{tc}_{ci}")
    ctx.pop_scope()


def emit_ln_col(ctx, tag, col_fn, col_key, out_fn, out_key):
    p = ctx.p
    ctx.push_scope()
    colsq = ctx.sb(ctx.u("lncol_sq"), [128, PC], F32)
    sqk = ctx.u("lncolsqk")
    for ci in range(PC):
        p.op("scalar", lambda e, ci=ci: e.activation(colsq[:, ci:ci + 1], col_fn(ci), AF.Square),
             reads=[col_key], writes=[sqk], tag=f"lncsq_{tag}_{ci}")
    pr = ctx.ps(ctx.u("lncol_ps"), [1, 2], F32)
    prk = ctx.u("lncolpsk")
    rows = ctx.sb(ctx.u("lncol_rows"), [1, 4], F32)
    rk = ctx.u("lncolrk")
    for ci in range(PC):
        p.op("tensor", lambda e, ci=ci: e.matmul(
                pr[:, 0:1], ctx.ones_col_f32[:], col_fn(ci),
                start=(ci == 0), stop=(ci == PC - 1)),
             reads=[col_key, "ones"], writes=[prk], tag=f"lncs_{tag}_{ci}")
    for ci in range(PC):
        p.op("tensor", lambda e, ci=ci: e.matmul(
                pr[:, 1:2], ctx.ones_col_f32[:], colsq[:, ci:ci + 1],
                start=(ci == 0), stop=(ci == PC - 1)),
             reads=[sqk, "ones"], writes=[prk], tag=f"lncq_{tag}_{ci}")
    p.op("vector", lambda e: e.tensor_scalar(rows[:, 0:2], pr[:], 1.0 / C, None, ALU.mult),
         reads=[prk], writes=[rk], tag=f"lnc_m_{tag}")
    p.op("vector", lambda e: e.tensor_mul(rows[:, 2:3], rows[:, 0:1], rows[:, 0:1]),
         reads=[rk], writes=[rk], tag=f"lnc_msq_{tag}")
    p.op("vector", lambda e: e.tensor_sub(rows[:, 2:3], rows[:, 1:2], rows[:, 2:3]),
         reads=[rk], writes=[rk], tag=f"lnc_var_{tag}")
    p.op("scalar", lambda e: e.activation(rows[:, 2:3], rows[:, 2:3], AF.Ln, bias=EPS),
         reads=[rk], writes=[rk], tag=f"lnc_ln_{tag}")
    p.op("scalar", lambda e: e.activation(rows[:, 2:3], rows[:, 2:3], AF.Exp, scale=-0.5),
         reads=[rk], writes=[rk], tag=f"lnc_exp_{tag}")
    p.op("vector", lambda e: e.tensor_mul(rows[:, 3:4], rows[:, 0:1], rows[:, 2:3]),
         reads=[rk], writes=[rk], tag=f"lnc_mr_{tag}")
    pb = ctx.ps(ctx.u("lncol_pb"), [128, 2], F32)
    pbk = ctx.u("lncolpbk")
    p.op("tensor", lambda e: e.matmul(pb[:], ctx.ones_row_f32[:], rows[:, 2:4],
                                      start=True, stop=True),
         reads=[rk, "ones"], writes=[pbk], tag=f"lnc_bc_{tag}")
    for ci in range(PC):
        p.op("vector", lambda e, ci=ci: e.scalar_tensor_tensor(
                out_fn(ci), col_fn(ci), 1.0, pb[:, 0:1], ALU.mult, ALU.mult),
             reads=[col_key, pbk], writes=[out_key], tag=f"lnc_a1_{tag}_{ci}")
        p.op("vector", lambda e, ci=ci: e.tensor_sub(
                out_fn(ci), out_fn(ci), pb[:, 1:2]),
             reads=[pbk, out_key], writes=[out_key], tag=f"lnc_a2_{tag}_{ci}")
    ctx.pop_scope()



CLAMP_MAX = float(np.log(1.0 / 0.01))
L1F_COLS = [(0, 1), (1, 3), (3, 7), (7, 15)]


def emit_l1_ffn(ctx, x_fn, x_key, inp):
    nc, p, tok = ctx.nc, ctx.p, ctx.tok
    TP = min(512, tok)
    n_pass = tok // TP
    ctx.push_scope()

    st_W = mk_stream(ctx, "l1fW", C)
    st_Wp = mk_stream(ctx, "l1fWp", 256)
    w_wdt = emit_weight_load(ctx, "l1fWd", inp["Wdt"], PC, 15)
    wu = ctx.sb("l1f_wu", [15, C], BF16)
    p.dma("gpsimd", wu[:], inp["Wu"], writes=["l1fWu"], tag="ld_l1fWu")
    sim_sb = ctx.sb("l1f_sim", [128, 2, 4], F32)
    p.dma("sync", sim_sb[:], inp["sim"].rearrange("(a p) e -> p a e", p=128),
          writes=["l1fsim"], tag="ld_l1fsim", allow_slow_non_contiguous=True)
    simb_sb = ctx.sb("l1f_simb", [128, 2, 4], BF16)
    temp_sb = ctx.sb("l1f_temp", [1, 1], F32)
    p.dma("sync", temp_sb[:], inp["temp"], writes=["l1ftemp"], tag="ld_l1ftemp")
    bp_sb = ctx.sb("l1f_bp", [128, 2], F32)
    p.dma("sync", bp_sb[:], inp["bp"], writes=["l1fbp"], tag="ld_l1fbp")

    factor_bc = ctx.sb("l1f_facbc", [128, 4], F32)
    ctx.push_scope()
    simsq = ctx.sb(ctx.u("l1f_simsq"), [128, 2, 4], F32)
    for a in range(2):
        p.op("scalar", lambda e, a=a: e.activation(simsq[:, a, :], sim_sb[:, a, :], AF.Square),
             reads=["l1fsim"], writes=["simsq"], tag=f"l1f_simsq{a}")
    srow = ctx.sb(ctx.u("l1f_srow"), [1, 8], F32)
    ps4 = ctx.ps(ctx.u("l1f_ps4"), [128, 16], F32)
    for a in range(2):
        p.op("tensor", lambda e, a=a: e.matmul(ps4[0:1, 0:4], ctx.ones_col_f32[:],
                                               simsq[:, a, :],
                                               start=(a == 0), stop=(a == 1)),
             reads=["simsq", "ones"], writes=["ps4k"], tag=f"l1f_simss{a}")
    p.op("vector", lambda e: e.tensor_copy(srow[:, 0:4], ps4[0:1, 0:4]),
         reads=["ps4k"], writes=["srowk"], tag="l1f_srow")
    p.op("scalar", lambda e: e.activation(srow[:, 0:4], srow[:, 0:4], AF.Ln),
         reads=["srowk"], writes=["srowk"], tag="l1f_sln")
    p.op("scalar", lambda e: e.activation(srow[:, 0:4], srow[:, 0:4], AF.Exp, scale=0.5),
         reads=["srowk"], writes=["srowk"], tag="l1f_sexp")
    p.op("vector", lambda e: e.tensor_scalar_max(srow[:, 0:4], srow[:, 0:4], 1e-12),
         reads=["srowk"], writes=["srowk"], tag="l1f_smax")
    p.op("vector", lambda e: e.tensor_scalar_min(srow[:, 4:5], temp_sb[:], CLAMP_MAX),
         reads=["l1ftemp"], writes=["srowk"], tag="l1f_tclamp")
    p.op("scalar", lambda e: e.activation(srow[:, 4:5], srow[:, 4:5], AF.Exp),
         reads=["srowk"], writes=["srowk"], tag="l1f_texp")
    p.op("vector", lambda e: e.reciprocal(srow[:, 0:4], srow[:, 0:4]),
         reads=["srowk"], writes=["srowk"], tag="l1f_srcp")
    p.op("vector", lambda e: e.tensor_scalar(srow[:, 0:4], srow[:, 0:4], srow[:, 4:5],
                                             None, ALU.mult),
         reads=["srowk"], writes=["srowk"], tag="l1f_sfac")
    p.op("tensor", lambda e: e.matmul(ps4[:, 4:8], ctx.ones_row_f32[:], srow[:, 0:4],
                                      start=True, stop=True),
         reads=["srowk", "ones"], writes=["ps4k"], tag="l1f_facbcm")
    p.op("vector", lambda e: e.tensor_copy(factor_bc[:], ps4[:, 4:8]),
         reads=["ps4k"], writes=["facbck"], tag="l1f_facbcc")
    for a in range(2):
        p.op("vector", lambda e, a=a: e.tensor_copy(simb_sb[:, a, :], sim_sb[:, a, :]),
             reads=["l1fsim"], writes=["l1fsimb"], tag=f"l1f_simcast{a}")
    ctx.pop_scope()

    def do_pass(pass_i):
        t0 = pass_i * TP
        n_tt = TP // 128
        ctx.push_scope()
        ln_p = ctx.sb(ctx.u("l1f_ln"), [128, PC, TP], BF16)
        lnk = ctx.u("l1f_lnk")
        emit_ln_fm(ctx, f"l1fln{pass_i}",
                   lambda ci, t0=t0: x_fn(ci)[:, t0:t0 + TP], x_key,
                   out_fn=lambda ci, ln_p=ln_p, lnk=lnk: (ln_p[:, ci, :], lnk), tok=TP)
        proj = ctx.sb(ctx.u("l1f_proj"), [128, 2, TP], BF16)
        projk = ctx.u("l1f_projk")
        projsq = ctx.sb(ctx.u("l1f_projsq"), [128, 2, TP], BF16)
        projsqk = ctx.u("l1f_projsqk")

        def proj_ep(mc, psum, pkey):
            p.op("vector", lambda e, mc=mc, psum=psum: e.tensor_scalar(
                    proj[:, mc, :], psum, bp_sb[:, mc:mc + 1], None, ALU.add),
                 reads=[pkey, "l1fbp"], writes=[projk], tag=f"l1f_pev_{pass_i}_{mc}")
            p.op("scalar", lambda e, mc=mc: e.activation(
                    projsq[:, mc, :], proj[:, mc, :], AF.Square),
                 reads=[projk], writes=[projsqk], tag=f"l1f_psq_{pass_i}_{mc}")
        emit_proj_stream(ctx, f"l1fp{pass_i}", st_Wp, inp["Wpt"], PC,
                         lambda kc, ln_p=ln_p: ln_p[:, kc, :], lnk, proj_ep, TP)
        ctx.push_scope()
        rows = ctx.sb(ctx.u("l1f_rows"), [1, 2, TP], F32)
        rwk = ctx.u("l1frowsk")
        prow = ctx.ps(ctx.u("l1f_prow"), [1, TP], F32)
        prk = ctx.u("l1fprk")
        for mc in range(2):
            p.op("tensor", lambda e, mc=mc: e.matmul(
                    prow[:], ctx.ones_col_bf[:], projsq[:, mc, :],
                    start=(mc == 0), stop=(mc == 1)),
                 reads=[projsqk, "ones"], writes=[prk], tag=f"l1f_ss_{pass_i}_{mc}")
        p.op("vector", lambda e: e.tensor_copy(rows[:, 0, :], prow[:]),
             reads=[prk], writes=[rwk], tag=f"l1f_ssr_{pass_i}")
        p.op("scalar", lambda e: e.activation(rows[:, 1, :], rows[:, 0, :], AF.Ln),
             reads=[rwk], writes=[rwk], tag=f"l1f_lnr_{pass_i}")
        p.op("scalar", lambda e: e.activation(rows[:, 1, :], rows[:, 1, :], AF.Exp, scale=0.5),
             reads=[rwk], writes=[rwk], tag=f"l1f_expr_{pass_i}")
        p.op("vector", lambda e: e.tensor_scalar_max(rows[:, 1, :], rows[:, 1, :], 1e-12),
             reads=[rwk], writes=[rwk], tag=f"l1f_maxr_{pass_i}")
        p.op("vector", lambda e: e.reciprocal(rows[:, 1, :], rows[:, 1, :]),
             reads=[rwk], writes=[rwk], tag=f"l1f_rcpr_{pass_i}")
        pbc = ctx.ps(ctx.u("l1f_pbc"), [128, TP], F32)
        pbk = ctx.u("l1fpbk")
        p.op("tensor", lambda e: e.matmul(pbc[:], ctx.ones_row_f32[:], rows[:, 1, :],
                                          start=True, stop=True),
             reads=[rwk, "ones"], writes=[pbk], tag=f"l1f_bc_{pass_i}")
        for mc in range(2):
            p.op("vector", lambda e, mc=mc: e.tensor_mul(proj[:, mc, :], proj[:, mc, :], pbc[:]),
                 reads=[pbk, projk], writes=[projk], tag=f"l1f_scale_{pass_i}_{mc}")
        ctx.pop_scope()
        ret = ctx.sb(ctx.u("l1f_ret"), [128, n_tt, 4], F32)
        retk = ctx.u("l1f_retk")
        ctx.push_scope()
        lg_ps = ctx.ps(ctx.u("l1f_lgps"), [128, 4], F32)
        lgk = ctx.u("lgpsk")
        red = ctx.sb(ctx.u("l1f_red"), [128, 2], F32)
        redk = ctx.u("l1fredk")
        for tt in range(n_tt):
            sl = slice(tt * 128, (tt + 1) * 128)
            for a in range(2):
                p.op("tensor", lambda e, a=a, sl=sl: e.matmul(
                        lg_ps[:], proj[:, a, sl], simb_sb[:, a, :],
                        start=(a == 0), stop=(a == 1)),
                     reads=[projk, "l1fsimb"], writes=[lgk], tag=f"l1f_lg_{pass_i}_{tt}_{a}")
            p.op("vector", lambda e: e.tensor_mul(lg_ps[:], lg_ps[:], factor_bc[:]),
                 reads=[lgk, "facbck"], writes=[lgk], tag=f"l1f_lgs_{pass_i}_{tt}")
            p.op("vector", lambda e: e.reduce_max(red[:, 0:1], lg_ps[:], AX.X, negate=True),
                 reads=[lgk], writes=[redk], tag=f"l1f_nmax_{pass_i}_{tt}")
            p.op("scalar", lambda e, tt=tt: e.activation(
                    ret[:, tt, :], lg_ps[:], AF.Exp, bias=red[:, 0:1],
                    accum_out=red[:, 1:2]),
                 reads=[lgk, redk], writes=[retk, redk], tag=f"l1f_sm_{pass_i}_{tt}")
            p.op("vector", lambda e: e.reciprocal(red[:, 1:2], red[:, 1:2]),
                 reads=[redk], writes=[redk], tag=f"l1f_rl_{pass_i}_{tt}")
            p.op("vector", lambda e, tt=tt: e.tensor_scalar(
                    ret[:, tt, :], ret[:, tt, :], red[:, 1:2], None, ALU.mult),
                 reads=[retk, redk], writes=[retk], tag=f"l1f_lw_{pass_i}_{tt}")
            p.op("vector", lambda e, tt=tt: e.reduce_max(red[:, 0:1], ret[:, tt, :], AX.X),
                 reads=[retk], writes=[redk], tag=f"l1f_lmax_{pass_i}_{tt}")
            p.op("vector", lambda e, tt=tt: e.tensor_scalar(
                    ctx.ln_tmp[:, 0:4], ret[:, tt, :], red[:, 0:1], None, ALU.is_ge),
                 reads=[retk, redk], writes=["ln_tmp"], tag=f"l1f_mask_{pass_i}_{tt}")
            p.op("vector", lambda e, tt=tt: e.tensor_mul(
                    ret[:, tt, :], ret[:, tt, :], ctx.ln_tmp[:, 0:4]),
                 reads=["ln_tmp", retk], writes=[retk], tag=f"l1f_ret_{pass_i}_{tt}")
        ctx.pop_scope()
        scaled_fm = ctx.sb(ctx.u("l1f_scfm"), [15, TP], BF16)
        scfmk = ctx.u("l1f_scfmk")
        ctx.push_scope()
        dn_ps = ctx.ps(ctx.u("l1f_dnps"), [128, 15], F32)
        dnk = ctx.u("dnpsk")
        dn_bf = ctx.sb(ctx.u("l1f_dnbf"), [128, 15], BF16)
        dnbk = ctx.u("dnbfk")
        tr_ps = ctx.ps(ctx.u("l1f_trps"), [15, 128], BF16)
        trk = ctx.u("trpsk")
        for tt in range(n_tt):
            sl = slice(tt * 128, (tt + 1) * 128)
            for kc in range(PC):
                p.op("tensor", lambda e, kc=kc, sl=sl: e.matmul(
                        dn_ps[:], ln_p[:, kc, sl], w_wdt[kc][:],
                        start=(kc == 0), stop=(kc == PC - 1)),
                     reads=[lnk, "l1fWd"], writes=[dnk], tag=f"l1f_dn_{pass_i}_{tt}_{kc}")
            for ei, (c0, c1) in enumerate(L1F_COLS):
                p.op("vector", lambda e, tt=tt, ei=ei, c0=c0, c1=c1: e.tensor_scalar(
                        dn_bf[:, c0:c1], dn_ps[:, c0:c1], ret[:, tt, ei:ei + 1],
                        None, ALU.mult),
                     reads=[dnk, retk], writes=[dnbk], tag=f"l1f_sc_{pass_i}_{tt}_{ei}")
            p.op("tensor", lambda e: e.transpose(tr_ps[:], dn_bf[:], ctx.ident[:]),
                 reads=[dnbk, "identk"], writes=[trk], tag=f"l1f_tr_{pass_i}_{tt}")
            p.op("vector", lambda e, sl=sl: e.tensor_copy(scaled_fm[:, sl], tr_ps[:]),
                 reads=[trk], writes=[scfmk], tag=f"l1f_trc_{pass_i}_{tt}")
        ctx.pop_scope()
        outp = ctx.sb(ctx.u("l1f_out"), [128, PC, TP], BF16)
        outk = ctx.u("l1f_outk")

        def extra(ps, pk):
            for mc in range(PC):
                p.op("tensor", lambda e, mc=mc, ps=ps: e.matmul(
                        ps[:, mc, :], wu[:, mc * 128:(mc + 1) * 128], scaled_fm[:],
                        start=False, stop=True),
                     reads=["l1fWu", scfmk], writes=[pk], tag=f"l1f_up_{pass_i}_{mc}")

        def base_ep(mc, psum, pkey):
            p.op("scalar", lambda e, mc=mc, psum=psum: e.copy(outp[:, mc, :], psum),
                 reads=[pkey], writes=[outk], tag=f"l1f_oev_{pass_i}_{mc}")
        emit_proj_stream(ctx, f"l1fb{pass_i}", st_W, inp["Wt"], PC,
                         lambda kc, ln_p=ln_p: ln_p[:, kc, :], lnk, base_ep, TP,
                         extra=extra)
        emit_ln_fm(ctx, f"l1flno{pass_i}", lambda ci, outp=outp: outp[:, ci, :], outk,
                   accum_res=(lambda ci, t0=t0: x_fn(ci)[:, t0:t0 + TP], x_key),
                   src_bf16=True, tok=TP)
        ctx.pop_scope()

    for _pi in range(n_pass):
        do_pass(_pi)
    ctx.pop_scope()



L1A_H = 16
CHUNK = 128


def emit_tm_proj(ctx, tag, stream, w_dram, n_kc, mcols_off, n_cols, in_fn, in_key,
                 out_cb, tp):
    p = ctx.p
    n_tt = tp // 128
    n_nb = n_cols // 512
    ctx.push_scope()
    ps = ctx.ps(ctx.u(f"ptm_{tag}_"), [128, n_tt, n_nb, 512], F32)
    pk = ctx.u(f"ptmk_{tag}_")
    bufs, keys = stream["bufs"], stream["keys"]

    def load(kc):
        p.dma("gpsimd", bufs[kc % 2][:], w_dram[kc * 128:(kc + 1) * 128, :],
              writes=[keys[kc % 2]], tag=f"ldtm_{tag}_{kc}", nobar=True)
    load(0)
    if n_kc > 1:
        load(1)
    for kc in range(n_kc):
        b = kc % 2
        for tt in range(n_tt):
            for nb in range(n_nb):
                p.op("tensor", lambda e, kc=kc, tt=tt, nb=nb, b=b: e.matmul(
                        ps[:, tt, nb, :], in_fn(kc)[:, tt * 128:(tt + 1) * 128],
                        bufs[b][:, mcols_off + nb * 512:mcols_off + (nb + 1) * 512],
                        start=(kc == 0), stop=(kc == n_kc - 1)),
                     reads=[keys[b], in_key], writes=[pk],
                     tag=f"tmm_{tag}_{kc}_{tt}_{nb}")
        if kc + 2 < n_kc:
            load(kc + 2)
    for tt in range(n_tt):
        for nb in range(n_nb):
            out_cb(tt, nb, ps[:, tt, nb, :], pk)
    ctx.pop_scope()


def emit_l1_attn(ctx, x_fn, x_key, inp, groups8):
    nc, p, tok = ctx.nc, ctx.p, ctx.tok
    TP = min(512, tok)
    n_pass = tok // TP
    n_ch = TP // CHUNK
    n_tt = TP // 128
    H = L1A_H
    ctx.push_scope()

    st_qkv = mk_stream(ctx, "l1aqkv", 3072)
    st_out = mk_stream(ctx, "l1aout", C)
    wqm = ctx.sb("l1a_wqm", [128, 64], BF16)
    wkm = ctx.sb("l1a_wkm", [128, 64], BF16)
    for po in (0, 64):
        p.dma("gpsimd", wqm[po:po + 64, :], inp["wq_mapT"], writes=["l1a_wqm"], tag=f"ld_wqm{po}")
        p.dma("gpsimd", wkm[po:po + 64, :], inp["wk_mapT"], writes=["l1a_wkm"], tag=f"ld_wkm{po}")
    bq = ctx.sb("l1a_bq", [128, 2], F32)
    for po in (0, 64):
        p.dma("sync", bq[po:po + 64, 0:1], inp["bq"], writes=["l1a_bq"], tag=f"ld_bq{po}")
    p.op("vector", lambda e: e.tensor_scalar(bq[:, 1:2], bq[:, 0:1], -1.0, None, ALU.mult),
         reads=["l1a_bq"], writes=["l1a_bq"], tag="negbq")
    bk_rep = ctx.sb("l1a_bk", [128, 64], F32)
    p.dma("sync", bk_rep[:], inp["bk_rep"], writes=["l1a_bk"], tag="ld_bk")
    maskI = ctx.sb("l1a_mask", [128, 128], BF16)
    p.dma("gpsimd", maskI[:], inp["mask_incl"], writes=["l1a_mask"], tag="ld_mask")
    foldm = ctx.sb("l1a_fold", [128, 8], F32)
    p.dma("sync", foldm[:], inp["foldmask"], writes=["l1a_fold"], tag="ld_foldl1")

    q_fm = ctx.sb("l1a_qfm", [128, PC, tok], BF16)
    numer_d = ctx.dram("l1a_numd", [tok // 128, 128, H * 65], BF16)
    ntile = ctx.sb("l1a_nt", [128, H, 65], BF16)
    ntileB = ctx.sb("l1a_ntB", [128, H, 65], BF16)
    kv = ctx.sb("l1a_kv", [128, H, 65], F32)
    kv_bf = ctx.sb("l1a_kvbf", [128, H, 65], BF16)
    s_in = ctx.sb("l1a_sin", [128, H, 65], F32)
    s_in_bf = ctx.sb("l1a_sinbf", [128, H, 65], BF16)
    p.op("vector", lambda e: e.memset(kv[:], 0.0), writes=["l1a_kv"], tag="kv0")

    def emit_eq(pass_i, eq_sb, eq_key, tagp):
        t0 = pass_i * TP
        ctx.push_scope()
        pp = ctx.ps(ctx.u("l1a_pp"), [64, TP], F32)
        ppk = ctx.u("ppk")
        lrow = ctx.sb(ctx.u("l1a_lrow"), [1, 3, TP], F32)
        lrk = ctx.u("lrk")
        prow1 = ctx.ps(ctx.u("l1a_prow"), [1, TP], F32)
        prk = ctx.u("prk")
        pbc = ctx.ps(ctx.u("l1a_pbcq"), [64, TP], F32)
        pbck = ctx.u("pbck")
        for h in range(H):
            ci, po = h // 2, (h % 2) * 64
            p.op("tensor", lambda e, ci=ci, po=po: e.matmul(
                    pp[:], wqm[po:po + 64, :], q_fm[po:po + 64, ci, t0:t0 + TP],
                    start=True, stop=True),
                 reads=["l1a_wqm", "l1a_qfm"], writes=[ppk], tag=f"{tagp}_map_{h}")
            p.op("scalar", lambda e, h=h: e.activation(
                    eq_sb[0:64, h, :], pp[:], AF.Exp, bias=bq[0:64, 0:1]),
                 reads=[ppk, "l1a_bq"], writes=[eq_key], tag=f"{tagp}_eqp_{h}")
            p.op("scalar", lambda e, h=h: e.activation(
                    eq_sb[64:128, h, :], pp[:], AF.Exp, bias=bq[0:64, 1:2], scale=-1.0),
                 reads=[ppk, "l1a_bq"], writes=[eq_key], tag=f"{tagp}_eqm_{h}")
            p.op("tensor", lambda e, h=h: e.matmul(
                    prow1[:], ctx.ones_col_bf[0:64, :], eq_sb[0:64, h, :],
                    start=True, stop=True),
                 reads=[eq_key, "ones"], writes=[prk], tag=f"{tagp}_l1_{h}")
            p.op("vector", lambda e: e.tensor_copy(lrow[:, 0, :], prow1[:]),
                 reads=[prk], writes=[lrk], tag=f"{tagp}_l1c_{h}")
            p.op("tensor", lambda e, h=h: e.matmul(
                    prow1[:], ctx.ones_col_bf[64:128, :], eq_sb[64:128, h, :],
                    start=True, stop=True),
                 reads=[eq_key, "ones"], writes=[prk], tag=f"{tagp}_l2_{h}")
            p.op("vector", lambda e: e.tensor_copy(lrow[:, 1, :], prow1[:]),
                 reads=[prk], writes=[lrk], tag=f"{tagp}_l2c_{h}")
            p.op("vector", lambda e: e.reciprocal(lrow[:, 1, :], lrow[:, 1, :]),
                 reads=[lrk], writes=[lrk], tag=f"{tagp}_l2r_{h}")
            p.op("vector", lambda e: e.tensor_mul(lrow[:, 2, :], lrow[:, 0, :], lrow[:, 1, :]),
                 reads=[lrk], writes=[lrk], tag=f"{tagp}_rho_{h}")
            p.op("tensor", lambda e: e.matmul(
                    pbc[:], ctx.ones_row_f32[:, 0:64], lrow[:, 2, :],
                    start=True, stop=True),
                 reads=[lrk, "ones"], writes=[pbck], tag=f"{tagp}_rhob_{h}")
            p.op("vector", lambda e, h=h: e.tensor_mul(
                    eq_sb[64:128, h, :], eq_sb[64:128, h, :], pbc[:]),
                 reads=[pbck, eq_key], writes=[eq_key], tag=f"{tagp}_eqs_{h}")
        ctx.pop_scope()

    def phase_a(pass_i):
        t0 = pass_i * TP
        ctx.push_scope()
        ln_p = ctx.sb(ctx.u("l1a_ln"), [128, PC, TP], BF16)
        lnk = ctx.u("l1a_lnp")
        emit_ln_fm(ctx, f"l1aln{pass_i}",
                   lambda ci, t0=t0: x_fn(ci)[:, t0:t0 + TP], x_key,
                   out_fn=lambda ci, ln_p=ln_p, lnk=lnk: (ln_p[:, ci, :], lnk), tok=TP)
        k_fm = ctx.sb(ctx.u("l1a_kfm"), [128, PC, TP], BF16)
        kk = ctx.u("l1a_kfmk")

        def qk_ep(mc, psum, pkey, t0=t0):
            if mc < PC:
                p.op("vector", lambda e, mc=mc, psum=psum, t0=t0: e.tensor_copy(
                        q_fm[:, mc, t0:t0 + TP], psum),
                     reads=[pkey], writes=["l1a_qfm"], tag=f"qev_{pass_i}_{mc}")
            else:
                p.op("vector", lambda e, mc=mc, psum=psum: e.tensor_copy(
                        k_fm[:, mc - PC, :], psum),
                     reads=[pkey], writes=[kk], tag=f"kev_{pass_i}_{mc}")
        emit_proj_stream(ctx, f"l1aqk{pass_i}", st_qkv, inp["wqkvT"], PC,
                         lambda kc, ln_p=ln_p: ln_p[:, kc, :], lnk, qk_ep, TP,
                         mcols_off=0, n_mc=16)
        vaug = ctx.sb(ctx.u("l1a_vaug"), [128, n_tt, H, 65], BF16)
        vk = ctx.u("l1a_vaugk")
        p.op("vector", lambda e: e.memset(
                vaug.ap().rearrange("p a h o -> p (a h) o")[:, :, 64:65], 1.0),
             writes=[vk], tag=f"vones_{pass_i}")

        def v_cb(tt, nb, psum, pkey):
            p.op("vector", lambda e, tt=tt, nb=nb, psum=psum: e.tensor_copy(
                    vaug[:, tt, nb * 8:(nb + 1) * 8, 0:64],
                    psum.rearrange("p (h d) -> p h d", d=64)),
                 reads=[pkey], writes=[vk], tag=f"vev_{pass_i}_{tt}_{nb}")
        emit_tm_proj(ctx, f"l1av{pass_i}", st_qkv, inp["wqkvT"], PC, 2048, 1024,
                     lambda kc, ln_p=ln_p: ln_p[:, kc, :], lnk, v_cb, TP)
        eq_sb = ctx.sb(ctx.u("l1a_eq"), [128, H, TP], BF16)
        eqk = ctx.u("l1a_eqk")
        emit_eq(pass_i, eq_sb, eqk, f"eqA{pass_i}")
        phik_fm = ctx.sb(ctx.u("l1a_phikfm"), [128, H, TP], BF16)
        pfk = ctx.u("l1a_phikfmk")
        phik_tm = ctx.sb(ctx.u("l1a_phiktm"), [128, n_tt, H, 128], BF16)
        ptmk = ctx.u("l1a_phiktmk")
        ctx.push_scope()
        pkt = ctx.ps(ctx.u("l1a_pkt"), [128, 64], F32)
        pktk = ctx.u("pktk")
        phtm = ctx.sb(ctx.u("l1a_phtm"), [128, 128], BF16)
        phtmk = ctx.u("phtmk")
        red2 = ctx.sb(ctx.u("l1a_red"), [128, 4], F32)
        redk = ctx.u("redk2")
        ptr = ctx.ps(ctx.u("l1a_ptr"), [128, 128], BF16)
        ptrk = ctx.u("ptrk")
        for tt in range(n_tt):
            for h in range(H):
                ci, po = h // 2, (h % 2) * 64
                p.op("tensor", lambda e, tt=tt, ci=ci, po=po: e.matmul(
                        pkt[:], k_fm[po:po + 64, ci, tt * 128:(tt + 1) * 128],
                        wkm[po:po + 64, :], start=True, stop=True),
                     reads=[kk, "l1a_wkm"], writes=[pktk], tag=f"kmap_{pass_i}_{tt}_{h}")
                p.op("vector", lambda e: e.tensor_add(pkt[:], pkt[:], bk_rep[:]),
                     reads=[pktk, "l1a_bk"], writes=[pktk], tag=f"kbk_{pass_i}_{tt}_{h}")
                p.op("vector", lambda e: e.reduce_max(red2[:, 0:1], pkt[:], AX.X, negate=True),
                     reads=[pktk], writes=[redk], tag=f"km1_{pass_i}_{tt}_{h}")
                p.op("scalar", lambda e: e.activation(
                        phtm[:, 0:64], pkt[:], AF.Exp, bias=red2[:, 0:1],
                        accum_out=red2[:, 1:2]),
                     reads=[pktk, redk], writes=[phtmk, redk], tag=f"km2_{pass_i}_{tt}_{h}")
                p.op("vector", lambda e: e.tensor_reduce(red2[:, 2:3], pkt[:], AX.X, ALU.min),
                     reads=[pktk], writes=[redk], tag=f"km3_{pass_i}_{tt}_{h}")
                p.op("scalar", lambda e: e.activation(
                        phtm[:, 64:128], pkt[:], AF.Exp, bias=red2[:, 2:3], scale=-1.0,
                        accum_out=red2[:, 3:4]),
                     reads=[pktk, redk], writes=[phtmk, redk], tag=f"km4_{pass_i}_{tt}_{h}")
                p.op("vector", lambda e: e.reciprocal(red2[:, 1:2], red2[:, 1:2]),
                     reads=[redk], writes=[redk], tag=f"km5_{pass_i}_{tt}_{h}")
                p.op("vector", lambda e: e.reciprocal(red2[:, 3:4], red2[:, 3:4]),
                     reads=[redk], writes=[redk], tag=f"km6_{pass_i}_{tt}_{h}")
                p.op("vector", lambda e: e.tensor_scalar(
                        phtm[:, 0:64], phtm[:, 0:64], red2[:, 1:2], None, ALU.mult),
                     reads=[phtmk, redk], writes=[phtmk], tag=f"km7_{pass_i}_{tt}_{h}")
                p.op("vector", lambda e: e.tensor_scalar(
                        phtm[:, 64:128], phtm[:, 64:128], red2[:, 3:4], None, ALU.mult),
                     reads=[phtmk, redk], writes=[phtmk], tag=f"km8_{pass_i}_{tt}_{h}")
                p.op("tensor", lambda e: e.transpose(ptr[:], phtm[:], ctx.ident[:]),
                     reads=[phtmk, "identk"], writes=[ptrk], tag=f"ktr_{pass_i}_{tt}_{h}")
                p.op("vector", lambda e, tt=tt, h=h: e.tensor_copy(
                        phik_fm[:, h, tt * 128:(tt + 1) * 128], ptr[:]),
                     reads=[ptrk], writes=[pfk], tag=f"ktrc_{pass_i}_{tt}_{h}")
                p.op("vector", lambda e, tt=tt, h=h: e.tensor_copy(
                        phik_tm[:, tt, h, :], phtm[:]),
                     reads=[phtmk], writes=[ptmk], tag=f"ktms_{pass_i}_{tt}_{h}")
        ctx.pop_scope()
        ctx.push_scope()
        at_sb = ctx.sb(ctx.u("l1a_at"), [128, 128], BF16)
        atk = ctx.u("atk")
        pat = ctx.ps(ctx.u("l1a_pat"), [128, 128], F32)
        patk = ctx.u("patk")
        py = ctx.ps(ctx.u("l1a_py"), [128, 65], F32)
        pyk = ctx.u("pyk")
        pu = ctx.ps(ctx.u("l1a_pu"), [128, 65], F32)
        puk = ctx.u("puk")
        for cc in range(n_ch):
            csl = slice(cc * 128, (cc + 1) * 128)
            p.op("vector", lambda e: e.tensor_copy(kv_bf[:], kv[:]),
                 reads=["l1a_kv"], writes=["l1a_kvbf"], tag=f"kvbf_{pass_i}_{cc}")
            for h in range(H):
                p.op("tensor", lambda e, csl=csl, h=h: e.matmul(
                        pat[:], phik_fm[:, h, csl], eq_sb[:, h, csl],
                        start=True, stop=True),
                     reads=[pfk, eqk], writes=[patk], tag=f"at_{pass_i}_{cc}_{h}")
                p.op("vector", lambda e: e.tensor_mul(at_sb[:], pat[:], maskI[:]),
                     reads=[patk, "l1a_mask"], writes=[atk], tag=f"atm_{pass_i}_{cc}_{h}")
                p.op("tensor", lambda e, cc=cc, h=h: e.matmul(
                        py[:], at_sb[:], vaug[:, cc, h, :], start=True, stop=False),
                     reads=[atk, vk], writes=[pyk], tag=f"av_{pass_i}_{cc}_{h}")
                p.op("tensor", lambda e, csl=csl, h=h: e.matmul(
                        py[:], eq_sb[:, h, csl], kv_bf[:, h, :], start=False, stop=True),
                     reads=[eqk, "l1a_kvbf"], writes=[pyk], tag=f"ycr_{pass_i}_{cc}_{h}")
                p.op("vector", lambda e, h=h: e.tensor_copy(
                        ntile[:, h, :], py[:]),
                     reads=[pyk], writes=["l1a_nt"], tag=f"yev_{pass_i}_{cc}_{h}")
                p.op("tensor", lambda e, cc=cc, h=h: e.matmul(
                        pu[:], phik_tm[:, cc, h, :], vaug[:, cc, h, :],
                        start=True, stop=True),
                     reads=[ptmk, vk], writes=[puk], tag=f"u_{pass_i}_{cc}_{h}")
                p.op("vector", lambda e, h=h: e.tensor_add(kv[:, h, :], kv[:, h, :], pu[:]),
                     reads=[puk, "l1a_kv"], writes=["l1a_kv"], tag=f"kvup_{pass_i}_{cc}_{h}")
            gt = (t0 + cc * 128) // 128
            p.dma("gpsimd", numer_d[gt], ntile.ap().rearrange("p h o -> p (h o)"),
                  reads=["l1a_nt"], writes=["l1a_numd"], tag=f"nst_{pass_i}_{cc}")
        ctx.pop_scope()
        ctx.pop_scope()

    for _pi in range(n_pass):
        phase_a(_pi)

    n_rank = len(groups8[0])
    ctx.push_scope()
    ag_in = ctx.dram(ctx.u("l1a_agin"), [128, H * 65], F32)
    ag_out = ctx.dram(ctx.u("l1a_agout"), [n_rank * 128, H * 65], F32)
    p.dma("sync", ag_in, kv.ap().rearrange("p h o -> p (h o)"),
          reads=["l1a_kv"], writes=["l1a_agin"], tag="ag_in_st")
    p.op("gpsimd", lambda e: e.collective_compute(
            "AllGather", ALU.bypass, groups8, [ag_in.opt()], [ag_out.opt()]),
         reads=["l1a_agin"], writes=["l1a_agout"], dma=True, tag="l1a_ag").ccinc1 = True
    kvj = ctx.sb(ctx.u("l1a_kvj"), [128, H * 65], F32)
    kvjk = ctx.u("l1akvjk")
    p.op("vector", lambda e: e.memset(s_in[:], 0.0), writes=["l1a_sin"], tag="sin0")
    for j in range(n_rank):
        p.dma("sync", kvj[:], ag_out[j * 128:(j + 1) * 128, :],
              reads=["l1a_agout"], writes=[kvjk], tag=f"ldkv_{j}")
        p.op("vector", lambda e, j=j: e.scalar_tensor_tensor(
                s_in.ap().rearrange("p h o -> p (h o)"), kvj[:], foldm[:, j:j + 1],
                s_in.ap().rearrange("p h o -> p (h o)"), ALU.mult, ALU.add),
             reads=[kvjk, "l1a_fold", "l1a_sin"], writes=["l1a_sin"],
             tag=f"fold_{j}")
    p.op("vector", lambda e: e.tensor_copy(s_in_bf[:], s_in[:]),
         reads=["l1a_sin"], writes=["l1a_sinbf"], tag="sinbf")
    ctx.pop_scope()

    def phase_b(pass_i):
        t0 = pass_i * TP
        ctx.push_scope()
        eq_sb = ctx.sb(ctx.u("l1a_eqB"), [128, H, TP], BF16)
        eqk = ctx.u("l1a_eqBk")
        emit_eq(pass_i, eq_sb, eqk, f"eqB{pass_i}")
        y_fm = ctx.sb(ctx.u("l1a_yfm"), [128, PC, TP], BF16)
        yfk = ctx.u("l1a_yfmk")
        ctx.push_scope()
        pc2 = ctx.ps(ctx.u("l1a_pc2"), [128, 65], F32)
        pck = ctx.u("pc2k")
        ytm = ctx.sb(ctx.u("l1a_ytm"), [128, 64], BF16)
        ytmk = ctx.u("ytmk")
        ptr2 = ctx.ps(ctx.u("l1a_ptr2"), [64, 128], BF16)
        ptr2k = ctx.u("ptr2k")
        rcp = ctx.sb(ctx.u("l1a_rcp"), [128, 1], F32)
        rcpk = ctx.u("rcpk")
        for tt in range(n_tt):
            gt = (t0 + tt * 128) // 128
            csl = slice(tt * 128, (tt + 1) * 128)
            p.dma("gpsimd", ntileB.ap().rearrange("p h o -> p (h o)"), numer_d[gt],
                  reads=["l1a_numd"], writes=["l1a_ntB"], tag=f"nld_{pass_i}_{tt}")
            for h in range(H):
                p.op("tensor", lambda e, csl=csl, h=h: e.matmul(
                        pc2[:], eq_sb[:, h, csl], s_in_bf[:, h, :], start=True, stop=True),
                     reads=[eqk, "l1a_sinbf"], writes=[pck], tag=f"corr_{pass_i}_{tt}_{h}")
                p.op("vector", lambda e, h=h: e.tensor_add(
                        pc2[:], pc2[:], ntileB[:, h, :]),
                     reads=["l1a_ntB", pck], writes=[pck], tag=f"cadd_{pass_i}_{tt}_{h}")
                p.op("vector", lambda e: e.tensor_scalar_max(rcp[:], pc2[:, 64:65], 1e-6),
                     reads=[pck], writes=[rcpk], tag=f"dmax_{pass_i}_{tt}_{h}")
                p.op("vector", lambda e: e.reciprocal(rcp[:], rcp[:]),
                     reads=[rcpk], writes=[rcpk], tag=f"drcp_{pass_i}_{tt}_{h}")
                p.op("vector", lambda e: e.tensor_scalar(
                        ytm[:], pc2[:, 0:64], rcp[:], None, ALU.mult),
                     reads=[pck, rcpk], writes=[ytmk], tag=f"ydiv_{pass_i}_{tt}_{h}")
                p.op("tensor", lambda e: e.transpose(ptr2[:], ytm[:], ctx.ident[:]),
                     reads=[ytmk, "identk"], writes=[ptr2k], tag=f"ytr_{pass_i}_{tt}_{h}")
                ci, po = h // 2, (h % 2) * 64
                p.op("vector", lambda e, ci=ci, po=po, csl=csl: e.tensor_copy(
                        y_fm[po:po + 64, ci, csl], ptr2[:]),
                     reads=[ptr2k], writes=[yfk], tag=f"ytrc_{pass_i}_{tt}_{h}")
        ctx.pop_scope()
        outp = ctx.sb(ctx.u("l1a_out"), [128, PC, TP], BF16)
        outk = ctx.u("l1a_outk")

        def o_ep(mc, psum, pkey):
            p.op("scalar", lambda e, mc=mc, psum=psum: e.copy(outp[:, mc, :], psum),
                 reads=[pkey], writes=[outk], tag=f"oev_{pass_i}_{mc}")
        emit_proj_stream(ctx, f"l1ao{pass_i}", st_out, inp["w_outT"], PC,
                         lambda kc, y_fm=y_fm: y_fm[:, kc, :], yfk, o_ep, TP)
        emit_ln_fm(ctx, f"l1alno{pass_i}", lambda ci, outp=outp: outp[:, ci, :], outk,
                   accum_res=(lambda ci, t0=t0: x_fn(ci)[:, t0:t0 + TP], x_key),
                   src_bf16=True, tok=TP)
        ctx.pop_scope()

    for _pi in range(n_pass):
        phase_b(_pi)
    ctx.pop_scope()



WKV_L = 128


def emit_l0_attn(ctx, x_fn, x_key, inp, groups8):
    nc, p, tok = ctx.nc, ctx.p, ctx.tok
    TP = min(512, tok)
    n_pass = tok // TP
    n_ch = TP // WKV_L
    n_tt = TP // 128
    NPT = 16
    NG = 16
    ctx.push_scope()

    st_pad = mk_stream(ctx, "l0apad", 2048)
    st_sq = mk_stream(ctx, "l0asq", C)
    w_maaw1 = emit_weight_load(ctx, "l0a_maaw1", inp["maaw1"], PC, 128)
    maaw2 = ctx.sb("l0a_maaw2s", [128, C], BF16)
    p.dma("gpsimd", maaw2[:], inp["maaw2"], writes=["l0a_maaw2"], tag="ld_maaw2")
    w_dw1 = emit_weight_load(ctx, "l0a_dw1", inp["decayw1"], PC, 64)
    dw2 = ctx.sb("l0a_dw2s", [64, C], BF16)
    p.dma("gpsimd", dw2[:], inp["decayw2"], writes=["l0a_dw2"], tag="ld_dw2")
    dw2p = ctx.sb("l0a_dw2ps", [64, 2048], BF16)
    p.dma("gpsimd", dw2p[:], inp["decayw2p"], writes=["l0a_dw2p"], tag="ld_dw2p")
    maa = {}
    for nm in ("maax", "maaw", "maak", "maav", "maar"):
        t = ctx.sb(f"l0a_{nm}s", [128, PC], F32)
        p.dma("sync", t[:], inp[nm], writes=["l0a_maa"], tag=f"ld_{nm}")
        maa[nm] = t
    decvec_rep = ctx.sb("l0a_dvrep", [128, C], F32)
    p.dma("sync", decvec_rep[:], inp["decvec_rep"], writes=["l0a_dvrep"], tag="ld_dvrep")
    decvec_pad = ctx.sb("l0a_dvpad", [128, NPT], F32)
    p.dma("sync", decvec_pad[:], inp["decvec_pad"], writes=["l0a_dvpad"], tag="ld_dvpad")
    ublk = ctx.sb("l0a_ublks", [128, 64], BF16)
    p.dma("gpsimd", ublk[:], inp["ublk"], writes=["l0a_ublk"], tag="ld_ublk")
    lng_rep = ctx.sb("l0a_lngrep", [128, C], BF16)
    lnb_rep = ctx.sb("l0a_lnbrep", [128, C], BF16)
    p.dma("gpsimd", lng_rep[:], inp["lng_rep"], writes=["l0a_lngb"], tag="ld_lng")
    p.dma("gpsimd", lnb_rep[:], inp["lnb_rep"], writes=["l0a_lngb"], tag="ld_lnb")
    trif = ctx.sb("l0a_trifs", [128, 128], F32)
    p.dma("sync", trif[:], inp["tri_incl"], writes=["l0a_trif"], tag="ld_trif")
    mask4 = ctx.sb("l0a_mask4s", [128, 4 * 128], BF16)
    p.dma("gpsimd", mask4[:], inp["mask_strict4"], writes=["l0a_mask4"], tag="ld_mask4")
    xprev = ctx.sb("l0a_xprevs", [128, PC], F32)
    p.dma("sync", xprev[:], inp["xprev"], writes=["l0a_xprev"], tag="ld_xprev")
    foldm = ctx.sb("l0a_folds", [128, 8], F32)
    p.dma("sync", foldm[:], inp["foldmask"], writes=["l0a_fold"], tag="ld_foldl0")
    zrow = ctx.sb("l0a_zrow", [1, 128], BF16)
    p.op("vector", lambda e: e.memset(zrow[:], 0.0), writes=["l0a_zrow"], tag="zrow")
    zrow512 = ctx.sb("l0a_zrow512", [1, 512], BF16)
    p.op("vector", lambda e: e.memset(zrow512[:], 0.0), writes=["l0a_zrow"], tag="zrow512")

    n_chunk_tot = tok // WKV_L
    lnprev_all = ctx.sb("l0a_lnprev_all", [128, PC, tok // TP], BF16)
    rawu_d = ctx.dram("l0a_rawud", [n_chunk_tot, 128, NG * 16], BF16)
    dstore = ctx.sb("l0a_dst", [128, n_chunk_tot, NG], F32)
    S = ctx.sb("l0a_S", [128, NG, 16], F32)
    Sblk = ctx.sb("l0a_Sblk", [128, NG, 64], BF16)
    p.op("vector", lambda e: e.memset(S[:], 0.0), writes=["l0a_S"], tag="S0")
    p.op("vector", lambda e: e.memset(Sblk[:], 0.0), writes=["l0a_Sblk"], tag="Sblk0")

    def mixes_prep(pass_i, phase):
        t0 = pass_i * TP
        ln_p = ctx.sb(ctx.u("l0a_ln"), [128, PC, TP], BF16)
        lnk = ctx.u("l0a_lnk")
        xx = ctx.sb(ctx.u("l0a_xx"), [128, PC, TP], BF16)
        xxk = ctx.u("l0a_xxk")
        m_fm = ctx.sb(ctx.u("l0a_m"), [128, TP], BF16)
        mk = ctx.u("l0a_mk")
        tmpf = ctx.sb(ctx.u("l0a_tmpf"), [128, TP], F32)
        tfk = ctx.u("tfk")
        ctx.push_scope()
        emit_ln_fm(ctx, f"l0aln{phase}{pass_i}",
                   lambda ci, t0=t0: x_fn(ci)[:, t0:t0 + TP], x_key,
                   out_fn=lambda ci, ln_p=ln_p, lnk=lnk: (ln_p[:, ci, :], lnk), tok=TP)
        lpk = "l0a_lnpa"
        if phase == "A":
            if pass_i == 0:
                emit_ln_col(ctx, f"lp{phase}{pass_i}",
                            lambda ci: xprev[:, ci:ci + 1], "l0a_xprev",
                            lambda ci, pi=pass_i: lnprev_all[:, ci, pi:pi + 1], lpk)
            else:
                emit_ln_col(ctx, f"lp{phase}{pass_i}",
                            lambda ci, t0=t0: x_fn(ci)[:, t0 - 1:t0], x_key,
                            lambda ci, pi=pass_i: lnprev_all[:, ci, pi:pi + 1], lpk)
        for ci in range(PC):
            p.op("vector", lambda e, ci=ci: e.tensor_sub(
                    xx[:, ci, 1:TP], ln_p[:, ci, 0:TP - 1], ln_p[:, ci, 1:TP]),
                 reads=[lnk], writes=[xxk], tag=f"xx_{phase}{pass_i}_{ci}")
            p.op("vector", lambda e, ci=ci, pi=pass_i: e.tensor_sub(
                    xx[:, ci, 0:1], lnprev_all[:, ci, pi:pi + 1], ln_p[:, ci, 0:1]),
                 reads=[lnk, lpk], writes=[xxk], tag=f"xx0_{phase}{pass_i}_{ci}")
        xxx_t = ctx.sb(ctx.u("l0a_xxx"), [128, 2, TP], BF16)
        xxxk = [ctx.u("l0a_xxxk") for _ in range(2)]
        pm = ctx.ps(ctx.u("l0a_pm"), [128, TP], F32)
        pmk = ctx.u("pmk")
        for kc in range(PC):
            b = kc % 2
            p.op("vector", lambda e, kc=kc, b=b: e.scalar_tensor_tensor(
                    xxx_t[:, b, :], xx[:, kc, :], maa["maax"][:, kc:kc + 1],
                    ln_p[:, kc, :], ALU.mult, ALU.add),
                 reads=[xxk, lnk, "l0a_maa"], writes=[xxxk[b]],
                 tag=f"xxx_{phase}{pass_i}_{kc}")
            p.op("tensor", lambda e, kc=kc, b=b: e.matmul(
                    pm[:], w_maaw1[kc][:], xxx_t[:, b, :],
                    start=(kc == 0), stop=(kc == PC - 1)),
                 reads=["l0a_maaw1", xxxk[b]], writes=[pmk], tag=f"mm_m_{phase}{pass_i}_{kc}")
        p.op("scalar", lambda e: e.activation(m_fm[:], pm[:], AF.Tanh),
             reads=[pmk], writes=[mk], tag=f"mtanh_{phase}{pass_i}")
        ctx.pop_scope()
        return {"ln": (ln_p, lnk), "xx": (xx, xxk), "m": (m_fm, mk),
                "tmpf": (tmpf, tfk), "phase": phase, "pass_i": pass_i}

    _forder = {"w": 0, "k": 1, "v": 2, "r": 3}

    def mix_field(prep, f, xf, xfk):
        fi = _forder[f]
        ln_p, lnk = prep["ln"]
        xx, xxk = prep["xx"]
        m_fm, mk = prep["m"]
        tmpf, tfk = prep["tmpf"]
        phase, pass_i = prep["phase"], prep["pass_i"]
        ctx.push_scope()
        pf = ctx.ps(ctx.u("l0a_pf"), [128, TP], F32)
        pfk = ctx.u("pfk")
        for mc in range(PC):
            p.op("tensor", lambda e, fi=fi, mc=mc: e.matmul(
                    pf[:], maaw2[32 * fi:32 * fi + 32, mc * 128:(mc + 1) * 128],
                    m_fm[32 * fi:32 * fi + 32, :], start=True, stop=True,
                    tile_position=(32 * fi, 0)),
                 reads=["l0a_maaw2", mk], writes=[pfk], tag=f"mf_{phase}{pass_i}_{f}_{mc}")
            p.op("vector", lambda e, mc=mc, f=f: e.tensor_scalar(
                    tmpf[:], pf[:], maa["maa" + f][:, mc:mc + 1], None, ALU.add),
                 reads=[pfk, "l0a_maa"], writes=[tfk], tag=f"mfa_{phase}{pass_i}_{f}_{mc}")
            p.op("vector", lambda e, mc=mc: e.tensor_mul(tmpf[:], tmpf[:], xx[:, mc, :]),
                 reads=[tfk, xxk], writes=[tfk], tag=f"mfb_{phase}{pass_i}_{f}_{mc}")
            p.op("gpsimd", lambda e, mc=mc, xf=xf: e.tensor_add(
                    xf[:, mc, :], tmpf[:], ln_p[:, mc, :]),
                 reads=[tfk, lnk], writes=[xfk], tag=f"mfc_{phase}{pass_i}_{f}_{mc}")
        ctx.pop_scope()

    def decay_tanh(xw, xwk, tagp):
        wt = ctx.sb(ctx.u("l0a_wt"), [64, TP], BF16)
        wtk = ctx.u("l0a_wtk")
        ctx.push_scope()
        pd = ctx.ps(ctx.u("l0a_pd"), [64, TP], F32)
        pdk = ctx.u("pdk")
        for kc in range(PC):
            p.op("tensor", lambda e, kc=kc: e.matmul(
                    pd[:], w_dw1[kc][:], xw[:, kc, :],
                    start=(kc == 0), stop=(kc == PC - 1)),
                 reads=["l0a_dw1", xwk], writes=[pdk], tag=f"dmm_{tagp}_{kc}")
        p.op("scalar", lambda e: e.activation(wt[:], pd[:], AF.Tanh),
             reads=[pdk], writes=[wtk], tag=f"dtanh_{tagp}")
        ctx.pop_scope()
        return wt, wtk

    def phase_a(pass_i):
        ctx.push_scope()
        ktm = ctx.sb(ctx.u("l0a_ktm"), [128, n_tt, C], BF16)
        ktmk = ctx.u("ktmk")
        vtm = ctx.sb(ctx.u("l0a_vtm"), [128, n_tt, C], BF16)
        vtmk = ctx.u("vtmk")
        xfb = ctx.sb(ctx.u("l0a_xfb"), [128, PC, TP], BF16)
        xfbk = ctx.u("l0a_xfbk")
        prep = mixes_prep(pass_i, "A")
        mix_field(prep, "w", xfb, xfbk)
        wt, wtk = decay_tanh(xfb, xfbk, f"A{pass_i}")
        mix_field(prep, "k", xfb, xfbk)

        def k_cb(tt, nb, psum, pkey):
            p.op("vector", lambda e, tt=tt, nb=nb, psum=psum: e.tensor_copy(
                    ktm[:, tt, nb * 512:(nb + 1) * 512], psum),
                 reads=[pkey], writes=[ktmk], tag=f"kte_{pass_i}_{tt}_{nb}")
        emit_tm_proj(ctx, f"l0akt{pass_i}", st_sq, inp["Wk"], PC, 0, C,
                     lambda kc: xfb[:, kc, :], xfbk, k_cb, TP)
        mix_field(prep, "v", xfb, xfbk)

        def v_cb(tt, nb, psum, pkey):
            p.op("vector", lambda e, tt=tt, nb=nb, psum=psum: e.tensor_copy(
                    vtm[:, tt, nb * 512:(nb + 1) * 512], psum),
                 reads=[pkey], writes=[vtmk], tag=f"vte_{pass_i}_{tt}_{nb}")
        emit_tm_proj(ctx, f"l0avt{pass_i}", st_sq, inp["Wv"], PC, 0, C,
                     lambda kc: xfb[:, kc, :], xfbk, v_cb, TP)
        ctx.push_scope()
        pwp = ctx.ps(ctx.u("l0a_pwp"), [128, TP], F32)
        pwpk = ctx.u("pwpk")
        ewpad = ctx.sb(ctx.u("l0a_ewpad"), [128, TP], F32)
        ewpk = ctx.u("ewpadk")
        gl = ctx.sb(ctx.u("l0a_gl"), [128, n_ch, NPT], F32)
        glk = ctx.u("glk")
        for pt in range(NPT):
            p.op("tensor", lambda e, pt=pt: e.matmul(
                    pwp[:], dw2p[:, pt * 128:(pt + 1) * 128], wt[:],
                    start=True, stop=True),
                 reads=["l0a_dw2p", wtk], writes=[pwpk], tag=f"dpp_{pass_i}_{pt}")
            p.op("scalar", lambda e, pt=pt: e.activation(
                    ewpad[:], pwp[:], AF.Exp, bias=decvec_pad[:, pt:pt + 1]),
                 reads=[pwpk, "l0a_dvpad"], writes=[ewpk], tag=f"dpe_{pass_i}_{pt}")
            for cc in range(n_ch):
                p.op("vector", lambda e, pt=pt, cc=cc: e.reduce_sum(
                        gl[:, cc, pt:pt + 1], ewpad[:, cc * WKV_L:(cc + 1) * WKV_L], AX.X),
                     reads=[ewpk], writes=[glk], tag=f"dps_{pass_i}_{pt}_{cc}")
        for cc in range(n_ch):
            gchunk = pass_i * n_ch + cc
            p.op("scalar", lambda e, cc=cc, gchunk=gchunk: e.activation(
                    dstore[:, gchunk, :], gl[:, cc, :], AF.Exp, scale=-1.0),
                 reads=[glk], writes=["l0a_dstore"], tag=f"dexp_{pass_i}_{cc}")
        pw = ctx.ps(ctx.u("l0a_pw"), [128, C], F32)
        pwk = ctx.u("pwk")
        ewtm = ctx.sb(ctx.u("l0a_ewtm"), [128, C], F32)
        ewk = ctx.u("ewtmk")
        gtm = ctx.sb(ctx.u("l0a_gtm"), [128, C], F32)
        gk = ctx.u("gtmk")
        pg = ctx.ps(ctx.u("l0a_pg"), [128, C], F32)
        pgk = ctx.u("pgk")
        kptm = ctx.sb(ctx.u("l0a_kptm"), [128, C], BF16)
        kpk = ctx.u("kptmk")
        pru = ctx.ps(ctx.u("l0a_pru"), [128, NG * 16], F32)
        pruk = ctx.u("pruk")
        rutile = ctx.sb(ctx.u("l0a_rutile"), [128, NG * 16], BF16)
        for tt in range(n_tt):
            for nb in range(2):
                p.op("tensor", lambda e, tt=tt, nb=nb: e.matmul(
                        pw[:, nb * 512:(nb + 1) * 512], wt[:, tt * 128:(tt + 1) * 128],
                        dw2[:, nb * 512:(nb + 1) * 512], start=True, stop=True),
                     reads=[wtk, "l0a_dw2"], writes=[pwk], tag=f"wtm_{pass_i}_{tt}_{nb}")
            p.op("vector", lambda e: e.tensor_add(ewtm[:], pw[:], decvec_rep[:]),
                 reads=[pwk, "l0a_dvrep"], writes=[ewk], tag=f"ewadd_{pass_i}_{tt}")
            p.op("scalar", lambda e: e.activation(ewtm[:], ewtm[:], AF.Exp),
                 reads=[ewk], writes=[ewk], tag=f"ewexp_{pass_i}_{tt}")
            for nb in range(2):
                p.op("tensor", lambda e, nb=nb: e.matmul(
                        pg[:, nb * 512:(nb + 1) * 512], trif[:],
                        ewtm[:, nb * 512:(nb + 1) * 512], start=True, stop=True),
                     reads=[ewk, "l0a_trif"], writes=[pgk], tag=f"gtm_{pass_i}_{tt}_{nb}")
            p.op("scalar", lambda e: e.activation(gtm[:], pg[:], AF.Exp),
                 reads=[pgk], writes=[gk], tag=f"gexp_{pass_i}_{tt}")
            p.op("vector", lambda e, tt=tt: e.tensor_mul(kptm[:], ktm[:, tt, :], gtm[:]),
                 reads=[gk, ktmk], writes=[kpk], tag=f"kp_{pass_i}_{tt}")
            gchunk = pass_i * n_ch + tt
            for g in range(NG):
                p.op("tensor", lambda e, g=g: e.matmul(
                        pru[:, g * 16:g * 16 + 16], zrow[:], zrow[:, 0:16],
                        start=True, stop=False),
                     reads=["l0a_zrow"], writes=[pruk], tag=f"ruz_{pass_i}_{tt}_{g}")
                for j in range(4):
                    h = g * 4 + j
                    p.op("tensor", lambda e, g=g, j=j, h=h, tt=tt: e.matmul(
                            pru[32 * j:32 * j + 16, g * 16:g * 16 + 16],
                            kptm[:, h * 16:h * 16 + 16],
                            vtm[:, tt, h * 16:h * 16 + 16],
                            start=False, stop=False,
                            tile_position=(0, 32 * j)),
                         reads=[kpk, vtmk], writes=[pruk], tag=f"ru_{pass_i}_{tt}_{g}_{j}")
                p.op("tensor", lambda e, g=g: e.matmul(
                        pru[:, g * 16:g * 16 + 16], zrow[:], zrow[:, 0:16],
                        start=False, stop=True),
                     reads=["l0a_zrow"], writes=[pruk], tag=f"ruc_{pass_i}_{tt}_{g}")
            p.op("vector", lambda e: e.tensor_copy(rutile[:], pru[:]),
                 reads=[pruk], writes=["l0a_rut"], tag=f"ruev_{pass_i}_{tt}")
            p.dma("gpsimd", rawu_d[gchunk], rutile[:],
                  reads=["l0a_rut"], writes=["l0a_rawu"], tag=f"rust_{pass_i}_{tt}")
            p.op("vector", lambda e: e.tensor_add(
                    S.ap().rearrange("p g j -> p (g j)"),
                    S.ap().rearrange("p g j -> p (g j)"), rutile[:]),
                 reads=["l0a_rut", "l0a_S"], writes=["l0a_S"], tag=f"sadd_{pass_i}_{tt}")
            p.op("vector", lambda e, gchunk=gchunk: e.tensor_mul(
                    S[:], S[:], dstore[:, gchunk, :].unsqueeze(2).broadcast_to([128, NG, 16])),
                 reads=["l0a_dstore", "l0a_S"], writes=["l0a_S"], tag=f"smul_{pass_i}_{tt}")
        ctx.pop_scope()
        ctx.pop_scope()

    for _pi in range(n_pass):
        phase_a(_pi)

    ctx.push_scope()
    dtot = ctx.sb(ctx.u("l0a_dtot"), [128, NG], F32)
    dtk = ctx.u("dtotk")
    p.op("vector", lambda e: e.tensor_copy(dtot[:], dstore[:, 0, :]),
         reads=["l0a_dstore"], writes=[dtk], tag="dtot0")
    for gc in range(1, n_chunk_tot):
        p.op("vector", lambda e, gc=gc: e.tensor_mul(dtot[:], dtot[:], dstore[:, gc, :]),
             reads=["l0a_dstore", dtk], writes=[dtk], tag=f"dtot{gc}")
    n_rank = len(groups8[0])
    ag_in = ctx.dram(ctx.u("l0a_agin"), [128, NG * 16 + NG], F32)
    ag_out = ctx.dram(ctx.u("l0a_agout"), [n_rank * 128, NG * 16 + NG], F32)
    p.dma("sync", ag_in[:, 0:NG * 16], S.ap().rearrange("p g j -> p (g j)"),
          reads=["l0a_S"], writes=["l0a_agin"], tag="l0a_agst1")
    p.dma("sync", ag_in[:, NG * 16:], dtot[:],
          reads=[dtk], writes=["l0a_agin2"], tag="l0a_agst2")
    p.op("gpsimd", lambda e: e.collective_compute(
            "AllGather", ALU.bypass, groups8, [ag_in.opt()], [ag_out.opt()]),
         reads=["l0a_agin", "l0a_agin2"], writes=["l0a_agout"], dma=True,
         tag="l0a_ag").ccinc1 = True
    sg = ctx.sb(ctx.u("l0a_sg"), [128, NG, 16], F32)
    sgk = ctx.u("sgk")
    buf = ctx.sb(ctx.u("l0a_agbuf"), [128, NG * 16 + NG], F32)
    bk2 = ctx.u("agbufk")
    dfac = ctx.sb(ctx.u("l0a_dfac"), [128, NG], F32)
    dfk = ctx.u("dfack")
    p.op("vector", lambda e: e.memset(sg[:], 0.0), writes=[sgk], tag="sg0")
    for j in range(n_rank):
        p.dma("sync", buf[:], ag_out[j * 128:(j + 1) * 128, :],
              reads=["l0a_agout"], writes=[bk2], tag=f"l0a_ldag{j}")
        p.op("vector", lambda e: e.tensor_scalar(dfac[:], buf[:, NG * 16:], -1.0, None, ALU.add),
             reads=[bk2], writes=[dfk], tag=f"l0a_df1_{j}")
        p.op("vector", lambda e, j=j: e.tensor_scalar(
                dfac[:], dfac[:], foldm[:, j:j + 1], 1.0, ALU.mult, ALU.add),
             reads=[dfk, "l0a_fold"], writes=[dfk], tag=f"l0a_df2_{j}")
        p.op("vector", lambda e: e.tensor_mul(
                sg[:], sg[:], dfac.ap().unsqueeze(2).broadcast_to([128, NG, 16])),
             reads=[dfk, sgk], writes=[sgk], tag=f"l0a_sgm_{j}")
        p.op("vector", lambda e, j=j: e.scalar_tensor_tensor(
                sg.ap().rearrange("p g j -> p (g j)"), buf[:, 0:NG * 16],
                foldm[:, j:j + 1], sg.ap().rearrange("p g j -> p (g j)"),
                ALU.mult, ALU.add),
             reads=[bk2, "l0a_fold", sgk], writes=[sgk], tag=f"l0a_sga_{j}")
    p.op("vector", lambda e: e.tensor_copy(S[:], sg[:]),
         reads=[sgk], writes=["l0a_S"], tag="l0a_sreset")
    ctx.pop_scope()

    def phase_b(pass_i):
        t0 = pass_i * TP
        ctx.push_scope()
        rpad = ctx.sb(ctx.u("l0a_rpad"), [128, NPT, TP], BF16)
        rpk = ctx.u("rpadk")
        kpad = ctx.sb(ctx.u("l0a_kpad"), [128, NPT, TP], BF16)
        kpdk = ctx.u("kpadk")
        vtm = ctx.sb(ctx.u("l0a_vtmB"), [128, n_tt, C], BF16)
        vtmk = ctx.u("vtmBk")
        du = ctx.sb(ctx.u("l0a_du"), [128, n_tt, 64], F32)
        duk = ctx.u("duk")
        ytm = ctx.sb(ctx.u("l0a_ytm"), [128, n_tt, C], BF16)
        ytmk = ctx.u("ytmk")
        y_fm = ctx.sb(ctx.u("l0a_yfm"), [128, PC, TP], BF16)
        yfk = ctx.u("yfmk")
        ctx.push_scope()
        xfb = ctx.sb(ctx.u("l0a_xfbB"), [128, PC, TP], BF16)
        xfbk = ctx.u("l0a_xfbBk")
        prep = mixes_prep(pass_i, "B")

        def dbg_dump(f):
            if "dbg_xr" in inp:
                p.dma("gpsimd",
                      inp["dbg_x" + f][:, t0:t0 + TP].rearrange("(a p) t -> p a t", p=128),
                      xfb[:], reads=[xfbk],
                      writes=["dbgx" + f], tag=f"dbgx_{f}_{pass_i}")
        mix_field(prep, "w", xfb, xfbk)
        dbg_dump("w")
        wt, wtk = decay_tanh(xfb, xfbk, f"B{pass_i}")
        mix_field(prep, "r", xfb, xfbk)
        dbg_dump("r")

        def rp_ep(mc, psum, pkey):
            p.op("vector", lambda e, mc=mc, psum=psum: e.tensor_copy(rpad[:, mc, :], psum),
                 reads=[pkey], writes=[rpk], tag=f"rpev_{pass_i}_{mc}")
        emit_proj_stream(ctx, f"l0arp{pass_i}", st_pad, inp["Wrp"], PC,
                         lambda kc: xfb[:, kc, :], xfbk, rp_ep, TP)
        mix_field(prep, "k", xfb, xfbk)
        dbg_dump("k")

        def kp_ep(mc, psum, pkey):
            p.op("vector", lambda e, mc=mc, psum=psum: e.tensor_copy(kpad[:, mc, :], psum),
                 reads=[pkey], writes=[kpdk], tag=f"kpev_{pass_i}_{mc}")
        emit_proj_stream(ctx, f"l0akp{pass_i}", st_pad, inp["Wkp"], PC,
                         lambda kc: xfb[:, kc, :], xfbk, kp_ep, TP)
        mix_field(prep, "v", xfb, xfbk)
        dbg_dump("v")

        def v_cb(tt, nb, psum, pkey):
            p.op("vector", lambda e, tt=tt, nb=nb, psum=psum: e.tensor_copy(
                    vtm[:, tt, nb * 512:(nb + 1) * 512], psum),
                 reads=[pkey], writes=[vtmk], tag=f"vteB_{pass_i}_{tt}_{nb}")
        emit_tm_proj(ctx, f"l0avtB{pass_i}", st_sq, inp["Wv"], PC, 0, C,
                     lambda kc: xfb[:, kc, :], xfbk, v_cb, TP)
        ctx.push_scope()
        rk_sb = ctx.sb(ctx.u("l0a_rk"), [128, TP], BF16)
        rkk = ctx.u("rkk")
        pdu = ctx.ps(ctx.u("l0a_pdu"), [128, 64], F32)
        pduk = ctx.u("pduk")
        for pt in range(NPT):
            p.op("vector", lambda e, pt=pt: e.tensor_mul(
                    rk_sb[:], rpad[:, pt, :], kpad[:, pt, :]),
                 reads=[rpk, kpdk], writes=[rkk], tag=f"rk_{pass_i}_{pt}")
            for cc in range(n_ch):
                p.op("tensor", lambda e, pt=pt, cc=cc: e.matmul(
                        pdu[:, pt * 4:pt * 4 + 4], rk_sb[:, cc * WKV_L:(cc + 1) * WKV_L],
                        ublk[:, pt * 4:pt * 4 + 4], start=True, stop=True),
                     reads=[rkk, "l0a_ublk"], writes=[pduk], tag=f"du_{pass_i}_{pt}_{cc}")
                p.op("vector", lambda e, pt=pt, cc=cc: e.tensor_copy(
                        du[:, cc, pt * 4:pt * 4 + 4], pdu[:, pt * 4:pt * 4 + 4]),
                     reads=[pduk], writes=[duk], tag=f"duev_{pass_i}_{pt}_{cc}")
        ctx.pop_scope()
        ctx.push_scope()
        pwp = ctx.ps(ctx.u("l0a_pwpB"), [128, TP], F32)
        pwpk = ctx.u("pwpBk")
        ew_f = ctx.sb(ctx.u("l0a_ewf"), [128, TP], F32)
        ewfk = ctx.u("ewfk")
        g_f = ctx.sb(ctx.u("l0a_gf"), [128, TP], F32)
        gfk = ctx.u("gfk")
        es_f = ctx.sb(ctx.u("l0a_esf"), [128, TP], F32)
        esk = ctx.u("esfk")
        for pt in range(NPT):
            p.op("tensor", lambda e, pt=pt: e.matmul(
                    pwp[:], dw2p[:, pt * 128:(pt + 1) * 128], wt[:],
                    start=True, stop=True),
                 reads=["l0a_dw2p", wtk], writes=[pwpk], tag=f"dppB_{pass_i}_{pt}")
            p.op("scalar", lambda e, pt=pt: e.activation(
                    ew_f[:], pwp[:], AF.Exp, bias=decvec_pad[:, pt:pt + 1]),
                 reads=[pwpk, "l0a_dvpad"], writes=[ewfk], tag=f"dpeB_{pass_i}_{pt}")
            for cc in range(n_ch):
                sl = slice(cc * WKV_L, (cc + 1) * WKV_L)
                p.op("vector", lambda e, sl=sl: e.tensor_tensor_scan(
                        g_f[:, sl], ctx.ones_col_f32.ap().broadcast_to([128, WKV_L]),
                        ew_f[:, sl], 0.0, ALU.mult, ALU.add),
                     reads=[ewfk, "ones"], writes=[gfk], tag=f"scan_{pass_i}_{pt}_{cc}")
            p.op("scalar", lambda e: e.activation(es_f[:], g_f[:], AF.Exp),
                 reads=[gfk], writes=[esk], tag=f"ekp_{pass_i}_{pt}")
            p.op("vector", lambda e, pt=pt: e.tensor_mul(
                    kpad[:, pt, :], kpad[:, pt, :], es_f[:]),
                 reads=[esk, kpdk], writes=[kpdk], tag=f"kps_{pass_i}_{pt}")
            p.op("vector", lambda e: e.tensor_sub(g_f[:], ew_f[:], g_f[:]),
                 reads=[ewfk, gfk], writes=[gfk], tag=f"qarg_{pass_i}_{pt}")
            p.op("scalar", lambda e: e.activation(es_f[:], g_f[:], AF.Exp),
                 reads=[gfk], writes=[esk], tag=f"eqp_{pass_i}_{pt}")
            p.op("vector", lambda e, pt=pt: e.tensor_mul(
                    rpad[:, pt, :], rpad[:, pt, :], es_f[:]),
                 reads=[esk, rpk], writes=[rpk], tag=f"qps_{pass_i}_{pt}")
        ctx.pop_scope()
        ctx.pop_scope()
        ctx.push_scope()
        pat4 = ctx.ps(ctx.u("l0a_pat"), [128, 4, 512], F32)
        patk = ctx.u("patk")
        at_sb = ctx.sb(ctx.u("l0a_at"), [128, 4 * 128], BF16)
        atk = ctx.u("atk")
        pY = ctx.ps(ctx.u("l0a_pY"), [128, C], F32)
        pYk = ctx.u("pYk")
        vdu = ctx.sb(ctx.u("l0a_vdu"), [128, C], F32)
        vduk = ctx.u("vduk")
        rutileB = ctx.sb(ctx.u("l0a_rutileB"), [128, NG * 16], BF16)
        for cc in range(n_ch):
            csl = slice(cc * WKV_L, (cc + 1) * WKV_L)
            gchunk = pass_i * n_ch + cc
            for j in range(4):
                p.op("vector", lambda e, j=j: e.tensor_copy(
                        Sblk[32 * j:32 * j + 32, :, j * 16:(j + 1) * 16],
                        S[32 * j:32 * j + 32, :, :]),
                     reads=["l0a_S"], writes=["l0a_Sblk"], tag=f"sbf_{pass_i}_{cc}_{j}")
            for hb in range(2):
                p.op("tensor", lambda e, hb=hb: e.matmul(
                        pY[:, hb * 512:(hb + 1) * 512], zrow[:], zrow512[:],
                        start=True, stop=False),
                     reads=["l0a_zrow"], writes=[pYk], tag=f"yz_{pass_i}_{cc}_{hb}")
            for pt in range(NPT):
                for j in range(4):
                    p.op("tensor", lambda e, pt=pt, j=j, csl=csl: e.matmul(
                            pat4[:, j, 0:128],
                            kpad[32 * j:32 * j + 32, pt, csl],
                            rpad[32 * j:32 * j + 32, pt, csl],
                            start=True, stop=True, tile_position=(32 * j, 0)),
                         reads=[kpdk, rpk], writes=[patk],
                         tag=f"at_{pass_i}_{cc}_{pt}_{j}")
                p.op("vector", lambda e: e.tensor_mul(
                        at_sb.ap().rearrange("p (a b) -> p a b", b=128),
                        pat4[:, :, 0:128],
                        mask4.ap().rearrange("p (a b) -> p a b", b=128)),
                     reads=[patk, "l0a_mask4"], writes=[atk],
                     tag=f"atm_{pass_i}_{cc}_{pt}")
                for j in range(4):
                    h = pt * 4 + j
                    p.op("tensor", lambda e, j=j, h=h, cc=cc: e.matmul(
                            pY[:, h * 16:h * 16 + 16], at_sb[:, j * 128:(j + 1) * 128],
                            vtm[:, cc, h * 16:h * 16 + 16], start=False, stop=False),
                         reads=[atk, vtmk], writes=[pYk], tag=f"av_{pass_i}_{cc}_{pt}_{j}")
                p.op("tensor", lambda e, pt=pt, csl=csl: e.matmul(
                        pY[:, pt * 64:(pt + 1) * 64], rpad[:, pt, csl],
                        Sblk[:, pt, :], start=False, stop=False),
                     reads=[rpk, "l0a_Sblk"], writes=[pYk], tag=f"ycr_{pass_i}_{cc}_{pt}")
            for hb in range(2):
                p.op("tensor", lambda e, hb=hb: e.matmul(
                        pY[:, hb * 512:(hb + 1) * 512], zrow[:], zrow512[:],
                        start=False, stop=True),
                     reads=["l0a_zrow", pYk], writes=[pYk], tag=f"yc_{pass_i}_{cc}_{hb}")
            p.op("vector", lambda e, cc=cc: e.tensor_mul(
                    vdu.ap().rearrange("p (h j) -> p h j", j=16),
                    vtm[:, cc, :].rearrange("p (h j) -> p h j", j=16),
                    du[:, cc, :].unsqueeze(2).broadcast_to([128, 64, 16])),
                 reads=[vtmk, duk], writes=[vduk], tag=f"vdu_{pass_i}_{cc}")
            p.op("vector", lambda e, cc=cc: e.tensor_add(ytm[:, cc, :], vdu[:], pY[:]),
                 reads=[vduk, pYk], writes=[ytmk], tag=f"yadd_{pass_i}_{cc}")
            if "dbg_y" in inp:
                p.dma("gpsimd", inp["dbg_y"][t0 + cc * WKV_L:t0 + (cc + 1) * WKV_L, :],
                      ytm[:, cc, :], reads=[ytmk], writes=["dbgy"],
                      tag=f"dbgy_{pass_i}_{cc}")
            p.dma("gpsimd", rutileB[:], rawu_d[gchunk],
                  reads=["l0a_rawu"], writes=["l0a_rutB"], tag=f"ruld_{pass_i}_{cc}")
            p.op("vector", lambda e: e.tensor_add(
                    S.ap().rearrange("p g j -> p (g j)"),
                    S.ap().rearrange("p g j -> p (g j)"), rutileB[:]),
                 reads=["l0a_rutB", "l0a_S"], writes=["l0a_S"], tag=f"saddB_{pass_i}_{cc}")
            p.op("vector", lambda e, gchunk=gchunk: e.tensor_mul(
                    S[:], S[:], dstore[:, gchunk, :].unsqueeze(2).broadcast_to([128, NG, 16])),
                 reads=["l0a_dstore", "l0a_S"], writes=["l0a_S"], tag=f"smulB_{pass_i}_{cc}")
        ctx.pop_scope()
        ctx.push_scope()
        st2 = ctx.sb(ctx.u("l0a_st2"), [128, 6], F32)
        stk = ctx.u("st2k")
        ysq = ctx.sb(ctx.u("l0a_ysq"), [128, C], BF16)
        ysqk = ctx.u("ysqk")
        ptr3 = ctx.ps(ctx.u("l0a_ptr3"), [128, 128], BF16)
        ptr3k = ctx.u("ptr3k")
        for tt in range(n_tt):
            p.op("vector", lambda e, tt=tt: e.reduce_sum(st2[:, 0:1], ytm[:, tt, :], AX.X),
                 reads=[ytmk], writes=[stk], tag=f"gn_s_{pass_i}_{tt}")
            p.op("scalar", lambda e, tt=tt: e.activation(ysq[:], ytm[:, tt, :], AF.Square,
                                                         accum_out=st2[:, 1:2]),
                 reads=[ytmk], writes=[ysqk, stk], tag=f"gn_sq_{pass_i}_{tt}")
            p.op("vector", lambda e: e.tensor_scalar(st2[:, 2:4], st2[:, 0:2], 1.0 / C, None, ALU.mult),
                 reads=[stk], writes=[stk], tag=f"gn_m_{pass_i}_{tt}")
            p.op("vector", lambda e: e.tensor_mul(st2[:, 4:5], st2[:, 2:3], st2[:, 2:3]),
                 reads=[stk], writes=[stk], tag=f"gn_m2_{pass_i}_{tt}")
            p.op("vector", lambda e: e.tensor_sub(st2[:, 4:5], st2[:, 3:4], st2[:, 4:5]),
                 reads=[stk], writes=[stk], tag=f"gn_v_{pass_i}_{tt}")
            p.op("scalar", lambda e: e.activation(st2[:, 4:5], st2[:, 4:5], AF.Ln, bias=EPS),
                 reads=[stk], writes=[stk], tag=f"gn_l_{pass_i}_{tt}")
            p.op("scalar", lambda e: e.activation(st2[:, 4:5], st2[:, 4:5], AF.Exp, scale=-0.5),
                 reads=[stk], writes=[stk], tag=f"gn_e_{pass_i}_{tt}")
            p.op("vector", lambda e, tt=tt: e.tensor_scalar(
                    ytm[:, tt, :], ytm[:, tt, :], st2[:, 2:3], None, ALU.subtract),
                 reads=[stk, ytmk], writes=[ytmk], tag=f"gn_a1_{pass_i}_{tt}")
            p.op("vector", lambda e, tt=tt: e.tensor_scalar(
                    ytm[:, tt, :], ytm[:, tt, :], st2[:, 4:5], None, ALU.mult),
                 reads=[stk, ytmk], writes=[ytmk], tag=f"gn_a2_{pass_i}_{tt}")
            p.op("gpsimd", lambda e, tt=tt: e.tensor_mul(ytm[:, tt, :], ytm[:, tt, :], lng_rep[:]),
                 reads=["l0a_lngb", ytmk], writes=[ytmk], tag=f"gn_g_{pass_i}_{tt}")
            p.op("gpsimd", lambda e, tt=tt: e.tensor_add(ytm[:, tt, :], ytm[:, tt, :], lnb_rep[:]),
                 reads=["l0a_lngb", ytmk], writes=[ytmk], tag=f"gn_b_{pass_i}_{tt}")
            for ci in range(PC):
                p.op("tensor", lambda e, tt=tt, ci=ci: e.transpose(
                        ptr3[:], ytm[:, tt, ci * 128:(ci + 1) * 128], ctx.ident[:]),
                     reads=[ytmk, "identk"], writes=[ptr3k], tag=f"ytr_{pass_i}_{tt}_{ci}")
                p.op("vector", lambda e, tt=tt, ci=ci: e.tensor_copy(
                        y_fm[:, ci, tt * 128:(tt + 1) * 128], ptr3[:]),
                     reads=[ptr3k], writes=[yfk], tag=f"ytrc_{pass_i}_{tt}_{ci}")
        ctx.pop_scope()

        def o_ep(mc, psum, pkey):
            p.op("scalar", lambda e, mc=mc, psum=psum: e.copy(y_fm[:, mc, :], psum),
                 reads=[pkey], writes=[yfk], tag=f"l0a_oev_{pass_i}_{mc}")
        emit_proj_stream(ctx, f"l0ao{pass_i}", st_sq, inp["Wo"], PC,
                         lambda kc, y_fm=y_fm: y_fm[:, kc, :], yfk, o_ep, TP)
        emit_ln_fm(ctx, f"l0alno{pass_i}", lambda ci: y_fm[:, ci, :], yfk,
                   accum_res=(lambda ci, t0=t0: x_fn(ci)[:, t0:t0 + TP], x_key),
                   src_bf16=True, tok=TP)
        ctx.pop_scope()

    for _pi in range(n_pass):
        phase_b(_pi)
    ctx.pop_scope()




def emit_moe_prep(ctx, my_w1, my_w2, groups_all):
    p = ctx.p
    n = sum(len(g) for g in groups_all) // len(groups_all)
    assert [len(g) for g in groups_all] == [n] * len(groups_all)
    w1loc = ctx.dram(ctx.u("moe_w1loc"), [C, 4096], BF16)
    w2loc = ctx.dram(ctx.u("moe_w2loc"), [4096, C], BF16)
    w1all = ctx.dram(ctx.u("moe_w1all"), [n * C, 4096], BF16)
    w2all = ctx.dram(ctx.u("moe_w2all"), [n * 4096, C], BF16)
    p.dma("gpsimd", w1loc, my_w1, writes=["moe_w1loc"], tag="moec1", nobar=True)
    p.dma("gpsimd", w2loc, my_w2, writes=["moe_w2loc"], tag="moec2", nobar=True)
    o1 = p.op("gpsimd", lambda e: e.collective_compute(
            "AllGather", ALU.bypass, groups_all, [w1loc.opt()], [w1all.opt()]),
         reads=["moe_w1loc"], writes=["moe_w1all"], dma=True, tag="moeag1")
    o1.ccinc1 = True
    o1.nobar = True
    o2 = p.op("gpsimd", lambda e: e.collective_compute(
            "AllGather", ALU.bypass, groups_all, [w2loc.opt()], [w2all.opt()]),
         reads=["moe_w2loc"], writes=["moe_w2all"], dma=True, tag="moeag2")
    o2.ccinc1 = True
    o2.nobar = True
    return w1all, w2all


def emit_l0_ffn(ctx, x_fn, x_key, inp, groups4, moe_w1, moe_w2,
                moe_rk=((), ())):
    nc, p, tok = ctx.nc, ctx.p, ctx.tok
    TP = min(512, tok)
    n_pass = tok // TP
    NR = len(groups4[0])
    HL = 16 // NR
    T_ALL = tok * NR
    n_kt_all = T_ALL // 128
    ctx.push_scope()

    st_w = mk_stream(ctx, "l0f_w", C)
    st_w1 = mk_stream(ctx, "l0f_w1", 1024)
    st_w2 = mk_stream(ctx, "l0f_w2", C)
    biases = ctx.sb("l0f_biases", [128, 4, PC], F32)
    p.dma("sync", biases[:], inp["bqkvo"], writes=["l0f_b"], tag="ld_l0fb")
    bv_rep = ctx.sb("l0f_bvrep", [128, C], F32)
    p.dma("sync", bv_rep[:], inp["bv_rep"], writes=["l0f_bv"], tag="ld_bv")
    ln1g = ctx.sb("l0f_ln1g", [128, 2, PC], F32)
    p.dma("sync", ln1g[:], inp["ln1gb"], writes=["l0f_ln1"], tag="ld_ln1")
    ln2g = ctx.sb("l0f_ln2g", [128, 2, PC], F32)
    p.dma("sync", ln2g[:], inp["ln2gb"], writes=["l0f_ln2"], tag="ld_ln2")
    wr_sb = emit_weight_load(ctx, "l0f_wr", inp["Wr"], PC, 8)
    br_rep = ctx.sb("l0f_br", [128, 8], F32)
    p.dma("sync", br_rep[:], inp["br_rep"], writes=["l0f_br"], tag="ld_br")
    b1_sb = ctx.sb("l0f_b1", [128, 8, 32], F32)
    p.dma("sync", b1_sb[:], inp["b1"], writes=["l0f_b1"], tag="ld_b1")
    b2_sb = ctx.sb("l0f_b2", [128, 8, 8], F32)
    p.dma("sync", b2_sb[:], inp["b2"], writes=["l0f_b2"], tag="ld_b2")
    dmask = ctx.sb("l0f_dmask", [128, 4, 512], BF16)
    p.dma("gpsimd", dmask[:], inp["dmask"], writes=["l0f_dmask"], tag="ld_dmask")
    idxs = {}
    for nm in ("gq_idx", "gk_idx", "gv_idx", "go_idx"):
        w = inp[nm].shape[1]
        t = ctx.sb(f"l0f_{nm}", [128, w], I32)
        p.dma("sync", t[:], inp[nm], writes=["l0f_idx"], tag=f"ld_{nm}")
        idxs[nm] = t

    a_sb = ctx.sb("l0f_a", [128, PC, tok], F32)

    contrib = ctx.dram(ctx.u("l0f_contrib"), [48 * 64, tok], BF16)
    ag1_out = ctx.dram(ctx.u("l0f_ag1"), [NR * 48 * 64, tok], BF16)
    myq = ctx.dram(ctx.u("l0f_myq"), [HL * 64 * NR, tok], BF16)
    myk = ctx.dram(ctx.u("l0f_myk"), [HL * 64 * NR, tok], BF16)
    myv = ctx.dram(ctx.u("l0f_myv"), [HL * NR * (tok // 128), 128 * 64], BF16)
    ocontrib = ctx.dram(ctx.u("l0f_oc"), [HL * 64, T_ALL], BF16)
    ag2_out = ctx.dram(ctx.u("l0f_ag2"), [NR * HL * 64, T_ALL], BF16)
    wdram = ctx.dram(ctx.u("l0f_wdram"), [8, tok], F32)

    vthd = contrib[2048:3072, :].rearrange("(h r) t -> h (r t)", h=16) \
                                .rearrange("h (t d) -> t h d", d=64)

    def f1_pass(pass_i):
        t0 = pass_i * TP
        ctx.push_scope()
        h_sb = ctx.sb(ctx.u("l0f_h"), [128, PC, TP], BF16)
        hk = ctx.u("l0f_hk")
        qtmp = ctx.sb(ctx.u("l0f_qtmp"), [128, TP], BF16)
        qtk = ctx.u("qtmpk")
        emit_ln_fm(ctx, f"l0fsl{pass_i}",
                   lambda ci, t0=t0: x_fn(ci)[:, t0:t0 + TP], x_key,
                   out_fn=lambda ci, t0=t0: (a_sb[:, ci, t0:t0 + TP], "l0f_a"),
                   tok=TP)
        emit_ln_fm(ctx, f"l0fl1{pass_i}",
                   lambda ci, t0=t0: a_sb[:, ci, t0:t0 + TP], "l0f_a",
                   out_fn=lambda ci: (h_sb[:, ci, :], hk),
                   gain_bias=(lambda ci: ln1g[:, 0, ci:ci + 1],
                              lambda ci: ln1g[:, 1, ci:ci + 1], "l0f_ln1"),
                   tok=TP)
        for wi, wname in ((0, "Wq"), (1, "Wk")):
            slab0 = 0 if wi == 0 else 16

            def ep(mc, psum, pkey, wi=wi, wname=wname, slab0=slab0, t0=t0):
                p.op("vector", lambda e, psum=psum, wi=wi, mc=mc: e.tensor_scalar(
                        qtmp[:], psum, biases[:, wi, mc:mc + 1], None, ALU.add),
                     reads=[pkey, "l0f_b"], writes=[qtk], tag=f"qev_{pass_i}_{wname}_{mc}")
                r0 = slab0 * 64 + mc * 128
                p.dma("gpsimd", contrib[r0:r0 + 128, t0:t0 + TP], qtmp[:],
                      reads=[qtk], writes=["l0f_contrib"], tag=f"qst_{pass_i}_{wname}_{mc}")
            emit_proj_stream(ctx, f"l0f{wname}{pass_i}", st_w, inp[wname], PC,
                             lambda kc: h_sb[:, kc, :], hk, ep, TP)

        def v_cb(tt, nb, psum, pkey, t0=t0):
            p.op("vector", lambda e, psum=psum, nb=nb: e.tensor_add(
                    qtmp[:], psum, bv_rep[:, nb * 512:(nb + 1) * 512]),
                 reads=[pkey, "l0f_bv"], writes=[qtk], tag=f"vev_{pass_i}_{tt}_{nb}")
            ts0 = t0 + tt * 128
            p.dma("gpsimd",
                  vthd[ts0:ts0 + 128, nb * 8:(nb + 1) * 8, :],
                  qtmp.ap().rearrange("t (h d) -> t h d", d=64),
                  reads=[qtk], writes=["l0f_contrib"], tag=f"vst_{pass_i}_{tt}_{nb}")
        emit_tm_proj(ctx, f"l0fv{pass_i}", st_w, inp["Wv"], PC, 0, C,
                     lambda kc: h_sb[:, kc, :], hk, v_cb, TP)
        ctx.pop_scope()

    for _pi in range(n_pass):
        f1_pass(_pi)

    ctx.push_scope()
    p.op("gpsimd", lambda e: e.collective_compute(
            "AllGather", ALU.bypass, groups4, [contrib.opt()], [ag1_out.opt()]),
         reads=["l0f_contrib"], writes=["l0f_ag1"], dma=True, tag="l0f_ag1").ccinc1 = True
    gbuf = ctx.sb(ctx.u("l0f_gbuf"), [128, tok], BF16)
    gbk = ctx.u("l0f_gbufk")
    gvbuf = ctx.sb(ctx.u("l0f_gvbuf"), [128, 128 * 64], BF16)
    gvk = ctx.u("l0f_gvbufk")
    n_gq = (HL * 64 * NR) // 128
    for nm, dst in (("gq_idx", myq), ("gk_idx", myk)):
        for g in range(n_gq):
            p.op("gpsimd", lambda e, g=g, nm=nm: e.indirect_dma_start(
                    gbuf[:], None, ag1_out,
                    IndirectOffsetOnAxis(idxs[nm][:, g:g + 1], 0)),
                 reads=["l0f_ag1", "l0f_idx"], writes=[gbk], dma=True, tag=f"g{nm}_{g}")
            p.dma("gpsimd", dst[g * 128:(g + 1) * 128, :], gbuf[:],
                  reads=[gbk], writes=["l0f_myqk"], tag=f"g{nm}s_{g}")
    ag1_pieces = ag1_out.rearrange("(a b) t -> a (b t)", b=8192 // tok)
    n_vrows = HL * NR * (tok // 128)
    g = 0
    while g * 128 < n_vrows:
        rows = min(128, n_vrows - g * 128)
        p.op("gpsimd", lambda e, g=g, rows=rows: e.indirect_dma_start(
                gvbuf[0:rows, :], None, ag1_pieces,
                IndirectOffsetOnAxis(idxs["gv_idx"][0:rows, g:g + 1], 0)),
             reads=["l0f_ag1", "l0f_idx"], writes=[gvk], dma=True, tag=f"gv_{g}")
        p.dma("gpsimd", myv[g * 128:g * 128 + rows, :], gvbuf[0:rows, :],
              reads=[gvk], writes=["l0f_myv"], tag=f"gvs_{g}")
        g += 1
    ctx.pop_scope()

    myq_v = myq.rearrange("(h d r) t -> h d r t", h=HL, r=NR)
    myk_v = myk.rearrange("(h d r) t -> h d r t", h=HL, r=NR)
    myv_v = myv.rearrange("(h b) e -> h b e", h=HL)

    def f3_head(h):
        ctx.push_scope()
        qh = ctx.sb(ctx.u("l0f_qh"), [64, T_ALL], BF16)
        qhk = ctx.u("qhk")
        kh = ctx.sb(ctx.u("l0f_kh"), [64, T_ALL], BF16)
        khk = ctx.u("khk")
        p.dma("sync", qh.ap().rearrange("d (r t) -> d r t", r=NR), myq_v[h],
              reads=["l0f_myqk"], writes=[qhk], tag=f"ldqh_{h}")
        p.dma("sync", kh.ap().rearrange("d (r t) -> d r t", r=NR), myk_v[h],
              reads=["l0f_myqk"], writes=[khk], tag=f"ldkh_{h}")
        vaug = ctx.sb(ctx.u("l0f_vaug"), [128, n_kt_all, 65], BF16)
        vbk = ctx.u("vaugk")
        p.dma("sync", vaug[:, :, 0:64],
              myv_v[h].rearrange("b (t d) -> t b d", d=64),
              reads=["l0f_myv"], writes=[vbk], tag=f"ldvh_{h}")
        ncol = ctx.sb(ctx.u("l0f_ncol"), [128, 2, n_kt_all], F32)
        nck = ctx.u("ncolk")
        negb = ctx.sb(ctx.u("l0f_negb"), [128, n_kt_all], F32)
        nbk = ctx.u("negbk")
        eb = ctx.sb(ctx.u("l0f_eb"), [128, n_kt_all], F32)
        ebk = ctx.u("ebk")
        otile = ctx.sb(ctx.u("l0f_otile"), [64, 2, 128], BF16)
        otk = ctx.u("otilek")
        ctx.push_scope()
        sqt = ctx.sb(ctx.u("l0f_sqt"), [64, 512], BF16)
        sqk = ctx.u("sqtk")
        pnb = ctx.ps(ctx.u("l0f_pnb"), [128, 2, n_kt_all], F32)
        pnk = ctx.u("pnbk")
        zr64 = ctx.sb(ctx.u("l0f_zr"), [1, 128], BF16)
        p.op("vector", lambda e: e.memset(zr64[:], 0.0), writes=["l0f_zr"],
             tag=f"zr_{h}")
        for which, buf, bk in ((0, qh, qhk), (1, kh, khk)):
            p.op("tensor", lambda e, which=which: e.matmul(
                    pnb[:, which, :], zr64[:], zr64[:, 0:n_kt_all],
                    start=True, stop=False),
                 reads=["l0f_zr"], writes=[pnk], tag=f"nz_{h}_{which}")
            for tc in range(T_ALL // 512):
                p.op("scalar", lambda e, buf=buf, tc=tc: e.activation(
                        sqt[:], buf[:, tc * 512:(tc + 1) * 512], AF.Square),
                     reads=[bk], writes=[sqk], tag=f"nsq_{h}_{which}_{tc}")
                for j in range(4):
                    b = tc * 4 + j
                    p.op("tensor", lambda e, j=j, b=b, which=which: e.matmul(
                            pnb[:, which, b:b + 1], sqt[:, j * 128:(j + 1) * 128],
                            ctx.ones_col_bf[0:64, :], start=False, stop=False),
                         reads=[sqk, "ones"], writes=[pnk], tag=f"nmm_{h}_{which}_{b}")
            p.op("tensor", lambda e, which=which: e.matmul(
                    pnb[:, which, :], zr64[:], zr64[:, 0:n_kt_all],
                    start=False, stop=True),
                 reads=["l0f_zr", pnk], writes=[pnk], tag=f"nc_{h}_{which}")
            p.op("vector", lambda e, which=which: e.tensor_copy(
                    ncol[:, which, :], pnb[:, which, :]),
                 reads=[pnk], writes=[nck], tag=f"nev_{h}_{which}")
        qmax = ctx.sb(ctx.u("l0f_qmax"), [128, 1], F32)
        qmk = ctx.u("qmaxk")
        qrow = ctx.sb(ctx.u("l0f_qrow"), [1, 129], F32)
        qrk = ctx.u("qrowk")
        ptq = ctx.ps(ctx.u("l0f_ptq"), [1, 128], F32)
        ptqk = ctx.u("ptqk")
        pbq = ctx.ps(ctx.u("l0f_pbq"), [128, 1], F32)
        pbqk = ctx.u("pbqk")
        p.op("vector", lambda e: e.reduce_max(qmax[:], ncol[:, 0, :], AX.X),
             reads=[nck], writes=[qmk], tag=f"qm1_{h}")
        p.op("tensor", lambda e: e.transpose(ptq[:], qmax[:], ctx.ident_f32[:]),
             reads=[qmk, "identfk"], writes=[ptqk], tag=f"qtr_{h}")
        p.op("vector", lambda e: e.tensor_copy(qrow[:, 0:128], ptq[:]),
             reads=[ptqk], writes=[qrk], tag=f"qcp_{h}")
        p.op("vector", lambda e: e.reduce_max(qrow[:, 128:129], qrow[:, 0:128], AX.X),
             reads=[qrk], writes=[qrk], tag=f"qm2_{h}")
        p.op("tensor", lambda e: e.matmul(pbq[:], ctx.ones_row_f32[:],
                                          qrow[:, 128:129], start=True, stop=True),
             reads=[qrk, "ones"], writes=[pbqk], tag=f"qbc_{h}")
        p.op("vector", lambda e: e.tensor_copy(qmax[:], pbq[:]),
             reads=[pbqk], writes=[qmk], tag=f"qbe_{h}")
        p.op("vector", lambda e: e.tensor_scalar(negb[:], ncol[:, 1, :], qmax[:],
                                                 -1.0 / 16.0, ALU.add, ALU.mult),
             reads=[nck, qmk], writes=[nbk], tag=f"nb_{h}")
        p.op("scalar", lambda e: e.activation(eb[:], negb[:], AF.Exp, scale=-1.0),
             reads=[nbk], writes=[ebk], tag=f"eb_{h}")
        ctx.pop_scope()
        p.op("vector", lambda e: e.tensor_mul(
                vaug[:, :, 0:64], vaug[:, :, 0:64],
                eb.ap().unsqueeze(2).broadcast_to([128, n_kt_all, 64])),
             reads=[ebk, vbk], writes=[vbk], tag=f"vsc_{h}")
        p.op("vector", lambda e: e.tensor_copy(vaug[:, :, 64:65], eb.ap().unsqueeze(2)),
             reads=[ebk, vbk], writes=[vbk], tag=f"vac_{h}")
        ctx.push_scope()
        pst = ctx.ps(ctx.u("l0f_pst"), [128, 2, 512], F32)
        pstk = ctx.u("pstk")
        pt_sb = ctx.sb(ctx.u("l0f_pt"), [128, 2, 512], BF16)
        ptk = ctx.u("ptsbk")
        po = ctx.ps(ctx.u("l0f_po"), [128, 4, 512], F32)
        pok = ctx.u("pok")
        oy = ctx.sb(ctx.u("l0f_oy"), [128, 64], BF16)
        oyk = ctx.u("oyk")
        rcl = ctx.sb(ctx.u("l0f_rcl"), [128, 1], F32)
        rck = ctx.u("rclk")
        ptro = ctx.ps(ctx.u("l0f_ptro"), [64, 128], BF16)
        ptrok = ctx.u("ptrok")
        for qc in range(T_ALL // 512):
            n_kt = (qc + 1) * 4
            for ktp in range((n_kt + 1) // 2):
                for sub in range(2):
                    kt = ktp * 2 + sub
                    if kt >= n_kt:
                        continue
                    diag = kt >= qc * 4
                    p.op("tensor", lambda e, kt=kt, sub=sub, qc=qc, diag=diag: e.matmul(
                            pst[:, sub, :], kh[:, kt * 128:(kt + 1) * 128],
                            qh[:, qc * 512:(qc + 1) * 512],
                            start=True, stop=not diag),
                         reads=[khk, qhk], writes=[pstk], tag=f"qk_{h}_{qc}_{kt}")
                    if diag:
                        p.op("tensor", lambda e, kt=kt, sub=sub, qc=qc: e.matmul(
                                pst[:, sub, :], ctx.ident[:],
                                dmask[:, kt - qc * 4, :], start=False, stop=True),
                             reads=["identk", "l0f_dmask", pstk], writes=[pstk],
                             tag=f"dm_{h}_{qc}_{kt}")
                    p.op("scalar", lambda e, sub=sub, kt=kt: e.activation(
                            pt_sb[:, sub, :], pst[:, sub, :], AF.Exp,
                            bias=negb[:, kt:kt + 1], scale=0.125),
                         reads=[pstk, nbk], writes=[ptk], tag=f"exp_{h}_{qc}_{kt}")
                for sub in range(2):
                    kt = ktp * 2 + sub
                    if kt >= n_kt:
                        continue
                    for qs in range(4):
                        p.op("tensor", lambda e, kt=kt, sub=sub, qs=qs, nk=n_kt: e.matmul(
                                po[:, qs, 0:65],
                                pt_sb[:, sub, qs * 128:(qs + 1) * 128],
                                vaug[:, kt, :],
                                start=(kt == 0), stop=(kt == nk - 1)),
                             reads=[ptk, vbk], writes=[pok], tag=f"av_{h}_{qc}_{kt}_{qs}")
            for qs in range(4):
                ob = qs % 2
                p.op("vector", lambda e, qs=qs: e.reciprocal(rcl[:], po[:, qs, 64:65]),
                     reads=[pok], writes=[rck], tag=f"rl_{h}_{qc}_{qs}")
                p.op("vector", lambda e, qs=qs: e.tensor_scalar(
                        oy[:], po[:, qs, 0:64], rcl[:], None, ALU.mult),
                     reads=[pok, rck], writes=[oyk], tag=f"onorm_{h}_{qc}_{qs}")
                p.op("tensor", lambda e: e.transpose(ptro[:], oy[:], ctx.ident[:]),
                     reads=[oyk, "identk"], writes=[ptrok], tag=f"otr_{h}_{qc}_{qs}")
                p.op("vector", lambda e, ob=ob: e.tensor_copy(otile[:, ob, :], ptro[:]),
                     reads=[ptrok], writes=[otk], tag=f"oev_{h}_{qc}_{qs}")
                c0 = qc * 512 + qs * 128
                p.dma("gpsimd", ocontrib[h * 64:(h + 1) * 64, c0:c0 + 128],
                      otile[:, ob, :], reads=[otk], writes=["l0f_oc"],
                      tag=f"ost_{h}_{qc}_{qs}")
        ctx.pop_scope()
        ctx.pop_scope()

    for h in range(HL):
        f3_head(h)

    p.op("gpsimd", lambda e: e.collective_compute(
            "AllGather", ALU.bypass, groups4, [ocontrib.opt()], [ag2_out.opt()]),
         reads=["l0f_oc"], writes=["l0f_ag2"], dma=True, tag="l0f_ag2").ccinc1 = True
    ag2_pieces = ag2_out.rearrange("r (n u t) -> (r n u) t", n=NR, t=TP)

    def f4_pass(pass_i):
        t0 = pass_i * TP
        ctx.push_scope()
        h2_sb = ctx.sb(ctx.u("l0f_h2"), [128, PC, TP], BF16)
        h2k = ctx.u("l0f_h2k")
        wbc_all = ctx.sb(ctx.u("l0f_wbc"), [128, 8, TP], BF16)
        wbck = ctx.u("wbck")
        mid_sb = ctx.sb(ctx.u("l0f_mid"), [128, 8, TP], BF16)
        midk = ctx.u("midk")
        eo_sb = ctx.sb(ctx.u("l0f_eo"), [128, 8, TP], BF16)
        eok = ctx.u("eok")
        tws = ctx.sb(ctx.u("l0f_tws"), [128, TP], F32)
        twk = ctx.u("twsk")
        ctx.push_scope()
        o_pass = ctx.sb(ctx.u("l0f_op"), [128, PC, TP], BF16)
        opk = ctx.u("l0f_opk")
        for g in range(PC):
            p.op("gpsimd", lambda e, g=g: e.indirect_dma_start(
                    o_pass[:, g, :], None, ag2_pieces,
                    IndirectOffsetOnAxis(idxs["go_idx"][:, pass_i * PC + g:pass_i * PC + g + 1], 0)),
                 reads=["l0f_ag2", "l0f_idx"], writes=[opk], dma=True,
                 tag=f"go_{pass_i}_{g}")

        def ep(mc, psum, pkey, t0=t0):
            p.op("vector", lambda e, mc=mc, psum=psum, t0=t0: e.scalar_tensor_tensor(
                    a_sb[:, mc, t0:t0 + TP], psum, biases[:, 3, mc:mc + 1],
                    a_sb[:, mc, t0:t0 + TP], ALU.add, ALU.add),
                 reads=[pkey, "l0f_b", "l0f_a"], writes=["l0f_a"],
                 tag=f"aev_{pass_i}_{mc}")
        emit_proj_stream(ctx, f"l0fo{pass_i}", st_w, inp["Wo"], PC,
                         lambda kc: o_pass[:, kc, :], opk, ep, TP)
        ctx.pop_scope()
        emit_ln_fm(ctx, f"l0fl2{pass_i}",
                   lambda ci, t0=t0: a_sb[:, ci, t0:t0 + TP], "l0f_a",
                   out_fn=lambda ci: (h2_sb[:, ci, :], h2k),
                   gain_bias=(lambda ci: ln2g[:, 0, ci:ci + 1],
                              lambda ci: ln2g[:, 1, ci:ci + 1], "l0f_ln2"),
                   tok=TP)
        ctx.push_scope()
        w_tm = ctx.sb(ctx.u("l0f_wtm"), [128, TP // 128, 8], F32)
        wtmk = ctx.u("wtmk")
        wfm = ctx.sb(ctx.u("l0f_wfm"), [8, TP], F32)
        wfmk = ctx.u("wfmk")
        plg = ctx.ps(ctx.u("l0f_plg"), [128, 8], F32)
        plgk = ctx.u("plgk")
        ptw = ctx.ps(ctx.u("l0f_ptw"), [8, 128], F32)
        ptwk = ctx.u("ptwk")
        red = ctx.sb(ctx.u("l0f_red"), [128, 4], F32)
        redk = ctx.u("redl0f")
        etmp = ctx.sb(ctx.u("l0f_etmp"), [128, 8], F32)
        etk = ctx.u("etmpk")
        for tt in range(TP // 128):
            sl = slice(tt * 128, (tt + 1) * 128)
            for kc in range(PC):
                p.op("tensor", lambda e, kc=kc, sl=sl: e.matmul(
                        plg[:], h2_sb[:, kc, sl], wr_sb[kc][:],
                        start=(kc == 0), stop=(kc == PC - 1)),
                     reads=[h2k, "l0f_wr"], writes=[plgk], tag=f"rt_{pass_i}_{tt}_{kc}")
            p.op("vector", lambda e: e.tensor_add(plg[:], plg[:], br_rep[:]),
                 reads=[plgk, "l0f_br"], writes=[plgk], tag=f"rtb_{pass_i}_{tt}")
            p.op("vector", lambda e: e.reduce_max(red[:, 0:1], plg[:], AX.X, negate=True),
                 reads=[plgk], writes=[redk], tag=f"rm1_{pass_i}_{tt}")
            p.op("vector", lambda e: e.tensor_scalar(etmp[:], plg[:], red[:, 0:1],
                                                     0.0, ALU.add, ALU.is_ge),
                 reads=[plgk, redk], writes=[etk], tag=f"req_{pass_i}_{tt}")
            p.op("vector", lambda e: e.scalar_tensor_tensor(
                    etmp[:], etmp[:], -1e30, plg[:], ALU.mult, ALU.add),
                 reads=[etk, plgk], writes=[etk], tag=f"rrm_{pass_i}_{tt}")
            p.op("vector", lambda e: e.reduce_max(red[:, 1:2], etmp[:], AX.X),
                 reads=[etk], writes=[redk], tag=f"rm2_{pass_i}_{tt}")
            p.op("vector", lambda e: e.tensor_scalar(etmp[:], plg[:], red[:, 1:2],
                                                     None, ALU.is_ge),
                 reads=[plgk, redk], writes=[etk], tag=f"rkm_{pass_i}_{tt}")
            p.op("scalar", lambda e, tt=tt: e.activation(
                    w_tm[:, tt, :], plg[:], AF.Exp, bias=red[:, 0:1]),
                 reads=[plgk, redk], writes=[wtmk], tag=f"rexp_{pass_i}_{tt}")
            p.op("vector", lambda e, tt=tt: e.tensor_mul(
                    w_tm[:, tt, :], w_tm[:, tt, :], etmp[:]),
                 reads=[etk, wtmk], writes=[wtmk], tag=f"rmask_{pass_i}_{tt}")
            p.op("vector", lambda e, tt=tt: e.reduce_sum(red[:, 2:3], w_tm[:, tt, :], AX.X),
                 reads=[wtmk], writes=[redk], tag=f"rsum_{pass_i}_{tt}")
            p.op("vector", lambda e: e.reciprocal(red[:, 2:3], red[:, 2:3]),
                 reads=[redk], writes=[redk], tag=f"rrc_{pass_i}_{tt}")
            p.op("vector", lambda e, tt=tt: e.tensor_scalar(
                    w_tm[:, tt, :], w_tm[:, tt, :], red[:, 2:3], None, ALU.mult),
                 reads=[wtmk, redk], writes=[wtmk], tag=f"rw_{pass_i}_{tt}")
            p.op("tensor", lambda e, tt=tt: e.transpose(
                    ptw[:], w_tm[:, tt, :], ctx.ident_f32[:]),
                 reads=[wtmk, "identfk"], writes=[ptwk], tag=f"rtr_{pass_i}_{tt}")
            p.op("vector", lambda e, sl=sl: e.tensor_copy(wfm[:, sl], ptw[:]),
                 reads=[ptwk], writes=[wfmk], tag=f"rtrc_{pass_i}_{tt}")
        p.dma("sync", wdram[:, t0:t0 + TP], wfm[:], reads=[wfmk],
              writes=["l0f_wdram"], tag=f"wst_{pass_i}")
        wrow = ctx.sb(ctx.u("l0f_wrow"), [1, TP], F32)
        wrk = ctx.u("wrowk")
        pwb = ctx.ps(ctx.u("l0f_pwb"), [128, TP], F32)
        pwbk = ctx.u("pwbk")
        for e_i in range(8):
            p.dma("sync", wrow[:], wdram[e_i:e_i + 1, t0:t0 + TP],
                  reads=["l0f_wdram"], writes=[wrk], tag=f"wld_{pass_i}_{e_i}")
            p.op("tensor", lambda e: e.matmul(pwb[:], ctx.ones_row_f32[:], wrow[:],
                                              start=True, stop=True),
                 reads=[wrk, "ones"], writes=[pwbk], tag=f"wbc_{pass_i}_{e_i}")
            p.op("scalar", lambda e, e_i=e_i: e.copy(wbc_all[:, e_i, :], pwb[:]),
                 reads=[pwbk], writes=[wbck], tag=f"wbe_{pass_i}_{e_i}")
        ctx.pop_scope()
        ctx.push_scope()
        pmm = ctx.ps(ctx.u("l0f_pmm"), [128, 8, TP], F32)
        pmmk = ctx.u("pmmk")
        qs_of = {0: "gpsimd", 1: "sync"}

        def loadw1(e_i, fb, kc):
            b = kc % 2
            p.dma(qs_of[b], st_w1["bufs"][b][:],
                  moe_w1[e_i * C + kc * 128:e_i * C + (kc + 1) * 128,
                         fb * 1024:(fb + 1) * 1024],
                  reads=list(moe_rk[0]),
                  writes=[st_w1["keys"][b]], tag=f"w1l_{pass_i}_{e_i}_{fb}_{kc}",
                  nobar=True)

        def loadw2(e_i, fb, kcf):
            b = kcf % 2
            p.dma(qs_of[b], st_w2["bufs"][b][:],
                  moe_w2[e_i * 4096 + fb * 1024 + kcf * 128:
                         e_i * 4096 + fb * 1024 + (kcf + 1) * 128, :],
                  reads=list(moe_rk[1]),
                  writes=[st_w2["keys"][b]], tag=f"w2l_{pass_i}_{e_i}_{fb}_{kcf}",
                  nobar=True)
        for e_i in range(8):
            for fb in range(4):
                loadw1(e_i, fb, 0)
                loadw1(e_i, fb, 1)
                for kc in range(PC):
                    b = kc % 2
                    for fc in range(8):
                        p.op("tensor", lambda e, kc=kc, fc=fc, b=b, t0=t0: e.matmul(
                                pmm[:, fc, :],
                                st_w1["bufs"][b][:, fc * 128:(fc + 1) * 128],
                                h2_sb[:, kc, :],
                                start=(kc == 0), stop=(kc == PC - 1)),
                             reads=[st_w1["keys"][b], h2k], writes=[pmmk],
                             tag=f"w1m_{pass_i}_{e_i}_{fb}_{kc}_{fc}")
                    if kc + 2 < PC:
                        loadw1(e_i, fb, kc + 2)
                loadw2(e_i, fb, 0)
                loadw2(e_i, fb, 1)
                for fc in range(8):
                    p.op("scalar", lambda e, fc=fc, fb=fb, e_i=e_i: e.activation(
                            mid_sb[:, fc, :], pmm[:, fc, :], AF.Gelu,
                            bias=b1_sb[:, e_i, fb * 8 + fc:fb * 8 + fc + 1]),
                         reads=[pmmk, "l0f_b1"], writes=[midk],
                         tag=f"gel_{pass_i}_{e_i}_{fb}_{fc}")
                for kcf in range(8):
                    b = kcf % 2
                    for mc in range(PC):
                        p.op("tensor", lambda e, kcf=kcf, mc=mc, b=b: e.matmul(
                                pmm[:, mc, :],
                                st_w2["bufs"][b][:, mc * 128:(mc + 1) * 128],
                                mid_sb[:, kcf, :],
                                start=(kcf == 0), stop=(kcf == PC - 1)),
                             reads=[st_w2["keys"][b], midk], writes=[pmmk],
                             tag=f"w2m_{pass_i}_{e_i}_{fb}_{kcf}_{mc}")
                    if kcf + 2 < 8:
                        loadw2(e_i, fb, kcf + 2)
                for mc in range(PC):
                    if fb == 0:
                        p.op("vector", lambda e, mc=mc: e.tensor_copy(
                                eo_sb[:, mc, :], pmm[:, mc, :]),
                             reads=[pmmk], writes=[eok],
                             tag=f"eoc_{pass_i}_{e_i}_{fb}_{mc}")
                    elif fb < 3:
                        p.op("vector", lambda e, mc=mc: e.tensor_add(
                                eo_sb[:, mc, :], eo_sb[:, mc, :], pmm[:, mc, :]),
                             reads=[pmmk, eok], writes=[eok],
                             tag=f"eoa_{pass_i}_{e_i}_{fb}_{mc}")
                    else:
                        p.op("vector", lambda e, mc=mc, e_i=e_i: e.scalar_tensor_tensor(
                                tws[:], pmm[:, mc, :], b2_sb[:, e_i, mc:mc + 1],
                                eo_sb[:, mc, :], ALU.add, ALU.add),
                             reads=[pmmk, "l0f_b2", eok], writes=[twk],
                             tag=f"eow_{pass_i}_{e_i}_{mc}")
                        p.op("vector", lambda e, mc=mc, e_i=e_i: e.tensor_mul(
                                tws[:], tws[:], wbc_all[:, e_i, :]),
                             reads=[twk, wbck], writes=[twk],
                             tag=f"eowm_{pass_i}_{e_i}_{mc}")
                        p.op("gpsimd", lambda e, mc=mc, t0=t0: e.tensor_add(
                                a_sb[:, mc, t0:t0 + TP], a_sb[:, mc, t0:t0 + TP],
                                tws[:]),
                             reads=[twk, "l0f_a"], writes=["l0f_a"],
                             tag=f"acc_{pass_i}_{e_i}_{mc}")
        ctx.pop_scope()
        emit_ln_fm(ctx, f"l0flno{pass_i}",
                   lambda ci, t0=t0: a_sb[:, ci, t0:t0 + TP], "l0f_a",
                   accum_res=(lambda ci, t0=t0: x_fn(ci)[:, t0:t0 + TP], x_key),
                   tok=TP)
        ctx.pop_scope()

    for _pi in range(n_pass):
        f4_pass(_pi)
    ctx.pop_scope()


import ml_dtypes
from concourse.bass_utils import run_bass_kernel_spmd

NCORES = 8
TOK = 2048
NR4 = 4
HL = 16 // NR4
N_PASS = TOK // 512
GROUPS4 = [[0, 1, 2, 3], [4, 5, 6, 7]]
GROUPS8 = [[0, 1, 2, 3, 4, 5, 6, 7]]
BF16NP = ml_dtypes.bfloat16


def _stage_shell(nc, emit_fn):
    p = Prog()
    ctx = Ctx(nc, p, TOK)
    xin = ctx.input("xin", [C, TOK])
    ident_in = ctx.input("ident_in", [128, 128], BF16)
    identf_in = ctx.input("identf_in", [128, 128])
    xout = ctx.output("xout", [C, TOK])
    x_sb = ctx.sb("x_res", [128, PC, TOK], F32, scope=0)
    ctx.setup_common(ident_in, identf_in)
    p.dma("sync", x_sb[:], xin.rearrange("(a p) t -> p a t", p=128),
          writes=["x"], tag="ldx")
    emit_fn(ctx, lambda ci: x_sb[:, ci, :], "x")
    p.dma("gpsimd", xout.rearrange("(a p) t -> p a t", p=128), x_sb[:],
          reads=["x"], writes=["xout"], tag="stx")
    p.fence("sync", ["xout"])
    with nc.Block() as block:
        p.emit_into(nc, block, ctx.scopes[0])
    ctx.scopes[0].close()


def build_l0a(nc):
    def emit(ctx, x_fn, x_key):
        names = {
            "maax": [128, 8], "maaw": [128, 8], "maak": [128, 8],
            "maav": [128, 8], "maar": [128, 8],
            "maaw1": [C, 128], "maaw2": [128, C],
            "decayw1": [C, 64], "decayw2": [64, C], "decayw2p": [64, 2048],
            "decvec_rep": [128, C], "decvec_pad": [128, 16],
            "Wrp": [C, 2048], "Wkp": [C, 2048], "Wk": [C, C], "Wv": [C, C],
            "Wo": [C, C], "ublk": [128, 64], "lng_rep": [128, C],
            "lnb_rep": [128, C], "tri_incl": [128, 128], "xprev": [128, 8],
            "foldmask": [128, 8],
        }
        inp = {k: ctx.input("i_" + k, s) for k, s in names.items()}
        inp["mask_strict4"] = ctx.input("i_mask_strict4", [128, 512], BF16)
        emit_l0_attn(ctx, x_fn, x_key, inp, GROUPS4)
    _stage_shell(nc, emit)


def feeds_l0a(ins, x):
    decay = np.asarray(ins["L0_attn_time_decay"], np.float32).reshape(C)
    dpad = np.full((2048, 1), -1e9, np.float32)
    for h in range(64):
        dpad[32 * h:32 * h + 16, 0] = decay[16 * h:16 * h + 16]

    def padcols(W):
        Wp = np.zeros((W.shape[0], 2048), np.float32)
        for h in range(64):
            Wp[:, 32 * h:32 * h + 16] = W[:, 16 * h:16 * h + 16]
        return Wp
    u = np.asarray(ins["L0_attn_time_faaaa"], np.float32)
    ublk = np.zeros((128, 64), np.float32)
    for pt in range(16):
        for j in range(4):
            h = pt * 4 + j
            ublk[32 * j:32 * j + 16, pt * 4 + j] = u[h]
    g = lambda n: np.asarray(ins["L0_attn_" + n], np.float32)
    com = {
        "i_maax": np.ascontiguousarray(g("time_maa_x").reshape(8, 128).T),
        "i_maaw": np.ascontiguousarray(g("time_maa_w").reshape(8, 128).T),
        "i_maak": np.ascontiguousarray(g("time_maa_k").reshape(8, 128).T),
        "i_maav": np.ascontiguousarray(g("time_maa_v").reshape(8, 128).T),
        "i_maar": np.ascontiguousarray(g("time_maa_r").reshape(8, 128).T),
        "i_maaw1": g("time_maa_w1"),
        "i_maaw2": g("time_maa_w2").reshape(128, C),
        "i_decayw1": g("time_decay_w1"),
        "i_decayw2": g("time_decay_w2"),
        "i_decayw2p": padcols(g("time_decay_w2")),
        "i_decvec_rep": np.broadcast_to(decay, (128, C)).copy(),
        "i_decvec_pad": np.ascontiguousarray(dpad.reshape(16, 128).T),
        "i_Wrp": padcols(g("W_r")),
        "i_Wkp": padcols(g("W_k")),
        "i_Wk": g("W_k"), "i_Wv": g("W_v"), "i_Wo": g("W_o"),
        "i_ublk": ublk,
        "i_lng_rep": np.broadcast_to(g("ln_g"), (128, C)).copy(),
        "i_lnb_rep": np.broadcast_to(g("ln_b"), (128, C)).copy(),
        "i_tri_incl": np.triu(np.ones((128, 128), np.float32)),
        "i_mask_strict4": np.tile(np.triu(np.ones((128, 128)), 1),
                                  (1, 4)).astype(BF16NP),
        "ident_in": np.eye(128, dtype=BF16NP),
        "identf_in": np.eye(128, dtype=np.float32),
    }
    feeds = []
    for c in range(NCORES):
        be, r = c // NR4, c % NR4
        fm = np.zeros((128, 8), np.float32)
        fm[:, :r] = 1.0
        xp = (np.zeros(C, np.float32) if r == 0
              else np.asarray(x[be, r * TOK - 1], np.float32))
        f = dict(com)
        f["i_foldmask"] = fm
        f["i_xprev"] = np.ascontiguousarray(xp.reshape(8, 128).T)
        feeds.append(f)
    return feeds


def build_l0f(nc):
    def emit(ctx, x_fn, x_key):
        names = {
            "bqkvo": ([128, 4, PC], F32), "bv_rep": ([128, C], F32),
            "ln1gb": ([128, 2, PC], F32), "ln2gb": ([128, 2, PC], F32),
            "Wq": ([C, C], F32), "Wk": ([C, C], F32), "Wv": ([C, C], F32),
            "Wo": ([C, C], F32), "Wr": ([C, 8], F32), "br_rep": ([128, 8], F32),
            "b1": ([128, 8, 32], F32), "b2": ([128, 8, 8], F32),
            "dmask": ([128, 4, 512], BF16),
            "gq_idx": ([128, (HL * 64 * NR4) // 128], I32),
            "gk_idx": ([128, (HL * 64 * NR4) // 128], I32),
            "gv_idx": ([128, max(1, (HL * NR4 * (TOK // 128)) // 128)], I32),
            "go_idx": ([128, N_PASS * PC], I32),
        }
        inp = {k: ctx.input("i_" + k, s, dt) for k, (s, dt) in names.items()}
        my_w1 = ctx.input("i_myw1", [C, 4096])
        my_w2 = ctx.input("i_myw2", [4096, C])
        w1all, w2all = emit_moe_prep(ctx, my_w1, my_w2, GROUPS8)
        emit_l0_ffn(ctx, x_fn, x_key, inp, GROUPS4, w1all, w2all,
                    moe_rk=(("moe_w1all",), ("moe_w2all",)))
    _stage_shell(nc, emit)


def _l0f_idxs(rank):
    n_gq = (HL * 64 * NR4) // 128
    gq = np.zeros((128, n_gq), np.int32)
    gk = np.zeros((128, n_gq), np.int32)
    for gg in range(n_gq):
        for pp in range(128):
            flat = gg * 128 + pp
            h = flat // (64 * NR4)
            rem = flat % (64 * NR4)
            d = rem // NR4
            src = rem % NR4
            h_g = rank * HL + h
            gq[pp, gg] = src * 3072 + h_g * 64 + d
            gk[pp, gg] = src * 3072 + 1024 + h_g * 64 + d
    n_vrows = HL * NR4 * (TOK // 128)
    gv = np.zeros((128, max(1, n_vrows // 128)), np.int32)
    P1, P2, P3 = 3072 * TOK // 8192, 2048 * TOK // 8192, 64 * TOK // 8192
    for flat in range(n_vrows):
        h = flat // (NR4 * (TOK // 128))
        rem = flat % (NR4 * (TOK // 128))
        src = rem // (TOK // 128)
        bb = rem % (TOK // 128)
        h_g = rank * HL + h
        gv[flat % 128, flat // 128] = src * P1 + P2 + h_g * P3 + bb
    go = np.zeros((128, N_PASS * PC), np.int32)
    for pi in range(N_PASS):
        for gg in range(PC):
            for pp in range(128):
                go[pp, pi * PC + gg] = ((gg * 128 + pp) * NR4 + rank) * N_PASS + pi
    return gq, gk, gv, go


def feeds_l0f(ins):
    g = lambda n: np.asarray(ins["L0_ffn_" + n], np.float32)
    dmask = np.full((128, 4, 512), -1e30, np.float32)
    for kk in range(128):
        for o_ in range(4):
            dmask[kk, o_, o_ * 128 + kk:] = 0.0
    com = {
        "i_bqkvo": np.stack([g(n).reshape(PC, 128).T
                             for n in ("bq", "bk", "bv", "bo")], axis=1),
        "i_bv_rep": np.broadcast_to(g("bv"), (128, C)).copy(),
        "i_ln1gb": np.stack([g("ln1_g").reshape(PC, 128).T,
                             g("ln1_b").reshape(PC, 128).T], 1),
        "i_ln2gb": np.stack([g("ln2_g").reshape(PC, 128).T,
                             g("ln2_b").reshape(PC, 128).T], 1),
        "i_Wq": g("Wq"), "i_Wk": g("Wk"), "i_Wv": g("Wv"), "i_Wo": g("Wo"),
        "i_Wr": g("Wr"),
        "i_br_rep": np.broadcast_to(g("br"), (128, 8)).copy(),
        "i_b1": g("e_b1").reshape(8, 32, 128).transpose(2, 0, 1).copy(),
        "i_b2": g("e_b2").reshape(8, 8, 128).transpose(2, 0, 1).copy(),
        "i_dmask": dmask.astype(BF16NP),
        "ident_in": np.eye(128, dtype=BF16NP),
        "identf_in": np.eye(128, dtype=np.float32),
    }
    e_w1 = np.asarray(ins["L0_ffn_e_w1"], np.float32)
    e_w2 = np.asarray(ins["L0_ffn_e_w2"], np.float32)
    feeds = []
    for c in range(NCORES):
        r = c % NR4
        gq, gk, gv, go = _l0f_idxs(r)
        f = dict(com)
        f["i_gq_idx"] = gq
        f["i_gk_idx"] = gk
        f["i_gv_idx"] = gv
        f["i_go_idx"] = go
        f["i_myw1"] = np.ascontiguousarray(e_w1[c])
        f["i_myw2"] = np.ascontiguousarray(e_w2[c])
        feeds.append(f)
    return feeds


def build_l1a(nc):
    def emit(ctx, x_fn, x_key):
        inp = {
            "wqkvT": ctx.input("i_wqkvT", [C, 3072]),
            "w_outT": ctx.input("i_woutT", [C, C]),
            "wq_mapT": ctx.input("i_wqmT", [64, 64]),
            "wk_mapT": ctx.input("i_wkmT", [64, 64]),
            "bq": ctx.input("i_bqm", [64, 1]),
            "bk_rep": ctx.input("i_bkm", [128, 64]),
            "mask_incl": ctx.input("i_maski", [128, 128], BF16),
            "foldmask": ctx.input("i_foldmask", [128, 8]),
        }
        emit_l1_attn(ctx, x_fn, x_key, inp, GROUPS4)
    _stage_shell(nc, emit)


def feeds_l1a(ins):
    g = lambda n: np.asarray(ins["L1_attn_" + n], np.float32)
    com = {
        "i_wqkvT": np.ascontiguousarray(g("w_qkv").T),
        "i_woutT": np.ascontiguousarray(g("w_out").T),
        "i_wqmT": np.ascontiguousarray(g("wq_map").T),
        "i_wkmT": np.ascontiguousarray(g("wk_map").T),
        "i_bqm": g("bq_map").reshape(64, 1),
        "i_bkm": np.broadcast_to(g("bk_map"), (128, 64)).copy(),
        "i_maski": np.triu(np.ones((128, 128))).astype(BF16NP),
        "ident_in": np.eye(128, dtype=BF16NP),
        "identf_in": np.eye(128, dtype=np.float32),
    }
    feeds = []
    for c in range(NCORES):
        r = c % NR4
        fm = np.zeros((128, 8), np.float32)
        fm[:, :r] = 1.0
        f = dict(com)
        f["i_foldmask"] = fm
        feeds.append(f)
    return feeds


def build_l1f(nc):
    def emit(ctx, x_fn, x_key):
        inp = {
            "Wt": ctx.input("i_Wt", [C, C]),
            "Wpt": ctx.input("i_Wpt", [C, 256]),
            "bp": ctx.input("i_bp", [128, 2]),
            "sim": ctx.input("i_sim", [256, 4]),
            "temp": ctx.input("i_temp", [1, 1]),
            "Wdt": ctx.input("i_Wdt", [C, 15]),
            "Wu": ctx.input("i_Wu", [15, C]),
        }
        emit_l1_ffn(ctx, x_fn, x_key, inp)
    _stage_shell(nc, emit)


def feeds_l1f(ins):
    g = lambda n: np.asarray(ins["L1_ffn_" + n], np.float32)
    com = {
        "i_Wt": np.ascontiguousarray(g("W").T),
        "i_Wpt": np.ascontiguousarray(g("Wp").T),
        "i_bp": np.ascontiguousarray(g("bp").reshape(2, 128).T),
        "i_sim": g("sim"),
        "i_temp": g("temp").reshape(1, 1),
        "i_Wdt": np.ascontiguousarray(
            np.concatenate([g(f"Wd{i}") for i in range(4)], 0).T),
        "i_Wu": np.concatenate([np.ascontiguousarray(g(f"Wu{i}").T)
                                for i in range(4)], 0),
        "ident_in": np.eye(128, dtype=BF16NP),
        "identf_in": np.eye(128, dtype=np.float32),
    }
    return [dict(com) for _ in range(NCORES)]


def _launch(build_fn, feeds, xs):
    nc = bass.Bass("TRN2", target_bir_lowering=False, num_devices=NCORES)
    build_fn(nc)
    in_maps = []
    for c in range(NCORES):
        f = dict(feeds[c])
        f["xin"] = xs[c]
        in_maps.append(f)
    res = run_bass_kernel_spmd(nc, in_maps, core_ids=list(range(NCORES)))
    return [np.ascontiguousarray(np.asarray(res.results[c]["xout"],
                                            np.float32))
            for c in range(NCORES)]


GSYNC = ["l1ftemp", "l1fbp", "l1a_bk", "l1a_bq", "l1a_fold", "l0a_dvrep",
         "l0a_dvpad", "l0a_trif", "l0a_xprev", "l0a_fold", "l0f_b", "l0f_bv",
         "l0f_ln1", "l0f_ln2", "l0f_br", "l0f_b1", "l0f_b2", "l0f_idx", "l1fsim"]
GGPS = ["l1fWu", "l1a_mask", "l0a_maaw2", "l0a_dw2", "l0a_dw2p",
        "l0a_ublk", "l0a_lngb", "l0a_mask4", "l0f_dmask", "l1fWd",
        "l0a_maaw1", "l0a_dw1", "l0f_wr", "l1a_wqm", "l1a_wkm"]


def build_fused(nc):
    def emit(ctx, x_fn, x_key):
        ctx.p.key_alias.update({k: "wlds" for k in GSYNC})
        ctx.p.key_alias.update({k: "wldg" for k in GGPS})
        names = {
            "maax": [128, 8], "maaw": [128, 8], "maak": [128, 8],
            "maav": [128, 8], "maar": [128, 8],
            "maaw1": [C, 128], "maaw2": [128, C],
            "decayw1": [C, 64], "decayw2": [64, C], "decayw2p": [64, 2048],
            "decvec_rep": [128, C], "decvec_pad": [128, 16],
            "Wrp": [C, 2048], "Wkp": [C, 2048], "Wk": [C, C], "Wv": [C, C],
            "Wo": [C, C], "ublk": [128, 64], "lng_rep": [128, C],
            "lnb_rep": [128, C], "tri_incl": [128, 128], "xprev": [128, 8],
            "foldmask": [128, 8],
        }
        inp_a = {k: ctx.input("i_" + k, s) for k, s in names.items()}
        inp_a["mask_strict4"] = ctx.input("i_mask_strict4", [128, 512], BF16)
        names_f = {
            "bqkvo": ([128, 4, PC], F32), "bv_rep": ([128, C], F32),
            "ln1gb": ([128, 2, PC], F32), "ln2gb": ([128, 2, PC], F32),
            "Wq": ([C, C], F32), "Wk": ([C, C], F32), "Wv": ([C, C], F32),
            "Wo": ([C, C], F32), "Wr": ([C, 8], F32), "br_rep": ([128, 8], F32),
            "b1": ([128, 8, 32], F32), "b2": ([128, 8, 8], F32),
            "dmask": ([128, 4, 512], BF16),
            "gq_idx": ([128, (HL * 64 * NR4) // 128], I32),
            "gk_idx": ([128, (HL * 64 * NR4) // 128], I32),
            "gv_idx": ([128, max(1, (HL * NR4 * (TOK // 128)) // 128)], I32),
            "go_idx": ([128, N_PASS * PC], I32),
        }
        inp_f = {k: ctx.input("i_f" + k, s, dt) for k, (s, dt) in names_f.items()}
        my_w1 = ctx.input("i_myw1", [C, 4096])
        my_w2 = ctx.input("i_myw2", [4096, C])
        w1all, w2all = emit_moe_prep(ctx, my_w1, my_w2, GROUPS8)
        inp_1a = {
            "wqkvT": ctx.input("i_wqkvT", [C, 3072]),
            "w_outT": ctx.input("i_woutT", [C, C]),
            "wq_mapT": ctx.input("i_wqmT", [64, 64]),
            "wk_mapT": ctx.input("i_wkmT", [64, 64]),
            "bq": ctx.input("i_bqm", [64, 1]),
            "bk_rep": ctx.input("i_bkm", [128, 64]),
            "mask_incl": ctx.input("i_maski", [128, 128], BF16),
            "foldmask": inp_a["foldmask"],
        }
        inp_1f = {
            "Wt": ctx.input("i_Wt", [C, C]),
            "Wpt": ctx.input("i_Wpt", [C, 256]),
            "bp": ctx.input("i_bp", [128, 2]),
            "sim": ctx.input("i_sim", [256, 4]),
            "temp": ctx.input("i_temp", [1, 1]),
            "Wdt": ctx.input("i_Wdt", [C, 15]),
            "Wu": ctx.input("i_Wu", [15, C]),
        }
        emit_l0_attn(ctx, x_fn, x_key, inp_a, GROUPS4)
        emit_l0_ffn(ctx, x_fn, x_key, inp_f, GROUPS4, w1all, w2all,
                    moe_rk=(("moe_w1all",), ("moe_w2all",)))
        emit_l1_attn(ctx, x_fn, x_key, inp_1a, GROUPS4)
        emit_l1_ffn(ctx, x_fn, x_key, inp_1f)
    _stage_shell(nc, emit)


def feeds_fused(ins, x):
    fa = feeds_l0a(ins, x)
    ff = feeds_l0f(ins)
    f1a = feeds_l1a(ins)
    f1f = feeds_l1f(ins)
    feeds = []
    for c in range(NCORES):
        f = dict(fa[c])
        for k, v in ff[c].items():
            if k in ("ident_in", "identf_in"):
                continue
            f["i_f" + k[2:]] = v if k.startswith("i_") else v
        f["i_myw1"] = ff[c]["i_myw1"]
        f["i_myw2"] = ff[c]["i_myw2"]
        for k, v in f1a[c].items():
            if k in ("ident_in", "identf_in", "i_foldmask"):
                continue
            f[k] = v
        for k, v in f1f[c].items():
            if k in ("ident_in", "identf_in"):
                continue
            f[k] = v
        feeds.append(f)
    return feeds


def kernel(**inputs):
    ins = {k: np.asarray(v) for k, v in inputs.items()}
    x = np.asarray(ins["L0_attn_x"], np.float32)
    xs = [np.ascontiguousarray(x[c // NR4,
                                 (c % NR4) * TOK:(c % NR4 + 1) * TOK, :].T)
          for c in range(NCORES)]
    xs = _launch(build_fused, feeds_fused(ins, x), xs)
    out = np.empty((2, 8192, C), np.float32)
    for c in range(NCORES):
        out[c // NR4, (c % NR4) * TOK:(c % NR4 + 1) * TOK, :] = xs[c].T
    return out


# version 04
# speedup vs baseline: 1.0225x; 1.0053x over previous
import numpy as np
from contextlib import ExitStack
import concourse.bass as bass
import concourse.mybir as mybir
from concourse.bass import IndirectOffsetOnAxis

F32 = mybir.dt.float32
BF16 = mybir.dt.bfloat16
I32 = mybir.dt.int32
AF = mybir.ActivationFunctionType
ALU = mybir.AluOpType
AX = mybir.AxisListType

ENGINES = ("sync", "scalar", "vector", "tensor", "gpsimd")

C = 1024
PC = C // 128
EPS = 1e-5



class _Op:
    __slots__ = ("engine", "fn", "reads", "writes", "dma", "tag", "idx",
                 "domain", "deps", "inc_needed", "wait_val", "ccinc1", "barrier",
                 "nobar")

    def __init__(self, engine, fn, reads, writes, dma, tag, idx):
        self.engine = engine
        self.fn = fn
        self.reads = tuple(reads)
        self.writes = tuple(writes)
        self.dma = dma
        self.tag = tag
        self.idx = idx
        if dma:
            assert len(self.writes) == 1, f"dma op {tag} needs exactly 1 write key"
            self.domain = "k_" + self.writes[0]
        else:
            self.domain = engine
        self.deps = []
        self.inc_needed = False
        self.wait_val = None
        self.ccinc1 = False
        self.barrier = False
        self.nobar = False


class Prog:
    def __init__(self):
        self.ops = []
        self.key_alias = {}

    def op(self, engine, fn, reads=(), writes=(), dma=False, tag=""):
        if self.key_alias:
            al = self.key_alias
            reads = [al.get(k, k) for k in reads]
            writes = [al.get(k, k) for k in writes]
        o = _Op(engine, fn, reads, writes, dma, tag, len(self.ops))
        self.ops.append(o)
        return o

    def dma(self, engine, out_ap, in_ap, reads=(), writes=(), tag="", nobar=False, **kw):
        def fn(eng):
            return eng.dma_start(out=out_ap, in_=in_ap, **kw)
        o = self.op(engine, fn, reads=reads, writes=writes, dma=True, tag=tag)
        o.nobar = nobar
        return o

    def fence(self, engine, keys, tag="fence"):
        return self.op(engine, lambda e: e.nop(nofuse=True), reads=tuple(keys), tag=tag)

    def barrier(self, tag="barrier"):
        for e in ("sync", "scalar", "vector", "tensor", "gpsimd"):
            o = self.op(e, lambda eng: eng.nop(nofuse=True), tag=f"{tag}:{e}")
            o.barrier = True

    @staticmethod
    def _needs_edge(prod, cons):
        if prod.domain != cons.domain:
            return True
        if prod.dma and cons.dma:
            return False
        if prod.engine == "tensor" and cons.engine == "tensor" and not prod.dma:
            return False
        return True

    def analyze(self):
        last_writer = {}
        readers = {}
        last_per_domain = {}
        for o in self.ops:
            deps = {}
            if o.barrier:
                for dom, w in last_per_domain.items():
                    if not w.nobar and w is not o:
                        deps[id(w)] = w
            else:
                for k in o.reads:
                    w = last_writer.get(k)
                    if w is not None and self._needs_edge(w, o):
                        deps[id(w)] = w
                for k in o.writes:
                    w = last_writer.get(k)
                    if w is not None and self._needs_edge(w, o):
                        deps[id(w)] = w
                    for r in readers.get(k, ()):
                        if r is not o and self._needs_edge(r, o):
                            deps[id(r)] = r
            o.deps = list(deps.values())
            for k in o.reads:
                readers.setdefault(k, []).append(o)
            for k in o.writes:
                last_writer[k] = o
                readers[k] = []
            last_per_domain[o.domain] = o
        for o in self.ops:
            for d in o.deps:
                d.inc_needed = True
        dmaq = {}
        counts = {}
        for o in self.ops:
            if o.dma:
                o.inc_needed = True
                q = "sw" if o.engine == "gpsimd" else "hw"
                assert dmaq.setdefault(o.domain, q) == q,                     f"dma key {o.domain} used from both SWDGE and HWDGE queues"
            if o.inc_needed:
                step = 16 if (o.dma and not o.ccinc1) else 1
                counts[o.domain] = counts.get(o.domain, 0) + step
                o.wait_val = counts[o.domain]
        self.final_counts = counts

    def emit_into(self, nc, block, stack):
        self.analyze()
        sems = {}
        for dom in self.final_counts:
            sems[dom] = stack.enter_context(nc.semaphore("s_" + dom))
        streams = {e: [] for e in ENGINES}
        for o in self.ops:
            streams[o.engine].append(o)

        def run_stream(eng_obj, engine_name):
            watermark = {}
            for o in streams[engine_name]:
                for d in o.deps:
                    dom, v = d.domain, d.wait_val
                    if watermark.get(dom, 0) < v:
                        eng_obj.wait_ge(sems[dom], v)
                        watermark[dom] = v
                inst = o.fn(eng_obj)
                if o.inc_needed:
                    assert inst is not None, f"op '{o.tag}' needs inc but returned None"
                    inst.then_inc(sems[o.domain], 16 if (o.dma and not o.ccinc1) else 1)

        for name in ENGINES:
            getattr(block, name)(lambda eng, n=name: run_stream(eng, n))
        return sems




class Ctx:
    def __init__(self, nc, p, tok):
        self.nc = nc
        self.p = p
        self.tok = tok
        self.uid = 0
        self.scopes = [ExitStack()]

    def u(self, base):
        self.uid += 1
        return f"{base}{self.uid}"

    def push_scope(self):
        self.scopes.append(ExitStack())

    def pop_scope(self, barrier=True):
        self.scopes.pop().close()
        if barrier:
            self.p.barrier()

    def sb(self, name, shape, dt=F32, scope=-1):
        return self.scopes[scope].enter_context(self.nc.sbuf_tensor(name, shape, dt))

    def ps(self, name, shape, dt=F32, scope=-1):
        return self.scopes[scope].enter_context(self.nc.psum_tensor(name, shape, dt))

    def dram(self, name, shape, dt=F32, kind="Internal"):
        return self.nc.dram_tensor(name, shape, dt, kind=kind).ap()

    def input(self, name, shape, dt=F32):
        return self.nc.dram_tensor(name, shape, dt, kind="ExternalInput").ap()

    def output(self, name, shape, dt=F32):
        return self.nc.dram_tensor(name, shape, dt, kind="ExternalOutput").ap()

    def setup_common(self, ident_in, identf_in, const_vals=()):
        p = self.p
        self.ident_in = ident_in
        self.identf_in = identf_in
        self.ones_col_bf = self.sb("ones_col_bf", [128, 1], BF16, scope=0)
        self.ones_col_f32 = self.sb("ones_col_f32", [128, 1], F32, scope=0)
        self.ones_row_f32 = self.sb("ones_row_f32", [1, 128], F32, scope=0)
        p.op("vector", lambda e: e.memset(self.ones_col_bf[:], 1.0), writes=["ones"], tag="ones1")
        p.op("vector", lambda e: e.memset(self.ones_col_f32[:], 1.0), writes=["ones"], tag="ones2")
        p.op("vector", lambda e: e.memset(self.ones_row_f32[:], 1.0), writes=["ones"], tag="ones3")
        vals = sorted(set(float(v) for v in const_vals) | {EPS})
        pool = self.sb("constpool", [128, len(vals)], F32, scope=0)
        for i, v in enumerate(vals):
            p.op("vector", lambda e, v=v, i=i: e.memset(pool[:, i:i + 1], v),
                 writes=["ones"], tag=f"const_{i}")
            self.nc.const_aps.register(F32, v, pool[:, i:i + 1])
        self.ident = self.sb("ident", [128, 128], BF16, scope=0)
        p.dma("gpsimd", self.ident[:], self.ident_in, writes=["identk"], tag="ld_ident")
        self.ident_f32 = self.sb("ident_f32", [128, 128], F32, scope=0)
        p.dma("sync", self.ident_f32[:], self.identf_in, writes=["identfk"], tag="ld_identf")
        self.ln_tmp = self.sb("ln_tmp", [128, 512], F32, scope=0)




def emit_weight_load(ctx, key, w_dram, n_kc, m_cols, engine="gpsimd", dt=BF16,
                     rows_per_kc=128):
    tiles = []
    for kc in range(n_kc):
        t = ctx.sb(ctx.u(f"w_{key}_"), [rows_per_kc, m_cols], dt)
        ctx.p.dma(engine, t[:],
                  w_dram[kc * rows_per_kc:(kc + 1) * rows_per_kc, :],
                  writes=[key], tag=f"ldw_{key}_{kc}")
        tiles.append(t)
    return tiles


def mk_stream(ctx, tag, m_cols):
    bufs, keys = [], []
    for i in range(2):
        bufs.append(ctx.sb(ctx.u(f"ws_{tag}_"), [128, m_cols], BF16))
        keys.append(ctx.u(f"wsk_{tag}_"))
    return {"bufs": bufs, "keys": keys, "m_cols": m_cols}


def emit_proj_stream(ctx, tag, stream, w_dram, n_kc, in_fn, in_key, epilogue,
                     tp, mcols_off=0, n_mc=None, extra=None):
    p = ctx.p
    m_cols = stream["m_cols"]
    n_mc = (m_cols - mcols_off) // 128 if n_mc is None else n_mc
    assert tp == 512, "accumulation regions must be bank-aligned (tp=512)"
    GM = 8
    ctx.push_scope()
    ps = ctx.ps(ctx.u(f"pp_{tag}_"), [128, min(n_mc, GM), tp], F32)
    pk = ctx.u(f"ppk_{tag}_")
    bufs, keys = stream["bufs"], stream["keys"]

    def load(kc, g):
        p.dma("gpsimd", bufs[kc % 2][:], w_dram[kc * 128:(kc + 1) * 128, :],
              writes=[keys[kc % 2]], tag=f"ldws_{tag}_{g}_{kc}", nobar=True)
    for g0 in range(0, n_mc, GM):
        gn = min(GM, n_mc - g0)
        load(0, g0)
        if n_kc > 1:
            load(1, g0)
        for kc in range(n_kc):
            b = kc % 2
            for mi in range(gn):
                mc = g0 + mi
                p.op("tensor", lambda e, kc=kc, mc=mc, mi=mi, b=b: e.matmul(
                        ps[:, mi, :],
                        bufs[b][:, mcols_off + mc * 128:mcols_off + (mc + 1) * 128],
                        in_fn(kc),
                        start=(kc == 0),
                        stop=(kc == n_kc - 1 and extra is None)),
                     reads=[keys[b], in_key(kc) if callable(in_key) else in_key],
                     writes=[pk], tag=f"mm_{tag}_{kc}_{mc}")
            if kc + 2 < n_kc:
                load(kc + 2, g0)
        if extra is not None:
            extra(ps, pk)
        for mi in range(gn):
            epilogue(g0 + mi, ps[:, mi, :], pk)
    ctx.pop_scope()


def emit_ln_fm(ctx, tag, src_fn, src_key, out_fn=None, accum_res=None,
               n_chunks=PC, src_bf16=False, tok=None, gain_bias=None):
    nc, p = ctx.nc, ctx.p
    tok = ctx.tok if tok is None else tok
    TT = min(512, tok)
    n_t = tok // TT
    ctx.push_scope()
    sqt = ctx.sb(ctx.u(f"lnsq{tag}_"), [128, 2, TT], BF16)
    sqk = [ctx.u(f"lnsqk{tag}") for _ in range(2)]
    rows = ctx.sb(ctx.u(f"lnrows{tag}_"), [1, 3, tok], F32)
    rk = ctx.u(f"lnrk{tag}")
    prow = ctx.ps(ctx.u(f"lnprow{tag}_"), [1, TT], F32)
    pk = ctx.u(f"lnpk{tag}")
    ones_src = ctx.ones_col_bf if src_bf16 else ctx.ones_col_f32
    for tc in range(n_t):
        for ci in range(n_chunks):
            p.op("tensor", lambda e, ci=ci, tc=tc: e.matmul(
                    prow[:], ones_src[:], src_fn(ci)[:, tc * TT:(tc + 1) * TT],
                    start=(ci == 0), stop=(ci == n_chunks - 1)),
                 reads=[src_key, "ones"], writes=[pk], tag=f"lnsx_{tag}_{tc}_{ci}")
        p.op("vector", lambda e, tc=tc: e.tensor_copy(rows[:, 0, tc * TT:(tc + 1) * TT], prow[:]),
             reads=[pk], writes=[rk], tag=f"lnr0_{tag}_{tc}")
        for ci in range(n_chunks):
            b = ci % 2
            p.op("scalar", lambda e, ci=ci, tc=tc, b=b: e.activation(
                    sqt[:, b, :], src_fn(ci)[:, tc * TT:(tc + 1) * TT], AF.Square),
                 reads=[src_key], writes=[sqk[b]], tag=f"lnsq_{tag}_{tc}_{ci}")
            p.op("tensor", lambda e, ci=ci, b=b: e.matmul(
                    prow[:], ctx.ones_col_bf[:], sqt[:, b, :],
                    start=(ci == 0), stop=(ci == n_chunks - 1)),
                 reads=[sqk[b], "ones"], writes=[pk], tag=f"lnsq2_{tag}_{tc}_{ci}")
        p.op("vector", lambda e, tc=tc: e.tensor_copy(rows[:, 1, tc * TT:(tc + 1) * TT], prow[:]),
             reads=[pk], writes=[rk], tag=f"lnr1_{tag}_{tc}")
    inv = 1.0 / (n_chunks * 128)
    p.op("vector", lambda e: e.tensor_scalar(rows[:, 0, :], rows[:, 0, :], inv, None, ALU.mult),
         reads=[rk], writes=[rk], tag=f"ln_m_{tag}")
    p.op("vector", lambda e: e.tensor_scalar(rows[:, 1, :], rows[:, 1, :], inv, None, ALU.mult),
         reads=[rk], writes=[rk], tag=f"ln_v2_{tag}")
    p.op("vector", lambda e: e.tensor_mul(rows[:, 2, :], rows[:, 0, :], rows[:, 0, :]),
         reads=[rk], writes=[rk], tag=f"ln_msq_{tag}")
    p.op("vector", lambda e: e.tensor_sub(rows[:, 1, :], rows[:, 1, :], rows[:, 2, :]),
         reads=[rk], writes=[rk], tag=f"ln_var_{tag}")
    p.op("scalar", lambda e: e.activation(rows[:, 1, :], rows[:, 1, :], AF.Ln, bias=EPS),
         reads=[rk], writes=[rk], tag=f"ln_ln_{tag}")
    p.op("scalar", lambda e: e.activation(rows[:, 1, :], rows[:, 1, :], AF.Exp, scale=-0.5),
         reads=[rk], writes=[rk], tag=f"ln_exp_{tag}")
    p.op("vector", lambda e: e.tensor_mul(rows[:, 2, :], rows[:, 0, :], rows[:, 1, :]),
         reads=[rk], writes=[rk], tag=f"ln_mr_{tag}")
    pbc = ctx.ps(ctx.u(f"lnbc{tag}_"), [128, 2, TT], F32)
    bck = ctx.u(f"lnbck{tag}")
    for tc in range(n_t):
        sl = slice(tc * TT, (tc + 1) * TT)
        p.op("tensor", lambda e, sl=sl: e.matmul(
                pbc[:, 0, :], ctx.ones_row_f32[:], rows[:, 1, sl], start=True, stop=True),
             reads=[rk, "ones"], writes=[bck], tag=f"lnbcr_{tag}_{tc}")
        p.op("tensor", lambda e, sl=sl: e.matmul(
                pbc[:, 1, :], ctx.ones_row_f32[:], rows[:, 2, sl], start=True, stop=True),
             reads=[rk, "ones"], writes=[bck], tag=f"lnbcm_{tag}_{tc}")
        for ci in range(n_chunks):
            if accum_res is None:
                out_ap, out_key = out_fn(ci)
                dst = out_ap[:, sl]
                if gain_bias is not None:
                    g_fn, b_fn, gb_key = gain_bias
                    tmp = ctx.ln_tmp
                    p.op("vector", lambda e, ci=ci, sl=sl: e.scalar_tensor_tensor(
                            tmp[:, 0:TT], src_fn(ci)[:, sl], 1.0, pbc[:, 0, :],
                            ALU.mult, ALU.mult),
                         reads=[src_key, bck], writes=["ln_tmp"],
                         tag=f"lnm1_{tag}_{tc}_{ci}")
                    p.op("vector", lambda e: e.tensor_sub(
                            tmp[:, 0:TT], tmp[:, 0:TT], pbc[:, 1, :]),
                         reads=[bck, "ln_tmp"], writes=["ln_tmp"],
                         tag=f"lnm2_{tag}_{tc}_{ci}")
                    p.op("vector", lambda e, ci=ci, dst=dst: e.tensor_scalar(
                            dst, tmp[:, 0:TT], g_fn(ci), b_fn(ci),
                            ALU.mult, ALU.add),
                         reads=["ln_tmp", gb_key], writes=[out_key],
                         tag=f"lnm3_{tag}_{tc}_{ci}")
                    continue
                p.op("vector", lambda e, ci=ci, sl=sl, dst=dst: e.scalar_tensor_tensor(
                        dst, src_fn(ci)[:, sl], 1.0, pbc[:, 0, :], ALU.mult, ALU.mult),
                     reads=[src_key, bck], writes=[out_key], tag=f"lnm1_{tag}_{tc}_{ci}")
                p.op("vector", lambda e, dst=dst: e.tensor_sub(dst, dst, pbc[:, 1, :]),
                     reads=[bck, out_key], writes=[out_key], tag=f"lnm2_{tag}_{tc}_{ci}")
            else:
                res_fn, res_key = accum_res
                tmp = ctx.ln_tmp
                p.op("vector", lambda e, ci=ci, sl=sl: e.scalar_tensor_tensor(
                        tmp[:, 0:TT], src_fn(ci)[:, sl], 1.0, pbc[:, 0, :],
                        ALU.mult, ALU.mult),
                     reads=[src_key, bck], writes=["ln_tmp"], tag=f"lnm1_{tag}_{tc}_{ci}")
                p.op("vector", lambda e: e.tensor_sub(tmp[:, 0:TT], tmp[:, 0:TT], pbc[:, 1, :]),
                     reads=[bck, "ln_tmp"], writes=["ln_tmp"], tag=f"lnm2_{tag}_{tc}_{ci}")
                p.op("vector", lambda e, ci=ci, sl=sl: e.tensor_add(
                        res_fn(ci)[:, sl], res_fn(ci)[:, sl], tmp[:, 0:TT]),
                     reads=["ln_tmp", res_key], writes=[res_key],
                     tag=f"lnacc_{tag}_{tc}_{ci}")
    ctx.pop_scope()


def emit_ln_col(ctx, tag, col_fn, col_key, out_fn, out_key):
    p = ctx.p
    ctx.push_scope()
    colsq = ctx.sb(ctx.u("lncol_sq"), [128, PC], F32)
    sqk = ctx.u("lncolsqk")
    for ci in range(PC):
        p.op("scalar", lambda e, ci=ci: e.activation(colsq[:, ci:ci + 1], col_fn(ci), AF.Square),
             reads=[col_key], writes=[sqk], tag=f"lncsq_{tag}_{ci}")
    pr = ctx.ps(ctx.u("lncol_ps"), [1, 2], F32)
    prk = ctx.u("lncolpsk")
    rows = ctx.sb(ctx.u("lncol_rows"), [1, 4], F32)
    rk = ctx.u("lncolrk")
    for ci in range(PC):
        p.op("tensor", lambda e, ci=ci: e.matmul(
                pr[:, 0:1], ctx.ones_col_f32[:], col_fn(ci),
                start=(ci == 0), stop=(ci == PC - 1)),
             reads=[col_key, "ones"], writes=[prk], tag=f"lncs_{tag}_{ci}")
    for ci in range(PC):
        p.op("tensor", lambda e, ci=ci: e.matmul(
                pr[:, 1:2], ctx.ones_col_f32[:], colsq[:, ci:ci + 1],
                start=(ci == 0), stop=(ci == PC - 1)),
             reads=[sqk, "ones"], writes=[prk], tag=f"lncq_{tag}_{ci}")
    p.op("vector", lambda e: e.tensor_scalar(rows[:, 0:2], pr[:], 1.0 / C, None, ALU.mult),
         reads=[prk], writes=[rk], tag=f"lnc_m_{tag}")
    p.op("vector", lambda e: e.tensor_mul(rows[:, 2:3], rows[:, 0:1], rows[:, 0:1]),
         reads=[rk], writes=[rk], tag=f"lnc_msq_{tag}")
    p.op("vector", lambda e: e.tensor_sub(rows[:, 2:3], rows[:, 1:2], rows[:, 2:3]),
         reads=[rk], writes=[rk], tag=f"lnc_var_{tag}")
    p.op("scalar", lambda e: e.activation(rows[:, 2:3], rows[:, 2:3], AF.Ln, bias=EPS),
         reads=[rk], writes=[rk], tag=f"lnc_ln_{tag}")
    p.op("scalar", lambda e: e.activation(rows[:, 2:3], rows[:, 2:3], AF.Exp, scale=-0.5),
         reads=[rk], writes=[rk], tag=f"lnc_exp_{tag}")
    p.op("vector", lambda e: e.tensor_mul(rows[:, 3:4], rows[:, 0:1], rows[:, 2:3]),
         reads=[rk], writes=[rk], tag=f"lnc_mr_{tag}")
    pb = ctx.ps(ctx.u("lncol_pb"), [128, 2], F32)
    pbk = ctx.u("lncolpbk")
    p.op("tensor", lambda e: e.matmul(pb[:], ctx.ones_row_f32[:], rows[:, 2:4],
                                      start=True, stop=True),
         reads=[rk, "ones"], writes=[pbk], tag=f"lnc_bc_{tag}")
    for ci in range(PC):
        p.op("vector", lambda e, ci=ci: e.scalar_tensor_tensor(
                out_fn(ci), col_fn(ci), 1.0, pb[:, 0:1], ALU.mult, ALU.mult),
             reads=[col_key, pbk], writes=[out_key], tag=f"lnc_a1_{tag}_{ci}")
        p.op("vector", lambda e, ci=ci: e.tensor_sub(
                out_fn(ci), out_fn(ci), pb[:, 1:2]),
             reads=[pbk, out_key], writes=[out_key], tag=f"lnc_a2_{tag}_{ci}")
    ctx.pop_scope()



CLAMP_MAX = float(np.log(1.0 / 0.01))
L1F_COLS = [(0, 1), (1, 3), (3, 7), (7, 15)]


def emit_l1_ffn(ctx, x_fn, x_key, inp):
    nc, p, tok = ctx.nc, ctx.p, ctx.tok
    TP = min(512, tok)
    n_pass = tok // TP
    ctx.push_scope()

    st_W = mk_stream(ctx, "l1fW", C)
    st_Wp = mk_stream(ctx, "l1fWp", 256)
    w_wdt = emit_weight_load(ctx, "l1fWd", inp["Wdt"], PC, 15)
    wu = ctx.sb("l1f_wu", [15, C], BF16)
    p.dma("gpsimd", wu[:], inp["Wu"], writes=["l1fWu"], tag="ld_l1fWu")
    sim_sb = ctx.sb("l1f_sim", [128, 2, 4], F32)
    p.dma("sync", sim_sb[:], inp["sim"].rearrange("(a p) e -> p a e", p=128),
          writes=["l1fsim"], tag="ld_l1fsim", allow_slow_non_contiguous=True)
    simb_sb = ctx.sb("l1f_simb", [128, 2, 4], BF16)
    temp_sb = ctx.sb("l1f_temp", [1, 1], F32)
    p.dma("sync", temp_sb[:], inp["temp"], writes=["l1ftemp"], tag="ld_l1ftemp")
    bp_sb = ctx.sb("l1f_bp", [128, 2], F32)
    p.dma("sync", bp_sb[:], inp["bp"], writes=["l1fbp"], tag="ld_l1fbp")

    factor_bc = ctx.sb("l1f_facbc", [128, 4], F32)
    ctx.push_scope()
    simsq = ctx.sb(ctx.u("l1f_simsq"), [128, 2, 4], F32)
    for a in range(2):
        p.op("scalar", lambda e, a=a: e.activation(simsq[:, a, :], sim_sb[:, a, :], AF.Square),
             reads=["l1fsim"], writes=["simsq"], tag=f"l1f_simsq{a}")
    srow = ctx.sb(ctx.u("l1f_srow"), [1, 8], F32)
    ps4 = ctx.ps(ctx.u("l1f_ps4"), [128, 16], F32)
    for a in range(2):
        p.op("tensor", lambda e, a=a: e.matmul(ps4[0:1, 0:4], ctx.ones_col_f32[:],
                                               simsq[:, a, :],
                                               start=(a == 0), stop=(a == 1)),
             reads=["simsq", "ones"], writes=["ps4k"], tag=f"l1f_simss{a}")
    p.op("vector", lambda e: e.tensor_copy(srow[:, 0:4], ps4[0:1, 0:4]),
         reads=["ps4k"], writes=["srowk"], tag="l1f_srow")
    p.op("scalar", lambda e: e.activation(srow[:, 0:4], srow[:, 0:4], AF.Ln),
         reads=["srowk"], writes=["srowk"], tag="l1f_sln")
    p.op("scalar", lambda e: e.activation(srow[:, 0:4], srow[:, 0:4], AF.Exp, scale=0.5),
         reads=["srowk"], writes=["srowk"], tag="l1f_sexp")
    p.op("vector", lambda e: e.tensor_scalar_max(srow[:, 0:4], srow[:, 0:4], 1e-12),
         reads=["srowk"], writes=["srowk"], tag="l1f_smax")
    p.op("vector", lambda e: e.tensor_scalar_min(srow[:, 4:5], temp_sb[:], CLAMP_MAX),
         reads=["l1ftemp"], writes=["srowk"], tag="l1f_tclamp")
    p.op("scalar", lambda e: e.activation(srow[:, 4:5], srow[:, 4:5], AF.Exp),
         reads=["srowk"], writes=["srowk"], tag="l1f_texp")
    p.op("vector", lambda e: e.reciprocal(srow[:, 0:4], srow[:, 0:4]),
         reads=["srowk"], writes=["srowk"], tag="l1f_srcp")
    p.op("vector", lambda e: e.tensor_scalar(srow[:, 0:4], srow[:, 0:4], srow[:, 4:5],
                                             None, ALU.mult),
         reads=["srowk"], writes=["srowk"], tag="l1f_sfac")
    p.op("tensor", lambda e: e.matmul(ps4[:, 4:8], ctx.ones_row_f32[:], srow[:, 0:4],
                                      start=True, stop=True),
         reads=["srowk", "ones"], writes=["ps4k"], tag="l1f_facbcm")
    p.op("vector", lambda e: e.tensor_copy(factor_bc[:], ps4[:, 4:8]),
         reads=["ps4k"], writes=["facbck"], tag="l1f_facbcc")
    for a in range(2):
        p.op("vector", lambda e, a=a: e.tensor_copy(simb_sb[:, a, :], sim_sb[:, a, :]),
             reads=["l1fsim"], writes=["l1fsimb"], tag=f"l1f_simcast{a}")
    ctx.pop_scope()

    def do_pass(pass_i):
        t0 = pass_i * TP
        n_tt = TP // 128
        ctx.push_scope()
        ln_p = ctx.sb(ctx.u("l1f_ln"), [128, PC, TP], BF16)
        lnk = ctx.u("l1f_lnk")
        emit_ln_fm(ctx, f"l1fln{pass_i}",
                   lambda ci, t0=t0: x_fn(ci)[:, t0:t0 + TP], x_key,
                   out_fn=lambda ci, ln_p=ln_p, lnk=lnk: (ln_p[:, ci, :], lnk), tok=TP)
        proj = ctx.sb(ctx.u("l1f_proj"), [128, 2, TP], BF16)
        projk = ctx.u("l1f_projk")
        projsq = ctx.sb(ctx.u("l1f_projsq"), [128, 2, TP], BF16)
        projsqk = ctx.u("l1f_projsqk")

        def proj_ep(mc, psum, pkey):
            p.op("vector", lambda e, mc=mc, psum=psum: e.tensor_scalar(
                    proj[:, mc, :], psum, bp_sb[:, mc:mc + 1], None, ALU.add),
                 reads=[pkey, "l1fbp"], writes=[projk], tag=f"l1f_pev_{pass_i}_{mc}")
            p.op("scalar", lambda e, mc=mc: e.activation(
                    projsq[:, mc, :], proj[:, mc, :], AF.Square),
                 reads=[projk], writes=[projsqk], tag=f"l1f_psq_{pass_i}_{mc}")
        emit_proj_stream(ctx, f"l1fp{pass_i}", st_Wp, inp["Wpt"], PC,
                         lambda kc, ln_p=ln_p: ln_p[:, kc, :], lnk, proj_ep, TP)
        ctx.push_scope()
        rows = ctx.sb(ctx.u("l1f_rows"), [1, 2, TP], F32)
        rwk = ctx.u("l1frowsk")
        prow = ctx.ps(ctx.u("l1f_prow"), [1, TP], F32)
        prk = ctx.u("l1fprk")
        for mc in range(2):
            p.op("tensor", lambda e, mc=mc: e.matmul(
                    prow[:], ctx.ones_col_bf[:], projsq[:, mc, :],
                    start=(mc == 0), stop=(mc == 1)),
                 reads=[projsqk, "ones"], writes=[prk], tag=f"l1f_ss_{pass_i}_{mc}")
        p.op("vector", lambda e: e.tensor_copy(rows[:, 0, :], prow[:]),
             reads=[prk], writes=[rwk], tag=f"l1f_ssr_{pass_i}")
        p.op("scalar", lambda e: e.activation(rows[:, 1, :], rows[:, 0, :], AF.Ln),
             reads=[rwk], writes=[rwk], tag=f"l1f_lnr_{pass_i}")
        p.op("scalar", lambda e: e.activation(rows[:, 1, :], rows[:, 1, :], AF.Exp, scale=0.5),
             reads=[rwk], writes=[rwk], tag=f"l1f_expr_{pass_i}")
        p.op("vector", lambda e: e.tensor_scalar_max(rows[:, 1, :], rows[:, 1, :], 1e-12),
             reads=[rwk], writes=[rwk], tag=f"l1f_maxr_{pass_i}")
        p.op("vector", lambda e: e.reciprocal(rows[:, 1, :], rows[:, 1, :]),
             reads=[rwk], writes=[rwk], tag=f"l1f_rcpr_{pass_i}")
        pbc = ctx.ps(ctx.u("l1f_pbc"), [128, TP], F32)
        pbk = ctx.u("l1fpbk")
        p.op("tensor", lambda e: e.matmul(pbc[:], ctx.ones_row_f32[:], rows[:, 1, :],
                                          start=True, stop=True),
             reads=[rwk, "ones"], writes=[pbk], tag=f"l1f_bc_{pass_i}")
        for mc in range(2):
            p.op("vector", lambda e, mc=mc: e.tensor_mul(proj[:, mc, :], proj[:, mc, :], pbc[:]),
                 reads=[pbk, projk], writes=[projk], tag=f"l1f_scale_{pass_i}_{mc}")
        ctx.pop_scope()
        ret = ctx.sb(ctx.u("l1f_ret"), [128, n_tt, 4], F32)
        retk = ctx.u("l1f_retk")
        ctx.push_scope()
        lg_ps = ctx.ps(ctx.u("l1f_lgps"), [128, 4], F32)
        lgk = ctx.u("lgpsk")
        red = ctx.sb(ctx.u("l1f_red"), [128, 2], F32)
        redk = ctx.u("l1fredk")
        for tt in range(n_tt):
            sl = slice(tt * 128, (tt + 1) * 128)
            for a in range(2):
                p.op("tensor", lambda e, a=a, sl=sl: e.matmul(
                        lg_ps[:], proj[:, a, sl], simb_sb[:, a, :],
                        start=(a == 0), stop=(a == 1)),
                     reads=[projk, "l1fsimb"], writes=[lgk], tag=f"l1f_lg_{pass_i}_{tt}_{a}")
            p.op("vector", lambda e: e.tensor_mul(lg_ps[:], lg_ps[:], factor_bc[:]),
                 reads=[lgk, "facbck"], writes=[lgk], tag=f"l1f_lgs_{pass_i}_{tt}")
            p.op("vector", lambda e: e.reduce_max(red[:, 0:1], lg_ps[:], AX.X, negate=True),
                 reads=[lgk], writes=[redk], tag=f"l1f_nmax_{pass_i}_{tt}")
            p.op("scalar", lambda e, tt=tt: e.activation(
                    ret[:, tt, :], lg_ps[:], AF.Exp, bias=red[:, 0:1],
                    accum_out=red[:, 1:2]),
                 reads=[lgk, redk], writes=[retk, redk], tag=f"l1f_sm_{pass_i}_{tt}")
            p.op("vector", lambda e: e.reciprocal(red[:, 1:2], red[:, 1:2]),
                 reads=[redk], writes=[redk], tag=f"l1f_rl_{pass_i}_{tt}")
            p.op("vector", lambda e, tt=tt: e.tensor_scalar(
                    ret[:, tt, :], ret[:, tt, :], red[:, 1:2], None, ALU.mult),
                 reads=[retk, redk], writes=[retk], tag=f"l1f_lw_{pass_i}_{tt}")
            p.op("vector", lambda e, tt=tt: e.reduce_max(red[:, 0:1], ret[:, tt, :], AX.X),
                 reads=[retk], writes=[redk], tag=f"l1f_lmax_{pass_i}_{tt}")
            p.op("vector", lambda e, tt=tt: e.tensor_scalar(
                    ctx.ln_tmp[:, 0:4], ret[:, tt, :], red[:, 0:1], None, ALU.is_ge),
                 reads=[retk, redk], writes=["ln_tmp"], tag=f"l1f_mask_{pass_i}_{tt}")
            p.op("vector", lambda e, tt=tt: e.tensor_mul(
                    ret[:, tt, :], ret[:, tt, :], ctx.ln_tmp[:, 0:4]),
                 reads=["ln_tmp", retk], writes=[retk], tag=f"l1f_ret_{pass_i}_{tt}")
        ctx.pop_scope()
        scaled_fm = ctx.sb(ctx.u("l1f_scfm"), [15, TP], BF16)
        scfmk = ctx.u("l1f_scfmk")
        ctx.push_scope()
        dn_ps = ctx.ps(ctx.u("l1f_dnps"), [128, 15], F32)
        dnk = ctx.u("dnpsk")
        dn_bf = ctx.sb(ctx.u("l1f_dnbf"), [128, 15], BF16)
        dnbk = ctx.u("dnbfk")
        tr_ps = ctx.ps(ctx.u("l1f_trps"), [15, 128], BF16)
        trk = ctx.u("trpsk")
        for tt in range(n_tt):
            sl = slice(tt * 128, (tt + 1) * 128)
            for kc in range(PC):
                p.op("tensor", lambda e, kc=kc, sl=sl: e.matmul(
                        dn_ps[:], ln_p[:, kc, sl], w_wdt[kc][:],
                        start=(kc == 0), stop=(kc == PC - 1)),
                     reads=[lnk, "l1fWd"], writes=[dnk], tag=f"l1f_dn_{pass_i}_{tt}_{kc}")
            for ei, (c0, c1) in enumerate(L1F_COLS):
                p.op("vector", lambda e, tt=tt, ei=ei, c0=c0, c1=c1: e.tensor_scalar(
                        dn_bf[:, c0:c1], dn_ps[:, c0:c1], ret[:, tt, ei:ei + 1],
                        None, ALU.mult),
                     reads=[dnk, retk], writes=[dnbk], tag=f"l1f_sc_{pass_i}_{tt}_{ei}")
            p.op("tensor", lambda e: e.transpose(tr_ps[:], dn_bf[:], ctx.ident[:]),
                 reads=[dnbk, "identk"], writes=[trk], tag=f"l1f_tr_{pass_i}_{tt}")
            p.op("vector", lambda e, sl=sl: e.tensor_copy(scaled_fm[:, sl], tr_ps[:]),
                 reads=[trk], writes=[scfmk], tag=f"l1f_trc_{pass_i}_{tt}")
        ctx.pop_scope()
        outp = ctx.sb(ctx.u("l1f_out"), [128, PC, TP], BF16)
        outk = ctx.u("l1f_outk")

        def extra(ps, pk):
            for mc in range(PC):
                p.op("tensor", lambda e, mc=mc, ps=ps: e.matmul(
                        ps[:, mc, :], wu[:, mc * 128:(mc + 1) * 128], scaled_fm[:],
                        start=False, stop=True),
                     reads=["l1fWu", scfmk], writes=[pk], tag=f"l1f_up_{pass_i}_{mc}")

        def base_ep(mc, psum, pkey):
            p.op("scalar", lambda e, mc=mc, psum=psum: e.copy(outp[:, mc, :], psum),
                 reads=[pkey], writes=[outk], tag=f"l1f_oev_{pass_i}_{mc}")
        emit_proj_stream(ctx, f"l1fb{pass_i}", st_W, inp["Wt"], PC,
                         lambda kc, ln_p=ln_p: ln_p[:, kc, :], lnk, base_ep, TP,
                         extra=extra)
        emit_ln_fm(ctx, f"l1flno{pass_i}", lambda ci, outp=outp: outp[:, ci, :], outk,
                   accum_res=(lambda ci, t0=t0: x_fn(ci)[:, t0:t0 + TP], x_key),
                   src_bf16=True, tok=TP)
        ctx.pop_scope()

    for _pi in range(n_pass):
        do_pass(_pi)
    ctx.pop_scope()



L1A_H = 16
CHUNK = 128


def emit_tm_proj(ctx, tag, stream, w_dram, n_kc, mcols_off, n_cols, in_fn, in_key,
                 out_cb, tp):
    p = ctx.p
    n_tt = tp // 128
    n_nb = n_cols // 512
    ctx.push_scope()
    ps = ctx.ps(ctx.u(f"ptm_{tag}_"), [128, n_tt, n_nb, 512], F32)
    pk = ctx.u(f"ptmk_{tag}_")
    bufs, keys = stream["bufs"], stream["keys"]

    def load(kc):
        p.dma("gpsimd", bufs[kc % 2][:], w_dram[kc * 128:(kc + 1) * 128, :],
              writes=[keys[kc % 2]], tag=f"ldtm_{tag}_{kc}", nobar=True)
    load(0)
    if n_kc > 1:
        load(1)
    for kc in range(n_kc):
        b = kc % 2
        for tt in range(n_tt):
            for nb in range(n_nb):
                p.op("tensor", lambda e, kc=kc, tt=tt, nb=nb, b=b: e.matmul(
                        ps[:, tt, nb, :], in_fn(kc)[:, tt * 128:(tt + 1) * 128],
                        bufs[b][:, mcols_off + nb * 512:mcols_off + (nb + 1) * 512],
                        start=(kc == 0), stop=(kc == n_kc - 1)),
                     reads=[keys[b], in_key], writes=[pk],
                     tag=f"tmm_{tag}_{kc}_{tt}_{nb}")
        if kc + 2 < n_kc:
            load(kc + 2)
    for tt in range(n_tt):
        for nb in range(n_nb):
            out_cb(tt, nb, ps[:, tt, nb, :], pk)
    ctx.pop_scope()


def emit_l1_attn(ctx, x_fn, x_key, inp, groups8):
    nc, p, tok = ctx.nc, ctx.p, ctx.tok
    TP = min(512, tok)
    n_pass = tok // TP
    n_ch = TP // CHUNK
    n_tt = TP // 128
    H = L1A_H
    ctx.push_scope()

    st_qkv = mk_stream(ctx, "l1aqkv", 3072)
    st_out = mk_stream(ctx, "l1aout", C)
    wqm = ctx.sb("l1a_wqm", [128, 64], BF16)
    wkm = ctx.sb("l1a_wkm", [128, 64], BF16)
    for po in (0, 64):
        p.dma("gpsimd", wqm[po:po + 64, :], inp["wq_mapT"], writes=["l1a_wqm"], tag=f"ld_wqm{po}")
        p.dma("gpsimd", wkm[po:po + 64, :], inp["wk_mapT"], writes=["l1a_wkm"], tag=f"ld_wkm{po}")
    bq = ctx.sb("l1a_bq", [128, 2], F32)
    for po in (0, 64):
        p.dma("sync", bq[po:po + 64, 0:1], inp["bq"], writes=["l1a_bq"], tag=f"ld_bq{po}")
    p.op("vector", lambda e: e.tensor_scalar(bq[:, 1:2], bq[:, 0:1], -1.0, None, ALU.mult),
         reads=["l1a_bq"], writes=["l1a_bq"], tag="negbq")
    bk_rep = ctx.sb("l1a_bk", [128, 64], F32)
    p.dma("sync", bk_rep[:], inp["bk_rep"], writes=["l1a_bk"], tag="ld_bk")
    maskI = ctx.sb("l1a_mask", [128, 128], BF16)
    p.dma("gpsimd", maskI[:], inp["mask_incl"], writes=["l1a_mask"], tag="ld_mask")
    foldm = ctx.sb("l1a_fold", [128, 8], F32)
    p.dma("sync", foldm[:], inp["foldmask"], writes=["l1a_fold"], tag="ld_foldl1")

    q_fm = ctx.sb("l1a_qfm", [128, PC, tok], BF16)
    numer_d = ctx.dram("l1a_numd", [tok // 128, 128, H * 65], BF16)
    ntile = ctx.sb("l1a_nt", [128, H, 65], BF16)
    ntileB = ctx.sb("l1a_ntB", [128, H, 65], BF16)
    kv = ctx.sb("l1a_kv", [128, H, 65], F32)
    kv_bf = ctx.sb("l1a_kvbf", [128, H, 65], BF16)
    s_in = ctx.sb("l1a_sin", [128, H, 65], F32)
    s_in_bf = ctx.sb("l1a_sinbf", [128, H, 65], BF16)
    p.op("vector", lambda e: e.memset(kv[:], 0.0), writes=["l1a_kv"], tag="kv0")

    def emit_eq(pass_i, eq_sb, eq_key, tagp):
        t0 = pass_i * TP
        ctx.push_scope()
        pp = ctx.ps(ctx.u("l1a_pp"), [64, TP], F32)
        ppk = ctx.u("ppk")
        lrow = ctx.sb(ctx.u("l1a_lrow"), [1, 3, TP], F32)
        lrk = ctx.u("lrk")
        prow1 = ctx.ps(ctx.u("l1a_prow"), [1, TP], F32)
        prk = ctx.u("prk")
        pbc = ctx.ps(ctx.u("l1a_pbcq"), [64, TP], F32)
        pbck = ctx.u("pbck")
        for h in range(H):
            ci, po = h // 2, (h % 2) * 64
            p.op("tensor", lambda e, ci=ci, po=po: e.matmul(
                    pp[:], wqm[po:po + 64, :], q_fm[po:po + 64, ci, t0:t0 + TP],
                    start=True, stop=True),
                 reads=["l1a_wqm", "l1a_qfm"], writes=[ppk], tag=f"{tagp}_map_{h}")
            p.op("scalar", lambda e, h=h: e.activation(
                    eq_sb[0:64, h, :], pp[:], AF.Exp, bias=bq[0:64, 0:1]),
                 reads=[ppk, "l1a_bq"], writes=[eq_key], tag=f"{tagp}_eqp_{h}")
            p.op("scalar", lambda e, h=h: e.activation(
                    eq_sb[64:128, h, :], pp[:], AF.Exp, bias=bq[0:64, 1:2], scale=-1.0),
                 reads=[ppk, "l1a_bq"], writes=[eq_key], tag=f"{tagp}_eqm_{h}")
            p.op("tensor", lambda e, h=h: e.matmul(
                    prow1[:], ctx.ones_col_bf[0:64, :], eq_sb[0:64, h, :],
                    start=True, stop=True),
                 reads=[eq_key, "ones"], writes=[prk], tag=f"{tagp}_l1_{h}")
            p.op("vector", lambda e: e.tensor_copy(lrow[:, 0, :], prow1[:]),
                 reads=[prk], writes=[lrk], tag=f"{tagp}_l1c_{h}")
            p.op("tensor", lambda e, h=h: e.matmul(
                    prow1[:], ctx.ones_col_bf[64:128, :], eq_sb[64:128, h, :],
                    start=True, stop=True),
                 reads=[eq_key, "ones"], writes=[prk], tag=f"{tagp}_l2_{h}")
            p.op("vector", lambda e: e.tensor_copy(lrow[:, 1, :], prow1[:]),
                 reads=[prk], writes=[lrk], tag=f"{tagp}_l2c_{h}")
            p.op("vector", lambda e: e.reciprocal(lrow[:, 1, :], lrow[:, 1, :]),
                 reads=[lrk], writes=[lrk], tag=f"{tagp}_l2r_{h}")
            p.op("vector", lambda e: e.tensor_mul(lrow[:, 2, :], lrow[:, 0, :], lrow[:, 1, :]),
                 reads=[lrk], writes=[lrk], tag=f"{tagp}_rho_{h}")
            p.op("tensor", lambda e: e.matmul(
                    pbc[:], ctx.ones_row_f32[:, 0:64], lrow[:, 2, :],
                    start=True, stop=True),
                 reads=[lrk, "ones"], writes=[pbck], tag=f"{tagp}_rhob_{h}")
            p.op("vector", lambda e, h=h: e.tensor_mul(
                    eq_sb[64:128, h, :], eq_sb[64:128, h, :], pbc[:]),
                 reads=[pbck, eq_key], writes=[eq_key], tag=f"{tagp}_eqs_{h}")
        ctx.pop_scope()

    def phase_a(pass_i):
        t0 = pass_i * TP
        ctx.push_scope()
        ln_p = ctx.sb(ctx.u("l1a_ln"), [128, PC, TP], BF16)
        lnk = ctx.u("l1a_lnp")
        emit_ln_fm(ctx, f"l1aln{pass_i}",
                   lambda ci, t0=t0: x_fn(ci)[:, t0:t0 + TP], x_key,
                   out_fn=lambda ci, ln_p=ln_p, lnk=lnk: (ln_p[:, ci, :], lnk), tok=TP)
        k_fm = ctx.sb(ctx.u("l1a_kfm"), [128, PC, TP], BF16)
        kk = ctx.u("l1a_kfmk")

        def qk_ep(mc, psum, pkey, t0=t0):
            if mc < PC:
                p.op("vector", lambda e, mc=mc, psum=psum, t0=t0: e.tensor_copy(
                        q_fm[:, mc, t0:t0 + TP], psum),
                     reads=[pkey], writes=["l1a_qfm"], tag=f"qev_{pass_i}_{mc}")
            else:
                p.op("vector", lambda e, mc=mc, psum=psum: e.tensor_copy(
                        k_fm[:, mc - PC, :], psum),
                     reads=[pkey], writes=[kk], tag=f"kev_{pass_i}_{mc}")
        emit_proj_stream(ctx, f"l1aqk{pass_i}", st_qkv, inp["wqkvT"], PC,
                         lambda kc, ln_p=ln_p: ln_p[:, kc, :], lnk, qk_ep, TP,
                         mcols_off=0, n_mc=16)
        vaug = ctx.sb(ctx.u("l1a_vaug"), [128, n_tt, H, 65], BF16)
        vk = ctx.u("l1a_vaugk")
        p.op("vector", lambda e: e.memset(
                vaug.ap().rearrange("p a h o -> p (a h) o")[:, :, 64:65], 1.0),
             writes=[vk], tag=f"vones_{pass_i}")

        def v_cb(tt, nb, psum, pkey):
            p.op("vector", lambda e, tt=tt, nb=nb, psum=psum: e.tensor_copy(
                    vaug[:, tt, nb * 8:(nb + 1) * 8, 0:64],
                    psum.rearrange("p (h d) -> p h d", d=64)),
                 reads=[pkey], writes=[vk], tag=f"vev_{pass_i}_{tt}_{nb}")
        emit_tm_proj(ctx, f"l1av{pass_i}", st_qkv, inp["wqkvT"], PC, 2048, 1024,
                     lambda kc, ln_p=ln_p: ln_p[:, kc, :], lnk, v_cb, TP)
        eq_sb = ctx.sb(ctx.u("l1a_eq"), [128, H, TP], BF16)
        eqk = ctx.u("l1a_eqk")
        emit_eq(pass_i, eq_sb, eqk, f"eqA{pass_i}")
        phik_fm = ctx.sb(ctx.u("l1a_phikfm"), [128, H, TP], BF16)
        pfk = ctx.u("l1a_phikfmk")
        phik_tm = ctx.sb(ctx.u("l1a_phiktm"), [128, n_tt, H, 128], BF16)
        ptmk = ctx.u("l1a_phiktmk")
        ctx.push_scope()
        pkt = ctx.ps(ctx.u("l1a_pkt"), [128, 64], F32)
        pktk = ctx.u("pktk")
        phtm = ctx.sb(ctx.u("l1a_phtm"), [128, 128], BF16)
        phtmk = ctx.u("phtmk")
        red2 = ctx.sb(ctx.u("l1a_red"), [128, 4], F32)
        redk = ctx.u("redk2")
        ptr = ctx.ps(ctx.u("l1a_ptr"), [128, 128], BF16)
        ptrk = ctx.u("ptrk")
        for tt in range(n_tt):
            for h in range(H):
                ci, po = h // 2, (h % 2) * 64
                p.op("tensor", lambda e, tt=tt, ci=ci, po=po: e.matmul(
                        pkt[:], k_fm[po:po + 64, ci, tt * 128:(tt + 1) * 128],
                        wkm[po:po + 64, :], start=True, stop=True),
                     reads=[kk, "l1a_wkm"], writes=[pktk], tag=f"kmap_{pass_i}_{tt}_{h}")
                p.op("vector", lambda e: e.tensor_add(pkt[:], pkt[:], bk_rep[:]),
                     reads=[pktk, "l1a_bk"], writes=[pktk], tag=f"kbk_{pass_i}_{tt}_{h}")
                p.op("vector", lambda e: e.reduce_max(red2[:, 0:1], pkt[:], AX.X, negate=True),
                     reads=[pktk], writes=[redk], tag=f"km1_{pass_i}_{tt}_{h}")
                p.op("scalar", lambda e: e.activation(
                        phtm[:, 0:64], pkt[:], AF.Exp, bias=red2[:, 0:1],
                        accum_out=red2[:, 1:2]),
                     reads=[pktk, redk], writes=[phtmk, redk], tag=f"km2_{pass_i}_{tt}_{h}")
                p.op("vector", lambda e: e.tensor_reduce(red2[:, 2:3], pkt[:], AX.X, ALU.min),
                     reads=[pktk], writes=[redk], tag=f"km3_{pass_i}_{tt}_{h}")
                p.op("scalar", lambda e: e.activation(
                        phtm[:, 64:128], pkt[:], AF.Exp, bias=red2[:, 2:3], scale=-1.0,
                        accum_out=red2[:, 3:4]),
                     reads=[pktk, redk], writes=[phtmk, redk], tag=f"km4_{pass_i}_{tt}_{h}")
                p.op("vector", lambda e: e.reciprocal(red2[:, 1:2], red2[:, 1:2]),
                     reads=[redk], writes=[redk], tag=f"km5_{pass_i}_{tt}_{h}")
                p.op("vector", lambda e: e.reciprocal(red2[:, 3:4], red2[:, 3:4]),
                     reads=[redk], writes=[redk], tag=f"km6_{pass_i}_{tt}_{h}")
                p.op("vector", lambda e: e.tensor_scalar(
                        phtm[:, 0:64], phtm[:, 0:64], red2[:, 1:2], None, ALU.mult),
                     reads=[phtmk, redk], writes=[phtmk], tag=f"km7_{pass_i}_{tt}_{h}")
                p.op("vector", lambda e: e.tensor_scalar(
                        phtm[:, 64:128], phtm[:, 64:128], red2[:, 3:4], None, ALU.mult),
                     reads=[phtmk, redk], writes=[phtmk], tag=f"km8_{pass_i}_{tt}_{h}")
                p.op("tensor", lambda e: e.transpose(ptr[:], phtm[:], ctx.ident[:]),
                     reads=[phtmk, "identk"], writes=[ptrk], tag=f"ktr_{pass_i}_{tt}_{h}")
                p.op("vector", lambda e, tt=tt, h=h: e.tensor_copy(
                        phik_fm[:, h, tt * 128:(tt + 1) * 128], ptr[:]),
                     reads=[ptrk], writes=[pfk], tag=f"ktrc_{pass_i}_{tt}_{h}")
                p.op("vector", lambda e, tt=tt, h=h: e.tensor_copy(
                        phik_tm[:, tt, h, :], phtm[:]),
                     reads=[phtmk], writes=[ptmk], tag=f"ktms_{pass_i}_{tt}_{h}")
        ctx.pop_scope()
        ctx.push_scope()
        at_sb = ctx.sb(ctx.u("l1a_at"), [128, 128], BF16)
        atk = ctx.u("atk")
        pat = ctx.ps(ctx.u("l1a_pat"), [128, 128], F32)
        patk = ctx.u("patk")
        py = ctx.ps(ctx.u("l1a_py"), [128, 65], F32)
        pyk = ctx.u("pyk")
        pu = ctx.ps(ctx.u("l1a_pu"), [128, 65], F32)
        puk = ctx.u("puk")
        for cc in range(n_ch):
            csl = slice(cc * 128, (cc + 1) * 128)
            p.op("vector", lambda e: e.tensor_copy(kv_bf[:], kv[:]),
                 reads=["l1a_kv"], writes=["l1a_kvbf"], tag=f"kvbf_{pass_i}_{cc}")
            for h in range(H):
                p.op("tensor", lambda e, csl=csl, h=h: e.matmul(
                        pat[:], phik_fm[:, h, csl], eq_sb[:, h, csl],
                        start=True, stop=True),
                     reads=[pfk, eqk], writes=[patk], tag=f"at_{pass_i}_{cc}_{h}")
                p.op("vector", lambda e: e.tensor_mul(at_sb[:], pat[:], maskI[:]),
                     reads=[patk, "l1a_mask"], writes=[atk], tag=f"atm_{pass_i}_{cc}_{h}")
                p.op("tensor", lambda e, cc=cc, h=h: e.matmul(
                        py[:], at_sb[:], vaug[:, cc, h, :], start=True, stop=False),
                     reads=[atk, vk], writes=[pyk], tag=f"av_{pass_i}_{cc}_{h}")
                p.op("tensor", lambda e, csl=csl, h=h: e.matmul(
                        py[:], eq_sb[:, h, csl], kv_bf[:, h, :], start=False, stop=True),
                     reads=[eqk, "l1a_kvbf"], writes=[pyk], tag=f"ycr_{pass_i}_{cc}_{h}")
                p.op("vector", lambda e, h=h: e.tensor_copy(
                        ntile[:, h, :], py[:]),
                     reads=[pyk], writes=["l1a_nt"], tag=f"yev_{pass_i}_{cc}_{h}")
                p.op("tensor", lambda e, cc=cc, h=h: e.matmul(
                        pu[:], phik_tm[:, cc, h, :], vaug[:, cc, h, :],
                        start=True, stop=True),
                     reads=[ptmk, vk], writes=[puk], tag=f"u_{pass_i}_{cc}_{h}")
                p.op("vector", lambda e, h=h: e.tensor_add(kv[:, h, :], kv[:, h, :], pu[:]),
                     reads=[puk, "l1a_kv"], writes=["l1a_kv"], tag=f"kvup_{pass_i}_{cc}_{h}")
            gt = (t0 + cc * 128) // 128
            p.dma("gpsimd", numer_d[gt], ntile.ap().rearrange("p h o -> p (h o)"),
                  reads=["l1a_nt"], writes=["l1a_numd"], tag=f"nst_{pass_i}_{cc}")
        ctx.pop_scope()
        ctx.pop_scope()

    for _pi in range(n_pass):
        phase_a(_pi)

    n_rank = len(groups8[0])
    ctx.push_scope()
    ag_in = ctx.dram(ctx.u("l1a_agin"), [128, H * 65], F32)
    ag_out = ctx.dram(ctx.u("l1a_agout"), [n_rank * 128, H * 65], F32)
    p.dma("sync", ag_in, kv.ap().rearrange("p h o -> p (h o)"),
          reads=["l1a_kv"], writes=["l1a_agin"], tag="ag_in_st")
    p.op("gpsimd", lambda e: e.collective_compute(
            "AllGather", ALU.bypass, groups8, [ag_in.opt()], [ag_out.opt()]),
         reads=["l1a_agin"], writes=["l1a_agout"], dma=True, tag="l1a_ag").ccinc1 = True
    kvj = ctx.sb(ctx.u("l1a_kvj"), [128, H * 65], F32)
    kvjk = ctx.u("l1akvjk")
    p.op("vector", lambda e: e.memset(s_in[:], 0.0), writes=["l1a_sin"], tag="sin0")
    for j in range(n_rank):
        p.dma("sync", kvj[:], ag_out[j * 128:(j + 1) * 128, :],
              reads=["l1a_agout"], writes=[kvjk], tag=f"ldkv_{j}")
        p.op("vector", lambda e, j=j: e.scalar_tensor_tensor(
                s_in.ap().rearrange("p h o -> p (h o)"), kvj[:], foldm[:, j:j + 1],
                s_in.ap().rearrange("p h o -> p (h o)"), ALU.mult, ALU.add),
             reads=[kvjk, "l1a_fold", "l1a_sin"], writes=["l1a_sin"],
             tag=f"fold_{j}")
    p.op("vector", lambda e: e.tensor_copy(s_in_bf[:], s_in[:]),
         reads=["l1a_sin"], writes=["l1a_sinbf"], tag="sinbf")
    ctx.pop_scope()

    def phase_b(pass_i):
        t0 = pass_i * TP
        ctx.push_scope()
        eq_sb = ctx.sb(ctx.u("l1a_eqB"), [128, H, TP], BF16)
        eqk = ctx.u("l1a_eqBk")
        emit_eq(pass_i, eq_sb, eqk, f"eqB{pass_i}")
        y_fm = ctx.sb(ctx.u("l1a_yfm"), [128, PC, TP], BF16)
        yfk = ctx.u("l1a_yfmk")
        ctx.push_scope()
        pc2 = ctx.ps(ctx.u("l1a_pc2"), [128, 65], F32)
        pck = ctx.u("pc2k")
        ytm = ctx.sb(ctx.u("l1a_ytm"), [128, 64], BF16)
        ytmk = ctx.u("ytmk")
        ptr2 = ctx.ps(ctx.u("l1a_ptr2"), [64, 128], BF16)
        ptr2k = ctx.u("ptr2k")
        rcp = ctx.sb(ctx.u("l1a_rcp"), [128, 1], F32)
        rcpk = ctx.u("rcpk")
        for tt in range(n_tt):
            gt = (t0 + tt * 128) // 128
            csl = slice(tt * 128, (tt + 1) * 128)
            p.dma("gpsimd", ntileB.ap().rearrange("p h o -> p (h o)"), numer_d[gt],
                  reads=["l1a_numd"], writes=["l1a_ntB"], tag=f"nld_{pass_i}_{tt}")
            for h in range(H):
                p.op("tensor", lambda e, csl=csl, h=h: e.matmul(
                        pc2[:], eq_sb[:, h, csl], s_in_bf[:, h, :], start=True, stop=True),
                     reads=[eqk, "l1a_sinbf"], writes=[pck], tag=f"corr_{pass_i}_{tt}_{h}")
                p.op("vector", lambda e, h=h: e.tensor_add(
                        pc2[:], pc2[:], ntileB[:, h, :]),
                     reads=["l1a_ntB", pck], writes=[pck], tag=f"cadd_{pass_i}_{tt}_{h}")
                p.op("vector", lambda e: e.tensor_scalar_max(rcp[:], pc2[:, 64:65], 1e-6),
                     reads=[pck], writes=[rcpk], tag=f"dmax_{pass_i}_{tt}_{h}")
                p.op("vector", lambda e: e.reciprocal(rcp[:], rcp[:]),
                     reads=[rcpk], writes=[rcpk], tag=f"drcp_{pass_i}_{tt}_{h}")
                p.op("vector", lambda e: e.tensor_scalar(
                        ytm[:], pc2[:, 0:64], rcp[:], None, ALU.mult),
                     reads=[pck, rcpk], writes=[ytmk], tag=f"ydiv_{pass_i}_{tt}_{h}")
                p.op("tensor", lambda e: e.transpose(ptr2[:], ytm[:], ctx.ident[:]),
                     reads=[ytmk, "identk"], writes=[ptr2k], tag=f"ytr_{pass_i}_{tt}_{h}")
                ci, po = h // 2, (h % 2) * 64
                p.op("vector", lambda e, ci=ci, po=po, csl=csl: e.tensor_copy(
                        y_fm[po:po + 64, ci, csl], ptr2[:]),
                     reads=[ptr2k], writes=[yfk], tag=f"ytrc_{pass_i}_{tt}_{h}")
        ctx.pop_scope()
        outp = ctx.sb(ctx.u("l1a_out"), [128, PC, TP], BF16)
        outk = ctx.u("l1a_outk")

        def o_ep(mc, psum, pkey):
            p.op("scalar", lambda e, mc=mc, psum=psum: e.copy(outp[:, mc, :], psum),
                 reads=[pkey], writes=[outk], tag=f"oev_{pass_i}_{mc}")
        emit_proj_stream(ctx, f"l1ao{pass_i}", st_out, inp["w_outT"], PC,
                         lambda kc, y_fm=y_fm: y_fm[:, kc, :], yfk, o_ep, TP)
        emit_ln_fm(ctx, f"l1alno{pass_i}", lambda ci, outp=outp: outp[:, ci, :], outk,
                   accum_res=(lambda ci, t0=t0: x_fn(ci)[:, t0:t0 + TP], x_key),
                   src_bf16=True, tok=TP)
        ctx.pop_scope()

    for _pi in range(n_pass):
        phase_b(_pi)
    ctx.pop_scope()



WKV_L = 128


def emit_l0_attn(ctx, x_fn, x_key, inp, groups8):
    nc, p, tok = ctx.nc, ctx.p, ctx.tok
    TP = min(512, tok)
    n_pass = tok // TP
    n_ch = TP // WKV_L
    n_tt = TP // 128
    NPT = 16
    NG = 16
    ctx.push_scope()

    st_pad = mk_stream(ctx, "l0apad", 2048)
    st_sq = mk_stream(ctx, "l0asq", C)
    w_maaw1 = emit_weight_load(ctx, "l0a_maaw1", inp["maaw1"], PC, 128)
    maaw2 = ctx.sb("l0a_maaw2s", [128, C], BF16)
    p.dma("gpsimd", maaw2[:], inp["maaw2"], writes=["l0a_maaw2"], tag="ld_maaw2")
    w_dw1 = emit_weight_load(ctx, "l0a_dw1", inp["decayw1"], PC, 64)
    dw2 = ctx.sb("l0a_dw2s", [64, C], BF16)
    p.dma("gpsimd", dw2[:], inp["decayw2"], writes=["l0a_dw2"], tag="ld_dw2")
    dw2p = ctx.sb("l0a_dw2ps", [64, 2048], BF16)
    p.dma("gpsimd", dw2p[:], inp["decayw2p"], writes=["l0a_dw2p"], tag="ld_dw2p")
    maa = {}
    for nm in ("maax", "maaw", "maak", "maav", "maar"):
        t = ctx.sb(f"l0a_{nm}s", [128, PC], F32)
        p.dma("sync", t[:], inp[nm], writes=["l0a_maa"], tag=f"ld_{nm}")
        maa[nm] = t
    decvec_rep = ctx.sb("l0a_dvrep", [128, C], F32)
    p.dma("sync", decvec_rep[:], inp["decvec_rep"], writes=["l0a_dvrep"], tag="ld_dvrep")
    decvec_pad = ctx.sb("l0a_dvpad", [128, NPT], F32)
    p.dma("sync", decvec_pad[:], inp["decvec_pad"], writes=["l0a_dvpad"], tag="ld_dvpad")
    ublk = ctx.sb("l0a_ublks", [128, 64], BF16)
    p.dma("gpsimd", ublk[:], inp["ublk"], writes=["l0a_ublk"], tag="ld_ublk")
    lng_rep = ctx.sb("l0a_lngrep", [128, C], BF16)
    lnb_rep = ctx.sb("l0a_lnbrep", [128, C], BF16)
    p.dma("gpsimd", lng_rep[:], inp["lng_rep"], writes=["l0a_lngb"], tag="ld_lng")
    p.dma("gpsimd", lnb_rep[:], inp["lnb_rep"], writes=["l0a_lngb"], tag="ld_lnb")
    trif = ctx.sb("l0a_trifs", [128, 128], F32)
    p.dma("sync", trif[:], inp["tri_incl"], writes=["l0a_trif"], tag="ld_trif")
    mask4 = ctx.sb("l0a_mask4s", [128, 4 * 128], BF16)
    p.dma("gpsimd", mask4[:], inp["mask_strict4"], writes=["l0a_mask4"], tag="ld_mask4")
    xprev = ctx.sb("l0a_xprevs", [128, PC], F32)
    p.dma("sync", xprev[:], inp["xprev"], writes=["l0a_xprev"], tag="ld_xprev")
    foldm = ctx.sb("l0a_folds", [128, 8], F32)
    p.dma("sync", foldm[:], inp["foldmask"], writes=["l0a_fold"], tag="ld_foldl0")
    zrow = ctx.sb("l0a_zrow", [1, 128], BF16)
    p.op("vector", lambda e: e.memset(zrow[:], 0.0), writes=["l0a_zrow"], tag="zrow")
    zrow512 = ctx.sb("l0a_zrow512", [1, 512], BF16)
    p.op("vector", lambda e: e.memset(zrow512[:], 0.0), writes=["l0a_zrow"], tag="zrow512")

    n_chunk_tot = tok // WKV_L
    lnprev_all = ctx.sb("l0a_lnprev_all", [128, PC, tok // TP], BF16)
    rawu_d = ctx.dram("l0a_rawud", [n_chunk_tot, 128, NG * 16], BF16)
    dstore = ctx.sb("l0a_dst", [128, n_chunk_tot, NG], F32)
    S = ctx.sb("l0a_S", [128, NG, 16], F32)
    Sblk = ctx.sb("l0a_Sblk", [128, NG, 64], BF16)
    p.op("vector", lambda e: e.memset(S[:], 0.0), writes=["l0a_S"], tag="S0")
    p.op("vector", lambda e: e.memset(Sblk[:], 0.0), writes=["l0a_Sblk"], tag="Sblk0")

    def mixes_prep(pass_i, phase):
        t0 = pass_i * TP
        ln_p = ctx.sb(ctx.u("l0a_ln"), [128, PC, TP], BF16)
        lnk = ctx.u("l0a_lnk")
        xx = ctx.sb(ctx.u("l0a_xx"), [128, PC, TP], BF16)
        xxk = ctx.u("l0a_xxk")
        m_fm = ctx.sb(ctx.u("l0a_m"), [128, TP], BF16)
        mk = ctx.u("l0a_mk")
        tmpf = ctx.sb(ctx.u("l0a_tmpf"), [128, TP], F32)
        tfk = ctx.u("tfk")
        ctx.push_scope()
        emit_ln_fm(ctx, f"l0aln{phase}{pass_i}",
                   lambda ci, t0=t0: x_fn(ci)[:, t0:t0 + TP], x_key,
                   out_fn=lambda ci, ln_p=ln_p, lnk=lnk: (ln_p[:, ci, :], lnk), tok=TP)
        lpk = "l0a_lnpa"
        if phase == "A":
            if pass_i == 0:
                emit_ln_col(ctx, f"lp{phase}{pass_i}",
                            lambda ci: xprev[:, ci:ci + 1], "l0a_xprev",
                            lambda ci, pi=pass_i: lnprev_all[:, ci, pi:pi + 1], lpk)
            else:
                emit_ln_col(ctx, f"lp{phase}{pass_i}",
                            lambda ci, t0=t0: x_fn(ci)[:, t0 - 1:t0], x_key,
                            lambda ci, pi=pass_i: lnprev_all[:, ci, pi:pi + 1], lpk)
        for ci in range(PC):
            p.op("vector", lambda e, ci=ci: e.tensor_sub(
                    xx[:, ci, 1:TP], ln_p[:, ci, 0:TP - 1], ln_p[:, ci, 1:TP]),
                 reads=[lnk], writes=[xxk], tag=f"xx_{phase}{pass_i}_{ci}")
            p.op("vector", lambda e, ci=ci, pi=pass_i: e.tensor_sub(
                    xx[:, ci, 0:1], lnprev_all[:, ci, pi:pi + 1], ln_p[:, ci, 0:1]),
                 reads=[lnk, lpk], writes=[xxk], tag=f"xx0_{phase}{pass_i}_{ci}")
        xxx_t = ctx.sb(ctx.u("l0a_xxx"), [128, 2, TP], BF16)
        xxxk = [ctx.u("l0a_xxxk") for _ in range(2)]
        pm = ctx.ps(ctx.u("l0a_pm"), [128, TP], F32)
        pmk = ctx.u("pmk")
        for kc in range(PC):
            b = kc % 2
            p.op("vector", lambda e, kc=kc, b=b: e.scalar_tensor_tensor(
                    xxx_t[:, b, :], xx[:, kc, :], maa["maax"][:, kc:kc + 1],
                    ln_p[:, kc, :], ALU.mult, ALU.add),
                 reads=[xxk, lnk, "l0a_maa"], writes=[xxxk[b]],
                 tag=f"xxx_{phase}{pass_i}_{kc}")
            p.op("tensor", lambda e, kc=kc, b=b: e.matmul(
                    pm[:], w_maaw1[kc][:], xxx_t[:, b, :],
                    start=(kc == 0), stop=(kc == PC - 1)),
                 reads=["l0a_maaw1", xxxk[b]], writes=[pmk], tag=f"mm_m_{phase}{pass_i}_{kc}")
        p.op("scalar", lambda e: e.activation(m_fm[:], pm[:], AF.Tanh),
             reads=[pmk], writes=[mk], tag=f"mtanh_{phase}{pass_i}")
        ctx.pop_scope()
        return {"ln": (ln_p, lnk), "xx": (xx, xxk), "m": (m_fm, mk),
                "tmpf": (tmpf, tfk), "phase": phase, "pass_i": pass_i}

    _forder = {"w": 0, "k": 1, "v": 2, "r": 3}

    def mix_field(prep, f, xf, xfk):
        fi = _forder[f]
        ln_p, lnk = prep["ln"]
        xx, xxk = prep["xx"]
        m_fm, mk = prep["m"]
        tmpf, tfk = prep["tmpf"]
        phase, pass_i = prep["phase"], prep["pass_i"]
        ctx.push_scope()
        pf = ctx.ps(ctx.u("l0a_pf"), [128, TP], F32)
        pfk = ctx.u("pfk")
        for mc in range(PC):
            p.op("tensor", lambda e, fi=fi, mc=mc: e.matmul(
                    pf[:], maaw2[32 * fi:32 * fi + 32, mc * 128:(mc + 1) * 128],
                    m_fm[32 * fi:32 * fi + 32, :], start=True, stop=True,
                    tile_position=(32 * fi, 0)),
                 reads=["l0a_maaw2", mk], writes=[pfk], tag=f"mf_{phase}{pass_i}_{f}_{mc}")
            p.op("vector", lambda e, mc=mc, f=f: e.tensor_scalar(
                    tmpf[:], pf[:], maa["maa" + f][:, mc:mc + 1], None, ALU.add),
                 reads=[pfk, "l0a_maa"], writes=[tfk], tag=f"mfa_{phase}{pass_i}_{f}_{mc}")
            p.op("vector", lambda e, mc=mc: e.tensor_mul(tmpf[:], tmpf[:], xx[:, mc, :]),
                 reads=[tfk, xxk], writes=[tfk], tag=f"mfb_{phase}{pass_i}_{f}_{mc}")
            p.op("gpsimd", lambda e, mc=mc, xf=xf: e.tensor_add(
                    xf[:, mc, :], tmpf[:], ln_p[:, mc, :]),
                 reads=[tfk, lnk], writes=[xfk], tag=f"mfc_{phase}{pass_i}_{f}_{mc}")
        ctx.pop_scope()

    def decay_tanh(xw, xwk, tagp):
        wt = ctx.sb(ctx.u("l0a_wt"), [64, TP], BF16)
        wtk = ctx.u("l0a_wtk")
        ctx.push_scope()
        pd = ctx.ps(ctx.u("l0a_pd"), [64, TP], F32)
        pdk = ctx.u("pdk")
        for kc in range(PC):
            p.op("tensor", lambda e, kc=kc: e.matmul(
                    pd[:], w_dw1[kc][:], xw[:, kc, :],
                    start=(kc == 0), stop=(kc == PC - 1)),
                 reads=["l0a_dw1", xwk], writes=[pdk], tag=f"dmm_{tagp}_{kc}")
        p.op("scalar", lambda e: e.activation(wt[:], pd[:], AF.Tanh),
             reads=[pdk], writes=[wtk], tag=f"dtanh_{tagp}")
        ctx.pop_scope()
        return wt, wtk

    def phase_a(pass_i):
        ctx.push_scope()
        ktm = ctx.sb(ctx.u("l0a_ktm"), [128, n_tt, C], BF16)
        ktmk = ctx.u("ktmk")
        vtm = ctx.sb(ctx.u("l0a_vtm"), [128, n_tt, C], BF16)
        vtmk = ctx.u("vtmk")
        xfb = ctx.sb(ctx.u("l0a_xfb"), [128, PC, TP], BF16)
        xfbk = ctx.u("l0a_xfbk")
        prep = mixes_prep(pass_i, "A")
        mix_field(prep, "w", xfb, xfbk)
        wt, wtk = decay_tanh(xfb, xfbk, f"A{pass_i}")
        mix_field(prep, "k", xfb, xfbk)

        def k_cb(tt, nb, psum, pkey):
            p.op("vector", lambda e, tt=tt, nb=nb, psum=psum: e.tensor_copy(
                    ktm[:, tt, nb * 512:(nb + 1) * 512], psum),
                 reads=[pkey], writes=[ktmk], tag=f"kte_{pass_i}_{tt}_{nb}")
        emit_tm_proj(ctx, f"l0akt{pass_i}", st_sq, inp["Wk"], PC, 0, C,
                     lambda kc: xfb[:, kc, :], xfbk, k_cb, TP)
        mix_field(prep, "v", xfb, xfbk)

        def v_cb(tt, nb, psum, pkey):
            p.op("vector", lambda e, tt=tt, nb=nb, psum=psum: e.tensor_copy(
                    vtm[:, tt, nb * 512:(nb + 1) * 512], psum),
                 reads=[pkey], writes=[vtmk], tag=f"vte_{pass_i}_{tt}_{nb}")
        emit_tm_proj(ctx, f"l0avt{pass_i}", st_sq, inp["Wv"], PC, 0, C,
                     lambda kc: xfb[:, kc, :], xfbk, v_cb, TP)
        ctx.push_scope()
        pwp = ctx.ps(ctx.u("l0a_pwp"), [128, TP], F32)
        pwpk = ctx.u("pwpk")
        ewpad = ctx.sb(ctx.u("l0a_ewpad"), [128, TP], F32)
        ewpk = ctx.u("ewpadk")
        gl = ctx.sb(ctx.u("l0a_gl"), [128, n_ch, NPT], F32)
        glk = ctx.u("glk")
        for pt in range(NPT):
            p.op("tensor", lambda e, pt=pt: e.matmul(
                    pwp[:], dw2p[:, pt * 128:(pt + 1) * 128], wt[:],
                    start=True, stop=True),
                 reads=["l0a_dw2p", wtk], writes=[pwpk], tag=f"dpp_{pass_i}_{pt}")
            p.op("scalar", lambda e, pt=pt: e.activation(
                    ewpad[:], pwp[:], AF.Exp, bias=decvec_pad[:, pt:pt + 1]),
                 reads=[pwpk, "l0a_dvpad"], writes=[ewpk], tag=f"dpe_{pass_i}_{pt}")
            for cc in range(n_ch):
                p.op("vector", lambda e, pt=pt, cc=cc: e.reduce_sum(
                        gl[:, cc, pt:pt + 1], ewpad[:, cc * WKV_L:(cc + 1) * WKV_L], AX.X),
                     reads=[ewpk], writes=[glk], tag=f"dps_{pass_i}_{pt}_{cc}")
        for cc in range(n_ch):
            gchunk = pass_i * n_ch + cc
            p.op("scalar", lambda e, cc=cc, gchunk=gchunk: e.activation(
                    dstore[:, gchunk, :], gl[:, cc, :], AF.Exp, scale=-1.0),
                 reads=[glk], writes=["l0a_dstore"], tag=f"dexp_{pass_i}_{cc}")
        pw = ctx.ps(ctx.u("l0a_pw"), [128, C], F32)
        pwk = ctx.u("pwk")
        ewtm = ctx.sb(ctx.u("l0a_ewtm"), [128, C], F32)
        ewk = ctx.u("ewtmk")
        gtm = ctx.sb(ctx.u("l0a_gtm"), [128, C], F32)
        gk = ctx.u("gtmk")
        pg = ctx.ps(ctx.u("l0a_pg"), [128, C], F32)
        pgk = ctx.u("pgk")
        kptm = ctx.sb(ctx.u("l0a_kptm"), [128, C], BF16)
        kpk = ctx.u("kptmk")
        pru = ctx.ps(ctx.u("l0a_pru"), [128, NG * 16], F32)
        pruk = ctx.u("pruk")
        rutile = ctx.sb(ctx.u("l0a_rutile"), [128, NG * 16], BF16)
        for tt in range(n_tt):
            for nb in range(2):
                p.op("tensor", lambda e, tt=tt, nb=nb: e.matmul(
                        pw[:, nb * 512:(nb + 1) * 512], wt[:, tt * 128:(tt + 1) * 128],
                        dw2[:, nb * 512:(nb + 1) * 512], start=True, stop=True),
                     reads=[wtk, "l0a_dw2"], writes=[pwk], tag=f"wtm_{pass_i}_{tt}_{nb}")
            p.op("vector", lambda e: e.tensor_add(ewtm[:], pw[:], decvec_rep[:]),
                 reads=[pwk, "l0a_dvrep"], writes=[ewk], tag=f"ewadd_{pass_i}_{tt}")
            p.op("scalar", lambda e: e.activation(ewtm[:], ewtm[:], AF.Exp),
                 reads=[ewk], writes=[ewk], tag=f"ewexp_{pass_i}_{tt}")
            for nb in range(2):
                p.op("tensor", lambda e, nb=nb: e.matmul(
                        pg[:, nb * 512:(nb + 1) * 512], trif[:],
                        ewtm[:, nb * 512:(nb + 1) * 512], start=True, stop=True),
                     reads=[ewk, "l0a_trif"], writes=[pgk], tag=f"gtm_{pass_i}_{tt}_{nb}")
            p.op("scalar", lambda e: e.activation(gtm[:], pg[:], AF.Exp),
                 reads=[pgk], writes=[gk], tag=f"gexp_{pass_i}_{tt}")
            p.op("vector", lambda e, tt=tt: e.tensor_mul(kptm[:], ktm[:, tt, :], gtm[:]),
                 reads=[gk, ktmk], writes=[kpk], tag=f"kp_{pass_i}_{tt}")
            gchunk = pass_i * n_ch + tt
            for g in range(NG):
                p.op("tensor", lambda e, g=g: e.matmul(
                        pru[:, g * 16:g * 16 + 16], zrow[:], zrow[:, 0:16],
                        start=True, stop=False),
                     reads=["l0a_zrow"], writes=[pruk], tag=f"ruz_{pass_i}_{tt}_{g}")
                for j in range(4):
                    h = g * 4 + j
                    p.op("tensor", lambda e, g=g, j=j, h=h, tt=tt: e.matmul(
                            pru[32 * j:32 * j + 16, g * 16:g * 16 + 16],
                            kptm[:, h * 16:h * 16 + 16],
                            vtm[:, tt, h * 16:h * 16 + 16],
                            start=False, stop=False,
                            tile_position=(0, 32 * j)),
                         reads=[kpk, vtmk], writes=[pruk], tag=f"ru_{pass_i}_{tt}_{g}_{j}")
                p.op("tensor", lambda e, g=g: e.matmul(
                        pru[:, g * 16:g * 16 + 16], zrow[:], zrow[:, 0:16],
                        start=False, stop=True),
                     reads=["l0a_zrow"], writes=[pruk], tag=f"ruc_{pass_i}_{tt}_{g}")
            p.op("vector", lambda e: e.tensor_copy(rutile[:], pru[:]),
                 reads=[pruk], writes=["l0a_rut"], tag=f"ruev_{pass_i}_{tt}")
            p.dma("gpsimd", rawu_d[gchunk], rutile[:],
                  reads=["l0a_rut"], writes=["l0a_rawu"], tag=f"rust_{pass_i}_{tt}")
            p.op("vector", lambda e: e.tensor_add(
                    S.ap().rearrange("p g j -> p (g j)"),
                    S.ap().rearrange("p g j -> p (g j)"), rutile[:]),
                 reads=["l0a_rut", "l0a_S"], writes=["l0a_S"], tag=f"sadd_{pass_i}_{tt}")
            p.op("vector", lambda e, gchunk=gchunk: e.tensor_mul(
                    S[:], S[:], dstore[:, gchunk, :].unsqueeze(2).broadcast_to([128, NG, 16])),
                 reads=["l0a_dstore", "l0a_S"], writes=["l0a_S"], tag=f"smul_{pass_i}_{tt}")
        ctx.pop_scope()
        ctx.pop_scope()

    for _pi in range(n_pass):
        phase_a(_pi)

    ctx.push_scope()
    dtot = ctx.sb(ctx.u("l0a_dtot"), [128, NG], F32)
    dtk = ctx.u("dtotk")
    p.op("vector", lambda e: e.tensor_copy(dtot[:], dstore[:, 0, :]),
         reads=["l0a_dstore"], writes=[dtk], tag="dtot0")
    for gc in range(1, n_chunk_tot):
        p.op("vector", lambda e, gc=gc: e.tensor_mul(dtot[:], dtot[:], dstore[:, gc, :]),
             reads=["l0a_dstore", dtk], writes=[dtk], tag=f"dtot{gc}")
    n_rank = len(groups8[0])
    ag_in = ctx.dram(ctx.u("l0a_agin"), [128, NG * 16 + NG], F32)
    ag_out = ctx.dram(ctx.u("l0a_agout"), [n_rank * 128, NG * 16 + NG], F32)
    p.dma("sync", ag_in[:, 0:NG * 16], S.ap().rearrange("p g j -> p (g j)"),
          reads=["l0a_S"], writes=["l0a_agin"], tag="l0a_agst1")
    p.dma("sync", ag_in[:, NG * 16:], dtot[:],
          reads=[dtk], writes=["l0a_agin2"], tag="l0a_agst2")
    p.op("gpsimd", lambda e: e.collective_compute(
            "AllGather", ALU.bypass, groups8, [ag_in.opt()], [ag_out.opt()]),
         reads=["l0a_agin", "l0a_agin2"], writes=["l0a_agout"], dma=True,
         tag="l0a_ag").ccinc1 = True
    sg = ctx.sb(ctx.u("l0a_sg"), [128, NG, 16], F32)
    sgk = ctx.u("sgk")
    buf = ctx.sb(ctx.u("l0a_agbuf"), [128, NG * 16 + NG], F32)
    bk2 = ctx.u("agbufk")
    dfac = ctx.sb(ctx.u("l0a_dfac"), [128, NG], F32)
    dfk = ctx.u("dfack")
    p.op("vector", lambda e: e.memset(sg[:], 0.0), writes=[sgk], tag="sg0")
    for j in range(n_rank):
        p.dma("sync", buf[:], ag_out[j * 128:(j + 1) * 128, :],
              reads=["l0a_agout"], writes=[bk2], tag=f"l0a_ldag{j}")
        p.op("vector", lambda e: e.tensor_scalar(dfac[:], buf[:, NG * 16:], -1.0, None, ALU.add),
             reads=[bk2], writes=[dfk], tag=f"l0a_df1_{j}")
        p.op("vector", lambda e, j=j: e.tensor_scalar(
                dfac[:], dfac[:], foldm[:, j:j + 1], 1.0, ALU.mult, ALU.add),
             reads=[dfk, "l0a_fold"], writes=[dfk], tag=f"l0a_df2_{j}")
        p.op("vector", lambda e: e.tensor_mul(
                sg[:], sg[:], dfac.ap().unsqueeze(2).broadcast_to([128, NG, 16])),
             reads=[dfk, sgk], writes=[sgk], tag=f"l0a_sgm_{j}")
        p.op("vector", lambda e, j=j: e.scalar_tensor_tensor(
                sg.ap().rearrange("p g j -> p (g j)"), buf[:, 0:NG * 16],
                foldm[:, j:j + 1], sg.ap().rearrange("p g j -> p (g j)"),
                ALU.mult, ALU.add),
             reads=[bk2, "l0a_fold", sgk], writes=[sgk], tag=f"l0a_sga_{j}")
    p.op("vector", lambda e: e.tensor_copy(S[:], sg[:]),
         reads=[sgk], writes=["l0a_S"], tag="l0a_sreset")
    ctx.pop_scope()

    def phase_b(pass_i):
        t0 = pass_i * TP
        ctx.push_scope()
        rpad = ctx.sb(ctx.u("l0a_rpad"), [128, NPT, TP], BF16)
        rpk = ctx.u("rpadk")
        kpad = ctx.sb(ctx.u("l0a_kpad"), [128, NPT, TP], BF16)
        kpdk = ctx.u("kpadk")
        vtm = ctx.sb(ctx.u("l0a_vtmB"), [128, n_tt, C], BF16)
        vtmk = ctx.u("vtmBk")
        du = ctx.sb(ctx.u("l0a_du"), [128, n_tt, 64], F32)
        duk = ctx.u("duk")
        ytm = ctx.sb(ctx.u("l0a_ytm"), [128, n_tt, C], BF16)
        ytmk = ctx.u("ytmk")
        y_fm = ctx.sb(ctx.u("l0a_yfm"), [128, PC, TP], BF16)
        yfk = ctx.u("yfmk")
        ctx.push_scope()
        xfb = ctx.sb(ctx.u("l0a_xfbB"), [128, PC, TP], BF16)
        xfbk = ctx.u("l0a_xfbBk")
        prep = mixes_prep(pass_i, "B")

        def dbg_dump(f):
            if "dbg_xr" in inp:
                p.dma("gpsimd",
                      inp["dbg_x" + f][:, t0:t0 + TP].rearrange("(a p) t -> p a t", p=128),
                      xfb[:], reads=[xfbk],
                      writes=["dbgx" + f], tag=f"dbgx_{f}_{pass_i}")
        mix_field(prep, "w", xfb, xfbk)
        dbg_dump("w")
        wt, wtk = decay_tanh(xfb, xfbk, f"B{pass_i}")
        mix_field(prep, "r", xfb, xfbk)
        dbg_dump("r")

        def rp_ep(mc, psum, pkey):
            p.op("vector", lambda e, mc=mc, psum=psum: e.tensor_copy(rpad[:, mc, :], psum),
                 reads=[pkey], writes=[rpk], tag=f"rpev_{pass_i}_{mc}")
        emit_proj_stream(ctx, f"l0arp{pass_i}", st_pad, inp["Wrp"], PC,
                         lambda kc: xfb[:, kc, :], xfbk, rp_ep, TP)
        mix_field(prep, "k", xfb, xfbk)
        dbg_dump("k")

        def kp_ep(mc, psum, pkey):
            p.op("vector", lambda e, mc=mc, psum=psum: e.tensor_copy(kpad[:, mc, :], psum),
                 reads=[pkey], writes=[kpdk], tag=f"kpev_{pass_i}_{mc}")
        emit_proj_stream(ctx, f"l0akp{pass_i}", st_pad, inp["Wkp"], PC,
                         lambda kc: xfb[:, kc, :], xfbk, kp_ep, TP)
        mix_field(prep, "v", xfb, xfbk)
        dbg_dump("v")

        def v_cb(tt, nb, psum, pkey):
            p.op("vector", lambda e, tt=tt, nb=nb, psum=psum: e.tensor_copy(
                    vtm[:, tt, nb * 512:(nb + 1) * 512], psum),
                 reads=[pkey], writes=[vtmk], tag=f"vteB_{pass_i}_{tt}_{nb}")
        emit_tm_proj(ctx, f"l0avtB{pass_i}", st_sq, inp["Wv"], PC, 0, C,
                     lambda kc: xfb[:, kc, :], xfbk, v_cb, TP)
        ctx.push_scope()
        rk_sb = ctx.sb(ctx.u("l0a_rk"), [128, TP], BF16)
        rkk = ctx.u("rkk")
        pdu = ctx.ps(ctx.u("l0a_pdu"), [128, 64], F32)
        pduk = ctx.u("pduk")
        for pt in range(NPT):
            p.op("vector", lambda e, pt=pt: e.tensor_mul(
                    rk_sb[:], rpad[:, pt, :], kpad[:, pt, :]),
                 reads=[rpk, kpdk], writes=[rkk], tag=f"rk_{pass_i}_{pt}")
            for cc in range(n_ch):
                p.op("tensor", lambda e, pt=pt, cc=cc: e.matmul(
                        pdu[:, pt * 4:pt * 4 + 4], rk_sb[:, cc * WKV_L:(cc + 1) * WKV_L],
                        ublk[:, pt * 4:pt * 4 + 4], start=True, stop=True),
                     reads=[rkk, "l0a_ublk"], writes=[pduk], tag=f"du_{pass_i}_{pt}_{cc}")
                p.op("vector", lambda e, pt=pt, cc=cc: e.tensor_copy(
                        du[:, cc, pt * 4:pt * 4 + 4], pdu[:, pt * 4:pt * 4 + 4]),
                     reads=[pduk], writes=[duk], tag=f"duev_{pass_i}_{pt}_{cc}")
        ctx.pop_scope()
        ctx.push_scope()
        pwp = ctx.ps(ctx.u("l0a_pwpB"), [128, TP], F32)
        pwpk = ctx.u("pwpBk")
        ew_f = ctx.sb(ctx.u("l0a_ewf"), [128, TP], F32)
        ewfk = ctx.u("ewfk")
        g_f = ctx.sb(ctx.u("l0a_gf"), [128, TP], F32)
        gfk = ctx.u("gfk")
        es_f = ctx.sb(ctx.u("l0a_esf"), [128, TP], F32)
        esk = ctx.u("esfk")
        for pt in range(NPT):
            p.op("tensor", lambda e, pt=pt: e.matmul(
                    pwp[:], dw2p[:, pt * 128:(pt + 1) * 128], wt[:],
                    start=True, stop=True),
                 reads=["l0a_dw2p", wtk], writes=[pwpk], tag=f"dppB_{pass_i}_{pt}")
            p.op("scalar", lambda e, pt=pt: e.activation(
                    ew_f[:], pwp[:], AF.Exp, bias=decvec_pad[:, pt:pt + 1]),
                 reads=[pwpk, "l0a_dvpad"], writes=[ewfk], tag=f"dpeB_{pass_i}_{pt}")
            for cc in range(n_ch):
                sl = slice(cc * WKV_L, (cc + 1) * WKV_L)
                p.op("vector", lambda e, sl=sl: e.tensor_tensor_scan(
                        g_f[:, sl], ctx.ones_col_f32.ap().broadcast_to([128, WKV_L]),
                        ew_f[:, sl], 0.0, ALU.mult, ALU.add),
                     reads=[ewfk, "ones"], writes=[gfk], tag=f"scan_{pass_i}_{pt}_{cc}")
            p.op("scalar", lambda e: e.activation(es_f[:], g_f[:], AF.Exp),
                 reads=[gfk], writes=[esk], tag=f"ekp_{pass_i}_{pt}")
            p.op("vector", lambda e, pt=pt: e.tensor_mul(
                    kpad[:, pt, :], kpad[:, pt, :], es_f[:]),
                 reads=[esk, kpdk], writes=[kpdk], tag=f"kps_{pass_i}_{pt}")
            p.op("vector", lambda e: e.tensor_sub(g_f[:], ew_f[:], g_f[:]),
                 reads=[ewfk, gfk], writes=[gfk], tag=f"qarg_{pass_i}_{pt}")
            p.op("scalar", lambda e: e.activation(es_f[:], g_f[:], AF.Exp),
                 reads=[gfk], writes=[esk], tag=f"eqp_{pass_i}_{pt}")
            p.op("vector", lambda e, pt=pt: e.tensor_mul(
                    rpad[:, pt, :], rpad[:, pt, :], es_f[:]),
                 reads=[esk, rpk], writes=[rpk], tag=f"qps_{pass_i}_{pt}")
        ctx.pop_scope()
        ctx.pop_scope()
        ctx.push_scope()
        pat4 = ctx.ps(ctx.u("l0a_pat"), [128, 4, 512], F32)
        patk = ctx.u("patk")
        at_sb = ctx.sb(ctx.u("l0a_at"), [128, 4 * 128], BF16)
        atk = ctx.u("atk")
        pY = ctx.ps(ctx.u("l0a_pY"), [128, C], F32)
        pYk = ctx.u("pYk")
        vdu = ctx.sb(ctx.u("l0a_vdu"), [128, C], F32)
        vduk = ctx.u("vduk")
        rutileB = ctx.sb(ctx.u("l0a_rutileB"), [128, NG * 16], BF16)
        for cc in range(n_ch):
            csl = slice(cc * WKV_L, (cc + 1) * WKV_L)
            gchunk = pass_i * n_ch + cc
            for j in range(4):
                p.op("vector", lambda e, j=j: e.tensor_copy(
                        Sblk[32 * j:32 * j + 32, :, j * 16:(j + 1) * 16],
                        S[32 * j:32 * j + 32, :, :]),
                     reads=["l0a_S"], writes=["l0a_Sblk"], tag=f"sbf_{pass_i}_{cc}_{j}")
            for hb in range(2):
                p.op("tensor", lambda e, hb=hb: e.matmul(
                        pY[:, hb * 512:(hb + 1) * 512], zrow[:], zrow512[:],
                        start=True, stop=False),
                     reads=["l0a_zrow"], writes=[pYk], tag=f"yz_{pass_i}_{cc}_{hb}")
            for pt in range(NPT):
                for j in range(4):
                    p.op("tensor", lambda e, pt=pt, j=j, csl=csl: e.matmul(
                            pat4[:, j, 0:128],
                            kpad[32 * j:32 * j + 32, pt, csl],
                            rpad[32 * j:32 * j + 32, pt, csl],
                            start=True, stop=True, tile_position=(32 * j, 0)),
                         reads=[kpdk, rpk], writes=[patk],
                         tag=f"at_{pass_i}_{cc}_{pt}_{j}")
                p.op("vector", lambda e: e.tensor_mul(
                        at_sb.ap().rearrange("p (a b) -> p a b", b=128),
                        pat4[:, :, 0:128],
                        mask4.ap().rearrange("p (a b) -> p a b", b=128)),
                     reads=[patk, "l0a_mask4"], writes=[atk],
                     tag=f"atm_{pass_i}_{cc}_{pt}")
                for j in range(4):
                    h = pt * 4 + j
                    p.op("tensor", lambda e, j=j, h=h, cc=cc: e.matmul(
                            pY[:, h * 16:h * 16 + 16], at_sb[:, j * 128:(j + 1) * 128],
                            vtm[:, cc, h * 16:h * 16 + 16], start=False, stop=False),
                         reads=[atk, vtmk], writes=[pYk], tag=f"av_{pass_i}_{cc}_{pt}_{j}")
                p.op("tensor", lambda e, pt=pt, csl=csl: e.matmul(
                        pY[:, pt * 64:(pt + 1) * 64], rpad[:, pt, csl],
                        Sblk[:, pt, :], start=False, stop=False),
                     reads=[rpk, "l0a_Sblk"], writes=[pYk], tag=f"ycr_{pass_i}_{cc}_{pt}")
            for hb in range(2):
                p.op("tensor", lambda e, hb=hb: e.matmul(
                        pY[:, hb * 512:(hb + 1) * 512], zrow[:], zrow512[:],
                        start=False, stop=True),
                     reads=["l0a_zrow", pYk], writes=[pYk], tag=f"yc_{pass_i}_{cc}_{hb}")
            p.op("vector", lambda e, cc=cc: e.tensor_mul(
                    vdu.ap().rearrange("p (h j) -> p h j", j=16),
                    vtm[:, cc, :].rearrange("p (h j) -> p h j", j=16),
                    du[:, cc, :].unsqueeze(2).broadcast_to([128, 64, 16])),
                 reads=[vtmk, duk], writes=[vduk], tag=f"vdu_{pass_i}_{cc}")
            p.op("vector", lambda e, cc=cc: e.tensor_add(ytm[:, cc, :], vdu[:], pY[:]),
                 reads=[vduk, pYk], writes=[ytmk], tag=f"yadd_{pass_i}_{cc}")
            if "dbg_y" in inp:
                p.dma("gpsimd", inp["dbg_y"][t0 + cc * WKV_L:t0 + (cc + 1) * WKV_L, :],
                      ytm[:, cc, :], reads=[ytmk], writes=["dbgy"],
                      tag=f"dbgy_{pass_i}_{cc}")
            p.dma("gpsimd", rutileB[:], rawu_d[gchunk],
                  reads=["l0a_rawu"], writes=["l0a_rutB"], tag=f"ruld_{pass_i}_{cc}")
            p.op("vector", lambda e: e.tensor_add(
                    S.ap().rearrange("p g j -> p (g j)"),
                    S.ap().rearrange("p g j -> p (g j)"), rutileB[:]),
                 reads=["l0a_rutB", "l0a_S"], writes=["l0a_S"], tag=f"saddB_{pass_i}_{cc}")
            p.op("vector", lambda e, gchunk=gchunk: e.tensor_mul(
                    S[:], S[:], dstore[:, gchunk, :].unsqueeze(2).broadcast_to([128, NG, 16])),
                 reads=["l0a_dstore", "l0a_S"], writes=["l0a_S"], tag=f"smulB_{pass_i}_{cc}")
        ctx.pop_scope()
        ctx.push_scope()
        st2 = ctx.sb(ctx.u("l0a_st2"), [128, 6], F32)
        stk = ctx.u("st2k")
        ysq = ctx.sb(ctx.u("l0a_ysq"), [128, C], BF16)
        ysqk = ctx.u("ysqk")
        ptr3 = ctx.ps(ctx.u("l0a_ptr3"), [128, 128], BF16)
        ptr3k = ctx.u("ptr3k")
        for tt in range(n_tt):
            p.op("vector", lambda e, tt=tt: e.reduce_sum(st2[:, 0:1], ytm[:, tt, :], AX.X),
                 reads=[ytmk], writes=[stk], tag=f"gn_s_{pass_i}_{tt}")
            p.op("scalar", lambda e, tt=tt: e.activation(ysq[:], ytm[:, tt, :], AF.Square,
                                                         accum_out=st2[:, 1:2]),
                 reads=[ytmk], writes=[ysqk, stk], tag=f"gn_sq_{pass_i}_{tt}")
            p.op("vector", lambda e: e.tensor_scalar(st2[:, 2:4], st2[:, 0:2], 1.0 / C, None, ALU.mult),
                 reads=[stk], writes=[stk], tag=f"gn_m_{pass_i}_{tt}")
            p.op("vector", lambda e: e.tensor_mul(st2[:, 4:5], st2[:, 2:3], st2[:, 2:3]),
                 reads=[stk], writes=[stk], tag=f"gn_m2_{pass_i}_{tt}")
            p.op("vector", lambda e: e.tensor_sub(st2[:, 4:5], st2[:, 3:4], st2[:, 4:5]),
                 reads=[stk], writes=[stk], tag=f"gn_v_{pass_i}_{tt}")
            p.op("scalar", lambda e: e.activation(st2[:, 4:5], st2[:, 4:5], AF.Ln, bias=EPS),
                 reads=[stk], writes=[stk], tag=f"gn_l_{pass_i}_{tt}")
            p.op("scalar", lambda e: e.activation(st2[:, 4:5], st2[:, 4:5], AF.Exp, scale=-0.5),
                 reads=[stk], writes=[stk], tag=f"gn_e_{pass_i}_{tt}")
            p.op("vector", lambda e, tt=tt: e.tensor_scalar(
                    ytm[:, tt, :], ytm[:, tt, :], st2[:, 2:3], None, ALU.subtract),
                 reads=[stk, ytmk], writes=[ytmk], tag=f"gn_a1_{pass_i}_{tt}")
            p.op("vector", lambda e, tt=tt: e.tensor_scalar(
                    ytm[:, tt, :], ytm[:, tt, :], st2[:, 4:5], None, ALU.mult),
                 reads=[stk, ytmk], writes=[ytmk], tag=f"gn_a2_{pass_i}_{tt}")
            p.op("gpsimd", lambda e, tt=tt: e.tensor_mul(ytm[:, tt, :], ytm[:, tt, :], lng_rep[:]),
                 reads=["l0a_lngb", ytmk], writes=[ytmk], tag=f"gn_g_{pass_i}_{tt}")
            p.op("gpsimd", lambda e, tt=tt: e.tensor_add(ytm[:, tt, :], ytm[:, tt, :], lnb_rep[:]),
                 reads=["l0a_lngb", ytmk], writes=[ytmk], tag=f"gn_b_{pass_i}_{tt}")
            for ci in range(PC):
                p.op("tensor", lambda e, tt=tt, ci=ci: e.transpose(
                        ptr3[:], ytm[:, tt, ci * 128:(ci + 1) * 128], ctx.ident[:]),
                     reads=[ytmk, "identk"], writes=[ptr3k], tag=f"ytr_{pass_i}_{tt}_{ci}")
                p.op("vector", lambda e, tt=tt, ci=ci: e.tensor_copy(
                        y_fm[:, ci, tt * 128:(tt + 1) * 128], ptr3[:]),
                     reads=[ptr3k], writes=[yfk], tag=f"ytrc_{pass_i}_{tt}_{ci}")
        ctx.pop_scope()

        def o_ep(mc, psum, pkey):
            p.op("scalar", lambda e, mc=mc, psum=psum: e.copy(y_fm[:, mc, :], psum),
                 reads=[pkey], writes=[yfk], tag=f"l0a_oev_{pass_i}_{mc}")
        emit_proj_stream(ctx, f"l0ao{pass_i}", st_sq, inp["Wo"], PC,
                         lambda kc, y_fm=y_fm: y_fm[:, kc, :], yfk, o_ep, TP)
        emit_ln_fm(ctx, f"l0alno{pass_i}", lambda ci: y_fm[:, ci, :], yfk,
                   accum_res=(lambda ci, t0=t0: x_fn(ci)[:, t0:t0 + TP], x_key),
                   src_bf16=True, tok=TP)
        ctx.pop_scope()

    for _pi in range(n_pass):
        phase_b(_pi)
    ctx.pop_scope()




def emit_moe_prep(ctx, my_w1, my_w2, groups_all):
    p = ctx.p
    n = sum(len(g) for g in groups_all) // len(groups_all)
    assert [len(g) for g in groups_all] == [n] * len(groups_all)
    w1loc = ctx.dram(ctx.u("moe_w1loc"), [C, 4096], BF16)
    w2loc = ctx.dram(ctx.u("moe_w2loc"), [4096, C], BF16)
    w1all = ctx.dram(ctx.u("moe_w1all"), [n * C, 4096], BF16)
    w2all = ctx.dram(ctx.u("moe_w2all"), [n * 4096, C], BF16)
    p.dma("gpsimd", w1loc, my_w1, writes=["moe_w1loc"], tag="moec1", nobar=True)
    p.dma("gpsimd", w2loc, my_w2, writes=["moe_w2loc"], tag="moec2", nobar=True)
    o1 = p.op("gpsimd", lambda e: e.collective_compute(
            "AllGather", ALU.bypass, groups_all, [w1loc.opt()], [w1all.opt()]),
         reads=["moe_w1loc"], writes=["moe_w1all"], dma=True, tag="moeag1")
    o1.ccinc1 = True
    o1.nobar = True
    o2 = p.op("gpsimd", lambda e: e.collective_compute(
            "AllGather", ALU.bypass, groups_all, [w2loc.opt()], [w2all.opt()]),
         reads=["moe_w2loc"], writes=["moe_w2all"], dma=True, tag="moeag2")
    o2.ccinc1 = True
    o2.nobar = True
    return w1all, w2all


def emit_l0_ffn(ctx, x_fn, x_key, inp, groups4, moe_w1, moe_w2,
                moe_rk=((), ())):
    nc, p, tok = ctx.nc, ctx.p, ctx.tok
    TP = min(512, tok)
    n_pass = tok // TP
    NR = len(groups4[0])
    HL = 16 // NR
    T_ALL = tok * NR
    n_kt_all = T_ALL // 128
    ctx.push_scope()

    st_w = mk_stream(ctx, "l0f_w", C)
    st_w1 = mk_stream(ctx, "l0f_w1", 1024)
    st_w2 = mk_stream(ctx, "l0f_w2", C)
    biases = ctx.sb("l0f_biases", [128, 4, PC], F32)
    p.dma("sync", biases[:], inp["bqkvo"], writes=["l0f_b"], tag="ld_l0fb")
    bv_rep = ctx.sb("l0f_bvrep", [128, C], F32)
    p.dma("sync", bv_rep[:], inp["bv_rep"], writes=["l0f_bv"], tag="ld_bv")
    ln1g = ctx.sb("l0f_ln1g", [128, 2, PC], F32)
    p.dma("sync", ln1g[:], inp["ln1gb"], writes=["l0f_ln1"], tag="ld_ln1")
    ln2g = ctx.sb("l0f_ln2g", [128, 2, PC], F32)
    p.dma("sync", ln2g[:], inp["ln2gb"], writes=["l0f_ln2"], tag="ld_ln2")
    wr_sb = emit_weight_load(ctx, "l0f_wr", inp["Wr"], PC, 8)
    br_rep = ctx.sb("l0f_br", [128, 8], F32)
    p.dma("sync", br_rep[:], inp["br_rep"], writes=["l0f_br"], tag="ld_br")
    b1_sb = ctx.sb("l0f_b1", [128, 8, 32], F32)
    p.dma("sync", b1_sb[:], inp["b1"], writes=["l0f_b1"], tag="ld_b1")
    b2_sb = ctx.sb("l0f_b2", [128, 8, 8], F32)
    p.dma("sync", b2_sb[:], inp["b2"], writes=["l0f_b2"], tag="ld_b2")
    dmask = ctx.sb("l0f_dmask", [128, 4, 512], BF16)
    p.dma("gpsimd", dmask[:], inp["dmask"], writes=["l0f_dmask"], tag="ld_dmask")
    idxs = {}
    for nm in ("gq_idx", "gk_idx", "gv_idx", "go_idx"):
        w = inp[nm].shape[1]
        t = ctx.sb(f"l0f_{nm}", [128, w], I32)
        p.dma("sync", t[:], inp[nm], writes=["l0f_idx"], tag=f"ld_{nm}")
        idxs[nm] = t

    a_sb = ctx.sb("l0f_a", [128, PC, tok], F32)

    contrib = ctx.dram(ctx.u("l0f_contrib"), [48 * 64, tok], BF16)
    ag1_out = ctx.dram(ctx.u("l0f_ag1"), [NR * 48 * 64, tok], BF16)
    myq = ctx.dram(ctx.u("l0f_myq"), [HL * 64 * NR, tok], BF16)
    myk = ctx.dram(ctx.u("l0f_myk"), [HL * 64 * NR, tok], BF16)
    myv = ctx.dram(ctx.u("l0f_myv"), [HL * NR * (tok // 128), 128 * 64], BF16)
    ocontrib = ctx.dram(ctx.u("l0f_oc"), [HL * 64, T_ALL], BF16)
    ag2_out = ctx.dram(ctx.u("l0f_ag2"), [NR * HL * 64, T_ALL], BF16)
    wdram = ctx.dram(ctx.u("l0f_wdram"), [8, tok], F32)

    vthd = contrib[2048:3072, :].rearrange("(h r) t -> h (r t)", h=16) \
                                .rearrange("h (t d) -> t h d", d=64)

    def f1_pass(pass_i):
        t0 = pass_i * TP
        ctx.push_scope()
        h_sb = ctx.sb(ctx.u("l0f_h"), [128, PC, TP], BF16)
        hk = ctx.u("l0f_hk")
        qtmp = ctx.sb(ctx.u("l0f_qtmp"), [128, TP], BF16)
        qtk = ctx.u("qtmpk")
        emit_ln_fm(ctx, f"l0fsl{pass_i}",
                   lambda ci, t0=t0: x_fn(ci)[:, t0:t0 + TP], x_key,
                   out_fn=lambda ci, t0=t0: (a_sb[:, ci, t0:t0 + TP], "l0f_a"),
                   tok=TP)
        emit_ln_fm(ctx, f"l0fl1{pass_i}",
                   lambda ci, t0=t0: a_sb[:, ci, t0:t0 + TP], "l0f_a",
                   out_fn=lambda ci: (h_sb[:, ci, :], hk),
                   gain_bias=(lambda ci: ln1g[:, 0, ci:ci + 1],
                              lambda ci: ln1g[:, 1, ci:ci + 1], "l0f_ln1"),
                   tok=TP)
        for wi, wname in ((0, "Wq"), (1, "Wk")):
            slab0 = 0 if wi == 0 else 16

            def ep(mc, psum, pkey, wi=wi, wname=wname, slab0=slab0, t0=t0):
                p.op("vector", lambda e, psum=psum, wi=wi, mc=mc: e.tensor_scalar(
                        qtmp[:], psum, biases[:, wi, mc:mc + 1], None, ALU.add),
                     reads=[pkey, "l0f_b"], writes=[qtk], tag=f"qev_{pass_i}_{wname}_{mc}")
                r0 = slab0 * 64 + mc * 128
                p.dma("gpsimd", contrib[r0:r0 + 128, t0:t0 + TP], qtmp[:],
                      reads=[qtk], writes=["l0f_c" + wname],
                      tag=f"qst_{pass_i}_{wname}_{mc}")
            emit_proj_stream(ctx, f"l0f{wname}{pass_i}", st_w, inp[wname], PC,
                             lambda kc: h_sb[:, kc, :], hk, ep, TP)

        def v_cb(tt, nb, psum, pkey, t0=t0):
            p.op("vector", lambda e, psum=psum, nb=nb: e.tensor_add(
                    qtmp[:], psum, bv_rep[:, nb * 512:(nb + 1) * 512]),
                 reads=[pkey, "l0f_bv"], writes=[qtk], tag=f"vev_{pass_i}_{tt}_{nb}")
            ts0 = t0 + tt * 128
            p.dma("gpsimd",
                  vthd[ts0:ts0 + 128, nb * 8:(nb + 1) * 8, :],
                  qtmp.ap().rearrange("t (h d) -> t h d", d=64),
                  reads=[qtk], writes=["l0f_cv"], tag=f"vst_{pass_i}_{tt}_{nb}")
        emit_tm_proj(ctx, f"l0fv{pass_i}", st_w, inp["Wv"], PC, 0, C,
                     lambda kc: h_sb[:, kc, :], hk, v_cb, TP)
        ctx.pop_scope()

    for _pi in range(n_pass):
        f1_pass(_pi)

    ctx.push_scope()
    p.op("gpsimd", lambda e: e.collective_compute(
            "AllGather", ALU.bypass, groups4, [contrib.opt()], [ag1_out.opt()]),
         reads=["l0f_cWq", "l0f_cWk", "l0f_cv"], writes=["l0f_ag1"], dma=True,
         tag="l0f_ag1").ccinc1 = True
    gbuf = ctx.sb(ctx.u("l0f_gbuf"), [128, tok], BF16)
    gbk = ctx.u("l0f_gbufk")
    gvbuf = ctx.sb(ctx.u("l0f_gvbuf"), [128, 128 * 64], BF16)
    gvk = ctx.u("l0f_gvbufk")
    n_gq = (HL * 64 * NR) // 128
    for nm, dst in (("gq_idx", myq), ("gk_idx", myk)):
        for g in range(n_gq):
            p.op("gpsimd", lambda e, g=g, nm=nm: e.indirect_dma_start(
                    gbuf[:], None, ag1_out,
                    IndirectOffsetOnAxis(idxs[nm][:, g:g + 1], 0)),
                 reads=["l0f_ag1", "l0f_idx"], writes=[gbk], dma=True, tag=f"g{nm}_{g}")
            p.dma("gpsimd", dst[g * 128:(g + 1) * 128, :], gbuf[:],
                  reads=[gbk], writes=["l0f_myqk"], tag=f"g{nm}s_{g}")
    ag1_pieces = ag1_out.rearrange("(a b) t -> a (b t)", b=8192 // tok)
    n_vrows = HL * NR * (tok // 128)
    g = 0
    while g * 128 < n_vrows:
        rows = min(128, n_vrows - g * 128)
        p.op("gpsimd", lambda e, g=g, rows=rows: e.indirect_dma_start(
                gvbuf[0:rows, :], None, ag1_pieces,
                IndirectOffsetOnAxis(idxs["gv_idx"][0:rows, g:g + 1], 0)),
             reads=["l0f_ag1", "l0f_idx"], writes=[gvk], dma=True, tag=f"gv_{g}")
        p.dma("gpsimd", myv[g * 128:g * 128 + rows, :], gvbuf[0:rows, :],
              reads=[gvk], writes=["l0f_myv"], tag=f"gvs_{g}")
        g += 1
    ctx.pop_scope()

    myq_v = myq.rearrange("(h d r) t -> h d r t", h=HL, r=NR)
    myk_v = myk.rearrange("(h d r) t -> h d r t", h=HL, r=NR)
    myv_v = myv.rearrange("(h b) e -> h b e", h=HL)

    def f3_head(h):
        ctx.push_scope()
        qh = ctx.sb(ctx.u("l0f_qh"), [64, T_ALL], BF16)
        qhk = ctx.u("qhk")
        kh = ctx.sb(ctx.u("l0f_kh"), [64, T_ALL], BF16)
        khk = ctx.u("khk")
        p.dma("sync", qh.ap().rearrange("d (r t) -> d r t", r=NR), myq_v[h],
              reads=["l0f_myqk"], writes=[qhk], tag=f"ldqh_{h}")
        p.dma("sync", kh.ap().rearrange("d (r t) -> d r t", r=NR), myk_v[h],
              reads=["l0f_myqk"], writes=[khk], tag=f"ldkh_{h}")
        vaug = ctx.sb(ctx.u("l0f_vaug"), [128, n_kt_all, 65], BF16)
        vbk = ctx.u("vaugk")
        p.dma("sync", vaug[:, :, 0:64],
              myv_v[h].rearrange("b (t d) -> t b d", d=64),
              reads=["l0f_myv"], writes=[vbk], tag=f"ldvh_{h}")
        ncol = ctx.sb(ctx.u("l0f_ncol"), [128, 2, n_kt_all], F32)
        nck = ctx.u("ncolk")
        negb = ctx.sb(ctx.u("l0f_negb"), [128, n_kt_all], F32)
        nbk = ctx.u("negbk")
        eb = ctx.sb(ctx.u("l0f_eb"), [128, n_kt_all], F32)
        ebk = ctx.u("ebk")
        otile = ctx.sb(ctx.u("l0f_otile"), [64, 2, 128], BF16)
        otk = ctx.u("otilek")
        ctx.push_scope()
        sqt = ctx.sb(ctx.u("l0f_sqt"), [64, 512], BF16)
        sqk = ctx.u("sqtk")
        pnb = ctx.ps(ctx.u("l0f_pnb"), [128, 2, n_kt_all], F32)
        pnk = ctx.u("pnbk")
        zr64 = ctx.sb(ctx.u("l0f_zr"), [1, 128], BF16)
        p.op("vector", lambda e: e.memset(zr64[:], 0.0), writes=["l0f_zr"],
             tag=f"zr_{h}")
        for which, buf, bk in ((0, qh, qhk), (1, kh, khk)):
            p.op("tensor", lambda e, which=which: e.matmul(
                    pnb[:, which, :], zr64[:], zr64[:, 0:n_kt_all],
                    start=True, stop=False),
                 reads=["l0f_zr"], writes=[pnk], tag=f"nz_{h}_{which}")
            for tc in range(T_ALL // 512):
                p.op("scalar", lambda e, buf=buf, tc=tc: e.activation(
                        sqt[:], buf[:, tc * 512:(tc + 1) * 512], AF.Square),
                     reads=[bk], writes=[sqk], tag=f"nsq_{h}_{which}_{tc}")
                for j in range(4):
                    b = tc * 4 + j
                    p.op("tensor", lambda e, j=j, b=b, which=which: e.matmul(
                            pnb[:, which, b:b + 1], sqt[:, j * 128:(j + 1) * 128],
                            ctx.ones_col_bf[0:64, :], start=False, stop=False),
                         reads=[sqk, "ones"], writes=[pnk], tag=f"nmm_{h}_{which}_{b}")
            p.op("tensor", lambda e, which=which: e.matmul(
                    pnb[:, which, :], zr64[:], zr64[:, 0:n_kt_all],
                    start=False, stop=True),
                 reads=["l0f_zr", pnk], writes=[pnk], tag=f"nc_{h}_{which}")
            p.op("vector", lambda e, which=which: e.tensor_copy(
                    ncol[:, which, :], pnb[:, which, :]),
                 reads=[pnk], writes=[nck], tag=f"nev_{h}_{which}")
        qmax = ctx.sb(ctx.u("l0f_qmax"), [128, 1], F32)
        qmk = ctx.u("qmaxk")
        qrow = ctx.sb(ctx.u("l0f_qrow"), [1, 129], F32)
        qrk = ctx.u("qrowk")
        ptq = ctx.ps(ctx.u("l0f_ptq"), [1, 128], F32)
        ptqk = ctx.u("ptqk")
        pbq = ctx.ps(ctx.u("l0f_pbq"), [128, 1], F32)
        pbqk = ctx.u("pbqk")
        p.op("vector", lambda e: e.reduce_max(qmax[:], ncol[:, 0, :], AX.X),
             reads=[nck], writes=[qmk], tag=f"qm1_{h}")
        p.op("tensor", lambda e: e.transpose(ptq[:], qmax[:], ctx.ident_f32[:]),
             reads=[qmk, "identfk"], writes=[ptqk], tag=f"qtr_{h}")
        p.op("vector", lambda e: e.tensor_copy(qrow[:, 0:128], ptq[:]),
             reads=[ptqk], writes=[qrk], tag=f"qcp_{h}")
        p.op("vector", lambda e: e.reduce_max(qrow[:, 128:129], qrow[:, 0:128], AX.X),
             reads=[qrk], writes=[qrk], tag=f"qm2_{h}")
        p.op("tensor", lambda e: e.matmul(pbq[:], ctx.ones_row_f32[:],
                                          qrow[:, 128:129], start=True, stop=True),
             reads=[qrk, "ones"], writes=[pbqk], tag=f"qbc_{h}")
        p.op("vector", lambda e: e.tensor_copy(qmax[:], pbq[:]),
             reads=[pbqk], writes=[qmk], tag=f"qbe_{h}")
        p.op("vector", lambda e: e.tensor_scalar(negb[:], ncol[:, 1, :], qmax[:],
                                                 -1.0 / 16.0, ALU.add, ALU.mult),
             reads=[nck, qmk], writes=[nbk], tag=f"nb_{h}")
        p.op("scalar", lambda e: e.activation(eb[:], negb[:], AF.Exp, scale=-1.0),
             reads=[nbk], writes=[ebk], tag=f"eb_{h}")
        ctx.pop_scope()
        p.op("vector", lambda e: e.tensor_mul(
                vaug[:, :, 0:64], vaug[:, :, 0:64],
                eb.ap().unsqueeze(2).broadcast_to([128, n_kt_all, 64])),
             reads=[ebk, vbk], writes=[vbk], tag=f"vsc_{h}")
        p.op("vector", lambda e: e.tensor_copy(vaug[:, :, 64:65], eb.ap().unsqueeze(2)),
             reads=[ebk, vbk], writes=[vbk], tag=f"vac_{h}")
        ctx.push_scope()
        pst = ctx.ps(ctx.u("l0f_pst"), [128, 2, 512], F32)
        pstk = ctx.u("pstk")
        pt_sb = ctx.sb(ctx.u("l0f_pt"), [128, 2, 512], BF16)
        ptk = ctx.u("ptsbk")
        po = ctx.ps(ctx.u("l0f_po"), [128, 4, 512], F32)
        pok = ctx.u("pok")
        oy = ctx.sb(ctx.u("l0f_oy"), [128, 64], BF16)
        oyk = ctx.u("oyk")
        rcl = ctx.sb(ctx.u("l0f_rcl"), [128, 1], F32)
        rck = ctx.u("rclk")
        ptro = ctx.ps(ctx.u("l0f_ptro"), [64, 128], BF16)
        ptrok = ctx.u("ptrok")
        for qc in range(T_ALL // 512):
            n_kt = (qc + 1) * 4
            for ktp in range((n_kt + 1) // 2):
                for sub in range(2):
                    kt = ktp * 2 + sub
                    if kt >= n_kt:
                        continue
                    diag = kt >= qc * 4
                    p.op("tensor", lambda e, kt=kt, sub=sub, qc=qc, diag=diag: e.matmul(
                            pst[:, sub, :], kh[:, kt * 128:(kt + 1) * 128],
                            qh[:, qc * 512:(qc + 1) * 512],
                            start=True, stop=not diag),
                         reads=[khk, qhk], writes=[pstk], tag=f"qk_{h}_{qc}_{kt}")
                    if diag:
                        p.op("tensor", lambda e, kt=kt, sub=sub, qc=qc: e.matmul(
                                pst[:, sub, :], ctx.ident[:],
                                dmask[:, kt - qc * 4, :], start=False, stop=True),
                             reads=["identk", "l0f_dmask", pstk], writes=[pstk],
                             tag=f"dm_{h}_{qc}_{kt}")
                    p.op("scalar", lambda e, sub=sub, kt=kt: e.activation(
                            pt_sb[:, sub, :], pst[:, sub, :], AF.Exp,
                            bias=negb[:, kt:kt + 1], scale=0.125),
                         reads=[pstk, nbk], writes=[ptk], tag=f"exp_{h}_{qc}_{kt}")
                for sub in range(2):
                    kt = ktp * 2 + sub
                    if kt >= n_kt:
                        continue
                    for qs in range(4):
                        p.op("tensor", lambda e, kt=kt, sub=sub, qs=qs, nk=n_kt: e.matmul(
                                po[:, qs, 0:65],
                                pt_sb[:, sub, qs * 128:(qs + 1) * 128],
                                vaug[:, kt, :],
                                start=(kt == 0), stop=(kt == nk - 1)),
                             reads=[ptk, vbk], writes=[pok], tag=f"av_{h}_{qc}_{kt}_{qs}")
            for qs in range(4):
                ob = qs % 2
                p.op("vector", lambda e, qs=qs: e.reciprocal(rcl[:], po[:, qs, 64:65]),
                     reads=[pok], writes=[rck], tag=f"rl_{h}_{qc}_{qs}")
                p.op("vector", lambda e, qs=qs: e.tensor_scalar(
                        oy[:], po[:, qs, 0:64], rcl[:], None, ALU.mult),
                     reads=[pok, rck], writes=[oyk], tag=f"onorm_{h}_{qc}_{qs}")
                p.op("tensor", lambda e: e.transpose(ptro[:], oy[:], ctx.ident[:]),
                     reads=[oyk, "identk"], writes=[ptrok], tag=f"otr_{h}_{qc}_{qs}")
                p.op("vector", lambda e, ob=ob: e.tensor_copy(otile[:, ob, :], ptro[:]),
                     reads=[ptrok], writes=[otk], tag=f"oev_{h}_{qc}_{qs}")
                c0 = qc * 512 + qs * 128
                p.dma("gpsimd", ocontrib[h * 64:(h + 1) * 64, c0:c0 + 128],
                      otile[:, ob, :], reads=[otk], writes=["l0f_oc"],
                      tag=f"ost_{h}_{qc}_{qs}")
        ctx.pop_scope()
        ctx.pop_scope()

    for h in range(HL):
        f3_head(h)

    p.op("gpsimd", lambda e: e.collective_compute(
            "AllGather", ALU.bypass, groups4, [ocontrib.opt()], [ag2_out.opt()]),
         reads=["l0f_oc"], writes=["l0f_ag2"], dma=True, tag="l0f_ag2").ccinc1 = True
    ag2_pieces = ag2_out.rearrange("r (n u t) -> (r n u) t", n=NR, t=TP)

    def f4_pass(pass_i):
        t0 = pass_i * TP
        ctx.push_scope()
        h2_sb = ctx.sb(ctx.u("l0f_h2"), [128, PC, TP], BF16)
        h2k = ctx.u("l0f_h2k")
        wbc_all = ctx.sb(ctx.u("l0f_wbc"), [128, 8, TP], BF16)
        wbck = ctx.u("wbck")
        mid_sb = ctx.sb(ctx.u("l0f_mid"), [128, 8, TP], BF16)
        midk = ctx.u("midk")
        eo_sb = ctx.sb(ctx.u("l0f_eo"), [128, 8, TP], BF16)
        eok = ctx.u("eok")
        tws = ctx.sb(ctx.u("l0f_tws"), [128, TP], F32)
        twk = ctx.u("twsk")
        ctx.push_scope()
        o_pass = ctx.sb(ctx.u("l0f_op"), [128, PC, TP], BF16)
        opk = ctx.u("l0f_opk")
        for g in range(PC):
            p.op("gpsimd", lambda e, g=g: e.indirect_dma_start(
                    o_pass[:, g, :], None, ag2_pieces,
                    IndirectOffsetOnAxis(idxs["go_idx"][:, pass_i * PC + g:pass_i * PC + g + 1], 0)),
                 reads=["l0f_ag2", "l0f_idx"], writes=[opk], dma=True,
                 tag=f"go_{pass_i}_{g}")

        def ep(mc, psum, pkey, t0=t0):
            p.op("vector", lambda e, mc=mc, psum=psum, t0=t0: e.scalar_tensor_tensor(
                    a_sb[:, mc, t0:t0 + TP], psum, biases[:, 3, mc:mc + 1],
                    a_sb[:, mc, t0:t0 + TP], ALU.add, ALU.add),
                 reads=[pkey, "l0f_b", "l0f_a"], writes=["l0f_a"],
                 tag=f"aev_{pass_i}_{mc}")
        emit_proj_stream(ctx, f"l0fo{pass_i}", st_w, inp["Wo"], PC,
                         lambda kc: o_pass[:, kc, :], opk, ep, TP)
        ctx.pop_scope()
        emit_ln_fm(ctx, f"l0fl2{pass_i}",
                   lambda ci, t0=t0: a_sb[:, ci, t0:t0 + TP], "l0f_a",
                   out_fn=lambda ci: (h2_sb[:, ci, :], h2k),
                   gain_bias=(lambda ci: ln2g[:, 0, ci:ci + 1],
                              lambda ci: ln2g[:, 1, ci:ci + 1], "l0f_ln2"),
                   tok=TP)
        ctx.push_scope()
        w_tm = ctx.sb(ctx.u("l0f_wtm"), [128, TP // 128, 8], F32)
        wtmk = ctx.u("wtmk")
        wfm = ctx.sb(ctx.u("l0f_wfm"), [8, TP], F32)
        wfmk = ctx.u("wfmk")
        plg = ctx.ps(ctx.u("l0f_plg"), [128, 8], F32)
        plgk = ctx.u("plgk")
        ptw = ctx.ps(ctx.u("l0f_ptw"), [8, 128], F32)
        ptwk = ctx.u("ptwk")
        red = ctx.sb(ctx.u("l0f_red"), [128, 4], F32)
        redk = ctx.u("redl0f")
        etmp = ctx.sb(ctx.u("l0f_etmp"), [128, 8], F32)
        etk = ctx.u("etmpk")
        for tt in range(TP // 128):
            sl = slice(tt * 128, (tt + 1) * 128)
            for kc in range(PC):
                p.op("tensor", lambda e, kc=kc, sl=sl: e.matmul(
                        plg[:], h2_sb[:, kc, sl], wr_sb[kc][:],
                        start=(kc == 0), stop=(kc == PC - 1)),
                     reads=[h2k, "l0f_wr"], writes=[plgk], tag=f"rt_{pass_i}_{tt}_{kc}")
            p.op("vector", lambda e: e.tensor_add(plg[:], plg[:], br_rep[:]),
                 reads=[plgk, "l0f_br"], writes=[plgk], tag=f"rtb_{pass_i}_{tt}")
            p.op("vector", lambda e: e.reduce_max(red[:, 0:1], plg[:], AX.X, negate=True),
                 reads=[plgk], writes=[redk], tag=f"rm1_{pass_i}_{tt}")
            p.op("vector", lambda e: e.tensor_scalar(etmp[:], plg[:], red[:, 0:1],
                                                     0.0, ALU.add, ALU.is_ge),
                 reads=[plgk, redk], writes=[etk], tag=f"req_{pass_i}_{tt}")
            p.op("vector", lambda e: e.scalar_tensor_tensor(
                    etmp[:], etmp[:], -1e30, plg[:], ALU.mult, ALU.add),
                 reads=[etk, plgk], writes=[etk], tag=f"rrm_{pass_i}_{tt}")
            p.op("vector", lambda e: e.reduce_max(red[:, 1:2], etmp[:], AX.X),
                 reads=[etk], writes=[redk], tag=f"rm2_{pass_i}_{tt}")
            p.op("vector", lambda e: e.tensor_scalar(etmp[:], plg[:], red[:, 1:2],
                                                     None, ALU.is_ge),
                 reads=[plgk, redk], writes=[etk], tag=f"rkm_{pass_i}_{tt}")
            p.op("scalar", lambda e, tt=tt: e.activation(
                    w_tm[:, tt, :], plg[:], AF.Exp, bias=red[:, 0:1]),
                 reads=[plgk, redk], writes=[wtmk], tag=f"rexp_{pass_i}_{tt}")
            p.op("vector", lambda e, tt=tt: e.tensor_mul(
                    w_tm[:, tt, :], w_tm[:, tt, :], etmp[:]),
                 reads=[etk, wtmk], writes=[wtmk], tag=f"rmask_{pass_i}_{tt}")
            p.op("vector", lambda e, tt=tt: e.reduce_sum(red[:, 2:3], w_tm[:, tt, :], AX.X),
                 reads=[wtmk], writes=[redk], tag=f"rsum_{pass_i}_{tt}")
            p.op("vector", lambda e: e.reciprocal(red[:, 2:3], red[:, 2:3]),
                 reads=[redk], writes=[redk], tag=f"rrc_{pass_i}_{tt}")
            p.op("vector", lambda e, tt=tt: e.tensor_scalar(
                    w_tm[:, tt, :], w_tm[:, tt, :], red[:, 2:3], None, ALU.mult),
                 reads=[wtmk, redk], writes=[wtmk], tag=f"rw_{pass_i}_{tt}")
            p.op("tensor", lambda e, tt=tt: e.transpose(
                    ptw[:], w_tm[:, tt, :], ctx.ident_f32[:]),
                 reads=[wtmk, "identfk"], writes=[ptwk], tag=f"rtr_{pass_i}_{tt}")
            p.op("vector", lambda e, sl=sl: e.tensor_copy(wfm[:, sl], ptw[:]),
                 reads=[ptwk], writes=[wfmk], tag=f"rtrc_{pass_i}_{tt}")
        p.dma("sync", wdram[:, t0:t0 + TP], wfm[:], reads=[wfmk],
              writes=["l0f_wdram"], tag=f"wst_{pass_i}")
        wrow = ctx.sb(ctx.u("l0f_wrow"), [1, TP], F32)
        wrk = ctx.u("wrowk")
        pwb = ctx.ps(ctx.u("l0f_pwb"), [128, TP], F32)
        pwbk = ctx.u("pwbk")
        for e_i in range(8):
            p.dma("sync", wrow[:], wdram[e_i:e_i + 1, t0:t0 + TP],
                  reads=["l0f_wdram"], writes=[wrk], tag=f"wld_{pass_i}_{e_i}")
            p.op("tensor", lambda e: e.matmul(pwb[:], ctx.ones_row_f32[:], wrow[:],
                                              start=True, stop=True),
                 reads=[wrk, "ones"], writes=[pwbk], tag=f"wbc_{pass_i}_{e_i}")
            p.op("scalar", lambda e, e_i=e_i: e.copy(wbc_all[:, e_i, :], pwb[:]),
                 reads=[pwbk], writes=[wbck], tag=f"wbe_{pass_i}_{e_i}")
        ctx.pop_scope()
        ctx.push_scope()
        pmm = ctx.ps(ctx.u("l0f_pmm"), [128, 8, TP], F32)
        pmmk = ctx.u("pmmk")
        qs_of = {0: "gpsimd", 1: "sync"}

        def loadw1(e_i, fb, kc):
            b = kc % 2
            p.dma(qs_of[b], st_w1["bufs"][b][:],
                  moe_w1[e_i * C + kc * 128:e_i * C + (kc + 1) * 128,
                         fb * 1024:(fb + 1) * 1024],
                  reads=list(moe_rk[0]),
                  writes=[st_w1["keys"][b]], tag=f"w1l_{pass_i}_{e_i}_{fb}_{kc}",
                  nobar=True)

        def loadw2(e_i, fb, kcf):
            b = kcf % 2
            p.dma(qs_of[b], st_w2["bufs"][b][:],
                  moe_w2[e_i * 4096 + fb * 1024 + kcf * 128:
                         e_i * 4096 + fb * 1024 + (kcf + 1) * 128, :],
                  reads=list(moe_rk[1]),
                  writes=[st_w2["keys"][b]], tag=f"w2l_{pass_i}_{e_i}_{fb}_{kcf}",
                  nobar=True)
        first = [True]
        for e_i in range(8):
            for fb in range(4):
                if first[0]:
                    loadw1(e_i, fb, 0)
                    loadw1(e_i, fb, 1)
                    first[0] = False
                for kc in range(PC):
                    b = kc % 2
                    for fc in range(8):
                        p.op("tensor", lambda e, kc=kc, fc=fc, b=b, t0=t0: e.matmul(
                                pmm[:, fc, :],
                                st_w1["bufs"][b][:, fc * 128:(fc + 1) * 128],
                                h2_sb[:, kc, :],
                                start=(kc == 0), stop=(kc == PC - 1)),
                             reads=[st_w1["keys"][b], h2k], writes=[pmmk],
                             tag=f"w1m_{pass_i}_{e_i}_{fb}_{kc}_{fc}")
                    if kc + 2 < PC:
                        loadw1(e_i, fb, kc + 2)
                loadw2(e_i, fb, 0)
                loadw2(e_i, fb, 1)
                nfb, ne = (fb + 1, e_i) if fb < 3 else (0, e_i + 1)
                if ne < 8:
                    loadw1(ne, nfb, 0)
                    loadw1(ne, nfb, 1)
                for fc in range(8):
                    p.op("scalar", lambda e, fc=fc, fb=fb, e_i=e_i: e.activation(
                            mid_sb[:, fc, :], pmm[:, fc, :], AF.Gelu,
                            bias=b1_sb[:, e_i, fb * 8 + fc:fb * 8 + fc + 1]),
                         reads=[pmmk, "l0f_b1"], writes=[midk],
                         tag=f"gel_{pass_i}_{e_i}_{fb}_{fc}")
                for kcf in range(8):
                    b = kcf % 2
                    for mc in range(PC):
                        p.op("tensor", lambda e, kcf=kcf, mc=mc, b=b: e.matmul(
                                pmm[:, mc, :],
                                st_w2["bufs"][b][:, mc * 128:(mc + 1) * 128],
                                mid_sb[:, kcf, :],
                                start=(kcf == 0), stop=(kcf == PC - 1)),
                             reads=[st_w2["keys"][b], midk], writes=[pmmk],
                             tag=f"w2m_{pass_i}_{e_i}_{fb}_{kcf}_{mc}")
                    if kcf + 2 < 8:
                        loadw2(e_i, fb, kcf + 2)
                for mc in range(PC):
                    if fb == 0:
                        p.op("vector", lambda e, mc=mc: e.tensor_copy(
                                eo_sb[:, mc, :], pmm[:, mc, :]),
                             reads=[pmmk], writes=[eok],
                             tag=f"eoc_{pass_i}_{e_i}_{fb}_{mc}")
                    elif fb < 3:
                        p.op("vector", lambda e, mc=mc: e.tensor_add(
                                eo_sb[:, mc, :], eo_sb[:, mc, :], pmm[:, mc, :]),
                             reads=[pmmk, eok], writes=[eok],
                             tag=f"eoa_{pass_i}_{e_i}_{fb}_{mc}")
                    else:
                        p.op("vector", lambda e, mc=mc, e_i=e_i: e.scalar_tensor_tensor(
                                tws[:], pmm[:, mc, :], b2_sb[:, e_i, mc:mc + 1],
                                eo_sb[:, mc, :], ALU.add, ALU.add),
                             reads=[pmmk, "l0f_b2", eok], writes=[twk],
                             tag=f"eow_{pass_i}_{e_i}_{mc}")
                        p.op("vector", lambda e, mc=mc, e_i=e_i: e.tensor_mul(
                                tws[:], tws[:], wbc_all[:, e_i, :]),
                             reads=[twk, wbck], writes=[twk],
                             tag=f"eowm_{pass_i}_{e_i}_{mc}")
                        p.op("gpsimd", lambda e, mc=mc, t0=t0: e.tensor_add(
                                a_sb[:, mc, t0:t0 + TP], a_sb[:, mc, t0:t0 + TP],
                                tws[:]),
                             reads=[twk, "l0f_a"], writes=["l0f_a"],
                             tag=f"acc_{pass_i}_{e_i}_{mc}")
        ctx.pop_scope()
        emit_ln_fm(ctx, f"l0flno{pass_i}",
                   lambda ci, t0=t0: a_sb[:, ci, t0:t0 + TP], "l0f_a",
                   accum_res=(lambda ci, t0=t0: x_fn(ci)[:, t0:t0 + TP], x_key),
                   tok=TP)
        ctx.pop_scope()

    for _pi in range(n_pass):
        f4_pass(_pi)
    ctx.pop_scope()


import ml_dtypes
from concourse.bass_utils import run_bass_kernel_spmd

NCORES = 8
TOK = 2048
NR4 = 4
HL = 16 // NR4
N_PASS = TOK // 512
GROUPS4 = [[0, 1, 2, 3], [4, 5, 6, 7]]
GROUPS8 = [[0, 1, 2, 3, 4, 5, 6, 7]]
BF16NP = ml_dtypes.bfloat16


def _stage_shell(nc, emit_fn):
    p = Prog()
    ctx = Ctx(nc, p, TOK)
    xin = ctx.input("xin", [C, TOK])
    ident_in = ctx.input("ident_in", [128, 128], BF16)
    identf_in = ctx.input("identf_in", [128, 128])
    xout = ctx.output("xout", [C, TOK])
    x_sb = ctx.sb("x_res", [128, PC, TOK], F32, scope=0)
    ctx.setup_common(ident_in, identf_in)
    p.dma("sync", x_sb[:], xin.rearrange("(a p) t -> p a t", p=128),
          writes=["x"], tag="ldx")
    emit_fn(ctx, lambda ci: x_sb[:, ci, :], "x")
    p.dma("gpsimd", xout.rearrange("(a p) t -> p a t", p=128), x_sb[:],
          reads=["x"], writes=["xout"], tag="stx")
    p.fence("sync", ["xout"])
    with nc.Block() as block:
        p.emit_into(nc, block, ctx.scopes[0])
    ctx.scopes[0].close()


def build_l0a(nc):
    def emit(ctx, x_fn, x_key):
        names = {
            "maax": [128, 8], "maaw": [128, 8], "maak": [128, 8],
            "maav": [128, 8], "maar": [128, 8],
            "maaw1": [C, 128], "maaw2": [128, C],
            "decayw1": [C, 64], "decayw2": [64, C], "decayw2p": [64, 2048],
            "decvec_rep": [128, C], "decvec_pad": [128, 16],
            "Wrp": [C, 2048], "Wkp": [C, 2048], "Wk": [C, C], "Wv": [C, C],
            "Wo": [C, C], "ublk": [128, 64], "lng_rep": [128, C],
            "lnb_rep": [128, C], "tri_incl": [128, 128], "xprev": [128, 8],
            "foldmask": [128, 8],
        }
        inp = {k: ctx.input("i_" + k, s) for k, s in names.items()}
        inp["mask_strict4"] = ctx.input("i_mask_strict4", [128, 512], BF16)
        emit_l0_attn(ctx, x_fn, x_key, inp, GROUPS4)
    _stage_shell(nc, emit)


def feeds_l0a(ins, x):
    decay = np.asarray(ins["L0_attn_time_decay"], np.float32).reshape(C)
    dpad = np.full((2048, 1), -1e9, np.float32)
    for h in range(64):
        dpad[32 * h:32 * h + 16, 0] = decay[16 * h:16 * h + 16]

    def padcols(W):
        Wp = np.zeros((W.shape[0], 2048), np.float32)
        for h in range(64):
            Wp[:, 32 * h:32 * h + 16] = W[:, 16 * h:16 * h + 16]
        return Wp
    u = np.asarray(ins["L0_attn_time_faaaa"], np.float32)
    ublk = np.zeros((128, 64), np.float32)
    for pt in range(16):
        for j in range(4):
            h = pt * 4 + j
            ublk[32 * j:32 * j + 16, pt * 4 + j] = u[h]
    g = lambda n: np.asarray(ins["L0_attn_" + n], np.float32)
    com = {
        "i_maax": np.ascontiguousarray(g("time_maa_x").reshape(8, 128).T),
        "i_maaw": np.ascontiguousarray(g("time_maa_w").reshape(8, 128).T),
        "i_maak": np.ascontiguousarray(g("time_maa_k").reshape(8, 128).T),
        "i_maav": np.ascontiguousarray(g("time_maa_v").reshape(8, 128).T),
        "i_maar": np.ascontiguousarray(g("time_maa_r").reshape(8, 128).T),
        "i_maaw1": g("time_maa_w1"),
        "i_maaw2": g("time_maa_w2").reshape(128, C),
        "i_decayw1": g("time_decay_w1"),
        "i_decayw2": g("time_decay_w2"),
        "i_decayw2p": padcols(g("time_decay_w2")),
        "i_decvec_rep": np.broadcast_to(decay, (128, C)).copy(),
        "i_decvec_pad": np.ascontiguousarray(dpad.reshape(16, 128).T),
        "i_Wrp": padcols(g("W_r")),
        "i_Wkp": padcols(g("W_k")),
        "i_Wk": g("W_k"), "i_Wv": g("W_v"), "i_Wo": g("W_o"),
        "i_ublk": ublk,
        "i_lng_rep": np.broadcast_to(g("ln_g"), (128, C)).copy(),
        "i_lnb_rep": np.broadcast_to(g("ln_b"), (128, C)).copy(),
        "i_tri_incl": np.triu(np.ones((128, 128), np.float32)),
        "i_mask_strict4": np.tile(np.triu(np.ones((128, 128)), 1),
                                  (1, 4)).astype(BF16NP),
        "ident_in": np.eye(128, dtype=BF16NP),
        "identf_in": np.eye(128, dtype=np.float32),
    }
    feeds = []
    for c in range(NCORES):
        be, r = c // NR4, c % NR4
        fm = np.zeros((128, 8), np.float32)
        fm[:, :r] = 1.0
        xp = (np.zeros(C, np.float32) if r == 0
              else np.asarray(x[be, r * TOK - 1], np.float32))
        f = dict(com)
        f["i_foldmask"] = fm
        f["i_xprev"] = np.ascontiguousarray(xp.reshape(8, 128).T)
        feeds.append(f)
    return feeds


def build_l0f(nc):
    def emit(ctx, x_fn, x_key):
        names = {
            "bqkvo": ([128, 4, PC], F32), "bv_rep": ([128, C], F32),
            "ln1gb": ([128, 2, PC], F32), "ln2gb": ([128, 2, PC], F32),
            "Wq": ([C, C], F32), "Wk": ([C, C], F32), "Wv": ([C, C], F32),
            "Wo": ([C, C], F32), "Wr": ([C, 8], F32), "br_rep": ([128, 8], F32),
            "b1": ([128, 8, 32], F32), "b2": ([128, 8, 8], F32),
            "dmask": ([128, 4, 512], BF16),
            "gq_idx": ([128, (HL * 64 * NR4) // 128], I32),
            "gk_idx": ([128, (HL * 64 * NR4) // 128], I32),
            "gv_idx": ([128, max(1, (HL * NR4 * (TOK // 128)) // 128)], I32),
            "go_idx": ([128, N_PASS * PC], I32),
        }
        inp = {k: ctx.input("i_" + k, s, dt) for k, (s, dt) in names.items()}
        my_w1 = ctx.input("i_myw1", [C, 4096])
        my_w2 = ctx.input("i_myw2", [4096, C])
        w1all, w2all = emit_moe_prep(ctx, my_w1, my_w2, GROUPS8)
        emit_l0_ffn(ctx, x_fn, x_key, inp, GROUPS4, w1all, w2all,
                    moe_rk=(("moe_w1all",), ("moe_w2all",)))
    _stage_shell(nc, emit)


def _l0f_idxs(rank):
    n_gq = (HL * 64 * NR4) // 128
    gq = np.zeros((128, n_gq), np.int32)
    gk = np.zeros((128, n_gq), np.int32)
    for gg in range(n_gq):
        for pp in range(128):
            flat = gg * 128 + pp
            h = flat // (64 * NR4)
            rem = flat % (64 * NR4)
            d = rem // NR4
            src = rem % NR4
            h_g = rank * HL + h
            gq[pp, gg] = src * 3072 + h_g * 64 + d
            gk[pp, gg] = src * 3072 + 1024 + h_g * 64 + d
    n_vrows = HL * NR4 * (TOK // 128)
    gv = np.zeros((128, max(1, n_vrows // 128)), np.int32)
    P1, P2, P3 = 3072 * TOK // 8192, 2048 * TOK // 8192, 64 * TOK // 8192
    for flat in range(n_vrows):
        h = flat // (NR4 * (TOK // 128))
        rem = flat % (NR4 * (TOK // 128))
        src = rem // (TOK // 128)
        bb = rem % (TOK // 128)
        h_g = rank * HL + h
        gv[flat % 128, flat // 128] = src * P1 + P2 + h_g * P3 + bb
    go = np.zeros((128, N_PASS * PC), np.int32)
    for pi in range(N_PASS):
        for gg in range(PC):
            for pp in range(128):
                go[pp, pi * PC + gg] = ((gg * 128 + pp) * NR4 + rank) * N_PASS + pi
    return gq, gk, gv, go


def feeds_l0f(ins):
    g = lambda n: np.asarray(ins["L0_ffn_" + n], np.float32)
    dmask = np.full((128, 4, 512), -1e30, np.float32)
    for kk in range(128):
        for o_ in range(4):
            dmask[kk, o_, o_ * 128 + kk:] = 0.0
    com = {
        "i_bqkvo": np.stack([g(n).reshape(PC, 128).T
                             for n in ("bq", "bk", "bv", "bo")], axis=1),
        "i_bv_rep": np.broadcast_to(g("bv"), (128, C)).copy(),
        "i_ln1gb": np.stack([g("ln1_g").reshape(PC, 128).T,
                             g("ln1_b").reshape(PC, 128).T], 1),
        "i_ln2gb": np.stack([g("ln2_g").reshape(PC, 128).T,
                             g("ln2_b").reshape(PC, 128).T], 1),
        "i_Wq": g("Wq"), "i_Wk": g("Wk"), "i_Wv": g("Wv"), "i_Wo": g("Wo"),
        "i_Wr": g("Wr"),
        "i_br_rep": np.broadcast_to(g("br"), (128, 8)).copy(),
        "i_b1": g("e_b1").reshape(8, 32, 128).transpose(2, 0, 1).copy(),
        "i_b2": g("e_b2").reshape(8, 8, 128).transpose(2, 0, 1).copy(),
        "i_dmask": dmask.astype(BF16NP),
        "ident_in": np.eye(128, dtype=BF16NP),
        "identf_in": np.eye(128, dtype=np.float32),
    }
    e_w1 = np.asarray(ins["L0_ffn_e_w1"], np.float32)
    e_w2 = np.asarray(ins["L0_ffn_e_w2"], np.float32)
    feeds = []
    for c in range(NCORES):
        r = c % NR4
        gq, gk, gv, go = _l0f_idxs(r)
        f = dict(com)
        f["i_gq_idx"] = gq
        f["i_gk_idx"] = gk
        f["i_gv_idx"] = gv
        f["i_go_idx"] = go
        f["i_myw1"] = np.ascontiguousarray(e_w1[c])
        f["i_myw2"] = np.ascontiguousarray(e_w2[c])
        feeds.append(f)
    return feeds


def build_l1a(nc):
    def emit(ctx, x_fn, x_key):
        inp = {
            "wqkvT": ctx.input("i_wqkvT", [C, 3072]),
            "w_outT": ctx.input("i_woutT", [C, C]),
            "wq_mapT": ctx.input("i_wqmT", [64, 64]),
            "wk_mapT": ctx.input("i_wkmT", [64, 64]),
            "bq": ctx.input("i_bqm", [64, 1]),
            "bk_rep": ctx.input("i_bkm", [128, 64]),
            "mask_incl": ctx.input("i_maski", [128, 128], BF16),
            "foldmask": ctx.input("i_foldmask", [128, 8]),
        }
        emit_l1_attn(ctx, x_fn, x_key, inp, GROUPS4)
    _stage_shell(nc, emit)


def feeds_l1a(ins):
    g = lambda n: np.asarray(ins["L1_attn_" + n], np.float32)
    com = {
        "i_wqkvT": np.ascontiguousarray(g("w_qkv").T),
        "i_woutT": np.ascontiguousarray(g("w_out").T),
        "i_wqmT": np.ascontiguousarray(g("wq_map").T),
        "i_wkmT": np.ascontiguousarray(g("wk_map").T),
        "i_bqm": g("bq_map").reshape(64, 1),
        "i_bkm": np.broadcast_to(g("bk_map"), (128, 64)).copy(),
        "i_maski": np.triu(np.ones((128, 128))).astype(BF16NP),
        "ident_in": np.eye(128, dtype=BF16NP),
        "identf_in": np.eye(128, dtype=np.float32),
    }
    feeds = []
    for c in range(NCORES):
        r = c % NR4
        fm = np.zeros((128, 8), np.float32)
        fm[:, :r] = 1.0
        f = dict(com)
        f["i_foldmask"] = fm
        feeds.append(f)
    return feeds


def build_l1f(nc):
    def emit(ctx, x_fn, x_key):
        inp = {
            "Wt": ctx.input("i_Wt", [C, C]),
            "Wpt": ctx.input("i_Wpt", [C, 256]),
            "bp": ctx.input("i_bp", [128, 2]),
            "sim": ctx.input("i_sim", [256, 4]),
            "temp": ctx.input("i_temp", [1, 1]),
            "Wdt": ctx.input("i_Wdt", [C, 15]),
            "Wu": ctx.input("i_Wu", [15, C]),
        }
        emit_l1_ffn(ctx, x_fn, x_key, inp)
    _stage_shell(nc, emit)


def feeds_l1f(ins):
    g = lambda n: np.asarray(ins["L1_ffn_" + n], np.float32)
    com = {
        "i_Wt": np.ascontiguousarray(g("W").T),
        "i_Wpt": np.ascontiguousarray(g("Wp").T),
        "i_bp": np.ascontiguousarray(g("bp").reshape(2, 128).T),
        "i_sim": g("sim"),
        "i_temp": g("temp").reshape(1, 1),
        "i_Wdt": np.ascontiguousarray(
            np.concatenate([g(f"Wd{i}") for i in range(4)], 0).T),
        "i_Wu": np.concatenate([np.ascontiguousarray(g(f"Wu{i}").T)
                                for i in range(4)], 0),
        "ident_in": np.eye(128, dtype=BF16NP),
        "identf_in": np.eye(128, dtype=np.float32),
    }
    return [dict(com) for _ in range(NCORES)]


def _launch(build_fn, feeds, xs):
    nc = bass.Bass("TRN2", target_bir_lowering=False, num_devices=NCORES)
    build_fn(nc)
    in_maps = []
    for c in range(NCORES):
        f = dict(feeds[c])
        f["xin"] = xs[c]
        in_maps.append(f)
    res = run_bass_kernel_spmd(nc, in_maps, core_ids=list(range(NCORES)))
    return [np.ascontiguousarray(np.asarray(res.results[c]["xout"],
                                            np.float32))
            for c in range(NCORES)]


GSYNC = ["l1ftemp", "l1fbp", "l1a_bk", "l1a_bq", "l1a_fold", "l0a_dvrep",
         "l0a_dvpad", "l0a_trif", "l0a_xprev", "l0a_fold", "l0f_b", "l0f_bv",
         "l0f_ln1", "l0f_ln2", "l0f_br", "l0f_b1", "l0f_b2", "l0f_idx", "l1fsim"]
GGPS = ["l1fWu", "l1a_mask", "l0a_maaw2", "l0a_dw2", "l0a_dw2p",
        "l0a_ublk", "l0a_lngb", "l0a_mask4", "l0f_dmask", "l1fWd",
        "l0a_maaw1", "l0a_dw1", "l0f_wr", "l1a_wqm", "l1a_wkm"]


def build_fused(nc):
    def emit(ctx, x_fn, x_key):
        ctx.p.key_alias.update({k: "wlds" for k in GSYNC})
        ctx.p.key_alias.update({k: "wldg" for k in GGPS})
        names = {
            "maax": [128, 8], "maaw": [128, 8], "maak": [128, 8],
            "maav": [128, 8], "maar": [128, 8],
            "maaw1": [C, 128], "maaw2": [128, C],
            "decayw1": [C, 64], "decayw2": [64, C], "decayw2p": [64, 2048],
            "decvec_rep": [128, C], "decvec_pad": [128, 16],
            "Wrp": [C, 2048], "Wkp": [C, 2048], "Wk": [C, C], "Wv": [C, C],
            "Wo": [C, C], "ublk": [128, 64], "lng_rep": [128, C],
            "lnb_rep": [128, C], "tri_incl": [128, 128], "xprev": [128, 8],
            "foldmask": [128, 8],
        }
        inp_a = {k: ctx.input("i_" + k, s) for k, s in names.items()}
        inp_a["mask_strict4"] = ctx.input("i_mask_strict4", [128, 512], BF16)
        names_f = {
            "bqkvo": ([128, 4, PC], F32), "bv_rep": ([128, C], F32),
            "ln1gb": ([128, 2, PC], F32), "ln2gb": ([128, 2, PC], F32),
            "Wq": ([C, C], F32), "Wk": ([C, C], F32), "Wv": ([C, C], F32),
            "Wo": ([C, C], F32), "Wr": ([C, 8], F32), "br_rep": ([128, 8], F32),
            "b1": ([128, 8, 32], F32), "b2": ([128, 8, 8], F32),
            "dmask": ([128, 4, 512], BF16),
            "gq_idx": ([128, (HL * 64 * NR4) // 128], I32),
            "gk_idx": ([128, (HL * 64 * NR4) // 128], I32),
            "gv_idx": ([128, max(1, (HL * NR4 * (TOK // 128)) // 128)], I32),
            "go_idx": ([128, N_PASS * PC], I32),
        }
        inp_f = {k: ctx.input("i_f" + k, s, dt) for k, (s, dt) in names_f.items()}
        my_w1 = ctx.input("i_myw1", [C, 4096])
        my_w2 = ctx.input("i_myw2", [4096, C])
        w1all, w2all = emit_moe_prep(ctx, my_w1, my_w2, GROUPS8)
        inp_1a = {
            "wqkvT": ctx.input("i_wqkvT", [C, 3072]),
            "w_outT": ctx.input("i_woutT", [C, C]),
            "wq_mapT": ctx.input("i_wqmT", [64, 64]),
            "wk_mapT": ctx.input("i_wkmT", [64, 64]),
            "bq": ctx.input("i_bqm", [64, 1]),
            "bk_rep": ctx.input("i_bkm", [128, 64]),
            "mask_incl": ctx.input("i_maski", [128, 128], BF16),
            "foldmask": inp_a["foldmask"],
        }
        inp_1f = {
            "Wt": ctx.input("i_Wt", [C, C]),
            "Wpt": ctx.input("i_Wpt", [C, 256]),
            "bp": ctx.input("i_bp", [128, 2]),
            "sim": ctx.input("i_sim", [256, 4]),
            "temp": ctx.input("i_temp", [1, 1]),
            "Wdt": ctx.input("i_Wdt", [C, 15]),
            "Wu": ctx.input("i_Wu", [15, C]),
        }
        emit_l0_attn(ctx, x_fn, x_key, inp_a, GROUPS4)
        emit_l0_ffn(ctx, x_fn, x_key, inp_f, GROUPS4, w1all, w2all,
                    moe_rk=(("moe_w1all",), ("moe_w2all",)))
        emit_l1_attn(ctx, x_fn, x_key, inp_1a, GROUPS4)
        emit_l1_ffn(ctx, x_fn, x_key, inp_1f)
    _stage_shell(nc, emit)


def feeds_fused(ins, x):
    fa = feeds_l0a(ins, x)
    ff = feeds_l0f(ins)
    f1a = feeds_l1a(ins)
    f1f = feeds_l1f(ins)
    feeds = []
    for c in range(NCORES):
        f = dict(fa[c])
        for k, v in ff[c].items():
            if k in ("ident_in", "identf_in"):
                continue
            f["i_f" + k[2:]] = v if k.startswith("i_") else v
        f["i_myw1"] = ff[c]["i_myw1"]
        f["i_myw2"] = ff[c]["i_myw2"]
        for k, v in f1a[c].items():
            if k in ("ident_in", "identf_in", "i_foldmask"):
                continue
            f[k] = v
        for k, v in f1f[c].items():
            if k in ("ident_in", "identf_in"):
                continue
            f[k] = v
        feeds.append(f)
    return feeds


def kernel(**inputs):
    ins = {k: np.asarray(v) for k, v in inputs.items()}
    x = np.asarray(ins["L0_attn_x"], np.float32)
    xs = [np.ascontiguousarray(x[c // NR4,
                                 (c % NR4) * TOK:(c % NR4 + 1) * TOK, :].T)
          for c in range(NCORES)]
    xs = _launch(build_fused, feeds_fused(ins, x), xs)
    out = np.empty((2, 8192, C), np.float32)
    for c in range(NCORES):
        out[c // NR4, (c % NR4) * TOK:(c % NR4 + 1) * TOK, :] = xs[c].T
    return out
